# version 02
# baseline (speedup 1.0000x reference)
import contextlib
import numpy as np
import ml_dtypes
import concourse.bass as bass
import concourse.mybir as mybir
from concourse.bass_utils import run_bass_kernel_spmd

F32 = mybir.dt.float32
BF16 = mybir.dt.bfloat16
F16 = mybir.dt.float16
I32 = mybir.dt.int32
AF = mybir.ActivationFunctionType
ALU = mybir.AluOpType

NCORES = 8
B = 2
L = 8192
D = 1024
TLOC = 2048
H = 16
HD = 64
DI = 2048
NST = 16
DTR = 64
EPS = 1e-5
ND = D // 128

BF = np.dtype(ml_dtypes.bfloat16)


class Sem:
    def __init__(self, handle):
        self.h = handle
        self.n = 0

    def inc(self, instr, k=1):
        instr.then_inc(self.h, k)
        self.n += k
        return self.n


def mk_sems(nc, stack, names):
    return {nm: Sem(stack.enter_context(nc.semaphore(nm))) for nm in names}


def ts(i, n):
    return slice(i * n, (i + 1) * n)


ENGS = ("sync", "gpsimd", "tensor", "vector", "scalar")
STRICT = True


class Op:
    __slots__ = ("eng", "emit", "reads", "writes", "sreads", "idx",
                 "marked", "cum", "deps", "tag", "dma", "cc", "lane")

    def __init__(self, eng, emit, reads, writes, sreads, tag):
        self.eng = eng
        self.emit = emit
        self.reads = reads
        self.writes = writes
        self.sreads = sreads
        self.tag = tag
        self.marked = False
        self.cum = None
        self.deps = []


class Sched:
    def __init__(self, is_dma_eng_weight=16):
        self.ops = []
        self.last_writer = {}
        self.readers = {}

    def add(self, eng, emit, reads=(), writes=(), sreads=(), dma=False, cc=False, tag=""):
        op = Op(eng, emit, tuple(reads), tuple(writes), tuple(sreads), tag)
        op.dma = dma
        op.cc = cc
        deps = []
        anyasync = dma or cc
        for r in tuple(reads) + tuple(sreads):
            w = self.last_writer.get(r)
            if w is not None and (w.eng != eng or r in op.sreads or w.dma or w.cc or anyasync or STRICT):
                deps.append(w)
        for r in writes:
            w = self.last_writer.get(r)
            if w is not None and (w.eng != eng or w.dma or w.cc or anyasync or STRICT):
                deps.append(w)
            for rd in self.readers.get(r, ()):
                if rd.eng != eng or rd.dma or rd.cc or anyasync or STRICT:
                    deps.append(rd)
        seen = set()
        for d in deps:
            if id(d) not in seen:
                seen.add(id(d))
                op.deps.append(d)
        for r in writes:
            self.last_writer[r] = op
            self.readers[r] = []
        for r in tuple(reads) + tuple(sreads):
            self.readers.setdefault(r, []).append(op)
        self.ops.append(op)
        return op

    def emit_all(self, nc, ctx):
        for op in self.ops:
            if op.dma or op.cc:
                op.marked = True
            for d in op.deps:
                d.marked = True
        NDMA = 6
        lanes = ([(e, 0) for e in ENGS] + [(e, 1 + j) for e in ENGS for j in range(NDMA)]
                 + [("gpsimd", 99)])
        cum = {ln: 0 for ln in lanes}
        dma_i = {e: 0 for e in ENGS}
        dma_last = {}
        cc_prev = None
        for op in self.ops:
            if op.cc:
                op.lane = ("gpsimd", 99)
                if cc_prev is not None and cc_prev not in op.deps:
                    op.deps.append(cc_prev)
                cc_prev = op
            elif op.dma:
                op.lane = (op.eng, 1 + (dma_i[op.eng] % NDMA))
                dma_i[op.eng] += 1
                prev = dma_last.get(op.lane)
                if prev is not None and prev not in op.deps:
                    op.deps.append(prev)
                dma_last[op.lane] = op
            else:
                op.lane = (op.eng, 0)
            if op.marked:
                cum[op.lane] += 16 if op.dma else 1
                op.cum = cum[op.lane]
        sems = {ln: ctx.enter_context(nc.semaphore(f"S_{ln[0]}_{ln[1]}"))
                for ln in lanes if cum[ln] > 0}
        by_eng = {e: [o for o in self.ops if o.eng == e] for e in ENGS}
        final_cum = dict(cum)

        def make_body(eng):
            def body(h):
                hw = {ln: 0 for ln in lanes}
                for op in by_eng[eng]:
                    need = {}
                    for d in op.deps:
                        assert d.cum is not None, f"unmarked dep {d.tag} of {op.tag}"
                        need[d.lane] = max(need.get(d.lane, 0), d.cum)
                    for ln, v in need.items():
                        if v > hw[ln]:
                            h.wait_ge(sems[ln], v)
                            hw[ln] = v
                    ins = op.emit(h)
                    if op.marked:
                        ins.then_inc(sems[op.lane], 16 if op.dma else 1)
            return body
        return sems, by_eng, make_body, final_cum

    def barrier(self):
        deps = []
        lastc = {}
        dmatail = {e: [] for e in ENGS}
        for op in self.ops:
            if op.cc or op.dma:
                dmatail[op.eng].append(op)
            else:
                lastc[op.eng] = op
        for e in ENGS:
            if e in lastc:
                deps.append(lastc[e])
            deps.extend(dmatail[e][-8:])
        bars = []
        for e in ENGS:
            o = self.add(e, lambda h: h.nop(nofuse=True), tag=f"bar_{e}")
            for d in deps:
                if d is not o and d not in o.deps:
                    o.deps.append(d)
            bars.append(o)
        return bars

    def run_block(self, nc, ctx):
        sems, by_eng, make_body, final_cum = self.emit_all(nc, ctx)
        with nc.Block() as block:
            for eng in ENGS:
                if by_eng[eng]:
                    getattr(block, eng)(make_body(eng))
        return sems, final_cum



def phase_ln_feature_major(S, nc, xt, xout, work, T, gcol=None, bcol=None, key=""):
    NB = T // 512
    onesc = work["onesc"]
    sqt = work["sqt"]; mu = work["mu"]; vart = work["vart"]
    rstd = work["rstd"]; nmr = work["nmr"]; tmp = work["tmp"]
    ps_a = work["ps_a"]; ps_b = work["ps_b"]
    K = key
    for i in range(ND):
        S.add("scalar", lambda h, i=i: h.activation(sqt[:], xt[i][:], AF.Square),
              reads=[K + f"x{i}"], writes=[K + "sqt"], tag=K + f"sq{i}")
        for b_ in range(NB):
            S.add("tensor", lambda h, i=i, b_=b_: h.matmul(
                ps_a[0:1, ts(b_, 512)], onesc[:, 0:1], xt[i][:, ts(b_, 512)],
                start=(i == 0), stop=(i == ND - 1)),
                reads=[K + f"x{i}", "onesc"], writes=[K + "psa"], tag=K + f"mmS{i}.{b_}")
        for b_ in range(NB):
            S.add("tensor", lambda h, i=i, b_=b_: h.matmul(
                ps_b[0:1, ts(b_, 512)], onesc[:, 0:1], sqt[:, ts(b_, 512)],
                start=(i == 0), stop=(i == ND - 1)),
                reads=[K + "sqt", "onesc"], writes=[K + "psb"], tag=K + f"mmQ{i}.{b_}")
    S.add("vector", lambda h: h.tensor_scalar(mu[:], ps_a[0:1, :], 1.0 / D, None, op0=ALU.mult),
          reads=[K + "psa"], writes=[K + "mu"], tag=K + "mu")
    S.add("vector", lambda h: h.tensor_tensor(vart[:], mu[:], mu[:], op=ALU.mult),
          reads=[K + "mu"], writes=[K + "vart"], tag=K + "mu2")
    S.add("vector", lambda h: h.tensor_scalar(vart[:], vart[:], EPS, None, op0=ALU.subtract),
          reads=[K + "vart"], writes=[K + "vart"], tag=K + "m2e")
    S.add("vector", lambda h: h.scalar_tensor_tensor(vart[:], ps_b[0:1, :], 1.0 / D, vart[:],
                                                     op0=ALU.mult, op1=ALU.subtract),
          reads=[K + "psb", K + "vart"], writes=[K + "vart"], tag=K + "var")
    S.add("scalar", lambda h: h.activation(vart[:], vart[:], AF.Ln),
          reads=[K + "vart"], writes=[K + "vart"], tag=K + "ln")
    S.add("scalar", lambda h: h.activation(rstd[:], vart[:], AF.Exp, scale=-0.5),
          reads=[K + "vart"], writes=[K + "rstd"], tag=K + "rstd")
    S.add("vector", lambda h: h.tensor_tensor(nmr[:], mu[:], rstd[:], op=ALU.mult),
          reads=[K + "mu", K + "rstd"], writes=[K + "nmr"], tag=K + "nmr1")
    S.add("vector", lambda h: h.tensor_scalar(nmr[:], nmr[:], -1.0, None, op0=ALU.mult),
          reads=[K + "nmr"], writes=[K + "nmr"], tag=K + "nmr2")
    for b_ in range(NB):
        S.add("tensor", lambda h, b_=b_: h.matmul(
            ps_a[:, 0:512], onesc[0:1, :], rstd[:, ts(b_, 512)], start=True, stop=True),
            reads=[K + "rstd", "onesc"], writes=[K + "psa"], tag=K + f"bcR{b_}")
        S.add("tensor", lambda h, b_=b_: h.matmul(
            ps_b[:, 0:512], onesc[0:1, :], nmr[:, ts(b_, 512)], start=True, stop=True),
            reads=[K + "nmr", "onesc"], writes=[K + "psb"], tag=K + f"bcM{b_}")
        for i in range(ND):
            S.add("vector", lambda h, i=i, b_=b_: h.tensor_tensor(
                tmp[:], xt[i][:, ts(b_, 512)], ps_a[:, 0:512], op=ALU.mult),
                reads=[K + f"x{i}", K + "psa"], writes=[K + "tmp"], tag=K + f"xr{i}.{b_}")
            if gcol is None:
                S.add("vector", lambda h, i=i, b_=b_: h.tensor_tensor(
                    xout[:, i * T + b_ * 512: i * T + b_ * 512 + 512],
                    tmp[:], ps_b[:, 0:512], op=ALU.add),
                    reads=[K + "tmp", K + "psb"], writes=[K + "xout"], tag=K + f"xo{i}.{b_}")
            else:
                S.add("vector", lambda h, i=i, b_=b_: h.tensor_tensor(
                    tmp[:], tmp[:], ps_b[:, 0:512], op=ALU.add),
                    reads=[K + "tmp", K + "psb"], writes=[K + "tmp"], tag=K + f"xo{i}.{b_}")
                S.add("vector", lambda h, i=i, b_=b_: h.tensor_scalar(
                    xout[:, i * T + b_ * 512: i * T + b_ * 512 + 512],
                    tmp[:], gcol[:, i:i + 1], bcol[:, i:i + 1],
                    op0=ALU.mult, op1=ALU.add),
                    reads=[K + "tmp"], writes=[K + "xout"],
                    sreads=[K + "gb"], tag=K + f"xa{i}.{b_}")


def mk_ln_work(nc, ctx, T, pref="", psum=None):
    sb = lambda nm, shp, dt2: ctx.enter_context(nc.sbuf_tensor(pref + nm, shp, dt2))
    w = dict(
        onesc=sb("onesc", [128, 128], F32),
        sqt=sb("ln_sqt", [128, T], F32),
        sqtb=sb("ln_sqtb", [128, T], BF16),
        mu=sb("ln_mu", [1, T], F32),
        vart=sb("ln_vart", [1, T], F32),
        rstd=sb("ln_rstd", [1, T], F32),
        nmr=sb("ln_nmr", [1, T], F32),
        tmp=sb("ln_tmp", [128, 512], F32),
    )
    if psum is None:
        w["ps_a"] = ctx.enter_context(nc.psum_tensor(pref + "ln_psa", [128, T], F32))
        w["ps_b"] = ctx.enter_context(nc.psum_tensor(pref + "ln_psb", [128, T], F32))
    else:
        w["ps_a"], w["ps_b"] = psum
    return w


def build_ln_launch():
    nc = bass.Bass("TRN2", target_bir_lowering=False)
    x0T = nc.dram_tensor("x0T", [D, TLOC], F32, kind="ExternalInput").ap()
    ones = nc.dram_tensor("ones128", [128, 128], F32, kind="ExternalInput").ap()
    xnT = nc.dram_tensor("xnT", [D, TLOC], BF16, kind="ExternalOutput").ap()

    with contextlib.ExitStack() as ctx:
        sb = lambda *a: ctx.enter_context(nc.sbuf_tensor(*a))
        xt = [sb(f"xt{i}", [128, TLOC], F32) for i in range(ND)]
        xout = sb("xout", [128, ND * TLOC], BF16)
        work = mk_ln_work(nc, ctx, TLOC)
        S = Sched()
        S.add("sync", lambda h: h.dma_start(out=work["onesc"][:], in_=ones),
              writes=["onesc"], dma=True, tag="ld_ones")
        for i in range(ND):
            S.add("sync", lambda h, i=i: h.dma_start(out=xt[i][:], in_=x0T[ts(i, 128), :]),
                  writes=[f"x{i}"], dma=True, tag=f"ld{i}")
        phase_ln_feature_major(S, nc, xt, xout, work, TLOC)
        S.add("sync", lambda h: h.dma_start(
            out=xnT.rearrange("(dt p) t -> p dt t", p=128),
            in_=xout[:].rearrange("p (dt t) -> p dt t", t=TLOC)),
            reads=["xout"], dma=True, tag="store")
        S.add("sync", lambda h: h.nop(), reads=[], writes=[], tag="fin")
        S.ops[-1].deps.append(S.ops[-2])
        S.run_block(nc, ctx)
    return nc


RR_C1 = 6.25
RR_C2 = 0.033203125
RR_C3 = -1.781782e-05
INV2PI = 0.15915494309189535
TWOPI = 6.283185307179586
PI_ = 3.1415927410125732
PIH = 1.5707963267948966


def build_attn_launch(span, npairs=8):
    nc = bass.Bass("TRN2", target_bir_lowering=False)
    xnT = nc.dram_tensor("xnT", [D, L], BF16, kind="ExternalInput").ap()
    w_qkv = nc.dram_tensor("w_qkv", [D, 3 * D], F32, kind="ExternalInput").ap()
    b_qkvT = nc.dram_tensor("b_qkvT", [128, 24], F32, kind="ExternalInput").ap()
    w_out = nc.dram_tensor("w_out", [D, D], F32, kind="ExternalInput").ap()
    b_outT = nc.dram_tensor("b_outT", [128, 8], F32, kind="ExternalInput").ap()
    fcolsd = nc.dram_tensor("freqcols2", [128, 8], F32, kind="ExternalInput").ap()
    posrep = nc.dram_tensor("posrep", [128, L], F32, kind="ExternalInput").ap()
    onesbf_d = nc.dram_tensor("onesbf", [128, 128], BF16, kind="ExternalInput").ap()
    idbf_d = nc.dram_tensor("idbf", [128, 128], BF16, kind="ExternalInput").ap()
    attn0T = nc.dram_tensor("attn0T", [D, TLOC], BF16, kind="ExternalOutput").ap()
    dbg_ka = nc.dram_tensor("dbg_ka", [65, 256], BF16, kind="ExternalOutput").ap()
    dbg_qa = nc.dram_tensor("dbg_qa", [65, 256], BF16, kind="ExternalOutput").ap()
    dbg_V = nc.dram_tensor("dbg_V", [128, 260], BF16, kind="ExternalOutput").ap()
    dbg_ao = nc.dram_tensor("dbg_ao", [128, 512], BF16, kind="ExternalOutput").ap()

    NP = npairs
    CH = 1024
    NCH = L // CH
    NMT = 64
    NRB = 4

    with contextlib.ExitStack() as ctx:
        sb = lambda *a: ctx.enter_context(nc.sbuf_tensor(*a))
        xchunk = sb("xchunk", [128, ND * CH], BF16)
        wq = sb("wq", [128, ND * 128], BF16)
        wk = sb("wk", [128, ND * 128], BF16)
        wv = sb("wv", [128, ND * 128], BF16)
        biasq = sb("biasq", [128, 24], F32)
        bias_o = sb("bias_o", [128, 8], F32)
        fq = sb("fq", [128, 8], F32)
        onesb = sb("onesb", [128, 128], BF16)
        idb = sb("idb", [128, 128], BF16)
        pihc = sb("pihc", [128, 1], F32)
        kpre = sb("kpre", [128, CH], BF16)
        kswp = sb("kswp", [128, CH], BF16)
        qpre = sb("qpre", [128, CH], BF16)
        qswp = sb("qswp", [128, CH], BF16)
        vpre = sb("vpre", [128, CH], BF16)
        ka = sb("ka", [65, L], BF16)
        kb = sb("kb", [65, L], BF16)
        qa = sb("qa", [65, TLOC], BF16)
        qb = sb("qb", [65, TLOC], BF16)
        V = sb("V", [128, NMT * 130], BF16)
        th = sb("th", [128, CH], F32)
        kfb = sb("kfb", [128, CH], F32)
        kib = sb("kib", [128, CH], I32)
        rr = sb("rr", [128, CH], F32)
        cosr = sb("cosr", [128, CH], BF16)
        sinr = sb("sinr", [128, CH], BF16)
        tmpa = sb("tmpa", [128, CH], BF16)
        tmpb = sb("tmpb", [128, CH], BF16)
        ksqs = sb("ksqs", [64, CH], BF16)
        sp512 = sb("sp512", [1, 512], F32)
        kmaxp = sb("kmaxp", [1, 64], F32)
        kms = sb("kms", [1, 2], F32)
        q2s = sb("q2s", [1, TLOC], BF16)
        qnmA = sb("qnmA", [1, TLOC], BF16)
        qnmB = sb("qnmB", [1, TLOC], BF16)
        P0 = sb("P0", [128, 1024], BF16)
        P1 = sb("P1", [128, 1024], BF16)
        sr = sb("sr", [1, 512], F32)
        rsf = sb("rsf", [1, 512], F32)
        rs = sb("rs", [1, 512], BF16)
        rsb = sb("rsb", [64, 512], BF16)
        aoT = sb("aoT", [128, NP * TLOC], BF16)
        wo = sb("wo", [128, 64 * 128], BF16)
        ostg = [sb(f"ostg{j}", [128, 512], BF16) for j in range(2)]
        PA = ctx.enter_context(nc.psum_tensor("PA", [128, 2048], F32))
        PB = ctx.enter_context(nc.psum_tensor("PB", [128, 2048], F32))

        S = Sched()
        A = S.add

        A("gpsimd", lambda h: h.dma_start(out=biasq[:], in_=b_qkvT), writes=["biasq"], dma=True, tag="ldbq")
        A("gpsimd", lambda h: h.dma_start(out=bias_o[:], in_=b_outT), writes=["bias_o"], dma=True, tag="ldbo")
        A("gpsimd", lambda h: h.dma_start(out=fq[:], in_=fcolsd), writes=["fq"], dma=True, tag="ldfq")
        A("gpsimd", lambda h: h.dma_start(out=onesb[:], in_=onesbf_d), writes=["onesb"], dma=True, tag="ldob")
        A("gpsimd", lambda h: h.dma_start(out=idb[:], in_=idbf_d), writes=["idb"], dma=True, tag="ldid")
        A("gpsimd", lambda h: h.memset(pihc[:], PIH), writes=["pihc"], tag="mspih")
        A("gpsimd", lambda h: h.memset(ka[64:65, :], 1.0), writes=["karow"], tag="mska")
        A("gpsimd", lambda h: h.memset(kb[64:65, :], 1.0), writes=["kbrow"], tag="mskb")
        vap = V[:].rearrange("p (m c) -> p m c", c=130)
        A("gpsimd", lambda h: h.memset(vap[:, :, 64:65], 1.0), writes=["Vones"], tag="msv1")
        A("gpsimd", lambda h: h.memset(vap[:, :, 129:130], 1.0), writes=["Vones"], tag="msv2")

        PSL = [PA[:, 0:1024], PA[:, 1024:2048]]
        PSO = [PB[:, 0:512], PB[:, 512:1024]]
        PRB = PB[:, 1024:1536]
        PST = PB[:, 1536:2048]
        Pbuf = [P0, P1]

        for p_ in range(NP):
            c0 = p_ * 128
            pk = f"p{p_}"
            for nm, wt, coff in (("wq", wq, c0), ("wk", wk, D + c0), ("wv", wv, 2 * D + c0)):
                A("gpsimd", lambda h, wt=wt, coff=coff: h.dma_start(
                    out=wt[:], in_=w_qkv[:, coff:coff + 128].rearrange("(dt p) c -> p dt c", p=128)),
                  writes=[nm], dma=True, tag=pk + "ld" + nm)

            for c_ in range(NCH):
                ck = pk + f"c{c_}"
                ccols = ts(c_, CH)
                is_q = (c_ // 2 == span)
                qoff = (c_ % 2) * CH
                A("sync", lambda h, c_=c_: h.dma_start(
                    out=xchunk[:], in_=xnT[:, ts(c_, CH)].rearrange("(dt p) t -> p dt t", p=128)),
                  writes=["xchunk"], dma=True, tag=ck + "ldx")
                streams = [("k", wk, kpre, 8), ("v", wv, vpre, 16)]
                if is_q:
                    streams.append(("q", wq, qpre, 0))
                qki = 0
                for tb in range(CH // 512):
                    for nm, wt, dst, boff in streams:
                        j4 = qki % 4
                        qki += 1
                        A("tensor", lambda h, wt=wt, tb=tb, j4=j4: [
                            h.matmul(PA[:, ts(j4, 512)], wt[:, ts(dt, 128)],
                                     xchunk[:, dt * CH + tb * 512: dt * CH + tb * 512 + 512],
                                     start=(dt == 0), stop=(dt == ND - 1))
                            for dt in range(ND)][-1],
                          reads=["xchunk", "w" + nm], writes=[f"qps{j4}"], tag=ck + f"mm{nm}{tb}")
                        A("scalar", lambda h, dst=dst, tb=tb, boff=boff, j4=j4, p_=p_: h.activation(
                            dst[:, ts(tb, 512)], PA[:, ts(j4, 512)], AF.Identity,
                            bias=biasq[:, boff + p_:boff + p_ + 1]),
                          reads=[f"qps{j4}"], writes=[nm + "pre"], sreads=["biasq"],
                          tag=ck + f"ev{nm}{tb}")
                for mt_ in range(CH // 128):
                    gmt = c_ * (CH // 128) + mt_
                    A("tensor", lambda h, mt_=mt_: h.transpose(
                        PST[:, 0:64].bitcast(BF16), vpre[:, ts(mt_, 128)], idb[:]),
                      reads=["vpre", "idb"], writes=["PST"], tag=ck + f"tr{mt_}")
                    A("vector", lambda h, gmt=gmt: h.tensor_copy(
                        V[:, gmt * 130: gmt * 130 + 130].rearrange("p (a c) -> p a c", a=2)[:, :, 0:64],
                        PST[:, 0:64].bitcast(BF16).rearrange("p (a c) -> p a c", a=2)),
                      reads=["PST"], writes=["Vdat"], tag=ck + f"te{mt_}")
                swaps = [("k", kpre, kswp)] + ([("q", qpre, qswp)] if is_q else [])
                for nm, src_, dst_ in swaps:
                    for g2 in range(2):
                        s0, s1 = 64 * g2, 64 * g2 + 32
                        A("gpsimd", lambda h, src_=src_, dst_=dst_, s0=s0, s1=s1: h.dma_start(
                            out=dst_[s0:s0 + 32, :], in_=src_[s1:s1 + 32, :]),
                          reads=[nm + "pre"], writes=[nm + f"swp{g2}a"], dma=True, tag=ck + f"sw{nm}{g2}a")
                        A("gpsimd", lambda h, src_=src_, dst_=dst_, s0=s0, s1=s1: h.dma_start(
                            out=dst_[s1:s1 + 32, :], in_=src_[s0:s0 + 32, :]),
                          reads=[nm + "pre"], writes=[nm + f"swp{g2}b"], dma=True, tag=ck + f"sw{nm}{g2}b")
                A("gpsimd", lambda h, c_=c_: h.dma_start(out=th[:], in_=posrep[:, ts(c_, CH)]),
                  writes=["th"], dma=True, tag=ck + "io")
                A("vector", lambda h, p_=p_: h.tensor_scalar(th[:], th[:], fq[:, p_:p_ + 1], None, op0=ALU.mult),
                  reads=["th"], writes=["th"], sreads=["fq"], tag=ck + "th")
                A("vector", lambda h: h.tensor_scalar(kfb[:], th[:], INV2PI, None, op0=ALU.mult),
                  reads=["th"], writes=["kfb"], tag=ck + "kf")
                A("vector", lambda h: h.tensor_copy(kib[:], kfb[:]), reads=["kfb"], writes=["kib"], tag=ck + "ki")
                A("vector", lambda h: h.tensor_copy(kfb[:], kib[:]), reads=["kib"], writes=["kfb"], tag=ck + "kf2")
                A("vector", lambda h: h.scalar_tensor_tensor(rr[:], kfb[:], -RR_C1, th[:],
                                                             op0=ALU.mult, op1=ALU.add),
                  reads=["kfb", "th"], writes=["rr"], tag=ck + "r1")
                A("vector", lambda h: h.scalar_tensor_tensor(rr[:], kfb[:], -RR_C2, rr[:],
                                                             op0=ALU.mult, op1=ALU.add),
                  reads=["kfb", "rr"], writes=["rr"], tag=ck + "r2")
                A("vector", lambda h: h.scalar_tensor_tensor(rr[:], kfb[:], -RR_C3, rr[:],
                                                             op0=ALU.mult, op1=ALU.add),
                  reads=["kfb", "rr"], writes=["rr"], tag=ck + "r3")
                for fi in range(2):
                    A("vector", lambda h: h.tensor_scalar(kfb[:], rr[:], PI_, None, op0=ALU.is_gt),
                      reads=["rr"], writes=["kfb"], tag=ck + f"fg{fi}")
                    A("vector", lambda h: h.scalar_tensor_tensor(rr[:], kfb[:], -TWOPI, rr[:],
                                                                 op0=ALU.mult, op1=ALU.add),
                      reads=["kfb", "rr"], writes=["rr"], tag=ck + f"fs{fi}")
                    A("vector", lambda h: h.tensor_scalar(kfb[:], rr[:], -PI_, None, op0=ALU.is_lt),
                      reads=["rr"], writes=["kfb"], tag=ck + f"fh{fi}")
                    A("vector", lambda h: h.scalar_tensor_tensor(rr[:], kfb[:], TWOPI, rr[:],
                                                                 op0=ALU.mult, op1=ALU.add),
                      reads=["kfb", "rr"], writes=["rr"], tag=ck + f"ft{fi}")
                A("vector", lambda h: h.tensor_scalar(rr[:], rr[:], -PI_, PI_, op0=ALU.max, op1=ALU.min),
                  reads=["rr"], writes=["rr"], tag=ck + "cl")
                A("scalar", lambda h: h.activation(sinr[:], rr[:], AF.Sin),
                  reads=["rr"], writes=["sinr"], tag=ck + "sin")
                A("scalar", lambda h: h.activation(th[:], rr[:], AF.Abs),
                  reads=["rr"], writes=["th"], tag=ck + "abs")
                A("scalar", lambda h: h.activation(cosr[:], th[:], AF.Sin, scale=-1.0,
                                                   bias=pihc[:, 0:1]),
                  reads=["th"], writes=["cosr"], sreads=["pihc"], tag=ck + "cos")
                combos = [("k", kpre, kswp, ka, kb, ccols)] + (
                    [("q", qpre, qswp, qa, qb, slice(qoff, qoff + CH))] if is_q else [])
                for nm, pre, swp, ta, tb_, ocols in combos:
                    A("vector", lambda h, pre=pre: h.tensor_tensor(tmpa[:], pre[:], cosr[:], op=ALU.mult),
                      reads=[nm + "pre", "cosr"], writes=["tmpa"], tag=ck + f"cb1{nm}")
                    A("vector", lambda h, swp=swp: h.tensor_tensor(tmpb[:], swp[:], sinr[:], op=ALU.mult),
                      reads=[nm + "swp0a", nm + "swp0b", nm + "swp1a", nm + "swp1b", "sinr"],
                      writes=["tmpb"], tag=ck + f"cb2{nm}")
                    A("vector", lambda h, ta=ta, ocols=ocols: h.tensor_tensor(
                        ta[0:64, ocols], tmpa[0:64, :], tmpb[0:64, :], op=ALU.add),
                      reads=["tmpa", "tmpb"], writes=[nm + "rotA"], tag=ck + f"cb3{nm}")
                    A("vector", lambda h, tb_=tb_, ocols=ocols: h.tensor_tensor(
                        tb_[0:64, ocols], tmpa[64:128, :], tmpb[64:128, :], op=ALU.add),
                      reads=["tmpa", "tmpb"], writes=[nm + "rotB"], tag=ck + f"cb4{nm}")
                for hh in range(2):
                    kx = (ka, kb)[hh]
                    A("scalar", lambda h, kx=kx, ccols=ccols: h.activation(
                        ksqs[:], kx[0:64, ccols], AF.Square),
                      reads=["krot" + "AB"[hh]], writes=["ksqs"], tag=ck + f"ksq{hh}")
                    for piece in range(CH // 512):
                        pidx = c_ * (CH // 512) + piece
                        A("tensor", lambda h, piece=piece: h.matmul(
                            PRB[0:1, 0:512], onesb[0:64, 0:1], ksqs[0:64, ts(piece, 512)],
                            start=True, stop=True),
                          reads=["ksqs", "onesb"], writes=["prb"], tag=ck + f"k2m{hh}.{piece}")
                        A("scalar", lambda h: h.activation(sp512[:], PRB[0:1, 0:512], AF.Copy),
                          reads=["prb"], writes=["sp512"], tag=ck + f"k2e{hh}.{piece}")
                        A("vector", lambda h, hh=hh, pidx=pidx: h.reduce_max(
                            kmaxp[0:1, hh * 16 + pidx:hh * 16 + pidx + 1], sp512[:],
                            axis=mybir.AxisListType.X),
                          reads=["sp512"], writes=["kmaxp"], tag=ck + f"k2r{hh}.{pidx}")
                if is_q:
                    for hh in range(2):
                        qx = (qa, qb)[hh]
                        A("scalar", lambda h, qx=qx, qoff=qoff: h.activation(
                            ksqs[:], qx[0:64, qoff:qoff + CH], AF.Square),
                          reads=["qrot" + "AB"[hh]], writes=["ksqs"], tag=ck + f"qsq{hh}")
                        for piece in range(CH // 512):
                            A("tensor", lambda h, piece=piece: h.matmul(
                                PRB[0:1, 0:512], onesb[0:64, 0:1], ksqs[0:64, ts(piece, 512)],
                                start=True, stop=True),
                              reads=["ksqs", "onesb"], writes=["prb"], tag=ck + f"q2m{hh}.{piece}")
                            A("scalar", lambda h, piece=piece, qoff=qoff, hh=hh: h.activation(
                                q2s[0:1, qoff + piece * 512: qoff + piece * 512 + 512]
                                if hh == 0 else
                                q2s[0:1, qoff + piece * 512: qoff + piece * 512 + 512],
                                PRB[0:1, 0:512], AF.Copy),
                              reads=["prb"], writes=["q2s" + "AB"[hh]], tag=ck + f"q2e{hh}.{piece}")
                        A("scalar", lambda h, qoff=qoff, hh=hh: h.activation(
                            (qnmA if hh == 0 else qnmB)[0:1, qoff:qoff + CH],
                            q2s[0:1, qoff:qoff + CH], AF.Sqrt),
                          reads=["q2s" + "AB"[hh]], writes=["qnm" + "AB"[hh]], tag=ck + f"qnm{hh}")

            for hh in range(2):
                A("vector", lambda h, hh=hh: h.reduce_max(
                    kms[0:1, hh:hh + 1], kmaxp[0:1, hh * 16:hh * 16 + 16],
                    axis=mybir.AxisListType.X),
                  reads=["kmaxp"], writes=["kms2"], tag=pk + f"kmx{hh}")
            A("scalar", lambda h: h.activation(kms[:], kms[:], AF.Sqrt),
              reads=["kms2"], writes=["kms"], tag=pk + "kmsq")
            for hh in range(2):
                qx = (qa, qb)[hh]
                qn = (qnmA, qnmB)[hh]
                A("vector", lambda h, qx=qx, qn=qn, hh=hh: h.tensor_scalar(
                    qx[64:65, :], qn[:], kms[0:1, hh:hh + 1], -1.0,
                    op0=ALU.mult, op1=ALU.mult),
                  reads=["qnm" + "AB"[hh]], writes=["q" + "ab"[hh] + "crow"],
                  sreads=["kms"], tag=pk + f"crow{hh}")

            for hh in range(2):
                kx = (ka, kb)[hh]
                qx = (qa, qb)[hh]
                krd = ["krot" + "AB"[hh], ("karow", "kbrow")[hh]]
                qrd = ["qrot" + "AB"[hh], "q" + "ab"[hh] + "crow"]
                for rb in range(NRB):
                    ob = rb % 2
                    rcols = ts(rb, 512)

                    def emit_qk(h, mp, kx=kx, qx=qx, rcols=rcols):
                        for j in range(2):
                            mt = 2 * mp + j
                            ins = h.matmul(PSL[mp % 2][:, ts(j, 512)], kx[:, ts(mt, 128)],
                                           qx[:, rcols], start=True, stop=True)
                        return ins

                    def emit_exp(h, mp):
                        return h.activation(Pbuf[mp % 2][:], PSL[mp % 2][:], AF.Exp, scale=0.125)

                    def emit_av(h, mp, hh=hh, ob=ob):
                        for j in range(2):
                            mt = 2 * mp + j
                            ins = h.matmul(PSO[ob][0:65, :],
                                           V[:, mt * 130 + 65 * hh: mt * 130 + 65 * hh + 65],
                                           Pbuf[mp % 2][:, ts(j, 512)],
                                           start=(mt == 0), stop=(mt == NMT - 1))
                        return ins

                    hk = pk + f"h{hh}rb{rb}"
                    PK = [["qps0", "qps1"], ["qps2", "qps3"]]
                    A("tensor", lambda h, f=emit_qk: f(h, 0), reads=krd + qrd,
                      writes=PK[0], tag=hk + "qk0")
                    A("tensor", lambda h, f=emit_qk: f(h, 1), reads=krd + qrd,
                      writes=PK[1], tag=hk + "qk1")
                    for mp in range(32):
                        A("scalar", lambda h, mp=mp, f=emit_exp: f(h, mp),
                          reads=PK[mp % 2], writes=[f"P{mp % 2}"], tag=hk + f"ex{mp}")
                        A("tensor", lambda h, mp=mp, f=emit_av: f(h, mp),
                          reads=[f"P{mp % 2}", "Vdat", "Vones"], writes=[f"pso{ob}"], tag=hk + f"av{mp}")
                        if mp + 2 < 32:
                            A("tensor", lambda h, mp=mp, f=emit_qk: f(h, mp + 2),
                              reads=krd + qrd, writes=PK[mp % 2], tag=hk + f"qk{mp + 2}")
                    A("scalar", lambda h, ob=ob: h.activation(sr[:], PSO[ob][64:65, :], AF.Copy),
                      reads=[f"pso{ob}"], writes=["sr"], tag=hk + "sr")
                    A("vector", lambda h: h.reciprocal(rsf[:], sr[:]),
                      reads=["sr"], writes=["rsf"], tag=hk + "rsf")
                    A("vector", lambda h: h.tensor_copy(rs[:], rsf[:]),
                      reads=["rsf"], writes=["rs"], tag=hk + "rs")
                    A("tensor", lambda h: h.matmul(PRB[0:64, 0:512], onesb[0:1, 0:64], rs[:],
                                                   start=True, stop=True),
                      reads=["rs", "onesb"], writes=["prb"], tag=hk + "rb")
                    A("scalar", lambda h: h.activation(rsb[:], PRB[0:64, 0:512], AF.Copy),
                      reads=["prb"], writes=["rsb"], tag=hk + "rc")
                    A("vector", lambda h, ob=ob, hh=hh, rb=rb, p_=p_: h.tensor_tensor(
                        aoT[64 * hh:64 * hh + 64, p_ * TLOC + rb * 512: p_ * TLOC + rb * 512 + 512],
                        PSO[ob][0:64, :], rsb[:], op=ALU.mult),
                      reads=[f"pso{ob}", "rsb"], writes=["aoT"], tag=hk + "oe")

        A("gpsimd", lambda h: h.dma_start(
            out=wo[:], in_=w_out.rearrange("(pp a) (dt c) -> a pp dt c", a=128, c=128)),
          writes=["wo"], dma=True, tag="ldwo")
        for dt in range(ND):
            for g in range(4):
                j2 = (dt * 4 + g) % 2
                A("tensor", lambda h, dt=dt, g=g: [
                    h.matmul(PST[:, 0:512], wo[:, (pp * 8 + dt) * 128:(pp * 8 + dt) * 128 + 128],
                             aoT[:, pp * TLOC + g * 512: pp * TLOC + g * 512 + 512],
                             start=(pp == 0), stop=(pp == NP - 1))
                    for pp in range(NP)][-1],
                  reads=["aoT", "wo"], writes=["PST"], tag=f"op{dt}.{g}")
                A("scalar", lambda h, dt=dt, j2=j2: h.activation(
                    ostg[j2][:], PST[:, 0:512], AF.Identity, bias=bias_o[:, dt:dt + 1]),
                  reads=["PST"], writes=[f"ostg{j2}"], sreads=["bias_o"], tag=f"oe{dt}.{g}")
                A("sync", lambda h, dt=dt, g=g, j2=j2: h.dma_start(
                    out=attn0T[ts(dt, 128), ts(g, 512)], in_=ostg[j2][:]),
                  reads=[f"ostg{j2}"], writes=["stored"], dma=True, tag=f"st{dt}.{g}")
        A("sync", lambda h: h.dma_start(out=dbg_ka, in_=ka[:, 0:256]),
          reads=["krotA", "karow"], writes=["stored"], dma=True, tag="stdbg1")
        A("sync", lambda h: h.dma_start(out=dbg_qa, in_=qa[:, 0:256]),
          reads=["qrotA", "qacrow"], writes=["stored"], dma=True, tag="stdbg2")
        A("sync", lambda h: h.dma_start(out=dbg_V, in_=V[:, 0:260]),
          reads=["Vdat", "Vones"], writes=["stored"], dma=True, tag="stdbg3")
        A("sync", lambda h: h.dma_start(out=dbg_ao, in_=aoT[:, 0:512]),
          reads=["aoT"], writes=["stored"], dma=True, tag="stdbg4")
        fin = A("sync", lambda h: h.nop(nofuse=True), reads=["stored"], tag="fin")
        for op in S.ops:
            if op.dma and op.tag.startswith("st"):
                if op not in fin.deps:
                    fin.deps.append(op)
        S.run_block(nc, ctx)
    return nc


def build_ffn0_launch():
    nc = bass.Bass("TRN2", target_bir_lowering=False)
    xnTd = nc.dram_tensor("xnT_loc", [D, TLOC], BF16, kind="ExternalInput").ap()
    x0Td = nc.dram_tensor("x0T_loc", [D, TLOC], F32, kind="ExternalInput").ap()
    at0d = nc.dram_tensor("attn0T", [D, TLOC], BF16, kind="ExternalInput").ap()
    gwd = nc.dram_tensor("gate_wT", [32, D, 128], F32, kind="ExternalInput").ap()
    uwd = nc.dram_tensor("up_wT", [32, D, 128], F32, kind="ExternalInput").ap()
    dwd = nc.dram_tensor("down_wT", [32, 128, D], F32, kind="ExternalInput").ap()
    sgd = nc.dram_tensor("sgate_wT", [D, 512], F32, kind="ExternalInput").ap()
    sud = nc.dram_tensor("sup_wT", [D, 512], F32, kind="ExternalInput").ap()
    sdd = nc.dram_tensor("sdown_wT", [512, D], F32, kind="ExternalInput").ap()
    rwd = nc.dram_tensor("router_wT", [D, 32], F32, kind="ExternalInput").ap()
    rbd = nc.dram_tensor("router_b", [1, 32], F32, kind="ExternalInput").ap()
    ohd = nc.dram_tensor("onehot32", [32, 32 * 128], BF16, kind="ExternalInput").ap()
    onesbf_d = nc.dram_tensor("onesbf", [128, 128], BF16, kind="ExternalInput").ap()
    idbf_d = nc.dram_tensor("idbf", [128, 128], BF16, kind="ExternalInput").ap()
    x1Td = nc.dram_tensor("x1T", [D, TLOC], F32, kind="ExternalOutput").ap()

    NT = TLOC // 128
    NB4 = TLOC // 512
    EGR = 8

    with contextlib.ExitStack() as ctx:
        sb = lambda *a: ctx.enter_context(nc.sbuf_tensor(*a))
        xnT = sb("xnT", [128, ND * TLOC], BF16)
        ffacc = sb("ffacc", [128, ND * TLOC], F32)
        hwt = [sb(f"hwt{e}", [128, TLOC], BF16) for e in range(EGR)]
        ge = sb("ge", [128, ND * 128], BF16)
        ue = sb("ue", [128, ND * 128], BF16)
        de = [sb(f"de{i}", [128, D], BF16) for i in range(8)]
        sgw = sb("sgw", [128, ND * 512], BF16)
        suw = sb("suw", [128, ND * 512], BF16)
        sdw = sb("sdw", [128, 4 * D], BF16)
        rww = sb("rww", [128, ND * 32], BF16)
        rbb = sb("rbb", [1, 32], BF16)
        oh32 = sb("oh32", [32, 32 * 128], BF16)
        onesb = sb("onesb", [128, 128], BF16)
        idb = sb("idb", [128, 128], BF16)
        lg = sb("lg", [128, 32], F32)
        m8 = sb("m8", [128, 8], F32)
        nv0 = sb("nv0", [128, 1], F32)
        el = sb("el", [128, 32], F32)
        msk = sb("msk", [128, 32], F32)
        ev4 = sb("ev4", [128, 4], F32)
        ds = sb("ds", [128, 1], F32)
        rden = sb("rden", [128, 1], F32)
        compb = sb("compb", [128, NT * 32], BF16)
        combT = sb("combT", [32, TLOC], BF16)
        sgt = sb("sgt", [128, 512], BF16)
        x0t = sb("x0t", [128, TLOC], F32)
        a0t = sb("a0t", [128, TLOC], BF16)
        x1t = sb("x1t", [128, TLOC], F32)
        PA = ctx.enter_context(nc.psum_tensor("PA", [128, 2048], F32))
        PB = ctx.enter_context(nc.psum_tensor("PB", [128, 2048], F32))

        S = Sched()
        A = S.add
        A("sync", lambda h: h.dma_start(out=xnT[:], in_=xnTd.rearrange("(dt p) t -> p dt t", p=128)),
          writes=["xnT"], dma=True, tag="ldxn")
        A("gpsimd", lambda h: h.dma_start(out=rww[:], in_=rwd.rearrange("(dt p) e -> p dt e", p=128)),
          writes=["rww"], dma=True, tag="ldrw")
        A("gpsimd", lambda h: h.dma_start(out=rbb[:], in_=rbd), writes=["rbb"], dma=True, tag="ldrb")
        A("gpsimd", lambda h: h.dma_start(out=oh32[:], in_=ohd), writes=["oh32"], dma=True, tag="ldoh")
        A("gpsimd", lambda h: h.dma_start(out=onesb[:], in_=onesbf_d), writes=["onesb"], dma=True, tag="ldon")
        A("gpsimd", lambda h: h.dma_start(out=idb[:], in_=idbf_d), writes=["idb"], dma=True, tag="ldid")
        A("gpsimd", lambda h: h.dma_start(out=sgw[:], in_=sgd.rearrange("(dt p) s -> p dt s", p=128)),
          writes=["sgw"], dma=True, tag="ldsg")
        A("gpsimd", lambda h: h.dma_start(out=suw[:], in_=sud.rearrange("(dt p) s -> p dt s", p=128)),
          writes=["suw"], dma=True, tag="ldsu")
        A("gpsimd", lambda h: h.dma_start(out=sdw[:], in_=sdd.rearrange("(ht p) d -> p ht d", p=128)),
          writes=["sdw"], dma=True, tag="ldsd")

        for tt in range(NT):
            tk = f"r{tt}"
            A("tensor", lambda h, tt=tt: [
                h.matmul(PA[0:128, 1536:1568], xnT[:, dt * TLOC + tt * 128: dt * TLOC + tt * 128 + 128],
                         rww[:, ts(dt, 32)], start=(dt == 0), stop=False)
                for dt in range(ND)][-1],
              reads=["xnT", "rww"], writes=["prt"], tag=tk + "mm")
            A("tensor", lambda h: h.matmul(PA[0:128, 1536:1568], onesb[0:1, :], rbb[:],
                                           start=False, stop=True),
              reads=["rbb", "onesb"], writes=["prt"], tag=tk + "bias")
            A("scalar", lambda h: h.activation(lg[:], PA[0:128, 1536:1568], AF.Copy),
              reads=["prt"], writes=["lg"], tag=tk + "ev")
            A("vector", lambda h: h.max(m8[:], lg[:]), reads=["lg"], writes=["m8"], tag=tk + "m8")
            A("vector", lambda h: h.tensor_scalar(nv0[:], m8[:, 0:1], -1.0, None, op0=ALU.mult),
              reads=["m8"], writes=["nv0"], tag=tk + "nv")
            A("scalar", lambda h: h.activation(el[:], lg[:], AF.Exp, bias=nv0[:, 0:1]),
              reads=["lg"], sreads=["nv0"], writes=["el"], tag=tk + "el")
            A("vector", lambda h: h.tensor_scalar(msk[:], lg[:], m8[:, 3:4], None, op0=ALU.is_ge),
              reads=["lg"], sreads=["m8"], writes=["msk"], tag=tk + "mk")
            A("scalar", lambda h: h.activation(ev4[:], m8[:, 0:4], AF.Exp, bias=nv0[:, 0:1]),
              reads=["m8"], sreads=["nv0"], writes=["ev4"], tag=tk + "e4")
            A("vector", lambda h: h.reduce_sum(ds[:], ev4[:], axis=mybir.AxisListType.X),
              reads=["ev4"], writes=["ds"], tag=tk + "ds")
            A("vector", lambda h: h.reciprocal(rden[:], ds[:]), reads=["ds"], writes=["rden"], tag=tk + "rd")
            A("vector", lambda h: h.tensor_tensor(el[:], el[:], msk[:], op=ALU.mult),
              reads=["el", "msk"], writes=["el"], tag=tk + "cm1")
            A("vector", lambda h, tt=tt: h.tensor_scalar(
                compb[:, ts(tt, 32)], el[:], rden[:, 0:1], None, op0=ALU.mult),
              reads=["el"], sreads=["rden"], writes=["compb"], tag=tk + "cm2")
            A("tensor", lambda h, tt=tt: h.transpose(
                PB[0:32, 512:576].bitcast(BF16), compb[:, ts(tt, 32)], idb[:]),
              reads=["compb", "idb"], writes=["ptr"], tag=tk + "tr")
            A("vector", lambda h, tt=tt: h.tensor_copy(
                combT[0:32, ts(tt, 128)], PB[0:32, 512:576].bitcast(BF16)),
              reads=["ptr"], writes=["combT"], tag=tk + "te")

        for grp in range(32 // EGR):
            for ei in range(EGR):
                e = grp * EGR + ei
                ek = f"e{e}"
                A("gpsimd", lambda h, e=e: h.dma_start(
                    out=ge[:], in_=gwd[e].rearrange("(dt p) c -> p dt c", p=128)),
                  writes=["ge"], dma=True, tag=ek + "ldg")
                A("gpsimd", lambda h, e=e: h.dma_start(
                    out=ue[:], in_=uwd[e].rearrange("(dt p) c -> p dt c", p=128)),
                  writes=["ue"], dma=True, tag=ek + "ldu")
                for tb in range(NB4):
                    A("tensor", lambda h, tb=tb: [
                        h.matmul(PA[:, 0:512], ge[:, ts(dt, 128)],
                                 xnT[:, dt * TLOC + tb * 512: dt * TLOC + tb * 512 + 512],
                                 start=(dt == 0), stop=(dt == ND - 1))
                        for dt in range(ND)][-1],
                      reads=["xnT", "ge"], writes=["pg"], tag=ek + f"g{tb}")
                    A("tensor", lambda h, tb=tb: [
                        h.matmul(PA[:, 512:1024], ue[:, ts(dt, 128)],
                                 xnT[:, dt * TLOC + tb * 512: dt * TLOC + tb * 512 + 512],
                                 start=(dt == 0), stop=(dt == ND - 1))
                        for dt in range(ND)][-1],
                      reads=["xnT", "ue"], writes=["pu"], tag=ek + f"u{tb}")
                    A("tensor", lambda h, e=e, tb=tb: h.matmul(
                        PA[:, 1024:1536], oh32[:, ts(e, 128)], combT[0:32, ts(tb, 512)],
                        start=True, stop=True),
                      reads=["combT", "oh32"], writes=["pc"], tag=ek + f"c{tb}")
                    A("scalar", lambda h: h.activation(sgt[:], PA[:, 0:512], AF.Silu),
                      reads=["pg"], writes=["sgt"], tag=ek + f"s{tb}")
                    A("vector", lambda h, ei=ei, tb=tb: h.tensor_tensor(
                        hwt[ei][:, ts(tb, 512)], PA[:, 512:1024], sgt[:], op=ALU.mult),
                      reads=["pu", "sgt"], writes=[f"hw{ei}"], tag=ek + f"m{tb}")
                    A("vector", lambda h, ei=ei, tb=tb: h.tensor_tensor(
                        hwt[ei][:, ts(tb, 512)], hwt[ei][:, ts(tb, 512)], PA[:, 1024:1536],
                        op=ALU.mult),
                      reads=[f"hw{ei}", "pc"], writes=[f"hw{ei}"], tag=ek + f"w{tb}")
                A("gpsimd", lambda h, e=e, ei=ei: h.dma_start(out=de[ei][:], in_=dwd[e]),
                  writes=[f"de{ei}"], dma=True, tag=ek + "ldd")
            for dt in range(ND):
                for tb in range(NB4):
                    A("tensor", lambda h, grp=grp, dt=dt, tb=tb: [
                        h.matmul(PB[:, 0:512], de[ei][:, ts(dt, 128)],
                                 hwt[ei][:, ts(tb, 512)], start=(ei == 0), stop=(ei == EGR - 1))
                        for ei in range(EGR)][-1],
                      reads=[f"hw{x}" for x in range(EGR)] + [f"de{x}" for x in range(EGR)],
                      writes=["pd"], tag=f"dn{grp}.{dt}.{tb}")
                    if grp == 0:
                        A("scalar", lambda h, dt=dt, tb=tb: h.activation(
                            ffacc[:, dt * TLOC + tb * 512: dt * TLOC + tb * 512 + 512],
                            PB[:, 0:512], AF.Copy),
                          reads=["pd"], writes=["ffacc"], tag=f"fc{grp}.{dt}.{tb}")
                    else:
                        A("vector", lambda h, dt=dt, tb=tb: h.tensor_tensor(
                            ffacc[:, dt * TLOC + tb * 512: dt * TLOC + tb * 512 + 512],
                            ffacc[:, dt * TLOC + tb * 512: dt * TLOC + tb * 512 + 512],
                            PB[:, 0:512], op=ALU.add),
                          reads=["pd", "ffacc"], writes=["ffacc"], tag=f"fc{grp}.{dt}.{tb}")

        for st in range(4):
            for tb in range(NB4):
                A("tensor", lambda h, st=st, tb=tb: [
                    h.matmul(PA[:, 0:512], sgw[:, dt * 512 + st * 128: dt * 512 + st * 128 + 128],
                             xnT[:, dt * TLOC + tb * 512: dt * TLOC + tb * 512 + 512],
                             start=(dt == 0), stop=(dt == ND - 1))
                    for dt in range(ND)][-1],
                  reads=["xnT", "sgw"], writes=["pg"], tag=f"sg{st}.{tb}")
                A("tensor", lambda h, st=st, tb=tb: [
                    h.matmul(PA[:, 512:1024], suw[:, dt * 512 + st * 128: dt * 512 + st * 128 + 128],
                             xnT[:, dt * TLOC + tb * 512: dt * TLOC + tb * 512 + 512],
                             start=(dt == 0), stop=(dt == ND - 1))
                    for dt in range(ND)][-1],
                  reads=["xnT", "suw"], writes=["pu"], tag=f"su{st}.{tb}")
                A("scalar", lambda h: h.activation(sgt[:], PA[:, 0:512], AF.Silu),
                  reads=["pg"], writes=["sgt"], tag=f"ss{st}.{tb}")
                A("vector", lambda h, st=st, tb=tb: h.tensor_tensor(
                    hwt[st][:, ts(tb, 512)], PA[:, 512:1024], sgt[:], op=ALU.mult),
                  reads=["pu", "sgt"], writes=[f"hw{st}"], tag=f"sm{st}.{tb}")
        for dt in range(ND):
            for tb in range(NB4):
                A("tensor", lambda h, dt=dt, tb=tb: [
                    h.matmul(PB[:, 0:512], sdw[:, st * D + dt * 128: st * D + dt * 128 + 128],
                             hwt[st][:, ts(tb, 512)], start=(st == 0), stop=(st == 3))
                    for st in range(4)][-1],
                  reads=["hw0", "hw1", "hw2", "hw3", "sdw"], writes=["pd"], tag=f"sd{dt}.{tb}")
                A("vector", lambda h, dt=dt, tb=tb: h.tensor_tensor(
                    ffacc[:, dt * TLOC + tb * 512: dt * TLOC + tb * 512 + 512],
                    ffacc[:, dt * TLOC + tb * 512: dt * TLOC + tb * 512 + 512],
                    PB[:, 0:512], op=ALU.add),
                  reads=["pd", "ffacc"], writes=["ffacc"], tag=f"sf{dt}.{tb}")

        for dt in range(ND):
            A("sync", lambda h, dt=dt: h.dma_start(out=x0t[:], in_=x0Td[ts(dt, 128), :]),
              writes=["x0t"], dma=True, tag=f"lx0{dt}")
            A("sync", lambda h, dt=dt: h.dma_start(out=a0t[:], in_=at0d[ts(dt, 128), :]),
              writes=["a0t"], dma=True, tag=f"la0{dt}")
            A("vector", lambda h, dt=dt: h.tensor_tensor(
                x1t[:], x0t[:], ffacc[:, ts(dt, TLOC)], op=ALU.add),
              reads=["x0t", "ffacc"], writes=["x1t"], tag=f"x1a{dt}")
            A("vector", lambda h, dt=dt: h.tensor_tensor(x1t[:], x1t[:], a0t[:], op=ALU.add),
              reads=["x1t", "a0t"], writes=["x1t"], tag=f"x1b{dt}")
            A("sync", lambda h, dt=dt: h.dma_start(out=x1Td[ts(dt, 128), :], in_=x1t[:]),
              reads=["x1t"], writes=["stored"], dma=True, tag=f"st{dt}")
        fin = A("sync", lambda h: h.nop(nofuse=True), reads=["stored"], tag="fin")
        for op in S.ops:
            if op.dma and op.tag.startswith("st") and op not in fin.deps:
                fin.deps.append(op)
        S.run_block(nc, ctx)
    return nc


def phase_ln_fm_bf16(S, nc, xt, xout, work, T, onesbb, gcol=None, bcol=None, key=""):
    NB = T // 512
    sqt = work["sqtb"]; mu = work["mu"]; vart = work["vart"]
    rstd = work["rstd"]; nmr = work["nmr"]; tmp = work["tmp"]
    ps_a = work["ps_a"]; ps_b = work["ps_b"]
    onesc = work["onesc"]
    K = key
    for i in range(ND):
        S.add("scalar", lambda h, i=i: h.activation(sqt[:], xt[i], AF.Square),
              reads=[K + f"x{i}"], writes=[K + "sqt"], tag=K + f"sq{i}")
        for b_ in range(NB):
            S.add("tensor", lambda h, i=i, b_=b_: h.matmul(
                ps_a[0:1, ts(b_, 512)], onesbb[:, 0:1], xt[i][:, ts(b_, 512)],
                start=(i == 0), stop=(i == ND - 1)),
                reads=[K + f"x{i}", "onesbb"], writes=[K + "psa"], tag=K + f"mmS{i}.{b_}")
        for b_ in range(NB):
            S.add("tensor", lambda h, i=i, b_=b_: h.matmul(
                ps_b[0:1, ts(b_, 512)], onesbb[:, 0:1], sqt[:, ts(b_, 512)],
                start=(i == 0), stop=(i == ND - 1)),
                reads=[K + "sqt", "onesbb"], writes=[K + "psb"], tag=K + f"mmQ{i}.{b_}")
    S.add("vector", lambda h: h.tensor_scalar(mu[:], ps_a[0:1, :], 1.0 / D, None, op0=ALU.mult),
          reads=[K + "psa"], writes=[K + "mu"], tag=K + "mu")
    S.add("vector", lambda h: h.tensor_tensor(vart[:], mu[:], mu[:], op=ALU.mult),
          reads=[K + "mu"], writes=[K + "vart"], tag=K + "mu2")
    S.add("vector", lambda h: h.tensor_scalar(vart[:], vart[:], EPS, None, op0=ALU.subtract),
          reads=[K + "vart"], writes=[K + "vart"], tag=K + "m2e")
    S.add("vector", lambda h: h.scalar_tensor_tensor(vart[:], ps_b[0:1, :], 1.0 / D, vart[:],
                                                     op0=ALU.mult, op1=ALU.subtract),
          reads=[K + "psb", K + "vart"], writes=[K + "vart"], tag=K + "var")
    S.add("scalar", lambda h: h.activation(vart[:], vart[:], AF.Ln),
          reads=[K + "vart"], writes=[K + "vart"], tag=K + "ln")
    S.add("scalar", lambda h: h.activation(rstd[:], vart[:], AF.Exp, scale=-0.5),
          reads=[K + "vart"], writes=[K + "rstd"], tag=K + "rstd")
    S.add("vector", lambda h: h.tensor_tensor(nmr[:], mu[:], rstd[:], op=ALU.mult),
          reads=[K + "mu", K + "rstd"], writes=[K + "nmr"], tag=K + "nmr1")
    S.add("vector", lambda h: h.tensor_scalar(nmr[:], nmr[:], -1.0, None, op0=ALU.mult),
          reads=[K + "nmr"], writes=[K + "nmr"], tag=K + "nmr2")
    for b_ in range(NB):
        S.add("tensor", lambda h, b_=b_: h.matmul(
            ps_a[:, 0:512], work["onesf"][0:1, :], rstd[:, ts(b_, 512)], start=True, stop=True),
            reads=[K + "rstd", "onesf"], writes=[K + "psa"], tag=K + f"bcR{b_}")
        S.add("tensor", lambda h, b_=b_: h.matmul(
            ps_b[:, 0:512], work["onesf"][0:1, :], nmr[:, ts(b_, 512)], start=True, stop=True),
            reads=[K + "nmr", "onesf"], writes=[K + "psb"], tag=K + f"bcM{b_}")
        for i in range(ND):
            S.add("vector", lambda h, i=i, b_=b_: h.tensor_tensor(
                tmp[:], xt[i][:, ts(b_, 512)], ps_a[:, 0:512], op=ALU.mult),
                reads=[K + f"x{i}", K + "psa"], writes=[K + "tmp"], tag=K + f"xr{i}.{b_}")
            if gcol is None:
                S.add("vector", lambda h, i=i, b_=b_: h.tensor_tensor(
                    xout[:, i * T + b_ * 512: i * T + b_ * 512 + 512],
                    tmp[:], ps_b[:, 0:512], op=ALU.add),
                    reads=[K + "tmp", K + "psb"], writes=[K + "xout"], tag=K + f"xo{i}.{b_}")
            else:
                S.add("vector", lambda h, i=i, b_=b_: h.tensor_tensor(
                    tmp[:], tmp[:], ps_b[:, 0:512], op=ALU.add),
                    reads=[K + "tmp", K + "psb"], writes=[K + "tmp"], tag=K + f"xp{i}.{b_}")
                S.add("vector", lambda h, i=i, b_=b_: h.tensor_scalar(
                    xout[:, i * T + b_ * 512: i * T + b_ * 512 + 512],
                    tmp[:], gcol[:, i:i + 1], bcol[:, i:i + 1], op0=ALU.mult, op1=ALU.add),
                    reads=[K + "tmp"], writes=[K + "xout"], sreads=[K + "gb"], tag=K + f"xa{i}.{b_}")


def build_front_a_launch():
    nc = bass.Bass("TRN2", target_bir_lowering=False)
    x1Td = nc.dram_tensor("x1T", [D, TLOC], F32, kind="ExternalInput").ap()
    wind = nc.dram_tensor("W_in", [D, 2 * DI], F32, kind="ExternalInput").ap()
    lngd = nc.dram_tensor("ln_gT", [128, ND], F32, kind="ExternalInput").ap()
    lnbd = nc.dram_tensor("ln_bT", [128, ND], F32, kind="ExternalInput").ap()
    onesd = nc.dram_tensor("ones128", [128, 128], F32, kind="ExternalInput").ap()
    onesbd = nc.dram_tensor("onesbf", [128, 128], BF16, kind="ExternalInput").ap()
    xn2d = nc.dram_tensor("xn2T", [D, TLOC], BF16, kind="ExternalOutput").ap()
    xhd = nc.dram_tensor("xhT", [DI, TLOC], BF16, kind="ExternalOutput").ap()
    szd = nc.dram_tensor("szT", [DI, TLOC], BF16, kind="ExternalOutput").ap()

    NB4 = TLOC // 512
    with contextlib.ExitStack() as ctx:
        sb = lambda *a: ctx.enter_context(nc.sbuf_tensor(*a))
        x1t = [sb(f"x1t{i}", [128, TLOC], F32) for i in range(ND)]
        xn2b = sb("xn2b", [128, ND * TLOC], BF16)
        xnnT = sb("xnnT", [128, ND * TLOC], BF16)
        gcol = sb("gcol", [128, ND], F32)
        bcol = sb("bcol", [128, ND], F32)
        wbuf = [sb(f"wbuf{j}", [128, ND * 128], BF16) for j in range(2)]
        ostg = [sb(f"ostg{j}", [128, TLOC], BF16) for j in range(2)]
        work = mk_ln_work(nc, ctx, TLOC)
        work["onesf"] = work["onesc"]
        onesbb = sb("onesbb", [128, 128], BF16)

        S = Sched()
        A = S.add
        A("sync", lambda h: h.dma_start(out=work["onesc"][:], in_=onesd),
          writes=["onesc", "onesf"], dma=True, tag="ldone")
        A("sync", lambda h: h.dma_start(out=onesbb[:], in_=onesbd), writes=["onesbb"], dma=True, tag="ldonb")
        A("gpsimd", lambda h: h.dma_start(out=gcol[:], in_=lngd), writes=["Lgb"], dma=True, tag="ldg")
        A("gpsimd", lambda h: h.dma_start(out=bcol[:], in_=lnbd), writes=["Lgb"], dma=True, tag="ldb")
        for i in range(ND):
            A("sync", lambda h, i=i: h.dma_start(out=x1t[i][:], in_=x1Td[ts(i, 128), :]),
              writes=[f"x{i}"], dma=True, tag=f"ldx{i}")
        phase_ln_feature_major(S, nc, x1t, xn2b, work, TLOC, key="")
        A("sync", lambda h: h.dma_start(
            out=xn2d.rearrange("(dt p) t -> p dt t", p=128),
            in_=xn2b[:].rearrange("p (dt t) -> p dt t", t=TLOC)),
          reads=["xout"], writes=["stxn"], dma=True, tag="sxn2")
        xt2 = [xn2b[:, ts(i, TLOC)] for i in range(ND)]
        for i in range(ND):
            S.last_writer["L" + f"x{i}"] = S.last_writer.get("xout")
        phase_ln_fm_bf16(S, nc, xt2, xnnT, work, TLOC, onesbb, gcol=gcol, bcol=bcol, key="L")
        A("tensor", lambda h: h.nop(),
          reads=["Lxout"], writes=["psa", "psb", "Lpsa", "Lpsb", "pp0", "pp1", "pp2", "pp3"],
          tag="fence")
        for ci in range(32):
            jb = ci % 2
            A("gpsimd", lambda h, ci=ci, jb=jb: h.dma_start(
                out=wbuf[jb][:], in_=wind[:, ts(ci, 128)].rearrange("(dt p) c -> p dt c", p=128)),
              writes=[f"wb{jb}"], dma=True, tag=f"ldw{ci}")
            for tb in range(NB4):
                A("tensor", lambda h, ci=ci, jb=jb, tb=tb: [
                    h.matmul(work["ps_a"][:, ts(tb, 512)],
                             wbuf[jb][:, ts(dt, 128)],
                             xnnT[:, dt * TLOC + tb * 512: dt * TLOC + tb * 512 + 512],
                             start=(dt == 0), stop=(dt == ND - 1))
                    for dt in range(ND)][-1],
                  reads=["Lxout", f"wb{jb}"], writes=[f"pp{tb}"], tag=f"mm{ci}.{tb}")
                if ci < 16:
                    A("scalar", lambda h, ci=ci, jb=jb, tb=tb: h.activation(
                        ostg[jb][:, ts(tb, 512)], work["ps_a"][:, ts(tb, 512)], AF.Copy),
                      reads=[f"pp{tb}"], writes=[f"os{jb}"], tag=f"ev{ci}.{tb}")
                else:
                    A("scalar", lambda h, ci=ci, jb=jb, tb=tb: h.activation(
                        ostg[jb][:, ts(tb, 512)], work["ps_a"][:, ts(tb, 512)], AF.Silu),
                      reads=[f"pp{tb}"], writes=[f"os{jb}"], tag=f"ev{ci}.{tb}")
            dst = xhd if ci < 16 else szd
            row = (ci % 16) * 128
            A("sync", lambda h, dst=dst, row=row, jb=jb: h.dma_start(
                out=dst[row:row + 128, :], in_=ostg[jb][:]),
              reads=[f"os{jb}"], writes=["stored"], dma=True, tag=f"st{ci}")
        fin = A("sync", lambda h: h.nop(nofuse=True), reads=["stored", "stxn"], tag="fin")
        for op in S.ops:
            if op.dma and (op.tag.startswith("st")) and op not in fin.deps:
                fin.deps.append(op)
        S.run_block(nc, ctx)
    return nc


def build_front_b_launch():
    nc = bass.Bass("TRN2", target_bir_lowering=False)
    xhd = nc.dram_tensor("xhT", [DI, TLOC], BF16, kind="ExternalInput").ap()
    xhhd = nc.dram_tensor("xh_halo", [DI, 3], BF16, kind="ExternalInput").ap()
    cwd = nc.dram_tensor("conv_wT", [128, 16 * 4], F32, kind="ExternalInput").ap()
    cbd = nc.dram_tensor("conv_bT", [128, 16], F32, kind="ExternalInput").ap()
    wxd = nc.dram_tensor("W_x", [DI, 96], F32, kind="ExternalInput").ap()
    wdtd = nc.dram_tensor("W_dt", [DTR, DI], F32, kind="ExternalInput").ap()
    bdtd = nc.dram_tensor("b_dtT", [128, 16], F32, kind="ExternalInput").ap()
    xcd = nc.dram_tensor("xcT", [DI, TLOC], BF16, kind="ExternalOutput").ap()
    dtd = nc.dram_tensor("dtT", [DI, TLOC], F16, kind="ExternalOutput").ap()
    bcd = nc.dram_tensor("BCT", [32, TLOC], F32, kind="ExternalOutput").ap()

    NDQ = DI // 128
    NB4 = TLOC // 512
    with contextlib.ExitStack() as ctx:
        sb = lambda *a: ctx.enter_context(nc.sbuf_tensor(*a))
        xht = sb("xht", [128, TLOC + 3], BF16)
        cw = sb("cw", [128, 64], F32)
        cb = sb("cb", [128, 16], F32)
        bdt = sb("bdt", [128, 16], F32)
        cacc = sb("cacc", [128, TLOC], F32)
        xcT = sb("xcTs", [128, NDQ * TLOC], BF16)
        wx = sb("wx", [128, NDQ * 96], BF16)
        wdt = sb("wdt", [64, DI], BF16)
        dbcb = sb("dbcb", [96, TLOC], BF16)
        bcs = sb("bcs", [32, TLOC], F32)
        xsp = sb("xsp", [128, TLOC], F32)
        ab2 = sb("ab2", [128, TLOC], F32)
        dtt = sb("dtt", [128, TLOC], F16)
        PA = ctx.enter_context(nc.psum_tensor("PA", [128, 2048], F32))
        PB = ctx.enter_context(nc.psum_tensor("PB", [128, 2048], F32))

        S = Sched()
        A = S.add
        A("gpsimd", lambda h: h.dma_start(out=cw[:], in_=cwd), writes=["cw"], dma=True, tag="ldcw")
        A("gpsimd", lambda h: h.dma_start(out=cb[:], in_=cbd), writes=["cb"], dma=True, tag="ldcb")
        A("gpsimd", lambda h: h.dma_start(out=bdt[:], in_=bdtd), writes=["bdt"], dma=True, tag="ldbd")
        A("gpsimd", lambda h: h.dma_start(out=wx[:], in_=wxd.rearrange("(dq p) c -> p dq c", p=128)),
          writes=["wx"], dma=True, tag="ldwx")
        A("gpsimd", lambda h: h.dma_start(out=wdt[:], in_=wdtd), writes=["wdt"], dma=True, tag="ldwdt")
        for dq in range(NDQ):
            A("sync", lambda h, dq=dq: h.dma_start(out=xht[:, 0:3], in_=xhhd[ts(dq, 128), :]),
              writes=["xh1"], dma=True, tag=f"lh{dq}")
            A("sync", lambda h, dq=dq: h.dma_start(out=xht[:, 3:], in_=xhd[ts(dq, 128), :]),
              writes=["xh2"], dma=True, tag=f"lx{dq}")
            A("vector", lambda h, dq=dq: h.tensor_scalar(
                cacc[:], xht[:, 0:TLOC], cw[:, dq * 4:dq * 4 + 1], None, op0=ALU.mult),
              reads=["xh1", "xh2"], sreads=["cw"], writes=["cacc"], tag=f"c0.{dq}")
            for k in range(1, 4):
                A("vector", lambda h, dq=dq, k=k: h.scalar_tensor_tensor(
                    cacc[:], xht[:, k:k + TLOC], cw[:, dq * 4 + k:dq * 4 + k + 1], cacc[:],
                    op0=ALU.mult, op1=ALU.add),
                  reads=["xh1", "xh2", "cacc"], sreads=["cw"], writes=["cacc"], tag=f"c{k}.{dq}")
            A("scalar", lambda h, dq=dq: h.activation(
                xcT[:, ts(dq, TLOC)], cacc[:], AF.Silu, bias=cb[:, dq:dq + 1]),
              reads=["cacc"], sreads=["cb"], writes=["xcT"], tag=f"si{dq}")
            A("sync", lambda h, dq=dq: h.dma_start(out=xcd[ts(dq, 128), :], in_=xcT[:, ts(dq, TLOC)]),
              reads=["xcT"], writes=["stored"], dma=True, tag=f"stc{dq}")
        for tb in range(NB4):
            A("tensor", lambda h, tb=tb: [
                h.matmul(PA[0:96, ts(0, 512)], wx[:, dq * 96:dq * 96 + 96],
                         xcT[:, dq * TLOC + tb * 512: dq * TLOC + tb * 512 + 512],
                         start=(dq == 0), stop=(dq == NDQ - 1))
                for dq in range(NDQ)][-1],
              reads=["xcT", "wx"], writes=["pdb"], tag=f"db{tb}")
            A("scalar", lambda h, tb=tb: h.activation(dbcb[:, ts(tb, 512)], PA[0:96, 0:512], AF.Copy),
              reads=["pdb"], writes=["dbcb"], tag=f"dbe{tb}")
            A("scalar", lambda h, tb=tb: h.activation(bcs[0:32, ts(tb, 512)], PA[64:96, 0:512], AF.Copy),
              reads=["pdb"], writes=["bcf"], tag=f"dbf{tb}")
        A("sync", lambda h: h.dma_start(out=bcd, in_=bcs[0:32, :]),
          reads=["bcf"], writes=["stored"], dma=True, tag="stbc")
        for dq in range(NDQ):
            for tb in range(NB4):
                A("tensor", lambda h, dq=dq, tb=tb: h.matmul(
                    PB[:, 512:1024],
                    wdt[:, ts(dq, 128)], dbcb[0:64, ts(tb, 512)], start=True, stop=True),
                  reads=["dbcb", "wdt"], writes=["pdt"], tag=f"dt{dq}.{tb}")
                A("vector", lambda h, dq=dq, tb=tb: h.tensor_scalar(
                    xsp[:, ts(tb, 512)], PB[:, 512:1024], bdt[:, dq:dq + 1], None, op0=ALU.add),
                  reads=["pdt"], sreads=["bdt"], writes=["xsp"], tag=f"dta{dq}.{tb}")
            A("scalar", lambda h: h.activation(ab2[:], xsp[:], AF.Abs),
              reads=["xsp"], writes=["ab2"], tag=f"dtb{dq}")
            A("scalar", lambda h: h.activation(ab2[:], ab2[:], AF.Exp, scale=-1.0),
              reads=["ab2"], writes=["ab2"], tag=f"dtc{dq}")
            A("vector", lambda h: h.tensor_scalar(ab2[:], ab2[:], 1.0, None, op0=ALU.add),
              reads=["ab2"], writes=["ab2"], tag=f"dtd{dq}")
            A("scalar", lambda h: h.activation(ab2[:], ab2[:], AF.Ln),
              reads=["ab2"], writes=["ab2"], tag=f"dte{dq}")
            A("vector", lambda h: h.tensor_scalar(xsp[:], xsp[:], 0.0, None, op0=ALU.max),
              reads=["xsp"], writes=["xsp"], tag=f"dtf{dq}")
            A("vector", lambda h: h.tensor_tensor(dtt[:], xsp[:], ab2[:], op=ALU.add),
              reads=["xsp", "ab2"], writes=["dtt"], tag=f"dtg{dq}")
            A("sync", lambda h, dq=dq: h.dma_start(out=dtd[ts(dq, 128), :], in_=dtt[:]),
              reads=["dtt"], writes=["stored"], dma=True, tag=f"std{dq}")
        fin = A("sync", lambda h: h.nop(nofuse=True), reads=["stored"], tag="fin")
        for op in S.ops:
            if op.dma and op.tag.startswith("st") and op not in fin.deps:
                fin.deps.append(op)
        S.run_block(nc, ctx)
    return nc


def build_scan_launch():
    nc = bass.Bass("TRN2", target_bir_lowering=False)
    xcd = nc.dram_tensor("xcq", [512, L], BF16, kind="ExternalInput").ap()
    dtd = nc.dram_tensor("dtq", [512, L], F16, kind="ExternalInput").ap()
    bcd = nc.dram_tensor("BCq", [32, L], F32, kind="ExternalInput").ap()
    aqd = nc.dram_tensor("Aq", [128, 64], F32, kind="ExternalInput").ap()
    oh16d = nc.dram_tensor("onehot16", [16, 16 * 128], BF16, kind="ExternalInput").ap()
    yqd = nc.dram_tensor("yq", [512, L], BF16, kind="ExternalOutput").ap()

    TB = 1024
    NTB = L // TB
    with contextlib.ExitStack() as ctx:
        sb = lambda *a: ctx.enter_context(nc.sbuf_tensor(*a))
        xct = [sb(f"xct{j}", [128, 4 * TB], BF16) for j in range(2)]
        dtt = [sb(f"dtt{j}", [128, 4 * TB], F16) for j in range(2)]
        bct = [sb(f"bct{j}", [16, TB], BF16) for j in range(2)]
        cct = [sb(f"cct{j}", [16, TB], BF16) for j in range(2)]
        oh16 = sb("oh16", [16, 16 * 128], BF16)
        aq = sb("aq", [128, 64], F32)
        acol = sb("acol", [128, 64], F32)
        dtx = sb("dtx", [128, TB], BF16)
        dA = sb("dA", [128, TB], F32)
        dBu = sb("dBu", [128, TB], F32)
        hb = [sb(f"hb{j}", [128, TB], F32) for j in range(2)]
        tmp = sb("tmps", [128, TB], F32)
        chain = sb("chain", [128, 64], F32)
        ystg = sb("ystg", [128, TB], BF16)
        PA = ctx.enter_context(nc.psum_tensor("PA", [128, 2048], F32))
        PB = ctx.enter_context(nc.psum_tensor("PB", [128, 2048], F32))

        S = Sched()
        A = S.add
        A("gpsimd", lambda h: h.dma_start(out=oh16[:], in_=oh16d), writes=["oh16"], dma=True, tag="ldoh")
        A("gpsimd", lambda h: h.dma_start(out=aq[:], in_=aqd), writes=["aq"], dma=True, tag="ldaq")
        A("scalar", lambda h: h.activation(acol[:], aq[:], AF.Exp),
          reads=["aq"], writes=["acol"], tag="aexp")
        A("vector", lambda h: h.tensor_scalar(acol[:], acol[:], -1.0, None, op0=ALU.mult),
          reads=["acol"], writes=["acol"], tag="aneg")
        A("vector", lambda h: h.memset(chain[:], 0.0), writes=["chain"], tag="ch0")

        for tb in range(NTB):
            j = tb % 2
            A("sync", lambda h, tb=tb, j=j: h.dma_start(
                out=xct[j][:], in_=xcd[:, ts(tb, TB)].rearrange("(dq p) t -> p dq t", p=128)),
              writes=[f"xc{j}"], dma=True, tag=f"lxc{tb}")
            A("sync", lambda h, tb=tb, j=j: h.dma_start(
                out=dtt[j][:], in_=dtd[:, ts(tb, TB)].rearrange("(dq p) t -> p dq t", p=128)),
              writes=[f"dt{j}"], dma=True, tag=f"ldt{tb}")
            A("gpsimd", lambda h, tb=tb, j=j: h.dma_start(out=bct[j][:], in_=bcd[0:16, ts(tb, TB)]),
              writes=[f"bc{j}"], dma=True, tag=f"lbc{tb}")
            A("gpsimd", lambda h, tb=tb, j=j: h.dma_start(out=cct[j][:], in_=bcd[16:32, ts(tb, TB)]),
              writes=[f"cc{j}"], dma=True, tag=f"lcc{tb}")
            for dq in range(4):
                A("vector", lambda h, j=j, dq=dq: h.tensor_tensor(
                    dtx[:], dtt[j][:, ts(dq, TB)], xct[j][:, ts(dq, TB)], op=ALU.mult),
                  reads=[f"dt{j}", f"xc{j}"], writes=["dtx"], tag=f"dx{tb}.{dq}")
                for n in range(NST):
                    kk = f"s{tb}.{dq}.{n}"
                    hj = hb[n % 2]
                    A("tensor", lambda h, j=j, n=n: [
                        h.matmul(PA[:, ts(half, 512)], oh16[:, ts(n, 128)],
                                 bct[j][0:16, ts(half, 512)], start=True, stop=True)
                        for half in range(2)][-1],
                      reads=[f"bc{j}", "oh16"], writes=["pBb"], tag=kk + "B")
                    A("tensor", lambda h, j=j, n=n: [
                        h.matmul(PB[:, ts(half, 512)], oh16[:, ts(n, 128)],
                                 cct[j][0:16, ts(half, 512)], start=True, stop=True)
                        for half in range(2)][-1],
                      reads=[f"cc{j}", "oh16"], writes=["pCb"], tag=kk + "C")
                    A("scalar", lambda h, j=j, dq=dq, n=n: h.activation(
                        dA[:], dtt[j][:, ts(dq, TB)], AF.Exp, scale=acol[:, dq * 16 + n:dq * 16 + n + 1]),
                      reads=[f"dt{j}"], sreads=["acol"], writes=["dA"], tag=kk + "a")
                    A("vector", lambda h: h.tensor_tensor(dBu[:], dtx[:], PA[:, 0:TB], op=ALU.mult),
                      reads=["dtx", "pBb"], writes=["dBu"], tag=kk + "u")
                    A("vector", lambda h, hj=hj, dq=dq, n=n: h.tensor_tensor_scan(
                        hj[:], dA[:], dBu[:], chain[:, dq * 16 + n:dq * 16 + n + 1],
                        op0=ALU.mult, op1=ALU.add),
                      reads=["dA", "dBu"], sreads=[f"ch{dq}.{n}"],
                      writes=[f"h{n % 2}"], tag=kk + "h")
                    A("vector", lambda h, hj=hj, dq=dq, n=n: h.tensor_copy(
                        chain[:, dq * 16 + n:dq * 16 + n + 1], hj[:, TB - 1:TB]),
                      reads=[f"h{n % 2}"], writes=[f"ch{dq}.{n}", "chain"], tag=kk + "cc")
                    if n == 0:
                        A("vector", lambda h, hj=hj: h.tensor_tensor(
                            PA[:, 1024:2048], hj[:], PB[:, 0:TB], op=ALU.mult),
                          reads=[f"h{n % 2}", "pCb"], writes=["pY"], tag=kk + "y0")
                    else:
                        A("vector", lambda h, hj=hj: h.tensor_tensor(
                            tmp[:], hj[:], PB[:, 0:TB], op=ALU.mult),
                          reads=[f"h{n % 2}", "pCb"], writes=["tmps"], tag=kk + "y1")
                        A("vector", lambda h: h.tensor_tensor(
                            PA[:, 1024:2048], PA[:, 1024:2048], tmp[:], op=ALU.add),
                          reads=["tmps", "pY"], writes=["pY"], tag=kk + "y2")
                A("scalar", lambda h: h.activation(ystg[:], PA[:, 1024:2048], AF.Copy),
                  reads=["pY"], writes=["ystg"], tag=f"ye{tb}.{dq}")
                A("sync", lambda h, tb=tb, dq=dq: h.dma_start(
                    out=yqd[ts(dq, 128), ts(tb, TB)], in_=ystg[:]),
                  reads=["ystg"], writes=["stored"], dma=True, tag=f"st{tb}.{dq}")
        fin = A("sync", lambda h: h.nop(nofuse=True), reads=["stored"], tag="fin")
        for op in S.ops:
            if op.dma and op.tag.startswith("st") and op not in fin.deps:
                fin.deps.append(op)
        S.run_block(nc, ctx)
    return nc


def build_back_launch():
    nc = bass.Bass("TRN2", target_bir_lowering=False)
    x1Td = nc.dram_tensor("x1T", [D, TLOC], F32, kind="ExternalInput").ap()
    xn2d = nc.dram_tensor("xn2T", [D, TLOC], BF16, kind="ExternalInput").ap()
    yTd = nc.dram_tensor("yT_loc", [DI, TLOC], BF16, kind="ExternalInput").ap()
    xcd = nc.dram_tensor("xcT", [DI, TLOC], BF16, kind="ExternalInput").ap()
    szd = nc.dram_tensor("szT", [DI, TLOC], BF16, kind="ExternalInput").ap()
    dcold = nc.dram_tensor("DcolT", [128, 16], F32, kind="ExternalInput").ap()
    wod = nc.dram_tensor("W_out1", [DI, D], F32, kind="ExternalInput").ap()
    sw1d = nc.dram_tensor("shared_w1T", [D, 4096], F32, kind="ExternalInput").ap()
    sb1d = nc.dram_tensor("shared_b1T", [128, 32], F32, kind="ExternalInput").ap()
    sw2d = nc.dram_tensor("shared_w2T", [4096, D], F32, kind="ExternalInput").ap()
    sb2d = nc.dram_tensor("shared_b2T", [128, 8], F32, kind="ExternalInput").ap()
    rwd = nc.dram_tensor("router1_wT", [D, 8], F32, kind="ExternalInput").ap()
    rbd = nc.dram_tensor("router1_b", [1, 8], F32, kind="ExternalInput").ap()
    w1d = nc.dram_tensor("spec_w1T", [8, D, 4096], F32, kind="ExternalInput").ap()
    b1d = nc.dram_tensor("spec_b1T", [128, 8 * 32], F32, kind="ExternalInput").ap()
    w2d = nc.dram_tensor("spec_w2T", [8, 4096, D], F32, kind="ExternalInput").ap()
    b2d = nc.dram_tensor("spec_b2h", [8, D], F32, kind="ExternalInput").ap()
    oh8d = nc.dram_tensor("onehot8", [8, 8 * 128], BF16, kind="ExternalInput").ap()
    onesbd = nc.dram_tensor("onesbf", [128, 128], BF16, kind="ExternalInput").ap()
    idbd = nc.dram_tensor("idbf", [128, 128], BF16, kind="ExternalInput").ap()
    outd = nc.dram_tensor("outT", [D, TLOC], F32, kind="ExternalOutput").ap()

    NDQ = DI // 128
    NB4 = TLOC // 512
    NT = TLOC // 128
    NFQ = 8
    with contextlib.ExitStack() as ctx:
        sb = lambda *a: ctx.enter_context(nc.sbuf_tensor(*a))
        xn2T = sb("xn2Ts", [128, ND * TLOC], BF16)
        accT = sb("accT", [128, ND * TLOC], F32)
        y3T = sb("y3T", [128, 8 * TLOC], BF16)
        wo = sb("wo", [128, 8 * ND * 128], BF16)
        yt = sb("yt", [128, TLOC], BF16)
        xct = sb("xct", [128, TLOC], BF16)
        szt = sb("szt", [128, TLOC], BF16)
        x1t = sb("x1t", [128, TLOC], F32)
        dcol = sb("dcol", [128, 16], F32)
        sb2c = sb("sb2c", [128, 8], F32)
        sb1c = sb("sb1c", [128, 32], F32)
        b1c = sb("b1c", [128, 8 * 32], F32)
        b2h = sb("b2h", [8, D], BF16)
        oh8 = sb("oh8", [8, 8 * 128], BF16)
        onesb = sb("onesb", [128, 128], BF16)
        idb = sb("idb", [128, 128], BF16)
        rww = sb("rww", [128, ND * 8], BF16)
        rbb = sb("rbb", [1, 8], BF16)
        lg = sb("lg", [128, 8], F32)
        m8 = sb("m8", [128, 8], F32)
        nv0 = sb("nv0", [128, 1], F32)
        el = sb("el", [128, 8], F32)
        msk = sb("msk", [128, 8], F32)
        ev2 = sb("ev2", [128, 2], F32)
        ds = sb("ds", [128, 1], F32)
        rden = sb("rden", [128, 1], F32)
        compb = sb("compb", [128, NT * 8], BF16)
        combT = sb("combT", [8, TLOC], BF16)
        w1b = [sb(f"w1b{j}", [128, ND * 512], BF16) for j in range(2)]
        w2b = wo
        PA = ctx.enter_context(nc.psum_tensor("PA", [128, 2048], F32))
        PB = ctx.enter_context(nc.psum_tensor("PB", [128, 2048], F32))

        S = Sched()
        A = S.add
        for nm, dst, srcd in [("dcol", dcol, dcold), ("sb2c", sb2c, sb2d), ("sb1c", sb1c, sb1d),
                              ("b1c", b1c, b1d), ("oh8", oh8, oh8d), ("onesb", onesb, onesbd),
                              ("idb", idb, idbd), ("rbb", rbb, rbd)]:
            A("gpsimd", lambda h, dst=dst, srcd=srcd: h.dma_start(out=dst[:], in_=srcd),
              writes=[nm], dma=True, tag="ld" + nm)
        A("gpsimd", lambda h: h.dma_start(out=b2h[:], in_=b2d), writes=["b2h"], dma=True, tag="ldb2h")
        A("gpsimd", lambda h: h.dma_start(out=rww[:], in_=rwd.rearrange("(dt p) e -> p dt e", p=128)),
          writes=["rww"], dma=True, tag="ldrw")

        A("sync", lambda h: h.dma_start(out=xn2T[:], in_=xn2d.rearrange("(dt p) t -> p dt t", p=128)),
          writes=["xn2T"], dma=True, tag="ldxn2")
        for dt in range(ND):
            A("sync", lambda h, dt=dt: h.dma_start(out=x1t[:], in_=x1Td[ts(dt, 128), :]),
              writes=["x1t"], dma=True, tag=f"lx1{dt}")
            A("vector", lambda h, dt=dt: h.tensor_tensor(
                accT[:, ts(dt, TLOC)], x1t[:], xn2T[:, ts(dt, TLOC)], op=ALU.add),
              reads=["x1t", "xn2T"], writes=["accT"], tag=f"ai{dt}")
            A("vector", lambda h, dt=dt: h.tensor_scalar(
                accT[:, ts(dt, TLOC)], accT[:, ts(dt, TLOC)], sb2c[:, dt:dt + 1], None, op0=ALU.add),
              reads=["accT"], sreads=["sb2c"], writes=["accT"], tag=f"ab{dt}")
        for half in range(2):
            A("gpsimd", lambda h, half=half: h.dma_start(
                out=wo[:], in_=wod[half * 1024:(half + 1) * 1024, :]
                .rearrange("(dq p) (dt c) -> p dq dt c", p=128, c=128)),
              writes=["wo"], dma=True, tag=f"ldwo{half}")
            for dq8 in range(8):
                dq = half * 8 + dq8
                A("sync", lambda h, dq=dq: h.dma_start(out=yt[:], in_=yTd[ts(dq, 128), :]),
                  writes=["yt"], dma=True, tag=f"ly{dq}")
                A("sync", lambda h, dq=dq: h.dma_start(out=xct[:], in_=xcd[ts(dq, 128), :]),
                  writes=["xctb"], dma=True, tag=f"lc{dq}")
                A("sync", lambda h, dq=dq: h.dma_start(out=szt[:], in_=szd[ts(dq, 128), :]),
                  writes=["sztb"], dma=True, tag=f"lz{dq}")
                A("vector", lambda h, dq=dq, dq8=dq8: h.scalar_tensor_tensor(
                    y3T[:, ts(dq8, TLOC)], xct[:], dcol[:, dq:dq + 1], yt[:], op0=ALU.mult, op1=ALU.add),
                  reads=["xctb", "yt"], sreads=["dcol"], writes=[f"y3_{dq8}"], tag=f"y2.{dq}")
                A("vector", lambda h, dq8=dq8: h.tensor_tensor(
                    y3T[:, ts(dq8, TLOC)], y3T[:, ts(dq8, TLOC)], szt[:], op=ALU.mult),
                  reads=[f"y3_{dq8}", "sztb"], writes=[f"y3_{dq8}"], tag=f"y3.{dq}")
            for dt in range(ND):
                for tb in range(NB4):
                    A("tensor", lambda h, dt=dt, tb=tb: [
                        h.matmul(PA[:, 0:512], wo[:, (dq8 * ND + dt) * 128:(dq8 * ND + dt) * 128 + 128],
                                 y3T[:, dq8 * TLOC + tb * 512: dq8 * TLOC + tb * 512 + 512],
                                 start=(dq8 == 0), stop=(dq8 == 7))
                        for dq8 in range(8)][-1],
                      reads=[f"y3_{x}" for x in range(8)] + ["wo"], writes=["pw"],
                      tag=f"wo{half}.{dt}.{tb}")
                    A("vector", lambda h, dt=dt, tb=tb: h.tensor_tensor(
                        accT[:, dt * TLOC + tb * 512: dt * TLOC + tb * 512 + 512],
                        accT[:, dt * TLOC + tb * 512: dt * TLOC + tb * 512 + 512],
                        PA[:, 0:512], op=ALU.add),
                      reads=["pw", "accT"], writes=["accT"], tag=f"wa{half}.{dt}.{tb}")
        for tt in range(NT):
            tk = f"r{tt}"
            A("tensor", lambda h, tt=tt: [
                h.matmul(PA[0:128, 1536:1544], xn2T[:, dt * TLOC + tt * 128: dt * TLOC + tt * 128 + 128],
                         rww[:, ts(dt, 8)], start=(dt == 0), stop=False)
                for dt in range(ND)][-1],
              reads=["xn2T", "rww"], writes=["prt"], tag=tk + "mm")
            A("tensor", lambda h: h.matmul(PA[0:128, 1536:1544], onesb[0:1, :], rbb[:],
                                           start=False, stop=True),
              reads=["rbb", "onesb"], writes=["prt"], tag=tk + "bias")
            A("scalar", lambda h: h.activation(lg[:], PA[0:128, 1536:1544], AF.Copy),
              reads=["prt"], writes=["lg"], tag=tk + "ev")
            A("vector", lambda h: h.max(m8[:], lg[:]), reads=["lg"], writes=["m8"], tag=tk + "m8")
            A("vector", lambda h: h.tensor_scalar(nv0[:], m8[:, 0:1], -1.0, None, op0=ALU.mult),
              reads=["m8"], writes=["nv0"], tag=tk + "nv")
            A("scalar", lambda h: h.activation(el[:], lg[:], AF.Exp, bias=nv0[:, 0:1]),
              reads=["lg"], sreads=["nv0"], writes=["el"], tag=tk + "el")
            A("vector", lambda h: h.tensor_scalar(msk[:], lg[:], m8[:, 1:2], None, op0=ALU.is_ge),
              reads=["lg"], sreads=["m8"], writes=["msk"], tag=tk + "mk")
            A("scalar", lambda h: h.activation(ev2[:], m8[:, 0:2], AF.Exp, bias=nv0[:, 0:1]),
              reads=["m8"], sreads=["nv0"], writes=["ev2"], tag=tk + "e2")
            A("vector", lambda h: h.reduce_sum(ds[:], ev2[:], axis=mybir.AxisListType.X),
              reads=["ev2"], writes=["ds"], tag=tk + "ds")
            A("vector", lambda h: h.reciprocal(rden[:], ds[:]), reads=["ds"], writes=["rden"], tag=tk + "rd")
            A("vector", lambda h: h.tensor_tensor(el[:], el[:], msk[:], op=ALU.mult),
              reads=["el", "msk"], writes=["el"], tag=tk + "cm1")
            A("vector", lambda h, tt=tt: h.tensor_scalar(
                compb[:, ts(tt, 8)], el[:], rden[:, 0:1], None, op0=ALU.mult),
              reads=["el"], sreads=["rden"], writes=["compb"], tag=tk + "cm2")
            A("tensor", lambda h, tt=tt: h.transpose(
                PB[0:8, 1536:1600].bitcast(BF16), compb[:, ts(tt, 8)], idb[:]),
              reads=["compb", "idb"], writes=["ptr"], tag=tk + "tr")
            A("vector", lambda h, tt=tt: h.tensor_copy(
                combT[0:8, ts(tt, 128)], PB[0:8, 1536:1600].bitcast(BF16)),
              reads=["ptr"], writes=["combT"], tag=tk + "te")
        A("tensor", lambda h: h.nop(nofuse=True),
          reads=["combT"], writes=["prt", "ptr", "pw", "ph", "pcb", "pd"], tag="fence")
        w1i = 0
        for e in range(-1, 8):
            ekk = f"E{e}"
            if e >= 0:
                A("tensor", lambda h, e=e: [
                    h.matmul(PB[:, half * 512: half * 512 + 512], oh8[:, ts(e, 128)],
                             combT[0:8, half * 512: half * 512 + 512], start=True, stop=True)
                    for half in range(4)][-1],
                  reads=["combT", "oh8"], writes=["pcb"], tag=ekk + "cb")
            for q4 in range(4):
                w1src = sw1d if e < 0 else w1d[e]
                w2src = sw2d if e < 0 else w2d[e]
                A("gpsimd", lambda h, w2src=w2src, q4=q4: h.dma_start(
                    out=w2b[:], in_=w2src[ts(q4, 1024), :].rearrange("(ft p) c -> p ft c", p=128)),
                  writes=["wo", "w2bk"], dma=True, tag=ekk + f"lw2{q4}")
                for qh in range(2):
                    jb = w1i % 2
                    w1i += 1
                    A("gpsimd", lambda h, jb=jb, w1src=w1src, q4=q4, qh=qh: h.dma_start(
                        out=w1b[jb][:],
                        in_=w1src[:, q4 * 1024 + qh * 512: q4 * 1024 + qh * 512 + 512]
                        .rearrange("(dt p) c -> p dt c", p=128)),
                      writes=[f"w1b{jb}"], dma=True, tag=ekk + f"lw1{q4}.{qh}")
                    for fh in range(4):
                        ft = qh * 4 + fh
                        gft = q4 * NFQ + ft
                        for tb in range(NB4):
                            A("tensor", lambda h, jb=jb, fh=fh, tb=tb: [
                                h.matmul(PA[:, 512:1024],
                                         w1b[jb][:, dt * 512 + fh * 128: dt * 512 + fh * 128 + 128],
                                         xn2T[:, dt * TLOC + tb * 512: dt * TLOC + tb * 512 + 512],
                                         start=(dt == 0), stop=(dt == ND - 1))
                                for dt in range(ND)][-1],
                              reads=["xn2T", f"w1b{jb}"], writes=["ph"], tag=ekk + f"h{gft}.{tb}")
                            bca = (sb1c, gft) if e < 0 else (b1c, e * 32 + gft)
                            A("scalar", lambda h, ft=ft, tb=tb, bca=bca: h.activation(
                                y3T[:, ft * TLOC + tb * 512: ft * TLOC + tb * 512 + 512],
                                PA[:, 512:1024], AF.Gelu_apprx_tanh,
                                bias=bca[0][:, bca[1]:bca[1] + 1]),
                              reads=["ph"], sreads=["sb1c" if e < 0 else "b1c"],
                              writes=[f"hq{ft}", f"y3_{ft}"], tag=ekk + f"g{gft}.{tb}")
                        if e >= 0:
                            A("vector", lambda h, ft=ft: h.tensor_tensor(
                                y3T[:, ft * TLOC: ft * TLOC + TLOC],
                                y3T[:, ft * TLOC: ft * TLOC + TLOC],
                                PB[:, 0:2048], op=ALU.mult),
                              reads=[f"hq{ft}", "pcb"], writes=[f"hq{ft}", f"y3_{ft}"],
                              tag=ekk + f"w{gft}")
                for dt in range(ND):
                    for tb in range(NB4):
                        def emit_down(h, dt=dt, tb=tb, e=e, q4=q4):
                            for ft in range(NFQ):
                                ins = h.matmul(
                                    PA[:, 1024:1536],
                                    w2b[:, ft * 1024 + dt * 128: ft * 1024 + dt * 128 + 128],
                                    y3T[:, ft * TLOC + tb * 512: ft * TLOC + tb * 512 + 512],
                                    start=(ft == 0), stop=(ft == NFQ - 1 and not (e >= 0 and q4 == 0)))
                            if e >= 0 and q4 == 0:
                                ins = h.matmul(PA[:, 1024:1536], b2h[:, ts(dt, 128)],
                                               combT[0:8, ts(tb, 512)], start=False, stop=True)
                            return ins
                        A("tensor", emit_down,
                          reads=[f"hq{x}" for x in range(NFQ)] + ["w2bk", "b2h", "combT"],
                          writes=["pd"], tag=ekk + f"d{q4}.{dt}.{tb}")
                        A("vector", lambda h, dt=dt, tb=tb: h.tensor_tensor(
                            accT[:, dt * TLOC + tb * 512: dt * TLOC + tb * 512 + 512],
                            accT[:, dt * TLOC + tb * 512: dt * TLOC + tb * 512 + 512],
                            PA[:, 1024:1536], op=ALU.add),
                          reads=["pd", "accT"], writes=["accT"], tag=ekk + f"a{q4}.{dt}.{tb}")
        for dt in range(ND):
            A("sync", lambda h, dt=dt: h.dma_start(out=outd[ts(dt, 128), :], in_=accT[:, ts(dt, TLOC)]),
              reads=["accT"], writes=["stored"], dma=True, tag=f"sto{dt}")
        fin = A("sync", lambda h: h.nop(nofuse=True), reads=["stored"], tag="fin")
        for op in S.ops:
            if op.dma and op.tag.startswith("st") and op not in fin.deps:
                fin.deps.append(op)
        S.run_block(nc, ctx)
    return nc


def prep_common(z):
    BFD = np.dtype(ml_dtypes.bfloat16)
    p = {}
    p["x0T"] = np.ascontiguousarray(z["L0_attn_x"].transpose(0, 2, 1))
    p["ones128"] = np.ones((128, 128), dtype=np.float32)
    p["onesbf"] = np.ones((128, 128), dtype=BFD)
    p["idbf"] = np.eye(128, dtype=BFD)
    p["w_qkv"] = z["L0_attn_w_qkv"]
    p["b_qkvT"] = np.ascontiguousarray(z["L0_attn_b_qkv"].reshape(24, 128).T)
    p["w_out"] = z["L0_attn_w_out"]
    p["b_outT"] = np.ascontiguousarray(z["L0_attn_b_out"].reshape(8, 128).T)
    freqs = z["L0_attn_rope_freqs"][:, :, 0]
    fc = np.zeros((128, 8), dtype=np.float32)
    for pp in range(8):
        fc[0:32, pp] = -freqs[2 * pp]
        fc[32:64, pp] = freqs[2 * pp]
        fc[64:96, pp] = -freqs[2 * pp + 1]
        fc[96:128, pp] = freqs[2 * pp + 1]
    p["freqcols2"] = fc
    pos = np.arange(L, dtype=np.float32)
    p["pos"] = pos
    p["gate_wT"] = np.ascontiguousarray(z["L0_ffn_gate_w"].transpose(0, 2, 1))
    p["up_wT"] = np.ascontiguousarray(z["L0_ffn_up_w"].transpose(0, 2, 1))
    p["down_wT"] = np.ascontiguousarray(z["L0_ffn_down_w"].transpose(0, 2, 1))
    p["sgate_wT"] = np.ascontiguousarray(z["L0_ffn_sgate_w"].T)
    p["sup_wT"] = np.ascontiguousarray(z["L0_ffn_sup_w"].T)
    p["sdown_wT"] = np.ascontiguousarray(z["L0_ffn_sdown_w"].T)
    p["router_wT"] = np.ascontiguousarray(z["L0_ffn_router_w"].T)
    p["router_b"] = np.ascontiguousarray(z["L0_ffn_router_bias"][None, :])
    oh32 = np.zeros((32, 32 * 128), dtype=BFD)
    for e in range(32):
        oh32[e, e * 128:(e + 1) * 128] = 1
    p["onehot32"] = oh32
    oh16 = np.zeros((16, 16 * 128), dtype=BFD)
    for n in range(16):
        oh16[n, n * 128:(n + 1) * 128] = 1
    p["onehot16"] = oh16
    oh8 = np.zeros((8, 8 * 128), dtype=BFD)
    for e in range(8):
        oh8[e, e * 128:(e + 1) * 128] = 1
    p["onehot8"] = oh8
    p["W_in"] = z["L1_attn_W_in"]
    p["ln_gT"] = np.ascontiguousarray(z["L1_attn_ln_g"].reshape(8, 128).T)
    p["ln_bT"] = np.ascontiguousarray(z["L1_attn_ln_b"].reshape(8, 128).T)
    p["conv_wT"] = np.ascontiguousarray(
        z["L1_attn_conv_w"].reshape(16, 128, 4).transpose(1, 0, 2).reshape(128, 64))
    p["conv_bT"] = np.ascontiguousarray(z["L1_attn_conv_b"].reshape(16, 128).T)
    p["W_x"] = z["L1_attn_W_x"]
    p["W_dt"] = z["L1_attn_W_dt"]
    p["b_dtT"] = np.ascontiguousarray(z["L1_attn_b_dt"].reshape(16, 128).T)
    p["A_log"] = z["L1_attn_A_log"]
    p["DcolT"] = np.ascontiguousarray(z["L1_attn_D"].reshape(16, 128).T)
    p["W_out1"] = z["L1_attn_W_out"]
    p["shared_w1T"] = np.ascontiguousarray(z["L1_ffn_shared_w1"].T)
    p["shared_b1T"] = np.ascontiguousarray(z["L1_ffn_shared_b1"].reshape(32, 128).T)
    p["shared_w2T"] = np.ascontiguousarray(z["L1_ffn_shared_w2"].T)
    p["shared_b2T"] = np.ascontiguousarray(z["L1_ffn_shared_b2"].reshape(8, 128).T)
    p["router1_wT"] = np.ascontiguousarray(z["L1_ffn_router_w"].T)
    p["router1_b"] = np.ascontiguousarray(z["L1_ffn_router_b"][None, :])
    p["spec_w1T"] = np.ascontiguousarray(z["L1_ffn_spec_w1"].transpose(0, 2, 1))
    p["spec_b1T"] = np.ascontiguousarray(
        z["L1_ffn_spec_b1"].reshape(8, 32, 128).transpose(2, 0, 1).reshape(128, 256))
    p["spec_w2T"] = np.ascontiguousarray(z["L1_ffn_spec_w2"].transpose(0, 2, 1))
    p["spec_b2h"] = np.ascontiguousarray(z["L1_ffn_spec_b2"])
    return p


def kernel_multilaunch(**inputs):
    BFD = np.dtype(ml_dtypes.bfloat16)
    p = prep_common(inputs)
    cores = [divmod(c, 4) for c in range(NCORES)]

    nc = build_ln_launch()
    ims = [dict(x0T=np.ascontiguousarray(p["x0T"][b_, :, ts(s, TLOC)]),
                ones128=p["ones128"]) for b_, s in cores]
    res = run_bass_kernel_spmd(nc, ims, core_ids=list(range(NCORES)))
    xnT_loc = [r["xnT"] for r in res.results]
    xnT_full = [np.concatenate([xnT_loc[b_ * 4 + s] for s in range(4)], axis=1) for b_ in range(B)]

    nc = build_attn_launch(0)
    ims = []
    for b_, s in cores:
        xr = np.ascontiguousarray(np.roll(xnT_full[b_], -s * TLOC, axis=1))
        pr = np.ascontiguousarray(np.broadcast_to(np.roll(p["pos"], -s * TLOC), (128, L)))
        ims.append(dict(xnT=xr, posrep=pr, w_qkv=p["w_qkv"], b_qkvT=p["b_qkvT"],
                        w_out=p["w_out"], b_outT=p["b_outT"], freqcols2=p["freqcols2"],
                        onesbf=p["onesbf"], idbf=p["idbf"]))
    res = run_bass_kernel_spmd(nc, ims, core_ids=list(range(NCORES)))
    attn0T = [r["attn0T"] for r in res.results]

    nc = build_ffn0_launch()
    ims = []
    for c, (b_, s) in enumerate(cores):
        ims.append(dict(xnT_loc=xnT_loc[c],
                        x0T_loc=np.ascontiguousarray(p["x0T"][b_, :, ts(s, TLOC)]),
                        attn0T=attn0T[c],
                        gate_wT=p["gate_wT"], up_wT=p["up_wT"], down_wT=p["down_wT"],
                        sgate_wT=p["sgate_wT"], sup_wT=p["sup_wT"], sdown_wT=p["sdown_wT"],
                        router_wT=p["router_wT"], router_b=p["router_b"],
                        onehot32=p["onehot32"], onesbf=p["onesbf"], idbf=p["idbf"]))
    res = run_bass_kernel_spmd(nc, ims, core_ids=list(range(NCORES)))
    x1T = [r["x1T"] for r in res.results]

    nc = build_front_a_launch()
    ims = [dict(x1T=x1T[c], W_in=p["W_in"], ln_gT=p["ln_gT"], ln_bT=p["ln_bT"],
                ones128=p["ones128"], onesbf=p["onesbf"]) for c in range(NCORES)]
    res = run_bass_kernel_spmd(nc, ims, core_ids=list(range(NCORES)))
    xn2T = [r["xn2T"] for r in res.results]
    xhT = [r["xhT"] for r in res.results]
    szT = [r["szT"] for r in res.results]

    nc = build_front_b_launch()
    ims = []
    for c, (b_, s) in enumerate(cores):
        if s == 0:
            halo = np.zeros((DI, 3), dtype=BFD)
        else:
            halo = np.ascontiguousarray(xhT[c - 1][:, TLOC - 3:])
        ims.append(dict(xhT=xhT[c], xh_halo=halo, conv_wT=p["conv_wT"], conv_bT=p["conv_bT"],
                        W_x=p["W_x"], W_dt=p["W_dt"], b_dtT=p["b_dtT"]))
    res = run_bass_kernel_spmd(nc, ims, core_ids=list(range(NCORES)))
    xcT = [r["xcT"] for r in res.results]
    dtT = [r["dtT"] for r in res.results]
    BCT = [r["BCT"] for r in res.results]

    nc = build_scan_launch()
    ims = []
    qcores = [divmod(c, 4) for c in range(NCORES)]
    for b_, q in qcores:
        dsl = slice(q * 512, (q + 1) * 512)
        xcq = np.concatenate([xcT[b_ * 4 + s][dsl] for s in range(4)], axis=1)
        dtq = np.concatenate([dtT[b_ * 4 + s][dsl] for s in range(4)], axis=1)
        BCq = np.concatenate([BCT[b_ * 4 + s] for s in range(4)], axis=1)
        Aq = np.ascontiguousarray(
            p["A_log"][dsl].reshape(4, 128, 16).transpose(1, 0, 2).reshape(128, 64))
        ims.append(dict(xcq=np.ascontiguousarray(xcq), dtq=np.ascontiguousarray(dtq),
                        BCq=np.ascontiguousarray(BCq), Aq=Aq, onehot16=p["onehot16"]))
    res = run_bass_kernel_spmd(nc, ims, core_ids=list(range(NCORES)))
    yq = [r["yq"] for r in res.results]

    nc = build_back_launch()
    ims = []
    for c, (b_, s) in enumerate(cores):
        yT = np.concatenate([yq[b_ * 4 + q][:, ts(s, TLOC)] for q in range(4)], axis=0)
        ims.append(dict(x1T=x1T[c], xn2T=xn2T[c], yT_loc=np.ascontiguousarray(yT),
                        xcT=xcT[c], szT=szT[c], DcolT=p["DcolT"], W_out1=p["W_out1"],
                        shared_w1T=p["shared_w1T"], shared_b1T=p["shared_b1T"],
                        shared_w2T=p["shared_w2T"], shared_b2T=p["shared_b2T"],
                        router1_wT=p["router1_wT"], router1_b=p["router1_b"],
                        spec_w1T=p["spec_w1T"], spec_b1T=p["spec_b1T"],
                        spec_w2T=p["spec_w2T"], spec_b2h=p["spec_b2h"],
                        onehot8=p["onehot8"], onesbf=p["onesbf"], idbf=p["idbf"]))
    res = run_bass_kernel_spmd(nc, ims, core_ids=list(range(NCORES)))
    outT = [r["outT"] for r in res.results]

    out = np.empty((B, L, D), dtype=np.float32)
    for c, (b_, s) in enumerate(cores):
        out[b_, ts(s, TLOC), :] = outT[c].T
    return out


def emit_ffn0_body(S, nc, ctx, io):
    A = S.add
    xnTd = io["xnT_src"]; x0Td = io["x0Td"]; at0d = io["at0d"]
    gwd = io["gwd"]; uwd = io["uwd"]; dwd = io["dwd"]
    sgd = io["sgd"]; sud = io["sud"]; sdd = io["sdd"]
    rwd = io["rwd"]; rbd = io["rbd"]; ohd = io["ohd"]
    onesbf_d = io["onesbf_d"]; idbf_d = io["idbf_d"]
    x1Td = io["x1Td"]; x1key = io["x1key"]; at0key = io.get("at0key")
    NT = TLOC // 128
    NB4 = TLOC // 512
    EGR = 8

    if True:
        sb = lambda nm, shp, dt2: ctx.enter_context(nc.sbuf_tensor("p3" + nm, shp, dt2))
        xnT = sb("xnT", [128, ND * TLOC], BF16)
        ffacc = sb("ffacc", [128, ND * TLOC], F32)
        hwt = [sb(f"hwt{e}", [128, TLOC], BF16) for e in range(EGR)]
        ge = sb("ge", [128, ND * 128], BF16)
        ue = sb("ue", [128, ND * 128], BF16)
        de = [sb(f"de{i}", [128, D], BF16) for i in range(8)]
        sgw = sb("sgw", [128, ND * 512], BF16)
        suw = sb("suw", [128, ND * 512], BF16)
        sdw = sb("sdw", [128, 4 * D], BF16)
        rww = sb("rww", [128, ND * 32], BF16)
        rbb = sb("rbb", [1, 32], BF16)
        oh32 = sb("oh32", [32, 32 * 128], BF16)
        onesb = sb("onesb", [128, 128], BF16)
        idb = sb("idb", [128, 128], BF16)
        lg = sb("lg", [128, 32], F32)
        m8 = sb("m8", [128, 8], F32)
        nv0 = sb("nv0", [128, 1], F32)
        el = sb("el", [128, 32], F32)
        msk = sb("msk", [128, 32], F32)
        ev4 = sb("ev4", [128, 4], F32)
        ds = sb("ds", [128, 1], F32)
        rden = sb("rden", [128, 1], F32)
        compb = sb("compb", [128, NT * 32], BF16)
        combT = sb("combT", [32, TLOC], BF16)
        sgt = sb("sgt", [128, 512], BF16)
        x0t = sb("x0t", [128, 1024], F32)
        a0t = sb("a0t", [128, 1024], BF16)
        x1t = sb("x1t", [128, 1024], F32)
        PA = io["PA"]; PB = io["PB"]

        A("sync", lambda h: h.dma_start(out=xnT[:], in_=xnTd.rearrange("(dt p) t -> p dt t", p=128)),
          reads=[io.get("xn_srckey")] if io.get("xn_srckey") else [],
          writes=["xnT"], dma=True, tag="p3ldxn")
        A("gpsimd", lambda h: h.dma_start(out=rww[:], in_=rwd.rearrange("(dt p) e -> p dt e", p=128)),
          writes=["rww"], dma=True, tag="ldrw")
        A("gpsimd", lambda h: h.dma_start(out=rbb[:], in_=rbd), writes=["rbb"], dma=True, tag="ldrb")
        A("gpsimd", lambda h: h.dma_start(out=oh32[:], in_=ohd), writes=["oh32"], dma=True, tag="ldoh")
        A("gpsimd", lambda h: h.dma_start(out=onesb[:], in_=onesbf_d), writes=["onesb"], dma=True, tag="ldon")
        A("gpsimd", lambda h: h.dma_start(out=idb[:], in_=idbf_d), writes=["idb"], dma=True, tag="ldid")
        A("gpsimd", lambda h: h.dma_start(out=sgw[:], in_=sgd.rearrange("(dt p) s -> p dt s", p=128)),
          writes=["sgw"], dma=True, tag="ldsg")
        A("gpsimd", lambda h: h.dma_start(out=suw[:], in_=sud.rearrange("(dt p) s -> p dt s", p=128)),
          writes=["suw"], dma=True, tag="ldsu")
        A("gpsimd", lambda h: h.dma_start(out=sdw[:], in_=sdd.rearrange("(ht p) d -> p ht d", p=128)),
          writes=["sdw"], dma=True, tag="ldsd")

        for tt in range(NT):
            tk = f"r{tt}"
            A("tensor", lambda h, tt=tt: [
                h.matmul(PA[0:128, 1536:1568], xnT[:, dt * TLOC + tt * 128: dt * TLOC + tt * 128 + 128],
                         rww[:, ts(dt, 32)], start=(dt == 0), stop=False)
                for dt in range(ND)][-1],
              reads=["xnT", "rww"], writes=["prt"], tag=tk + "mm")
            A("tensor", lambda h: h.matmul(PA[0:128, 1536:1568], onesb[0:1, :], rbb[:],
                                           start=False, stop=True),
              reads=["rbb", "onesb"], writes=["prt"], tag=tk + "bias")
            A("scalar", lambda h: h.activation(lg[:], PA[0:128, 1536:1568], AF.Copy),
              reads=["prt"], writes=["lg"], tag=tk + "ev")
            A("vector", lambda h: h.max(m8[:], lg[:]), reads=["lg"], writes=["m8"], tag=tk + "m8")
            A("vector", lambda h: h.tensor_scalar(nv0[:], m8[:, 0:1], -1.0, None, op0=ALU.mult),
              reads=["m8"], writes=["nv0"], tag=tk + "nv")
            A("scalar", lambda h: h.activation(el[:], lg[:], AF.Exp, bias=nv0[:, 0:1]),
              reads=["lg"], sreads=["nv0"], writes=["el"], tag=tk + "el")
            A("vector", lambda h: h.tensor_scalar(msk[:], lg[:], m8[:, 3:4], None, op0=ALU.is_ge),
              reads=["lg"], sreads=["m8"], writes=["msk"], tag=tk + "mk")
            A("scalar", lambda h: h.activation(ev4[:], m8[:, 0:4], AF.Exp, bias=nv0[:, 0:1]),
              reads=["m8"], sreads=["nv0"], writes=["ev4"], tag=tk + "e4")
            A("vector", lambda h: h.reduce_sum(ds[:], ev4[:], axis=mybir.AxisListType.X),
              reads=["ev4"], writes=["ds"], tag=tk + "ds")
            A("vector", lambda h: h.reciprocal(rden[:], ds[:]), reads=["ds"], writes=["rden"], tag=tk + "rd")
            A("vector", lambda h: h.tensor_tensor(el[:], el[:], msk[:], op=ALU.mult),
              reads=["el", "msk"], writes=["el"], tag=tk + "cm1")
            A("vector", lambda h, tt=tt: h.tensor_scalar(
                compb[:, ts(tt, 32)], el[:], rden[:, 0:1], None, op0=ALU.mult),
              reads=["el"], sreads=["rden"], writes=["compb"], tag=tk + "cm2")
            A("tensor", lambda h, tt=tt: h.transpose(
                PB[0:32, 512:576].bitcast(BF16), compb[:, ts(tt, 32)], idb[:]),
              reads=["compb", "idb"], writes=["ptr"], tag=tk + "tr")
            A("vector", lambda h, tt=tt: h.tensor_copy(
                combT[0:32, ts(tt, 128)], PB[0:32, 512:576].bitcast(BF16)),
              reads=["ptr"], writes=["combT"], tag=tk + "te")

        for grp in range(32 // EGR):
            for ei in range(EGR):
                e = grp * EGR + ei
                ek = f"e{e}"
                A("gpsimd", lambda h, e=e: h.dma_start(
                    out=ge[:], in_=gwd[e].rearrange("(dt p) c -> p dt c", p=128)),
                  writes=["ge"], dma=True, tag=ek + "ldg")
                A("gpsimd", lambda h, e=e: h.dma_start(
                    out=ue[:], in_=uwd[e].rearrange("(dt p) c -> p dt c", p=128)),
                  writes=["ue"], dma=True, tag=ek + "ldu")
                for tb in range(NB4):
                    A("tensor", lambda h, tb=tb: [
                        h.matmul(PA[:, 0:512], ge[:, ts(dt, 128)],
                                 xnT[:, dt * TLOC + tb * 512: dt * TLOC + tb * 512 + 512],
                                 start=(dt == 0), stop=(dt == ND - 1))
                        for dt in range(ND)][-1],
                      reads=["xnT", "ge"], writes=["pg"], tag=ek + f"g{tb}")
                    A("tensor", lambda h, tb=tb: [
                        h.matmul(PA[:, 512:1024], ue[:, ts(dt, 128)],
                                 xnT[:, dt * TLOC + tb * 512: dt * TLOC + tb * 512 + 512],
                                 start=(dt == 0), stop=(dt == ND - 1))
                        for dt in range(ND)][-1],
                      reads=["xnT", "ue"], writes=["pu"], tag=ek + f"u{tb}")
                    A("tensor", lambda h, e=e, tb=tb: h.matmul(
                        PA[:, 1024:1536], oh32[:, ts(e, 128)], combT[0:32, ts(tb, 512)],
                        start=True, stop=True),
                      reads=["combT", "oh32"], writes=["pc"], tag=ek + f"c{tb}")
                    A("scalar", lambda h: h.activation(sgt[:], PA[:, 0:512], AF.Silu),
                      reads=["pg"], writes=["sgt"], tag=ek + f"s{tb}")
                    A("vector", lambda h, ei=ei, tb=tb: h.tensor_tensor(
                        hwt[ei][:, ts(tb, 512)], PA[:, 512:1024], sgt[:], op=ALU.mult),
                      reads=["pu", "sgt"], writes=[f"hw{ei}"], tag=ek + f"m{tb}")
                    A("vector", lambda h, ei=ei, tb=tb: h.tensor_tensor(
                        hwt[ei][:, ts(tb, 512)], hwt[ei][:, ts(tb, 512)], PA[:, 1024:1536],
                        op=ALU.mult),
                      reads=[f"hw{ei}", "pc"], writes=[f"hw{ei}"], tag=ek + f"w{tb}")
                A("gpsimd", lambda h, e=e, ei=ei: h.dma_start(out=de[ei][:], in_=dwd[e]),
                  writes=[f"de{ei}"], dma=True, tag=ek + "ldd")
            for dt in range(ND):
                for tb in range(NB4):
                    A("tensor", lambda h, grp=grp, dt=dt, tb=tb: [
                        h.matmul(PB[:, 0:512], de[ei][:, ts(dt, 128)],
                                 hwt[ei][:, ts(tb, 512)], start=(ei == 0), stop=(ei == EGR - 1))
                        for ei in range(EGR)][-1],
                      reads=[f"hw{x}" for x in range(EGR)] + [f"de{x}" for x in range(EGR)],
                      writes=["pd"], tag=f"dn{grp}.{dt}.{tb}")
                    if grp == 0:
                        A("scalar", lambda h, dt=dt, tb=tb: h.activation(
                            ffacc[:, dt * TLOC + tb * 512: dt * TLOC + tb * 512 + 512],
                            PB[:, 0:512], AF.Copy),
                          reads=["pd"], writes=["ffacc"], tag=f"fc{grp}.{dt}.{tb}")
                    else:
                        A("vector", lambda h, dt=dt, tb=tb: h.tensor_tensor(
                            ffacc[:, dt * TLOC + tb * 512: dt * TLOC + tb * 512 + 512],
                            ffacc[:, dt * TLOC + tb * 512: dt * TLOC + tb * 512 + 512],
                            PB[:, 0:512], op=ALU.add),
                          reads=["pd", "ffacc"], writes=["ffacc"], tag=f"fc{grp}.{dt}.{tb}")

        for st in range(4):
            for tb in range(NB4):
                A("tensor", lambda h, st=st, tb=tb: [
                    h.matmul(PA[:, 0:512], sgw[:, dt * 512 + st * 128: dt * 512 + st * 128 + 128],
                             xnT[:, dt * TLOC + tb * 512: dt * TLOC + tb * 512 + 512],
                             start=(dt == 0), stop=(dt == ND - 1))
                    for dt in range(ND)][-1],
                  reads=["xnT", "sgw"], writes=["pg"], tag=f"sg{st}.{tb}")
                A("tensor", lambda h, st=st, tb=tb: [
                    h.matmul(PA[:, 512:1024], suw[:, dt * 512 + st * 128: dt * 512 + st * 128 + 128],
                             xnT[:, dt * TLOC + tb * 512: dt * TLOC + tb * 512 + 512],
                             start=(dt == 0), stop=(dt == ND - 1))
                    for dt in range(ND)][-1],
                  reads=["xnT", "suw"], writes=["pu"], tag=f"su{st}.{tb}")
                A("scalar", lambda h: h.activation(sgt[:], PA[:, 0:512], AF.Silu),
                  reads=["pg"], writes=["sgt"], tag=f"ss{st}.{tb}")
                A("vector", lambda h, st=st, tb=tb: h.tensor_tensor(
                    hwt[st][:, ts(tb, 512)], PA[:, 512:1024], sgt[:], op=ALU.mult),
                  reads=["pu", "sgt"], writes=[f"hw{st}"], tag=f"sm{st}.{tb}")
        for dt in range(ND):
            for tb in range(NB4):
                A("tensor", lambda h, dt=dt, tb=tb: [
                    h.matmul(PB[:, 0:512], sdw[:, st * D + dt * 128: st * D + dt * 128 + 128],
                             hwt[st][:, ts(tb, 512)], start=(st == 0), stop=(st == 3))
                    for st in range(4)][-1],
                  reads=["hw0", "hw1", "hw2", "hw3", "sdw"], writes=["pd"], tag=f"sd{dt}.{tb}")
                A("vector", lambda h, dt=dt, tb=tb: h.tensor_tensor(
                    ffacc[:, dt * TLOC + tb * 512: dt * TLOC + tb * 512 + 512],
                    ffacc[:, dt * TLOC + tb * 512: dt * TLOC + tb * 512 + 512],
                    PB[:, 0:512], op=ALU.add),
                  reads=["pd", "ffacc"], writes=["ffacc"], tag=f"sf{dt}.{tb}")

        for dt in range(ND):
          for hf in range(2):
            hsl = slice(hf * 1024, hf * 1024 + 1024)
            A("sync", lambda h, dt=dt, hsl=hsl: h.dma_start(out=x0t[:], in_=x0Td[ts(dt, 128), hsl]),
              writes=["x0t"], dma=True, tag=f"lx0{dt}.{hf}")
            A("sync", lambda h, dt=dt, hsl=hsl: h.dma_start(out=a0t[:], in_=at0d[ts(dt, 128), hsl]),
              reads=[at0key] if at0key else [], writes=["a0t"], dma=True, tag=f"p3la0{dt}.{hf}")
            A("vector", lambda h, dt=dt, hf=hf: h.tensor_tensor(
                x1t[:], x0t[:], ffacc[:, dt * TLOC + hf * 1024: dt * TLOC + hf * 1024 + 1024],
                op=ALU.add),
              reads=["x0t", "ffacc"], writes=["x1t"], tag=f"x1a{dt}.{hf}")
            A("vector", lambda h: h.tensor_tensor(x1t[:], x1t[:], a0t[:], op=ALU.add),
              reads=["x1t", "a0t"], writes=["x1t"], tag=f"x1b{dt}.{hf}")
            A("sync", lambda h, dt=dt, hsl=hsl: h.dma_start(out=x1Td[ts(dt, 128), hsl], in_=x1t[:]),
              reads=["x1t"], writes=[x1key], dma=True, tag=f"p3st{dt}.{hf}")


def emit_front_a_body(S, nc, ctx, io):
    A = S.add
    x1Td = io["x1Td"]; wind = io["wind"]; lngd = io["lngd"]; lnbd = io["lnbd"]
    onesd = io["onesd"]; onesbd = io["onesbd"]
    xn2d = io["xn2d"]; xhd = io["xhd"]; szd = io["szd"]
    x1key = io.get("x1key")
    NB4 = TLOC // 512
    if True:
        sb = lambda nm, shp, dt2: ctx.enter_context(nc.sbuf_tensor("p4" + nm, shp, dt2))
        x1t = [sb(f"x1t{i}", [128, TLOC], F32) for i in range(ND)]
        xn2b = sb("xn2b", [128, ND * TLOC], BF16)
        xnnT = sb("xnnT", [128, ND * TLOC], BF16)
        gcol = sb("gcol", [128, ND], F32)
        bcol = sb("bcol", [128, ND], F32)
        wbuf = [sb(f"wbuf{j}", [128, ND * 128], BF16) for j in range(2)]
        ostg = [sb(f"ostg{j}", [128, TLOC], BF16) for j in range(2)]
        work = mk_ln_work(nc, ctx, TLOC, pref="p4", psum=(io["PA"], io["PB"]))
        work["onesf"] = work["onesc"]
        onesbb = sb("onesbb", [128, 128], BF16)

        A("sync", lambda h: h.dma_start(out=work["onesc"][:], in_=onesd),
          writes=["onesc", "onesf"], dma=True, tag="ldone")
        A("sync", lambda h: h.dma_start(out=onesbb[:], in_=onesbd), writes=["onesbb"], dma=True, tag="ldonb")
        A("gpsimd", lambda h: h.dma_start(out=gcol[:], in_=lngd), writes=["p4Lgb"], dma=True, tag="p4ldg")
        A("gpsimd", lambda h: h.dma_start(out=bcol[:], in_=lnbd), writes=["p4Lgb"], dma=True, tag="p4ldb")
        for i in range(ND):
            A("sync", lambda h, i=i: h.dma_start(out=x1t[i][:], in_=x1Td[ts(i, 128), :]),
              reads=[x1key] if x1key else [], writes=[f"p4x{i}"], dma=True, tag=f"p4ldx{i}")
        phase_ln_feature_major(S, nc, x1t, xn2b, work, TLOC, key="p4")
        A("sync", lambda h: h.dma_start(
            out=xn2d.rearrange("(dt p) t -> p dt t", p=128),
            in_=xn2b[:].rearrange("p (dt t) -> p dt t", t=TLOC)),
          reads=["p4xout"], writes=["xn2dk"], dma=True, tag="p4sxn2")
        xt2 = [xn2b[:, ts(i, TLOC)] for i in range(ND)]
        for i in range(ND):
            S.last_writer["p4L" + f"x{i}"] = S.last_writer.get("p4xout")
        phase_ln_fm_bf16(S, nc, xt2, xnnT, work, TLOC, onesbb, gcol=gcol, bcol=bcol, key="p4L")
        A("tensor", lambda h: h.nop(nofuse=True),
          reads=["p4Lxout"], writes=["p4psa", "p4psb", "p4Lpsa", "p4Lpsb", "pp0", "pp1", "pp2", "pp3"],
          tag="p4fence")
        for ci in range(32):
            jb = ci % 2
            A("gpsimd", lambda h, ci=ci, jb=jb: h.dma_start(
                out=wbuf[jb][:], in_=wind[:, ts(ci, 128)].rearrange("(dt p) c -> p dt c", p=128)),
              writes=[f"wb{jb}"], dma=True, tag=f"ldw{ci}")
            for tb in range(NB4):
                A("tensor", lambda h, ci=ci, jb=jb, tb=tb: [
                    h.matmul(work["ps_a"][:, ts(tb, 512)],
                             wbuf[jb][:, ts(dt, 128)],
                             xnnT[:, dt * TLOC + tb * 512: dt * TLOC + tb * 512 + 512],
                             start=(dt == 0), stop=(dt == ND - 1))
                    for dt in range(ND)][-1],
                  reads=["p4Lxout", f"wb{jb}"], writes=[f"pp{tb}"], tag=f"mm{ci}.{tb}")
                if ci < 16:
                    A("scalar", lambda h, ci=ci, jb=jb, tb=tb: h.activation(
                        ostg[jb][:, ts(tb, 512)], work["ps_a"][:, ts(tb, 512)], AF.Copy),
                      reads=[f"pp{tb}"], writes=[f"os{jb}"], tag=f"ev{ci}.{tb}")
                else:
                    A("scalar", lambda h, ci=ci, jb=jb, tb=tb: h.activation(
                        ostg[jb][:, ts(tb, 512)], work["ps_a"][:, ts(tb, 512)], AF.Silu),
                      reads=[f"pp{tb}"], writes=[f"os{jb}"], tag=f"ev{ci}.{tb}")
            dst = xhd if ci < 16 else szd
            row = (ci % 16) * 128
            A("sync", lambda h, dst=dst, row=row, jb=jb: h.dma_start(
                out=dst[row:row + 128, :], in_=ostg[jb][:]),
              reads=[f"os{jb}"], writes=["xhd" if ci < 16 else "szd"], dma=True, tag=f"p4st{ci}")
            if ci < 16 and io.get("tail") is not None:
                A("sync", lambda h, row=row, jb=jb: h.dma_start(
                    out=io["tail"][row:row + 128, 0:8], in_=ostg[jb][:, TLOC - 8:]),
                  reads=[f"os{jb}"], writes=["tld"], dma=True, tag=f"p4tl{ci}")


def emit_front_b_body(S, nc, ctx, io):
    A = S.add
    xhd = io["xhd"]; cwd = io["cwd"]; cbd = io["cbd"]
    wxd = io["wxd"]; wdtd = io["wdtd"]; bdtd = io["bdtd"]
    xcd = io["xcd"]; dtd = io["dtd"]; bcd = io["bcd"]
    xhkey = io.get("xhkey")
    NDQ = DI // 128
    NB4 = TLOC // 512
    if True:
        sb = lambda nm, shp, dt2: ctx.enter_context(nc.sbuf_tensor("p5" + nm, shp, dt2))
        xht = sb("xht", [128, TLOC + 3], BF16)
        cw = sb("cw", [128, 64], F32)
        cb = sb("cb", [128, 16], F32)
        bdt = sb("bdt", [128, 16], F32)
        cacc = sb("cacc", [128, TLOC], F32)
        xcT = sb("xcTs", [128, NDQ * TLOC], BF16)
        wx = sb("wx", [128, NDQ * 96], BF16)
        wdt = sb("wdt", [64, DI], BF16)
        dbcb = sb("dbcb", [96, TLOC], BF16)
        bcs = sb("bcs", [32, TLOC], F32)
        xsp = sb("xsp", [128, TLOC], F32)
        ab2 = sb("ab2", [128, TLOC], F32)
        dtt = sb("dtt", [128, TLOC], F16)
        PA = io["PA"]; PB = io["PB"]
        xhalo = sb("xhalo", [128, 8], BF16)

        A("gpsimd", lambda h: h.dma_start(out=cw[:], in_=cwd), writes=["cw"], dma=True, tag="ldcw")
        A("gpsimd", lambda h: h.dma_start(out=cb[:], in_=cbd), writes=["cb"], dma=True, tag="ldcb")
        A("gpsimd", lambda h: h.dma_start(out=bdt[:], in_=bdtd), writes=["bdt"], dma=True, tag="ldbd")
        A("gpsimd", lambda h: h.dma_start(out=wx[:], in_=wxd.rearrange("(dq p) c -> p dq c", p=128)),
          writes=["wx"], dma=True, tag="ldwx")
        A("gpsimd", lambda h: h.dma_start(out=wdt[:], in_=wdtd), writes=["wdt"], dma=True, tag="ldwdt")
        for dq in range(NDQ):
            A("gpsimd", lambda h: h.memset(xhalo[:], 0.0), writes=["xhalo"], tag=f"p5hm{dq}")
            A("gpsimd", io["gather"](xhalo[:], io["tl_out"], io["idxh"][:, dq:dq + 1],
                                     NCORES * DI - 1),
              reads=["tloutd", "idxh", "xhalo"], writes=["xhalo"], dma=True, tag=f"p5hg{dq}")
            A("vector", lambda h: h.tensor_copy(xht[:, 0:3], xhalo[:, 5:8]),
              reads=["xhalo"], writes=["xh1"], tag=f"p5hc{dq}")
            A("sync", lambda h, dq=dq: h.dma_start(out=xht[:, 3:], in_=xhd[ts(dq, 128), :]),
              reads=[xhkey] if xhkey else [], writes=["xh2"], dma=True, tag=f"p5lx{dq}")
            A("vector", lambda h, dq=dq: h.tensor_scalar(
                cacc[:], xht[:, 0:TLOC], cw[:, dq * 4:dq * 4 + 1], None, op0=ALU.mult),
              reads=["xh1", "xh2"], sreads=["cw"], writes=["cacc"], tag=f"c0.{dq}")
            for k in range(1, 4):
                A("vector", lambda h, dq=dq, k=k: h.scalar_tensor_tensor(
                    cacc[:], xht[:, k:k + TLOC], cw[:, dq * 4 + k:dq * 4 + k + 1], cacc[:],
                    op0=ALU.mult, op1=ALU.add),
                  reads=["xh1", "xh2", "cacc"], sreads=["cw"], writes=["cacc"], tag=f"c{k}.{dq}")
            A("scalar", lambda h, dq=dq: h.activation(
                xcT[:, ts(dq, TLOC)], cacc[:], AF.Silu, bias=cb[:, dq:dq + 1]),
              reads=["cacc"], sreads=["cb"], writes=["xcT"], tag=f"si{dq}")
            A("sync", lambda h, dq=dq: h.dma_start(out=xcd[ts(dq, 128), :], in_=xcT[:, ts(dq, TLOC)]),
              reads=["xcT"], writes=["xcd"], dma=True, tag=f"p5stc{dq}")
        for tb in range(NB4):
            A("tensor", lambda h, tb=tb: [
                h.matmul(PA[0:96, ts(0, 512)], wx[:, dq * 96:dq * 96 + 96],
                         xcT[:, dq * TLOC + tb * 512: dq * TLOC + tb * 512 + 512],
                         start=(dq == 0), stop=(dq == NDQ - 1))
                for dq in range(NDQ)][-1],
              reads=["xcT", "wx"], writes=["pdb"], tag=f"db{tb}")
            A("scalar", lambda h, tb=tb: h.activation(dbcb[:, ts(tb, 512)], PA[0:96, 0:512], AF.Copy),
              reads=["pdb"], writes=["dbcb"], tag=f"dbe{tb}")
            A("scalar", lambda h, tb=tb: h.activation(bcs[0:32, ts(tb, 512)], PA[64:96, 0:512], AF.Copy),
              reads=["pdb"], writes=["bcf"], tag=f"dbf{tb}")
        A("sync", lambda h: h.dma_start(out=bcd, in_=bcs[0:32, :]),
          reads=["bcf"], writes=["bcd"], dma=True, tag="p5stbc")
        for dq in range(NDQ):
            for tb in range(NB4):
                A("tensor", lambda h, dq=dq, tb=tb: h.matmul(
                    PB[:, 512:1024],
                    wdt[:, ts(dq, 128)], dbcb[0:64, ts(tb, 512)], start=True, stop=True),
                  reads=["dbcb", "wdt"], writes=["pdt"], tag=f"dt{dq}.{tb}")
                A("vector", lambda h, dq=dq, tb=tb: h.tensor_scalar(
                    xsp[:, ts(tb, 512)], PB[:, 512:1024], bdt[:, dq:dq + 1], None, op0=ALU.add),
                  reads=["pdt"], sreads=["bdt"], writes=["xsp"], tag=f"dta{dq}.{tb}")
            A("scalar", lambda h: h.activation(ab2[:], xsp[:], AF.Abs),
              reads=["xsp"], writes=["ab2"], tag=f"dtb{dq}")
            A("scalar", lambda h: h.activation(ab2[:], ab2[:], AF.Exp, scale=-1.0),
              reads=["ab2"], writes=["ab2"], tag=f"dtc{dq}")
            A("vector", lambda h: h.tensor_scalar(ab2[:], ab2[:], 1.0, None, op0=ALU.add),
              reads=["ab2"], writes=["ab2"], tag=f"dtd{dq}")
            A("scalar", lambda h: h.activation(ab2[:], ab2[:], AF.Ln),
              reads=["ab2"], writes=["ab2"], tag=f"dte{dq}")
            A("vector", lambda h: h.tensor_scalar(xsp[:], xsp[:], 0.0, None, op0=ALU.max),
              reads=["xsp"], writes=["xsp"], tag=f"dtf{dq}")
            A("vector", lambda h: h.tensor_tensor(dtt[:], xsp[:], ab2[:], op=ALU.add),
              reads=["xsp", "ab2"], writes=["dtt"], tag=f"dtg{dq}")
            A("sync", lambda h, dq=dq: h.dma_start(out=dtd[ts(dq, 128), :], in_=dtt[:]),
              reads=["dtt"], writes=["dtd"], dma=True, tag=f"p5std{dq}")


def emit_scan_body(S, nc, ctx, io):
    A = S.add
    aqd = io["aqd"]; oh16d = io["oh16d"]; yqd = io["yqd"]
    TB = 1024
    NTB = L // TB
    if True:
        sb = lambda nm, shp, dt2: ctx.enter_context(nc.sbuf_tensor("p6" + nm, shp, dt2))
        xct = [sb(f"xct{j}", [128, 4 * TB], BF16) for j in range(2)]
        dtt = [sb(f"dtt{j}", [128, 4 * TB], F16) for j in range(2)]
        bct = [sb(f"bct{j}", [16, TB], BF16) for j in range(2)]
        cct = [sb(f"cct{j}", [16, TB], BF16) for j in range(2)]
        oh16 = sb("oh16", [16, 16 * 128], BF16)
        aq = sb("aq", [128, 64], F32)
        acol = sb("acol", [128, 64], F32)
        dtx = sb("dtx", [128, TB], BF16)
        dA = sb("dA", [128, TB], F32)
        dBu = sb("dBu", [128, TB], F32)
        hb = [sb(f"hb{j}", [128, TB], F32) for j in range(2)]
        tmp = sb("tmps", [128, TB], F32)
        chain = sb("chain", [128, 64], F32)
        ystg = sb("ystg", [128, TB], BF16)
        PA = io["PA"]; PB = io["PB"]

        A("gpsimd", lambda h: h.dma_start(out=oh16[:], in_=oh16d), writes=["oh16"], dma=True, tag="p6ldoh")
        A("gpsimd", lambda h: h.dma_start(out=aq[:], in_=aqd), writes=["aq"], dma=True, tag="ldaq")
        A("scalar", lambda h: h.activation(acol[:], aq[:], AF.Exp),
          reads=["aq"], writes=["acol"], tag="aexp")
        A("vector", lambda h: h.tensor_scalar(acol[:], acol[:], -1.0, None, op0=ALU.mult),
          reads=["acol"], writes=["acol"], tag="aneg")
        A("vector", lambda h: h.memset(chain[:], 0.0), writes=["chain"], tag="ch0")

        for tb in range(NTB):
            j = tb % 2
            for dqi in range(4):
                A("gpsimd", io["gather"](xct[j][:, ts(dqi, TB)], io["xc_out"],
                                         io["idxs"][:, tb * 4 + dqi:tb * 4 + dqi + 1]),
                  reads=["xcoutd", "idxs"], writes=[f"xc{j}"], dma=True, tag=f"p6lxc{tb}.{dqi}")
                A("gpsimd", io["gather"](dtt[j][:, ts(dqi, TB)], io["dt_out"],
                                         io["idxs"][:, tb * 4 + dqi:tb * 4 + dqi + 1]),
                  reads=["dtoutd", "idxs"], writes=[f"dt{j}"], dma=True, tag=f"p6ldt{tb}.{dqi}")
            A("gpsimd", io["gather"](bct[j][:], io["bc_out"], io["idxb"][0:16, tb:tb + 1]),
              reads=["bcoutd", "idxb"], writes=[f"bc{j}"], dma=True, tag=f"p6lbc{tb}")
            A("gpsimd", io["gather"](cct[j][:], io["bc_out"], io["idxb"][0:16, 8 + tb:8 + tb + 1]),
              reads=["bcoutd", "idxb"], writes=[f"cc{j}"], dma=True, tag=f"p6lcc{tb}")
            for dq in range(4):
                A("vector", lambda h, j=j, dq=dq: h.tensor_tensor(
                    dtx[:], dtt[j][:, ts(dq, TB)], xct[j][:, ts(dq, TB)], op=ALU.mult),
                  reads=[f"dt{j}", f"xc{j}"], writes=["dtx"], tag=f"dx{tb}.{dq}")
                for n in range(NST):
                    kk = f"s{tb}.{dq}.{n}"
                    hj = hb[n % 2]
                    A("tensor", lambda h, j=j, n=n: [
                        h.matmul(PA[:, ts(half, 512)], oh16[:, ts(n, 128)],
                                 bct[j][0:16, ts(half, 512)], start=True, stop=True)
                        for half in range(2)][-1],
                      reads=[f"bc{j}", "oh16"], writes=["pBb"], tag=kk + "B")
                    A("tensor", lambda h, j=j, n=n: [
                        h.matmul(PB[:, ts(half, 512)], oh16[:, ts(n, 128)],
                                 cct[j][0:16, ts(half, 512)], start=True, stop=True)
                        for half in range(2)][-1],
                      reads=[f"cc{j}", "oh16"], writes=["pCb"], tag=kk + "C")
                    A("scalar", lambda h, j=j, dq=dq, n=n: h.activation(
                        dA[:], dtt[j][:, ts(dq, TB)], AF.Exp, scale=acol[:, dq * 16 + n:dq * 16 + n + 1]),
                      reads=[f"dt{j}"], sreads=["acol"], writes=["dA"], tag=kk + "a")
                    A("vector", lambda h: h.tensor_tensor(dBu[:], dtx[:], PA[:, 0:TB], op=ALU.mult),
                      reads=["dtx", "pBb"], writes=["dBu"], tag=kk + "u")
                    A("vector", lambda h, hj=hj, dq=dq, n=n: h.tensor_tensor_scan(
                        hj[:], dA[:], dBu[:], chain[:, dq * 16 + n:dq * 16 + n + 1],
                        op0=ALU.mult, op1=ALU.add),
                      reads=["dA", "dBu"], sreads=[f"ch{dq}.{n}"],
                      writes=[f"h{n % 2}"], tag=kk + "h")
                    A("vector", lambda h, hj=hj, dq=dq, n=n: h.tensor_copy(
                        chain[:, dq * 16 + n:dq * 16 + n + 1], hj[:, TB - 1:TB]),
                      reads=[f"h{n % 2}"], writes=[f"ch{dq}.{n}", "chain"], tag=kk + "cc")
                    if n == 0:
                        A("vector", lambda h, hj=hj: h.tensor_tensor(
                            PA[:, 1024:2048], hj[:], PB[:, 0:TB], op=ALU.mult),
                          reads=[f"h{n % 2}", "pCb"], writes=["pY"], tag=kk + "y0")
                    else:
                        A("vector", lambda h, hj=hj: h.tensor_tensor(
                            tmp[:], hj[:], PB[:, 0:TB], op=ALU.mult),
                          reads=[f"h{n % 2}", "pCb"], writes=["tmps"], tag=kk + "y1")
                        A("vector", lambda h: h.tensor_tensor(
                            PA[:, 1024:2048], PA[:, 1024:2048], tmp[:], op=ALU.add),
                          reads=["tmps", "pY"], writes=["pY"], tag=kk + "y2")
                A("scalar", lambda h: h.activation(ystg[:], PA[:, 1024:2048], AF.Copy),
                  reads=["pY"], writes=["ystg"], tag=f"ye{tb}.{dq}")
                A("sync", lambda h, tb=tb, dq=dq: h.dma_start(
                    out=yqd[ts(dq, 128), ts(tb, TB)], in_=ystg[:]),
                  reads=["ystg"], writes=["yqd"], dma=True, tag=f"p6st{tb}.{dq}")


def emit_back_body(S, nc, ctx, io):
    A = S.add
    x1Td = io["x1Td"]; xn2d = io["xn2d"]; xcd = io["xcd"]; szd = io["szd"]
    dcold = io["dcold"]; wod = io["wod"]
    sw1d = io["sw1d"]; sb1d = io["sb1d"]; sw2d = io["sw2d"]; sb2d = io["sb2d"]
    rwd = io["rwd"]; rbd = io["rbd"]
    w1d = io["w1d"]; b1d = io["b1d"]; w2d = io["w2d"]; b2d = io["b2d"]
    oh8d = io["oh8d"]; onesbd = io["onesbd"]; idbd = io["idbd"]; outd = io["outd"]
    NDQ = DI // 128
    NB4 = TLOC // 512
    NT = TLOC // 128
    NFQ = 8
    if True:
        sb = lambda nm, shp, dt2: ctx.enter_context(nc.sbuf_tensor("p7" + nm, shp, dt2))
        xn2T = sb("xn2Ts", [128, ND * TLOC], BF16)
        accT = sb("accT", [128, ND * TLOC], F32)
        y3T = sb("y3T", [128, 8 * TLOC], BF16)
        wo = sb("wo", [128, 8 * ND * 128], BF16)
        yt = sb("yt", [128, TLOC], BF16)
        xct = sb("xct", [128, TLOC], BF16)
        szt = sb("szt", [128, TLOC], BF16)
        x1t = sb("x1t", [128, TLOC], F32)
        dcol = sb("dcol", [128, 16], F32)
        sb2c = sb("sb2c", [128, 8], F32)
        sb1c = sb("sb1c", [128, 32], F32)
        b1c = sb("b1c", [128, 8 * 32], F32)
        b2h = sb("b2h", [8, D], BF16)
        oh8 = sb("oh8", [8, 8 * 128], BF16)
        onesb = sb("onesb", [128, 128], BF16)
        idb = sb("idb", [128, 128], BF16)
        rww = sb("rww", [128, ND * 8], BF16)
        rbb = sb("rbb", [1, 8], BF16)
        lg = sb("lg", [128, 8], F32)
        m8 = sb("m8", [128, 8], F32)
        nv0 = sb("nv0", [128, 1], F32)
        el = sb("el", [128, 8], F32)
        msk = sb("msk", [128, 8], F32)
        ev2 = sb("ev2", [128, 2], F32)
        ds = sb("ds", [128, 1], F32)
        rden = sb("rden", [128, 1], F32)
        compb = sb("compb", [128, NT * 8], BF16)
        combT = sb("combT", [8, TLOC], BF16)
        w1b = [sb(f"w1b{j}", [128, ND * 512], BF16) for j in range(2)]
        w2b = wo
        PA = io["PA"]; PB = io["PB"]

        for nm, dst, srcd in [("dcol", dcol, dcold), ("sb2c", sb2c, sb2d), ("sb1c", sb1c, sb1d),
                              ("b1c", b1c, b1d), ("oh8", oh8, oh8d), ("onesb", onesb, onesbd),
                              ("idb", idb, idbd), ("rbb", rbb, rbd)]:
            A("gpsimd", lambda h, dst=dst, srcd=srcd: h.dma_start(out=dst[:], in_=srcd),
              writes=[nm], dma=True, tag="ld" + nm)
        A("gpsimd", lambda h: h.dma_start(out=b2h[:], in_=b2d), writes=["b2h"], dma=True, tag="ldb2h")
        A("gpsimd", lambda h: h.dma_start(out=rww[:], in_=rwd.rearrange("(dt p) e -> p dt e", p=128)),
          writes=["rww"], dma=True, tag="ldrw")

        A("sync", lambda h: h.dma_start(out=xn2T[:], in_=xn2d.rearrange("(dt p) t -> p dt t", p=128)),
          reads=["xn2dk"], writes=["xn2T"], dma=True, tag="p7ldxn2")
        for dt in range(ND):
            A("sync", lambda h, dt=dt: h.dma_start(out=x1t[:], in_=x1Td[ts(dt, 128), :]),
              reads=["x1d"], writes=["x1t"], dma=True, tag=f"p7lx1{dt}")
            A("vector", lambda h, dt=dt: h.tensor_tensor(
                accT[:, ts(dt, TLOC)], x1t[:], xn2T[:, ts(dt, TLOC)], op=ALU.add),
              reads=["x1t", "xn2T"], writes=["accT"], tag=f"ai{dt}")
            A("vector", lambda h, dt=dt: h.tensor_scalar(
                accT[:, ts(dt, TLOC)], accT[:, ts(dt, TLOC)], sb2c[:, dt:dt + 1], None, op0=ALU.add),
              reads=["accT"], sreads=["sb2c"], writes=["accT"], tag=f"ab{dt}")
        for half in range(2):
            A("gpsimd", lambda h, half=half: h.dma_start(
                out=wo[:], in_=wod[half * 1024:(half + 1) * 1024, :]
                .rearrange("(dq p) (dt c) -> p dq dt c", p=128, c=128)),
              writes=["wo"], dma=True, tag=f"ldwo{half}")
            for dq8 in range(8):
                dq = half * 8 + dq8
                A("gpsimd", io["gather"](yt[:], io["y_out"], io["idxy"][:, dq:dq + 1]),
                  reads=["youtd", "idxy"], writes=["yt"], dma=True, tag=f"p7ly{dq}")
                A("sync", lambda h, dq=dq: h.dma_start(out=xct[:], in_=xcd[ts(dq, 128), :]),
                  reads=["xcd"], writes=["xctb"], dma=True, tag=f"p7lc{dq}")
                A("sync", lambda h, dq=dq: h.dma_start(out=szt[:], in_=szd[ts(dq, 128), :]),
                  reads=["szd"], writes=["sztb"], dma=True, tag=f"p7lz{dq}")
                A("vector", lambda h, dq=dq, dq8=dq8: h.scalar_tensor_tensor(
                    y3T[:, ts(dq8, TLOC)], xct[:], dcol[:, dq:dq + 1], yt[:], op0=ALU.mult, op1=ALU.add),
                  reads=["xctb", "yt"], sreads=["dcol"], writes=[f"y3_{dq8}"], tag=f"y2.{dq}")
                A("vector", lambda h, dq8=dq8: h.tensor_tensor(
                    y3T[:, ts(dq8, TLOC)], y3T[:, ts(dq8, TLOC)], szt[:], op=ALU.mult),
                  reads=[f"y3_{dq8}", "sztb"], writes=[f"y3_{dq8}"], tag=f"y3.{dq}")
            for dt in range(ND):
                for tb in range(NB4):
                    A("tensor", lambda h, dt=dt, tb=tb: [
                        h.matmul(PA[:, 0:512], wo[:, (dq8 * ND + dt) * 128:(dq8 * ND + dt) * 128 + 128],
                                 y3T[:, dq8 * TLOC + tb * 512: dq8 * TLOC + tb * 512 + 512],
                                 start=(dq8 == 0), stop=(dq8 == 7))
                        for dq8 in range(8)][-1],
                      reads=[f"y3_{x}" for x in range(8)] + ["wo"], writes=["pw"],
                      tag=f"wo{half}.{dt}.{tb}")
                    A("vector", lambda h, dt=dt, tb=tb: h.tensor_tensor(
                        accT[:, dt * TLOC + tb * 512: dt * TLOC + tb * 512 + 512],
                        accT[:, dt * TLOC + tb * 512: dt * TLOC + tb * 512 + 512],
                        PA[:, 0:512], op=ALU.add),
                      reads=["pw", "accT"], writes=["accT"], tag=f"wa{half}.{dt}.{tb}")
        for tt in range(NT):
            tk = f"r{tt}"
            A("tensor", lambda h, tt=tt: [
                h.matmul(PA[0:128, 1536:1544], xn2T[:, dt * TLOC + tt * 128: dt * TLOC + tt * 128 + 128],
                         rww[:, ts(dt, 8)], start=(dt == 0), stop=False)
                for dt in range(ND)][-1],
              reads=["xn2T", "rww"], writes=["prt"], tag=tk + "mm")
            A("tensor", lambda h: h.matmul(PA[0:128, 1536:1544], onesb[0:1, :], rbb[:],
                                           start=False, stop=True),
              reads=["rbb", "onesb"], writes=["prt"], tag=tk + "bias")
            A("scalar", lambda h: h.activation(lg[:], PA[0:128, 1536:1544], AF.Copy),
              reads=["prt"], writes=["lg"], tag=tk + "ev")
            A("vector", lambda h: h.max(m8[:], lg[:]), reads=["lg"], writes=["m8"], tag=tk + "m8")
            A("vector", lambda h: h.tensor_scalar(nv0[:], m8[:, 0:1], -1.0, None, op0=ALU.mult),
              reads=["m8"], writes=["nv0"], tag=tk + "nv")
            A("scalar", lambda h: h.activation(el[:], lg[:], AF.Exp, bias=nv0[:, 0:1]),
              reads=["lg"], sreads=["nv0"], writes=["el"], tag=tk + "el")
            A("vector", lambda h: h.tensor_scalar(msk[:], lg[:], m8[:, 1:2], None, op0=ALU.is_ge),
              reads=["lg"], sreads=["m8"], writes=["msk"], tag=tk + "mk")
            A("scalar", lambda h: h.activation(ev2[:], m8[:, 0:2], AF.Exp, bias=nv0[:, 0:1]),
              reads=["m8"], sreads=["nv0"], writes=["ev2"], tag=tk + "e2")
            A("vector", lambda h: h.reduce_sum(ds[:], ev2[:], axis=mybir.AxisListType.X),
              reads=["ev2"], writes=["ds"], tag=tk + "ds")
            A("vector", lambda h: h.reciprocal(rden[:], ds[:]), reads=["ds"], writes=["rden"], tag=tk + "rd")
            A("vector", lambda h: h.tensor_tensor(el[:], el[:], msk[:], op=ALU.mult),
              reads=["el", "msk"], writes=["el"], tag=tk + "cm1")
            A("vector", lambda h, tt=tt: h.tensor_scalar(
                compb[:, ts(tt, 8)], el[:], rden[:, 0:1], None, op0=ALU.mult),
              reads=["el"], sreads=["rden"], writes=["compb"], tag=tk + "cm2")
            A("tensor", lambda h, tt=tt: h.transpose(
                PB[0:8, 1536:1600].bitcast(BF16), compb[:, ts(tt, 8)], idb[:]),
              reads=["compb", "idb"], writes=["ptr"], tag=tk + "tr")
            A("vector", lambda h, tt=tt: h.tensor_copy(
                combT[0:8, ts(tt, 128)], PB[0:8, 1536:1600].bitcast(BF16)),
              reads=["ptr"], writes=["combT"], tag=tk + "te")
        A("tensor", lambda h: h.nop(nofuse=True),
          reads=["combT"], writes=["prt", "ptr", "pw", "ph", "pcb", "pd"], tag="fence")
        w1i = 0
        for e in range(-1, 8):
            ekk = f"E{e}"
            if e >= 0:
                A("tensor", lambda h, e=e: [
                    h.matmul(PB[:, half * 512: half * 512 + 512], oh8[:, ts(e, 128)],
                             combT[0:8, half * 512: half * 512 + 512], start=True, stop=True)
                    for half in range(4)][-1],
                  reads=["combT", "oh8"], writes=["pcb"], tag=ekk + "cb")
            for q4 in range(4):
                w1src = sw1d if e < 0 else w1d[e]
                w2src = sw2d if e < 0 else w2d[e]
                A("gpsimd", lambda h, w2src=w2src, q4=q4: h.dma_start(
                    out=w2b[:], in_=w2src[ts(q4, 1024), :].rearrange("(ft p) c -> p ft c", p=128)),
                  writes=["wo", "w2bk"], dma=True, tag=ekk + f"lw2{q4}")
                for qh in range(2):
                    jb = w1i % 2
                    w1i += 1
                    A("gpsimd", lambda h, jb=jb, w1src=w1src, q4=q4, qh=qh: h.dma_start(
                        out=w1b[jb][:],
                        in_=w1src[:, q4 * 1024 + qh * 512: q4 * 1024 + qh * 512 + 512]
                        .rearrange("(dt p) c -> p dt c", p=128)),
                      writes=[f"w1b{jb}"], dma=True, tag=ekk + f"lw1{q4}.{qh}")
                    for fh in range(4):
                        ft = qh * 4 + fh
                        gft = q4 * NFQ + ft
                        for tb in range(NB4):
                            A("tensor", lambda h, jb=jb, fh=fh, tb=tb: [
                                h.matmul(PA[:, 512:1024],
                                         w1b[jb][:, dt * 512 + fh * 128: dt * 512 + fh * 128 + 128],
                                         xn2T[:, dt * TLOC + tb * 512: dt * TLOC + tb * 512 + 512],
                                         start=(dt == 0), stop=(dt == ND - 1))
                                for dt in range(ND)][-1],
                              reads=["xn2T", f"w1b{jb}"], writes=["ph"], tag=ekk + f"h{gft}.{tb}")
                            bca = (sb1c, gft) if e < 0 else (b1c, e * 32 + gft)
                            A("scalar", lambda h, ft=ft, tb=tb, bca=bca: h.activation(
                                y3T[:, ft * TLOC + tb * 512: ft * TLOC + tb * 512 + 512],
                                PA[:, 512:1024], AF.Gelu_apprx_tanh,
                                bias=bca[0][:, bca[1]:bca[1] + 1]),
                              reads=["ph"], sreads=["sb1c" if e < 0 else "b1c"],
                              writes=[f"hq{ft}", f"y3_{ft}"], tag=ekk + f"g{gft}.{tb}")
                        if e >= 0:
                            A("vector", lambda h, ft=ft: h.tensor_tensor(
                                y3T[:, ft * TLOC: ft * TLOC + TLOC],
                                y3T[:, ft * TLOC: ft * TLOC + TLOC],
                                PB[:, 0:2048], op=ALU.mult),
                              reads=[f"hq{ft}", "pcb"], writes=[f"hq{ft}", f"y3_{ft}"],
                              tag=ekk + f"w{gft}")
                for dt in range(ND):
                    for tb in range(NB4):
                        def emit_down(h, dt=dt, tb=tb, e=e, q4=q4):
                            for ft in range(NFQ):
                                ins = h.matmul(
                                    PA[:, 1024:1536],
                                    w2b[:, ft * 1024 + dt * 128: ft * 1024 + dt * 128 + 128],
                                    y3T[:, ft * TLOC + tb * 512: ft * TLOC + tb * 512 + 512],
                                    start=(ft == 0), stop=(ft == NFQ - 1 and not (e >= 0 and q4 == 0)))
                            if e >= 0 and q4 == 0:
                                ins = h.matmul(PA[:, 1024:1536], b2h[:, ts(dt, 128)],
                                               combT[0:8, ts(tb, 512)], start=False, stop=True)
                            return ins
                        A("tensor", emit_down,
                          reads=[f"hq{x}" for x in range(NFQ)] + ["w2bk", "b2h", "combT"],
                          writes=["pd"], tag=ekk + f"d{q4}.{dt}.{tb}")
                        A("vector", lambda h, dt=dt, tb=tb: h.tensor_tensor(
                            accT[:, dt * TLOC + tb * 512: dt * TLOC + tb * 512 + 512],
                            accT[:, dt * TLOC + tb * 512: dt * TLOC + tb * 512 + 512],
                            PA[:, 1024:1536], op=ALU.add),
                          reads=["pd", "accT"], writes=["accT"], tag=ekk + f"a{q4}.{dt}.{tb}")
        for dt in range(ND):
            A("sync", lambda h, dt=dt: h.dma_start(out=outd[ts(dt, 128), :], in_=accT[:, ts(dt, TLOC)]),
              reads=["accT"], writes=["stored"], dma=True, tag=f"p7sto{dt}")
        fin = A("sync", lambda h: h.nop(nofuse=True), reads=["stored"], tag="fin")
        for op in S.ops:
            if op.dma and op.tag.startswith("p7sto") and op not in fin.deps:
                fin.deps.append(op)


def emit_attn_body(S, nc, env):
    A = env["A"]
    xchunk = env["xchunk"]
    wq = env["wq"]
    wk = env["wk"]
    wv = env["wv"]
    biasq = env["biasq"]
    bias_o = env["bias_o"]
    fq = env["fq"]
    onesb = env["onesb"]
    idb = env["idb"]
    pihc = env["pihc"]
    kpre = env["kpre"]
    kswp = env["kswp"]
    qpre = env["qpre"]
    qswp = env["qswp"]
    vpre = env["vpre"]
    ka = env["ka"]
    kb = env["kb"]
    qa = env["qa"]
    qb = env["qb"]
    V = env["V"]
    th = env["th"]
    kfb = env["kfb"]
    kib = env["kib"]
    rr = env["rr"]
    cosr = env["cosr"]
    sinr = env["sinr"]
    tmpa = env["tmpa"]
    tmpb = env["tmpb"]
    ksqs = env["ksqs"]
    sp512 = env["sp512"]
    kmaxp = env["kmaxp"]
    kms = env["kms"]
    q2s = env["q2s"]
    qnmA = env["qnmA"]
    qnmB = env["qnmB"]
    P0 = env["P0"]
    P1v = env["P1v"]
    sr = env["sr"]
    rsf = env["rsf"]
    rs = env["rs"]
    rsb = env["rsb"]
    aoT = env["aoT"]
    wo = env["wo"]
    ostg = env["ostg"]
    PA = env["PA"]
    PB = env["PB"]
    b_qkvT = env["b_qkvT"]
    b_outT = env["b_outT"]
    fcolsd = env["fcolsd"]
    onesbf_d = env["onesbf_d"]
    idbf_d = env["idbf_d"]
    posrep = env["posrep"]
    w_qkv = env["w_qkv"]
    w_outd = env["w_outd"]
    xn_view = env["xn_view"]
    idxa = env["idxa"]
    at0_i = env["at0_i"]
    gather = env["gather"]
    w_out = w_outd
    at0_st = at0_i
    span = 0
    NP = 8; CH = 1024; NCH = L // CH; NMT = 64; NRB = 4

    A("gpsimd", lambda h: h.dma_start(out=biasq[:], in_=b_qkvT), writes=["biasq"], dma=True, tag="ldbq")
    A("gpsimd", lambda h: h.dma_start(out=bias_o[:], in_=b_outT), writes=["bias_o"], dma=True, tag="ldbo")
    A("gpsimd", lambda h: h.dma_start(out=fq[:], in_=fcolsd), writes=["fq"], dma=True, tag="ldfq")
    A("gpsimd", lambda h: h.dma_start(out=onesb[:], in_=onesbf_d), writes=["onesb"], dma=True, tag="ldob")
    A("gpsimd", lambda h: h.dma_start(out=idb[:], in_=idbf_d), writes=["idb"], dma=True, tag="ldid")
    A("gpsimd", lambda h: h.memset(pihc[:], PIH), writes=["pihc"], tag="mspih")
    A("gpsimd", lambda h: h.memset(ka[64:65, :], 1.0), writes=["karow"], tag="mska")
    A("gpsimd", lambda h: h.memset(kb[64:65, :], 1.0), writes=["kbrow"], tag="mskb")
    vap = V[:].rearrange("p (m c) -> p m c", c=130)
    A("gpsimd", lambda h: h.memset(vap[:, :, 64:65], 1.0), writes=["Vones"], tag="msv1")
    A("gpsimd", lambda h: h.memset(vap[:, :, 129:130], 1.0), writes=["Vones"], tag="msv2")

    PSL = [PA[:, 0:1024], PA[:, 1024:2048]]
    PSO = [PB[:, 0:512], PB[:, 512:1024]]
    PRB = PB[:, 1024:1536]
    PST = PB[:, 1536:2048]
    Pbuf = [P0, P1v]

    for p_ in range(NP):
        c0 = p_ * 128
        pk = f"p{p_}"
        for nm, wt, coff in (("wq", wq, c0), ("wk", wk, D + c0), ("wv", wv, 2 * D + c0)):
            A("gpsimd", lambda h, wt=wt, coff=coff: h.dma_start(
                out=wt[:], in_=w_qkv[:, coff:coff + 128].rearrange("(dt p) c -> p dt c", p=128)),
              writes=[nm], dma=True, tag=pk + "ld" + nm)

        for c_ in range(NCH):
            ck = pk + f"c{c_}"
            ccols = ts(c_, CH)
            is_q = (c_ // 2 == span)
            qoff = (c_ % 2) * CH
            for dt_i in range(ND):
                A("gpsimd", gather(xchunk[:, ts(dt_i, CH)], xn_view,
                                   idxa[:, c_ * 8 + dt_i:c_ * 8 + dt_i + 1]),
                  reads=["xnoutd", "idxa"], writes=["xchunk"], dma=True,
                  tag=ck + f"ldx{dt_i}")
            streams = [("k", wk, kpre, 8), ("v", wv, vpre, 16)]
            if is_q:
                streams.append(("q", wq, qpre, 0))
            qki = 0
            for tb in range(CH // 512):
                for nm, wt, dst, boff in streams:
                    j4 = qki % 4
                    qki += 1
                    A("tensor", lambda h, wt=wt, tb=tb, j4=j4: [
                        h.matmul(PA[:, ts(j4, 512)], wt[:, ts(dt, 128)],
                                 xchunk[:, dt * CH + tb * 512: dt * CH + tb * 512 + 512],
                                 start=(dt == 0), stop=(dt == ND - 1))
                        for dt in range(ND)][-1],
                      reads=["xchunk", "w" + nm], writes=[f"qps{j4}"], tag=ck + f"mm{nm}{tb}")
                    A("scalar", lambda h, dst=dst, tb=tb, boff=boff, j4=j4, p_=p_: h.activation(
                        dst[:, ts(tb, 512)], PA[:, ts(j4, 512)], AF.Identity,
                        bias=biasq[:, boff + p_:boff + p_ + 1]),
                      reads=[f"qps{j4}"], writes=[nm + "pre"], sreads=["biasq"],
                      tag=ck + f"ev{nm}{tb}")
            for mt_ in range(CH // 128):
                gmt = c_ * (CH // 128) + mt_
                A("tensor", lambda h, mt_=mt_: h.transpose(
                    PST[:, 0:64].bitcast(BF16), vpre[:, ts(mt_, 128)], idb[:]),
                  reads=["vpre", "idb"], writes=["PST"], tag=ck + f"tr{mt_}")
                A("vector", lambda h, gmt=gmt: h.tensor_copy(
                    V[:, gmt * 130: gmt * 130 + 130].rearrange("p (a c) -> p a c", a=2)[:, :, 0:64],
                    PST[:, 0:64].bitcast(BF16).rearrange("p (a c) -> p a c", a=2)),
                  reads=["PST"], writes=["Vdat"], tag=ck + f"te{mt_}")
            swaps = [("k", kpre, kswp)] + ([("q", qpre, qswp)] if is_q else [])
            for nm, src_, dst_ in swaps:
                for g2 in range(2):
                    s0, s1 = 64 * g2, 64 * g2 + 32
                    A("gpsimd", lambda h, src_=src_, dst_=dst_, s0=s0, s1=s1: h.dma_start(
                        out=dst_[s0:s0 + 32, :], in_=src_[s1:s1 + 32, :]),
                      reads=[nm + "pre"], writes=[nm + f"swp{g2}a"], dma=True, tag=ck + f"sw{nm}{g2}a")
                    A("gpsimd", lambda h, src_=src_, dst_=dst_, s0=s0, s1=s1: h.dma_start(
                        out=dst_[s1:s1 + 32, :], in_=src_[s0:s0 + 32, :]),
                      reads=[nm + "pre"], writes=[nm + f"swp{g2}b"], dma=True, tag=ck + f"sw{nm}{g2}b")
            A("gpsimd", lambda h, c_=c_: h.dma_start(out=th[:], in_=posrep[:, ts(c_, CH)]),
              writes=["th"], dma=True, tag=ck + "io")
            A("vector", lambda h, p_=p_: h.tensor_scalar(th[:], th[:], fq[:, p_:p_ + 1], None, op0=ALU.mult),
              reads=["th"], writes=["th"], sreads=["fq"], tag=ck + "th")
            A("vector", lambda h: h.tensor_scalar(kfb[:], th[:], INV2PI, None, op0=ALU.mult),
              reads=["th"], writes=["kfb"], tag=ck + "kf")
            A("vector", lambda h: h.tensor_copy(kib[:], kfb[:]), reads=["kfb"], writes=["kib"], tag=ck + "ki")
            A("vector", lambda h: h.tensor_copy(kfb[:], kib[:]), reads=["kib"], writes=["kfb"], tag=ck + "kf2")
            A("vector", lambda h: h.scalar_tensor_tensor(rr[:], kfb[:], -RR_C1, th[:],
                                                         op0=ALU.mult, op1=ALU.add),
              reads=["kfb", "th"], writes=["rr"], tag=ck + "r1")
            A("vector", lambda h: h.scalar_tensor_tensor(rr[:], kfb[:], -RR_C2, rr[:],
                                                         op0=ALU.mult, op1=ALU.add),
              reads=["kfb", "rr"], writes=["rr"], tag=ck + "r2")
            A("vector", lambda h: h.scalar_tensor_tensor(rr[:], kfb[:], -RR_C3, rr[:],
                                                         op0=ALU.mult, op1=ALU.add),
              reads=["kfb", "rr"], writes=["rr"], tag=ck + "r3")
            for fi in range(2):
                A("vector", lambda h: h.tensor_scalar(kfb[:], rr[:], PI_, None, op0=ALU.is_gt),
                  reads=["rr"], writes=["kfb"], tag=ck + f"fg{fi}")
                A("vector", lambda h: h.scalar_tensor_tensor(rr[:], kfb[:], -TWOPI, rr[:],
                                                             op0=ALU.mult, op1=ALU.add),
                  reads=["kfb", "rr"], writes=["rr"], tag=ck + f"fs{fi}")
                A("vector", lambda h: h.tensor_scalar(kfb[:], rr[:], -PI_, None, op0=ALU.is_lt),
                  reads=["rr"], writes=["kfb"], tag=ck + f"fh{fi}")
                A("vector", lambda h: h.scalar_tensor_tensor(rr[:], kfb[:], TWOPI, rr[:],
                                                             op0=ALU.mult, op1=ALU.add),
                  reads=["kfb", "rr"], writes=["rr"], tag=ck + f"ft{fi}")
            A("vector", lambda h: h.tensor_scalar(rr[:], rr[:], -PI_, PI_, op0=ALU.max, op1=ALU.min),
              reads=["rr"], writes=["rr"], tag=ck + "cl")
            A("scalar", lambda h: h.activation(sinr[:], rr[:], AF.Sin),
              reads=["rr"], writes=["sinr"], tag=ck + "sin")
            A("scalar", lambda h: h.activation(th[:], rr[:], AF.Abs),
              reads=["rr"], writes=["th"], tag=ck + "abs")
            A("scalar", lambda h: h.activation(cosr[:], th[:], AF.Sin, scale=-1.0,
                                               bias=pihc[:, 0:1]),
              reads=["th"], writes=["cosr"], sreads=["pihc"], tag=ck + "cos")
            combos = [("k", kpre, kswp, ka, kb, ccols)] + (
                [("q", qpre, qswp, qa, qb, slice(qoff, qoff + CH))] if is_q else [])
            for nm, pre, swp, ta, tb_, ocols in combos:
                A("vector", lambda h, pre=pre: h.tensor_tensor(tmpa[:], pre[:], cosr[:], op=ALU.mult),
                  reads=[nm + "pre", "cosr"], writes=["tmpa"], tag=ck + f"cb1{nm}")
                A("vector", lambda h, swp=swp: h.tensor_tensor(tmpb[:], swp[:], sinr[:], op=ALU.mult),
                  reads=[nm + "swp0a", nm + "swp0b", nm + "swp1a", nm + "swp1b", "sinr"],
                  writes=["tmpb"], tag=ck + f"cb2{nm}")
                A("vector", lambda h, ta=ta, ocols=ocols: h.tensor_tensor(
                    ta[0:64, ocols], tmpa[0:64, :], tmpb[0:64, :], op=ALU.add),
                  reads=["tmpa", "tmpb"], writes=[nm + "rotA"], tag=ck + f"cb3{nm}")
                A("vector", lambda h, tb_=tb_, ocols=ocols: h.tensor_tensor(
                    tb_[0:64, ocols], tmpa[64:128, :], tmpb[64:128, :], op=ALU.add),
                  reads=["tmpa", "tmpb"], writes=[nm + "rotB"], tag=ck + f"cb4{nm}")
            for hh in range(2):
                kx = (ka, kb)[hh]
                A("scalar", lambda h, kx=kx, ccols=ccols: h.activation(
                    ksqs[:], kx[0:64, ccols], AF.Square),
                  reads=["krot" + "AB"[hh]], writes=["ksqs"], tag=ck + f"ksq{hh}")
                for piece in range(CH // 512):
                    pidx = c_ * (CH // 512) + piece
                    A("tensor", lambda h, piece=piece: h.matmul(
                        PRB[0:1, 0:512], onesb[0:64, 0:1], ksqs[0:64, ts(piece, 512)],
                        start=True, stop=True),
                      reads=["ksqs", "onesb"], writes=["prb"], tag=ck + f"k2m{hh}.{piece}")
                    A("scalar", lambda h: h.activation(sp512[:], PRB[0:1, 0:512], AF.Copy),
                      reads=["prb"], writes=["sp512"], tag=ck + f"k2e{hh}.{piece}")
                    A("vector", lambda h, hh=hh, pidx=pidx: h.reduce_max(
                        kmaxp[0:1, hh * 16 + pidx:hh * 16 + pidx + 1], sp512[:],
                        axis=mybir.AxisListType.X),
                      reads=["sp512"], writes=["kmaxp"], tag=ck + f"k2r{hh}.{pidx}")
            if is_q:
                for hh in range(2):
                    qx = (qa, qb)[hh]
                    A("scalar", lambda h, qx=qx, qoff=qoff: h.activation(
                        ksqs[:], qx[0:64, qoff:qoff + CH], AF.Square),
                      reads=["qrot" + "AB"[hh]], writes=["ksqs"], tag=ck + f"qsq{hh}")
                    for piece in range(CH // 512):
                        A("tensor", lambda h, piece=piece: h.matmul(
                            PRB[0:1, 0:512], onesb[0:64, 0:1], ksqs[0:64, ts(piece, 512)],
                            start=True, stop=True),
                          reads=["ksqs", "onesb"], writes=["prb"], tag=ck + f"q2m{hh}.{piece}")
                        A("scalar", lambda h, piece=piece, qoff=qoff, hh=hh: h.activation(
                            q2s[0:1, qoff + piece * 512: qoff + piece * 512 + 512]
                            if hh == 0 else
                            q2s[0:1, qoff + piece * 512: qoff + piece * 512 + 512],
                            PRB[0:1, 0:512], AF.Copy),
                          reads=["prb"], writes=["q2s" + "AB"[hh]], tag=ck + f"q2e{hh}.{piece}")
                    A("scalar", lambda h, qoff=qoff, hh=hh: h.activation(
                        (qnmA if hh == 0 else qnmB)[0:1, qoff:qoff + CH],
                        q2s[0:1, qoff:qoff + CH], AF.Sqrt),
                      reads=["q2s" + "AB"[hh]], writes=["qnm" + "AB"[hh]], tag=ck + f"qnm{hh}")

        for hh in range(2):
            A("vector", lambda h, hh=hh: h.reduce_max(
                kms[0:1, hh:hh + 1], kmaxp[0:1, hh * 16:hh * 16 + 16],
                axis=mybir.AxisListType.X),
              reads=["kmaxp"], writes=["kms2"], tag=pk + f"kmx{hh}")
        A("scalar", lambda h: h.activation(kms[:], kms[:], AF.Sqrt),
          reads=["kms2"], writes=["kms"], tag=pk + "kmsq")
        for hh in range(2):
            qx = (qa, qb)[hh]
            qn = (qnmA, qnmB)[hh]
            A("vector", lambda h, qx=qx, qn=qn, hh=hh: h.tensor_scalar(
                qx[64:65, :], qn[:], kms[0:1, hh:hh + 1], -1.0,
                op0=ALU.mult, op1=ALU.mult),
              reads=["qnm" + "AB"[hh]], writes=["q" + "ab"[hh] + "crow"],
              sreads=["kms"], tag=pk + f"crow{hh}")

        for hh in range(2):
            kx = (ka, kb)[hh]
            qx = (qa, qb)[hh]
            krd = ["krot" + "AB"[hh], ("karow", "kbrow")[hh]]
            qrd = ["qrot" + "AB"[hh], "q" + "ab"[hh] + "crow"]
            for rb in range(NRB):
                ob = rb % 2
                rcols = ts(rb, 512)

                def emit_qk(h, mp, kx=kx, qx=qx, rcols=rcols):
                    for j in range(2):
                        mt = 2 * mp + j
                        ins = h.matmul(PSL[mp % 2][:, ts(j, 512)], kx[:, ts(mt, 128)],
                                       qx[:, rcols], start=True, stop=True)
                    return ins

                def emit_exp(h, mp):
                    return h.activation(Pbuf[mp % 2][:], PSL[mp % 2][:], AF.Exp, scale=0.125)

                def emit_av(h, mp, hh=hh, ob=ob):
                    for j in range(2):
                        mt = 2 * mp + j
                        ins = h.matmul(PSO[ob][0:65, :],
                                       V[:, mt * 130 + 65 * hh: mt * 130 + 65 * hh + 65],
                                       Pbuf[mp % 2][:, ts(j, 512)],
                                       start=(mt == 0), stop=(mt == NMT - 1))
                    return ins

                hk = pk + f"h{hh}rb{rb}"
                PK = [["qps0", "qps1"], ["qps2", "qps3"]]
                A("tensor", lambda h, f=emit_qk: f(h, 0), reads=krd + qrd,
                  writes=PK[0], tag=hk + "qk0")
                A("tensor", lambda h, f=emit_qk: f(h, 1), reads=krd + qrd,
                  writes=PK[1], tag=hk + "qk1")
                for mp in range(32):
                    A("scalar", lambda h, mp=mp, f=emit_exp: f(h, mp),
                      reads=PK[mp % 2], writes=[f"P{mp % 2}"], tag=hk + f"ex{mp}")
                    A("tensor", lambda h, mp=mp, f=emit_av: f(h, mp),
                      reads=[f"P{mp % 2}", "Vdat", "Vones"], writes=[f"pso{ob}"], tag=hk + f"av{mp}")
                    if mp + 2 < 32:
                        A("tensor", lambda h, mp=mp, f=emit_qk: f(h, mp + 2),
                          reads=krd + qrd, writes=PK[mp % 2], tag=hk + f"qk{mp + 2}")
                A("scalar", lambda h, ob=ob: h.activation(sr[:], PSO[ob][64:65, :], AF.Copy),
                  reads=[f"pso{ob}"], writes=["sr"], tag=hk + "sr")
                A("vector", lambda h: h.reciprocal(rsf[:], sr[:]),
                  reads=["sr"], writes=["rsf"], tag=hk + "rsf")
                A("vector", lambda h: h.tensor_copy(rs[:], rsf[:]),
                  reads=["rsf"], writes=["rs"], tag=hk + "rs")
                A("tensor", lambda h: h.matmul(PRB[0:64, 0:512], onesb[0:1, 0:64], rs[:],
                                               start=True, stop=True),
                  reads=["rs", "onesb"], writes=["prb"], tag=hk + "rb")
                A("scalar", lambda h: h.activation(rsb[:], PRB[0:64, 0:512], AF.Copy),
                  reads=["prb"], writes=["rsb"], tag=hk + "rc")
                A("vector", lambda h, ob=ob, hh=hh, rb=rb, p_=p_: h.tensor_tensor(
                    aoT[64 * hh:64 * hh + 64, p_ * TLOC + rb * 512: p_ * TLOC + rb * 512 + 512],
                    PSO[ob][0:64, :], rsb[:], op=ALU.mult),
                  reads=[f"pso{ob}", "rsb"], writes=["aoT"], tag=hk + "oe")

    A("gpsimd", lambda h: h.dma_start(
        out=wo[:], in_=w_out.rearrange("(pp a) (dt c) -> a pp dt c", a=128, c=128)),
      writes=["wo"], dma=True, tag="ldwo")
    for dt in range(ND):
        for g in range(4):
            j2 = (dt * 4 + g) % 2
            A("tensor", lambda h, dt=dt, g=g: [
                h.matmul(PST[:, 0:512], wo[:, (pp * 8 + dt) * 128:(pp * 8 + dt) * 128 + 128],
                         aoT[:, pp * TLOC + g * 512: pp * TLOC + g * 512 + 512],
                         start=(pp == 0), stop=(pp == NP - 1))
                for pp in range(NP)][-1],
              reads=["aoT", "wo"], writes=["PST"], tag=f"op{dt}.{g}")
            A("scalar", lambda h, dt=dt, j2=j2: h.activation(
                ostg[j2][:], PST[:, 0:512], AF.Identity, bias=bias_o[:, dt:dt + 1]),
              reads=["PST"], writes=[f"ostg{j2}"], sreads=["bias_o"], tag=f"oe{dt}.{g}")
            A("sync", lambda h, dt=dt, g=g, j2=j2: h.dma_start(
                out=at0_st[ts(dt, 128), ts(g, 512)], in_=ostg[j2][:]),
              reads=[f"ostg{j2}"], writes=["at0d"], dma=True, tag=f"p2st{dt}.{g}")

def build_fused():
    nc = bass.Bass("TRN2", target_bir_lowering=False, num_devices=NCORES)
    EI, EO = "ExternalInput", "ExternalOutput"
    dt_ = nc.dram_tensor
    x0Td = dt_("x0T_loc", [D, TLOC], F32, kind=EI).ap()
    posrep = dt_("posrep", [128, L], F32, kind=EI).ap()
    w_qkv = dt_("w_qkv", [D, 3 * D], F32, kind=EI).ap()
    b_qkvT = dt_("b_qkvT", [128, 24], F32, kind=EI).ap()
    w_outd = dt_("w_out", [D, D], F32, kind=EI).ap()
    b_outT = dt_("b_outT", [128, 8], F32, kind=EI).ap()
    fcolsd = dt_("freqcols2", [128, 8], F32, kind=EI).ap()
    onesd = dt_("ones128", [128, 128], F32, kind=EI).ap()
    onesbf_d = dt_("onesbf", [128, 128], BF16, kind=EI).ap()
    idbf_d = dt_("idbf", [128, 128], BF16, kind=EI).ap()
    gwd = dt_("gate_wT", [32, D, 128], F32, kind=EI).ap()
    uwd = dt_("up_wT", [32, D, 128], F32, kind=EI).ap()
    dwd = dt_("down_wT", [32, 128, D], F32, kind=EI).ap()
    sgd = dt_("sgate_wT", [D, 512], F32, kind=EI).ap()
    sud = dt_("sup_wT", [D, 512], F32, kind=EI).ap()
    sdd = dt_("sdown_wT", [512, D], F32, kind=EI).ap()
    rw0d = dt_("router_wT", [D, 32], F32, kind=EI).ap()
    rb0d = dt_("router_b", [1, 32], F32, kind=EI).ap()
    ohd32 = dt_("onehot32", [32, 32 * 128], BF16, kind=EI).ap()
    oh16d = dt_("onehot16", [16, 16 * 128], BF16, kind=EI).ap()
    oh8d = dt_("onehot8", [8, 8 * 128], BF16, kind=EI).ap()
    wind = dt_("W_in", [D, 2 * DI], F32, kind=EI).ap()
    lngd = dt_("ln_gT", [128, ND], F32, kind=EI).ap()
    lnbd = dt_("ln_bT", [128, ND], F32, kind=EI).ap()
    cwd = dt_("conv_wT", [128, 16 * 4], F32, kind=EI).ap()
    cbd = dt_("conv_bT", [128, 16], F32, kind=EI).ap()
    wxd = dt_("W_x", [DI, 96], F32, kind=EI).ap()
    wdtd = dt_("W_dt", [DTR, DI], F32, kind=EI).ap()
    bdtd = dt_("b_dtT", [128, 16], F32, kind=EI).ap()
    aqd = dt_("Aq", [128, 64], F32, kind=EI).ap()
    dcold = dt_("DcolT", [128, 16], F32, kind=EI).ap()
    wo1d = dt_("W_out1", [DI, D], F32, kind=EI).ap()
    sw1d = dt_("shared_w1T", [D, 4096], F32, kind=EI).ap()
    sb1d = dt_("shared_b1T", [128, 32], F32, kind=EI).ap()
    sw2d = dt_("shared_w2T", [4096, D], F32, kind=EI).ap()
    sb2d = dt_("shared_b2T", [128, 8], F32, kind=EI).ap()
    rw1d = dt_("router1_wT", [D, 8], F32, kind=EI).ap()
    rb1d = dt_("router1_b", [1, 8], F32, kind=EI).ap()
    w1d = dt_("spec_w1T", [8, D, 4096], F32, kind=EI).ap()
    b1d = dt_("spec_b1T", [128, 8 * 32], F32, kind=EI).ap()
    w2d = dt_("spec_w2T", [8, 4096, D], F32, kind=EI).ap()
    b2d = dt_("spec_b2h", [8, D], F32, kind=EI).ap()
    U32 = mybir.dt.uint32
    idxad = dt_("idx_attn", [128, 64], U32, kind=EI).ap()
    idxhd = dt_("idx_halo", [128, 16], U32, kind=EI).ap()
    idxsd = dt_("idx_scan", [128, 32], U32, kind=EI).ap()
    idxbd = dt_("idx_bc", [16, 16], U32, kind=EI).ap()
    idxyd = dt_("idx_y", [128, 16], U32, kind=EI).ap()
    outd = dt_("outT", [D, TLOC], F32, kind=EO).ap()
    IN = "Internal"
    ag_xn_in = dt_("ag_xn_in", [D, TLOC], BF16, kind=IN)
    ag_xn_out = dt_("ag_xn_out", [NCORES * D, TLOC], BF16, kind=IN)
    ag_tl_in = dt_("ag_tl_in", [DI, 8], BF16, kind=IN)
    ag_tl_out = dt_("ag_tl_out", [NCORES * DI, 8], BF16, kind=IN)
    ag_xc_in = dt_("ag_xc_in", [DI, TLOC], BF16, kind=IN)
    ag_xc_out = dt_("ag_xc_out", [NCORES * DI, TLOC], BF16, kind=IN)
    ag_dt_in = dt_("ag_dt_in", [DI, TLOC], F16, kind=IN)
    ag_dt_out = dt_("ag_dt_out", [NCORES * DI, TLOC], F16, kind=IN)
    ag_bc_in = dt_("ag_bc_in", [32, TLOC], F32, kind=IN)
    ag_bc_out = dt_("ag_bc_out", [NCORES * 32, TLOC], F32, kind=IN)
    ag_y_in = dt_("ag_y_in", [512, L], BF16, kind=IN)
    ag_y_out = dt_("ag_y_out", [NCORES * 512, L], BF16, kind=IN)
    at0_i = dt_("at0_i", [D, TLOC], BF16, kind=IN).ap()
    x1_i = dt_("x1_i", [D, TLOC], F32, kind=IN).ap()
    xn2_i = dt_("xn2_i", [D, TLOC], BF16, kind=IN).ap()
    xh_i = dt_("xh_i", [DI, TLOC], BF16, kind=IN).ap()
    sz_i = dt_("sz_i", [DI, TLOC], BF16, kind=IN).ap()

    S = Sched()
    A = S.add
    RG = [list(range(NCORES))]

    def gather(out_ap, in_ap, idx_ap, oobmax=None):
        def f(h, out_ap=out_ap, in_ap=in_ap, idx_ap=idx_ap):
            import bass_rust as _br
            return h.indirect_dma_start(
                out=out_ap, out_offset=None, in_=in_ap,
                in_offset=_br.IndirectOffsetOnAxis(ap=idx_ap, axis=0),
                bounds_check=oobmax, oob_is_err=False if oobmax is not None else True)
        return f

    with contextlib.ExitStack() as octx:
        sbo = lambda *a: octx.enter_context(nc.sbuf_tensor(*a))
        PA = octx.enter_context(nc.psum_tensor("PA", [128, 2048], F32))
        PB = octx.enter_context(nc.psum_tensor("PB", [128, 2048], F32))
        idxa = sbo("idxa", [128, 64], U32)
        idxh = sbo("idxh", [128, 16], U32)
        idxs = sbo("idxs", [128, 32], U32)
        idxb = sbo("idxb", [16, 16], U32)
        idxy = sbo("idxy", [128, 16], U32)
        for nm, dst, srcd in [("idxa", idxa, idxad), ("idxh", idxh, idxhd),
                              ("idxs", idxs, idxsd), ("idxb", idxb, idxbd),
                              ("idxy", idxy, idxyd)]:
            A("gpsimd", lambda h, dst=dst, srcd=srcd: h.dma_start(out=dst[:], in_=srcd),
              writes=[nm], dma=True, tag="ld" + nm)

        with contextlib.ExitStack() as ctx:
            sb = lambda *a: ctx.enter_context(nc.sbuf_tensor(*a))
            xt = [sb(f"p1xt{i}", [128, TLOC], F32) for i in range(ND)]
            xout = sb("p1xout", [128, ND * TLOC], BF16)
            work = mk_ln_work(nc, ctx, TLOC, pref="p1", psum=(PA, PB))
            A("sync", lambda h: h.dma_start(out=work["onesc"][:], in_=onesd),
              writes=["onesc"], dma=True, tag="p1ldones")
            for i in range(ND):
                A("sync", lambda h, i=i: h.dma_start(out=xt[i][:], in_=x0Td[ts(i, 128), :]),
                  writes=[f"x{i}"], dma=True, tag=f"p1ld{i}")
            phase_ln_feature_major(S, nc, xt, xout, work, TLOC, key="")
            A("sync", lambda h: h.dma_start(
                out=ag_xn_in.ap().rearrange("(dt p) t -> p dt t", p=128),
                in_=xout[:].rearrange("p (dt t) -> p dt t", t=TLOC)),
              reads=["xout"], writes=["xnind"], dma=True, tag="p1st")
            S.barrier()
        A("gpsimd", lambda h: h.collective_compute(
            "AllGather", ALU.bypass, replica_groups=RG,
            ins=[ag_xn_in.ap().opt()], outs=[ag_xn_out.ap().opt()]),
          reads=["xnind"], writes=["xnoutd"], cc=True, tag="CC_xn")

        NP = 8
        CH = 1024
        NCH = L // CH
        NMT = 64
        NRB = 4
        xn_view = ag_xn_out.ap().rearrange("r (h c) -> (r h) c", c=CH)
        with contextlib.ExitStack() as ctx:
            sb = lambda *a: ctx.enter_context(nc.sbuf_tensor(*a))
            xchunk = sb("xchunk", [128, ND * CH], BF16)
            wq = sb("wq", [128, ND * 128], BF16)
            wk = sb("wk", [128, ND * 128], BF16)
            wv = sb("wv", [128, ND * 128], BF16)
            biasq = sb("biasq", [128, 24], F32)
            bias_o = sb("bias_o", [128, 8], F32)
            fq = sb("fq", [128, 8], F32)
            onesb = sb("onesb", [128, 128], BF16)
            idb = sb("idb", [128, 128], BF16)
            pihc = sb("pihc", [128, 1], F32)
            kpre = sb("kpre", [128, CH], BF16)
            kswp = sb("kswp", [128, CH], BF16)
            qpre = sb("qpre", [128, CH], BF16)
            qswp = sb("qswp", [128, CH], BF16)
            vpre = sb("vpre", [128, CH], BF16)
            ka = sb("ka", [65, L], BF16)
            kb = sb("kb", [65, L], BF16)
            qa = sb("qa", [65, TLOC], BF16)
            qb = sb("qb", [65, TLOC], BF16)
            V = sb("V", [128, NMT * 130], BF16)
            th = sb("th", [128, CH], F32)
            kfb = sb("kfb", [128, CH], F32)
            kib = sb("kib", [128, CH], I32)
            rr = sb("rr", [128, CH], F32)
            cosr = sb("cosr", [128, CH], BF16)
            sinr = sb("sinr", [128, CH], BF16)
            tmpa = sb("tmpa", [128, CH], BF16)
            tmpb = sb("tmpb", [128, CH], BF16)
            ksqs = sb("ksqs", [64, CH], BF16)
            sp512 = sb("sp512", [1, 512], F32)
            kmaxp = sb("kmaxp", [1, 64], F32)
            kms = sb("kms", [1, 2], F32)
            q2s = sb("q2s", [1, TLOC], BF16)
            qnmA = sb("qnmA", [1, TLOC], BF16)
            qnmB = sb("qnmB", [1, TLOC], BF16)
            P0 = sb("P0", [128, 1024], BF16)
            P1v = sb("P1v", [128, 1024], BF16)
            sr = sb("sr", [1, 512], F32)
            rsf = sb("rsf", [1, 512], F32)
            rs = sb("rs", [1, 512], BF16)
            rsb = sb("rsb", [64, 512], BF16)
            aoT = sb("aoT", [128, NP * TLOC], BF16)
            wo = sb("wo", [128, 64 * 128], BF16)
            ostg = [sb(f"ostg{j}", [128, 512], BF16) for j in range(2)]
            emit_attn_body(S, nc, locals())
            S.barrier()

        with contextlib.ExitStack() as ctx:
            sb = lambda *a: ctx.enter_context(nc.sbuf_tensor(*a))
            emit_ffn0_body(S, nc, ctx, dict(
                PA=PA, PB=PB,
                xnT_src=ag_xn_in.ap(), xn_srckey="xnind",
                x0Td=x0Td, at0d=at0_i, at0key="at0d",
                gwd=gwd, uwd=uwd, dwd=dwd, sgd=sgd, sud=sud, sdd=sdd,
                rwd=rw0d, rbd=rb0d, ohd=ohd32, onesbf_d=onesbf_d, idbf_d=idbf_d,
                x1Td=x1_i, x1key="x1d"))
            S.barrier()

        with contextlib.ExitStack() as ctx:
            emit_front_a_body(S, nc, ctx, dict(
                PA=PA, PB=PB,
                x1Td=x1_i, x1key="x1d", wind=wind, lngd=lngd, lnbd=lnbd,
                onesd=onesd, onesbd=onesbf_d,
                xn2d=xn2_i, xhd=xh_i, szd=sz_i, tail=ag_tl_in.ap()))
            S.barrier()
        A("gpsimd", lambda h: h.collective_compute(
            "AllGather", ALU.bypass, replica_groups=RG,
            ins=[ag_tl_in.ap().opt()], outs=[ag_tl_out.ap().opt()]),
          reads=["tld"], writes=["tloutd"], cc=True, tag="CC_tl")

        with contextlib.ExitStack() as ctx:
            emit_front_b_body(S, nc, ctx, dict(
                PA=PA, PB=PB, gather=gather,
                xhd=xh_i, xhkey="xhd", tl_out=ag_tl_out.ap(), idxh=idxh,
                cwd=cwd, cbd=cbd, wxd=wxd, wdtd=wdtd, bdtd=bdtd,
                xcd=ag_xc_in.ap(), dtd=ag_dt_in.ap(), bcd=ag_bc_in.ap()))
            S.barrier()
        for nm, ain, aout, rk, wk_ in [("xc", ag_xc_in, ag_xc_out, "xcd", "xcoutd"),
                                       ("dtq", ag_dt_in, ag_dt_out, "dtd", "dtoutd"),
                                       ("bc", ag_bc_in, ag_bc_out, "bcd", "bcoutd")]:
            A("gpsimd", lambda h, ain=ain, aout=aout: h.collective_compute(
                "AllGather", ALU.bypass, replica_groups=RG,
                ins=[ain.ap().opt()], outs=[aout.ap().opt()]),
              reads=[rk], writes=[wk_], cc=True, tag="CC_" + nm)

        with contextlib.ExitStack() as ctx:
            emit_scan_body(S, nc, ctx, dict(
                PA=PA, PB=PB, gather=gather,
                xc_out=ag_xc_out.ap().rearrange("r (h c) -> (r h) c", c=1024),
                dt_out=ag_dt_out.ap().rearrange("r (h c) -> (r h) c", c=1024),
                bc_out=ag_bc_out.ap().rearrange("r (h c) -> (r h) c", c=1024),
                idxs=idxs, idxb=idxb, aqd=aqd, oh16d=oh16d,
                yqd=ag_y_in.ap()))
            S.barrier()
        A("gpsimd", lambda h: h.collective_compute(
            "AllGather", ALU.bypass, replica_groups=RG,
            ins=[ag_y_in.ap().opt()], outs=[ag_y_out.ap().opt()]),
          reads=["yqd"], writes=["youtd"], cc=True, tag="CC_y")

        with contextlib.ExitStack() as ctx:
            emit_back_body(S, nc, ctx, dict(
                PA=PA, PB=PB, gather=gather,
                x1Td=x1_i, xn2d=xn2_i, y_out=ag_y_out.ap().rearrange("r (h c) -> (r h) c", c=TLOC),
                idxy=idxy, xcd=ag_xc_in.ap(), szd=sz_i,
                dcold=dcold, wod=wo1d, sw1d=sw1d, sb1d=sb1d, sw2d=sw2d, sb2d=sb2d,
                rwd=rw1d, rbd=rb1d, w1d=w1d, b1d=b1d, w2d=w2d, b2d=b2d,
                oh8d=oh8d, onesbd=onesbf_d, idbd=idbf_d, outd=outd))
        S.run_block(nc, octx)
    return nc


def fused_core_inputs(p, c):
    b_, s = divmod(c, 4)
    q = s
    d = {}
    d["x0T_loc"] = np.ascontiguousarray(p["x0T"][b_, :, ts(s, TLOC)])
    d["posrep"] = np.ascontiguousarray(
        np.broadcast_to(np.roll(p["pos"], -s * TLOC), (128, L)))
    for k_ in ["w_qkv", "b_qkvT", "w_out", "b_outT", "freqcols2", "ones128", "onesbf",
               "idbf", "gate_wT", "up_wT", "down_wT", "sgate_wT", "sup_wT", "sdown_wT",
               "router_wT", "router_b", "onehot32", "onehot16", "onehot8", "W_in",
               "ln_gT", "ln_bT", "conv_wT", "conv_bT", "W_x", "W_dt", "b_dtT",
               "DcolT", "W_out1", "shared_w1T", "shared_b1T", "shared_w2T", "shared_b2T",
               "router1_wT", "router1_b", "spec_w1T", "spec_b1T", "spec_w2T", "spec_b2h"]:
        d[k_] = p[k_]
    dsl = slice(q * 512, (q + 1) * 512)
    d["Aq"] = np.ascontiguousarray(
        p["A_log"][dsl].reshape(4, 128, 16).transpose(1, 0, 2).reshape(128, 64))
    pvec = np.arange(128, dtype=np.uint64)
    ia = np.zeros((128, 64), dtype=np.uint32)
    for c_ in range(8):
        o = (c_ + 2 * s) % 8
        for dt in range(8):
            ia[:, c_ * 8 + dt] = (((4 * b_ + o // 2) * 1024 + dt * 128 + pvec) * 2
                                  + (o % 2)).astype(np.uint32)
    d["idx_attn"] = ia
    ih = np.zeros((128, 16), dtype=np.uint32)
    for dq in range(16):
        if s == 0:
            ih[:, dq] = 0x7FFFFFFF
        else:
            ih[:, dq] = ((4 * b_ + s - 1) * 2048 + dq * 128 + pvec).astype(np.uint32)
    d["idx_halo"] = ih
    isc = np.zeros((128, 32), dtype=np.uint32)
    for tb in range(8):
        for dqi in range(4):
            src_s = tb // 2
            isc[:, tb * 4 + dqi] = (((4 * b_ + src_s) * 2048 + q * 512 + dqi * 128 + pvec) * 2
                                    + (tb % 2)).astype(np.uint32)
    d["idx_scan"] = isc
    ib = np.zeros((16, 16), dtype=np.uint32)
    r16 = np.arange(16, dtype=np.uint64)
    for tb in range(8):
        src_s = tb // 2
        ib[:, tb] = (((4 * b_ + src_s) * 32 + r16) * 2 + (tb % 2)).astype(np.uint32)
        ib[:, 8 + tb] = (((4 * b_ + src_s) * 32 + 16 + r16) * 2 + (tb % 2)).astype(np.uint32)
    d["idx_bc"] = ib
    iy = np.zeros((128, 16), dtype=np.uint32)
    for dq in range(16):
        qq, dq8 = divmod(dq, 4)
        iy[:, dq] = (((4 * b_ + qq) * 512 + dq8 * 128 + pvec) * 4 + s).astype(np.uint32)
    d["idx_y"] = iy
    return d


def kernel(**inputs):
    p = prep_common(inputs)
    p["A_log"] = inputs["L1_attn_A_log"]
    nc = build_fused()
    ims = [fused_core_inputs(p, c) for c in range(NCORES)]
    res = run_bass_kernel_spmd(nc, ims, core_ids=list(range(NCORES)))
    out = np.empty((B, L, D), dtype=np.float32)
    for c in range(NCORES):
        b_, s = divmod(c, 4)
        out[b_, ts(s, TLOC), :] = res.results[c]["outT"].T
    return out


# version 06
# speedup vs baseline: 1.1126x; 1.1126x over previous
import contextlib
import numpy as np
import ml_dtypes
import concourse.bass as bass
import concourse.mybir as mybir
from concourse.bass_utils import run_bass_kernel_spmd

F32 = mybir.dt.float32
BF16 = mybir.dt.bfloat16
F16 = mybir.dt.float16
I32 = mybir.dt.int32
AF = mybir.ActivationFunctionType
ALU = mybir.AluOpType

NCORES = 8
B = 2
L = 8192
D = 1024
TLOC = 2048
H = 16
HD = 64
DI = 2048
NST = 16
DTR = 64
EPS = 1e-5
ND = D // 128

BF = np.dtype(ml_dtypes.bfloat16)


class Sem:
    def __init__(self, handle):
        self.h = handle
        self.n = 0

    def inc(self, instr, k=1):
        instr.then_inc(self.h, k)
        self.n += k
        return self.n


def mk_sems(nc, stack, names):
    return {nm: Sem(stack.enter_context(nc.semaphore(nm))) for nm in names}


def ts(i, n):
    return slice(i * n, (i + 1) * n)


ENGS = ("sync", "gpsimd", "tensor", "vector", "scalar")
STRICT = True


class Op:
    __slots__ = ("eng", "emit", "reads", "writes", "sreads", "idx",
                 "marked", "cum", "deps", "tag", "dma", "cc", "lane")

    def __init__(self, eng, emit, reads, writes, sreads, tag):
        self.eng = eng
        self.emit = emit
        self.reads = reads
        self.writes = writes
        self.sreads = sreads
        self.tag = tag
        self.marked = False
        self.cum = None
        self.deps = []


class Sched:
    def __init__(self, is_dma_eng_weight=16):
        self.ops = []
        self.last_writer = {}
        self.readers = {}

    def add(self, eng, emit, reads=(), writes=(), sreads=(), dma=False, cc=False, tag=""):
        op = Op(eng, emit, tuple(reads), tuple(writes), tuple(sreads), tag)
        op.dma = dma
        op.cc = cc
        deps = []
        anyasync = dma or cc
        for r in tuple(reads) + tuple(sreads):
            w = self.last_writer.get(r)
            if w is not None and (w.eng != eng or r in op.sreads or w.dma or w.cc or anyasync or STRICT):
                deps.append(w)
        for r in writes:
            w = self.last_writer.get(r)
            if w is not None and (w.eng != eng or w.dma or w.cc or anyasync or STRICT):
                deps.append(w)
            for rd in self.readers.get(r, ()):
                if rd.eng != eng or rd.dma or rd.cc or anyasync or STRICT:
                    deps.append(rd)
        seen = set()
        for d in deps:
            if id(d) not in seen:
                seen.add(id(d))
                op.deps.append(d)
        for r in writes:
            self.last_writer[r] = op
            self.readers[r] = []
        for r in tuple(reads) + tuple(sreads):
            self.readers.setdefault(r, []).append(op)
        self.ops.append(op)
        return op

    def emit_all(self, nc, ctx):
        for op in self.ops:
            if op.dma or op.cc:
                op.marked = True
            for d in op.deps:
                d.marked = True
        NDMA = 6
        lanes = ([(e, 0) for e in ENGS] + [(e, 1 + j) for e in ENGS for j in range(NDMA)]
                 + [("gpsimd", 99)])
        cum = {ln: 0 for ln in lanes}
        dma_i = {e: 0 for e in ENGS}
        dma_last = {}
        cc_prev = None
        for op in self.ops:
            if op.cc:
                op.lane = ("gpsimd", 99)
                if cc_prev is not None and cc_prev not in op.deps:
                    op.deps.append(cc_prev)
                cc_prev = op
            elif op.dma:
                op.lane = (op.eng, 1 + (dma_i[op.eng] % NDMA))
                dma_i[op.eng] += 1
                prev = dma_last.get(op.lane)
                if prev is not None and prev not in op.deps:
                    op.deps.append(prev)
                dma_last[op.lane] = op
            else:
                op.lane = (op.eng, 0)
            if op.marked:
                cum[op.lane] += 16 if op.dma else 1
                op.cum = cum[op.lane]
        sems = {ln: ctx.enter_context(nc.semaphore(f"S_{ln[0]}_{ln[1]}"))
                for ln in lanes if cum[ln] > 0}
        by_eng = {e: [o for o in self.ops if o.eng == e] for e in ENGS}
        final_cum = dict(cum)

        def make_body(eng):
            def body(h):
                hw = {ln: 0 for ln in lanes}
                for op in by_eng[eng]:
                    need = {}
                    for d in op.deps:
                        assert d.cum is not None, f"unmarked dep {d.tag} of {op.tag}"
                        need[d.lane] = max(need.get(d.lane, 0), d.cum)
                    for ln, v in need.items():
                        if v > hw[ln]:
                            h.wait_ge(sems[ln], v)
                            hw[ln] = v
                    ins = op.emit(h)
                    if op.marked:
                        ins.then_inc(sems[op.lane], 16 if op.dma else 1)
            return body
        return sems, by_eng, make_body, final_cum

    def barrier(self):
        deps = []
        lastc = {}
        dmatail = {e: [] for e in ENGS}
        for op in self.ops:
            if op.cc or op.dma:
                dmatail[op.eng].append(op)
            else:
                lastc[op.eng] = op
        for e in ENGS:
            if e in lastc:
                deps.append(lastc[e])
            deps.extend(dmatail[e][-8:])
        bars = []
        for e in ENGS:
            o = self.add(e, lambda h: h.nop(nofuse=True), tag=f"bar_{e}")
            for d in deps:
                if d is not o and d not in o.deps:
                    o.deps.append(d)
            bars.append(o)
        return bars

    def run_block(self, nc, ctx):
        sems, by_eng, make_body, final_cum = self.emit_all(nc, ctx)
        with nc.Block() as block:
            for eng in ENGS:
                if by_eng[eng]:
                    getattr(block, eng)(make_body(eng))
        return sems, final_cum



def phase_ln_feature_major(S, nc, xt, xout, work, T, gcol=None, bcol=None, key=""):
    NB = T // 512
    onesc = work["onesc"]
    sqt = work["sqt"]; mu = work["mu"]; vart = work["vart"]
    rstd = work["rstd"]; nmr = work["nmr"]; tmp = work["tmp"]
    ps_a = work["ps_a"]; ps_b = work["ps_b"]
    K = key
    for i in range(ND):
        S.add("scalar", lambda h, i=i: h.activation(sqt[:], xt[i][:], AF.Square),
              reads=[K + f"x{i}"], writes=[K + "sqt"], tag=K + f"sq{i}")
        for b_ in range(NB):
            S.add("tensor", lambda h, i=i, b_=b_: h.matmul(
                ps_a[0:1, ts(b_, 512)], onesc[:, 0:1], xt[i][:, ts(b_, 512)],
                start=(i == 0), stop=(i == ND - 1)),
                reads=[K + f"x{i}", "onesc"], writes=[K + "psa"], tag=K + f"mmS{i}.{b_}")
        for b_ in range(NB):
            S.add("tensor", lambda h, i=i, b_=b_: h.matmul(
                ps_b[0:1, ts(b_, 512)], onesc[:, 0:1], sqt[:, ts(b_, 512)],
                start=(i == 0), stop=(i == ND - 1)),
                reads=[K + "sqt", "onesc"], writes=[K + "psb"], tag=K + f"mmQ{i}.{b_}")
    S.add("vector", lambda h: h.tensor_scalar(mu[:], ps_a[0:1, :], 1.0 / D, None, op0=ALU.mult),
          reads=[K + "psa"], writes=[K + "mu"], tag=K + "mu")
    S.add("vector", lambda h: h.tensor_tensor(vart[:], mu[:], mu[:], op=ALU.mult),
          reads=[K + "mu"], writes=[K + "vart"], tag=K + "mu2")
    S.add("vector", lambda h: h.tensor_scalar(vart[:], vart[:], EPS, None, op0=ALU.subtract),
          reads=[K + "vart"], writes=[K + "vart"], tag=K + "m2e")
    S.add("vector", lambda h: h.scalar_tensor_tensor(vart[:], ps_b[0:1, :], 1.0 / D, vart[:],
                                                     op0=ALU.mult, op1=ALU.subtract),
          reads=[K + "psb", K + "vart"], writes=[K + "vart"], tag=K + "var")
    S.add("scalar", lambda h: h.activation(vart[:], vart[:], AF.Ln),
          reads=[K + "vart"], writes=[K + "vart"], tag=K + "ln")
    S.add("scalar", lambda h: h.activation(rstd[:], vart[:], AF.Exp, scale=-0.5),
          reads=[K + "vart"], writes=[K + "rstd"], tag=K + "rstd")
    S.add("vector", lambda h: h.tensor_tensor(nmr[:], mu[:], rstd[:], op=ALU.mult),
          reads=[K + "mu", K + "rstd"], writes=[K + "nmr"], tag=K + "nmr1")
    S.add("vector", lambda h: h.tensor_scalar(nmr[:], nmr[:], -1.0, None, op0=ALU.mult),
          reads=[K + "nmr"], writes=[K + "nmr"], tag=K + "nmr2")
    for b_ in range(NB):
        S.add("tensor", lambda h, b_=b_: h.matmul(
            ps_a[:, 0:512], onesc[0:1, :], rstd[:, ts(b_, 512)], start=True, stop=True),
            reads=[K + "rstd", "onesc"], writes=[K + "psa"], tag=K + f"bcR{b_}")
        S.add("tensor", lambda h, b_=b_: h.matmul(
            ps_b[:, 0:512], onesc[0:1, :], nmr[:, ts(b_, 512)], start=True, stop=True),
            reads=[K + "nmr", "onesc"], writes=[K + "psb"], tag=K + f"bcM{b_}")
        for i in range(ND):
            S.add("vector", lambda h, i=i, b_=b_: h.tensor_tensor(
                tmp[:], xt[i][:, ts(b_, 512)], ps_a[:, 0:512], op=ALU.mult),
                reads=[K + f"x{i}", K + "psa"], writes=[K + "tmp"], tag=K + f"xr{i}.{b_}")
            if gcol is None:
                S.add("vector", lambda h, i=i, b_=b_: h.tensor_tensor(
                    xout[:, i * T + b_ * 512: i * T + b_ * 512 + 512],
                    tmp[:], ps_b[:, 0:512], op=ALU.add),
                    reads=[K + "tmp", K + "psb"], writes=[K + "xout"], tag=K + f"xo{i}.{b_}")
            else:
                S.add("vector", lambda h, i=i, b_=b_: h.tensor_tensor(
                    tmp[:], tmp[:], ps_b[:, 0:512], op=ALU.add),
                    reads=[K + "tmp", K + "psb"], writes=[K + "tmp"], tag=K + f"xo{i}.{b_}")
                S.add("vector", lambda h, i=i, b_=b_: h.tensor_scalar(
                    xout[:, i * T + b_ * 512: i * T + b_ * 512 + 512],
                    tmp[:], gcol[:, i:i + 1], bcol[:, i:i + 1],
                    op0=ALU.mult, op1=ALU.add),
                    reads=[K + "tmp"], writes=[K + "xout"],
                    sreads=[K + "gb"], tag=K + f"xa{i}.{b_}")


def mk_ln_work(nc, ctx, T, pref="", psum=None):
    sb = lambda nm, shp, dt2: ctx.enter_context(nc.sbuf_tensor(pref + nm, shp, dt2))
    w = dict(
        onesc=sb("onesc", [128, 128], F32),
        sqt=sb("ln_sqt", [128, T], F32),
        sqtb=sb("ln_sqtb", [128, T], BF16),
        mu=sb("ln_mu", [1, T], F32),
        vart=sb("ln_vart", [1, T], F32),
        rstd=sb("ln_rstd", [1, T], F32),
        nmr=sb("ln_nmr", [1, T], F32),
        tmp=sb("ln_tmp", [128, 512], F32),
    )
    if psum is None:
        w["ps_a"] = ctx.enter_context(nc.psum_tensor(pref + "ln_psa", [128, T], F32))
        w["ps_b"] = ctx.enter_context(nc.psum_tensor(pref + "ln_psb", [128, T], F32))
    else:
        w["ps_a"], w["ps_b"] = psum
    return w


def build_ln_launch():
    nc = bass.Bass("TRN2", target_bir_lowering=False)
    x0T = nc.dram_tensor("x0T", [D, TLOC], F32, kind="ExternalInput").ap()
    ones = nc.dram_tensor("ones128", [128, 128], F32, kind="ExternalInput").ap()
    xnT = nc.dram_tensor("xnT", [D, TLOC], BF16, kind="ExternalOutput").ap()

    with contextlib.ExitStack() as ctx:
        sb = lambda *a: ctx.enter_context(nc.sbuf_tensor(*a))
        xt = [sb(f"xt{i}", [128, TLOC], F32) for i in range(ND)]
        xout = sb("xout", [128, ND * TLOC], BF16)
        work = mk_ln_work(nc, ctx, TLOC)
        S = Sched()
        S.add("sync", lambda h: h.dma_start(out=work["onesc"][:], in_=ones),
              writes=["onesc"], dma=True, tag="ld_ones")
        for i in range(ND):
            S.add("sync", lambda h, i=i: h.dma_start(out=xt[i][:], in_=x0T[ts(i, 128), :]),
                  writes=[f"x{i}"], dma=True, tag=f"ld{i}")
        phase_ln_feature_major(S, nc, xt, xout, work, TLOC)
        S.add("sync", lambda h: h.dma_start(
            out=xnT.rearrange("(dt p) t -> p dt t", p=128),
            in_=xout[:].rearrange("p (dt t) -> p dt t", t=TLOC)),
            reads=["xout"], dma=True, tag="store")
        S.add("sync", lambda h: h.nop(), reads=[], writes=[], tag="fin")
        S.ops[-1].deps.append(S.ops[-2])
        S.run_block(nc, ctx)
    return nc


RR_C1 = 6.25
RR_C2 = 0.033203125
RR_C3 = -1.781782e-05
INV2PI = 0.15915494309189535
TWOPI = 6.283185307179586
PI_ = 3.1415927410125732
PIH = 1.5707963267948966


def build_attn_launch(span, npairs=8):
    nc = bass.Bass("TRN2", target_bir_lowering=False)
    xnT = nc.dram_tensor("xnT", [D, L], BF16, kind="ExternalInput").ap()
    w_qkv = nc.dram_tensor("w_qkv", [D, 3 * D], F32, kind="ExternalInput").ap()
    b_qkvT = nc.dram_tensor("b_qkvT", [128, 24], F32, kind="ExternalInput").ap()
    w_out = nc.dram_tensor("w_out", [D, D], F32, kind="ExternalInput").ap()
    b_outT = nc.dram_tensor("b_outT", [128, 8], F32, kind="ExternalInput").ap()
    fcolsd = nc.dram_tensor("freqcols2", [128, 8], F32, kind="ExternalInput").ap()
    posrep = nc.dram_tensor("posrep", [128, L], F32, kind="ExternalInput").ap()
    onesbf_d = nc.dram_tensor("onesbf", [128, 128], BF16, kind="ExternalInput").ap()
    idbf_d = nc.dram_tensor("idbf", [128, 128], BF16, kind="ExternalInput").ap()
    attn0T = nc.dram_tensor("attn0T", [D, TLOC], BF16, kind="ExternalOutput").ap()
    dbg_ka = nc.dram_tensor("dbg_ka", [65, 256], BF16, kind="ExternalOutput").ap()
    dbg_qa = nc.dram_tensor("dbg_qa", [65, 256], BF16, kind="ExternalOutput").ap()
    dbg_V = nc.dram_tensor("dbg_V", [128, 260], BF16, kind="ExternalOutput").ap()
    dbg_ao = nc.dram_tensor("dbg_ao", [128, 512], BF16, kind="ExternalOutput").ap()

    NP = npairs
    CH = 1024
    NCH = L // CH
    NMT = 64
    NRB = 4

    with contextlib.ExitStack() as ctx:
        sb = lambda *a: ctx.enter_context(nc.sbuf_tensor(*a))
        xchunk = sb("xchunk", [128, ND * CH], BF16)
        wq = sb("wq", [128, ND * 128], BF16)
        wk = sb("wk", [128, ND * 128], BF16)
        wv = sb("wv", [128, ND * 128], BF16)
        biasq = sb("biasq", [128, 24], F32)
        bias_o = sb("bias_o", [128, 8], F32)
        fq = sb("fq", [128, 8], F32)
        onesb = sb("onesb", [128, 128], BF16)
        idb = sb("idb", [128, 128], BF16)
        pihc = sb("pihc", [128, 1], F32)
        kpre = sb("kpre", [128, CH], BF16)
        kswp = sb("kswp", [128, CH], BF16)
        qpre = sb("qpre", [128, CH], BF16)
        qswp = sb("qswp", [128, CH], BF16)
        vpre = sb("vpre", [128, CH], BF16)
        ka = sb("ka", [65, L], BF16)
        kb = sb("kb", [65, L], BF16)
        qa = sb("qa", [65, TLOC], BF16)
        qb = sb("qb", [65, TLOC], BF16)
        V = sb("V", [128, NMT * 130], BF16)
        th = sb("th", [128, CH], F32)
        kfb = sb("kfb", [128, CH], F32)
        kib = sb("kib", [128, CH], I32)
        rr = sb("rr", [128, CH], F32)
        cosr = sb("cosr", [128, CH], BF16)
        sinr = sb("sinr", [128, CH], BF16)
        tmpa = sb("tmpa", [128, CH], BF16)
        tmpb = sb("tmpb", [128, CH], BF16)
        ksqs = sb("ksqs", [64, CH], BF16)
        sp512 = sb("sp512", [1, 512], F32)
        kmaxp = sb("kmaxp", [1, 64], F32)
        kms = sb("kms", [1, 2], F32)
        q2s = sb("q2s", [1, TLOC], BF16)
        qnmA = sb("qnmA", [1, TLOC], BF16)
        qnmB = sb("qnmB", [1, TLOC], BF16)
        P0 = sb("P0", [128, 1024], BF16)
        P1 = sb("P1", [128, 1024], BF16)
        sr = sb("sr", [1, 512], F32)
        rsf = sb("rsf", [1, 512], F32)
        rs = sb("rs", [1, 512], BF16)
        rsb = sb("rsb", [64, 512], BF16)
        aoT = sb("aoT", [128, NP * TLOC], BF16)
        wo = sb("wo", [128, 64 * 128], BF16)
        ostg = [sb(f"ostg{j}", [128, 512], BF16) for j in range(2)]
        PA = ctx.enter_context(nc.psum_tensor("PA", [128, 2048], F32))
        PB = ctx.enter_context(nc.psum_tensor("PB", [128, 2048], F32))

        S = Sched()
        A = S.add

        A("gpsimd", lambda h: h.dma_start(out=biasq[:], in_=b_qkvT), writes=["biasq"], dma=True, tag="ldbq")
        A("gpsimd", lambda h: h.dma_start(out=bias_o[:], in_=b_outT), writes=["bias_o"], dma=True, tag="ldbo")
        A("gpsimd", lambda h: h.dma_start(out=fq[:], in_=fcolsd), writes=["fq"], dma=True, tag="ldfq")
        A("gpsimd", lambda h: h.dma_start(out=onesb[:], in_=onesbf_d), writes=["onesb"], dma=True, tag="ldob")
        A("gpsimd", lambda h: h.dma_start(out=idb[:], in_=idbf_d), writes=["idb"], dma=True, tag="ldid")
        A("gpsimd", lambda h: h.memset(pihc[:], PIH), writes=["pihc"], tag="mspih")
        A("gpsimd", lambda h: h.memset(ka[64:65, :], 1.0), writes=["karow"], tag="mska")
        A("gpsimd", lambda h: h.memset(kb[64:65, :], 1.0), writes=["kbrow"], tag="mskb")
        vap = V[:].rearrange("p (m c) -> p m c", c=130)
        A("gpsimd", lambda h: h.memset(vap[:, :, 64:65], 1.0), writes=["Vones"], tag="msv1")
        A("gpsimd", lambda h: h.memset(vap[:, :, 129:130], 1.0), writes=["Vones"], tag="msv2")

        PSL = [PA[:, 0:1024], PA[:, 1024:2048]]
        PSO = [PB[:, 0:512], PB[:, 512:1024]]
        PRB = PB[:, 1024:1536]
        PST = PB[:, 1536:2048]
        Pbuf = [P0, P1]

        for p_ in range(NP):
            c0 = p_ * 128
            pk = f"p{p_}"
            for nm, wt, coff in (("wq", wq, c0), ("wk", wk, D + c0), ("wv", wv, 2 * D + c0)):
                A("gpsimd", lambda h, wt=wt, coff=coff: h.dma_start(
                    out=wt[:], in_=w_qkv[:, coff:coff + 128].rearrange("(dt p) c -> p dt c", p=128)),
                  writes=[nm], dma=True, tag=pk + "ld" + nm)

            for c_ in range(NCH):
                ck = pk + f"c{c_}"
                ccols = ts(c_, CH)
                is_q = (c_ // 2 == span)
                qoff = (c_ % 2) * CH
                A("sync", lambda h, c_=c_: h.dma_start(
                    out=xchunk[:], in_=xnT[:, ts(c_, CH)].rearrange("(dt p) t -> p dt t", p=128)),
                  writes=["xchunk"], dma=True, tag=ck + "ldx")
                streams = [("k", wk, kpre, 8), ("v", wv, vpre, 16)]
                if is_q:
                    streams.append(("q", wq, qpre, 0))
                qki = 0
                for tb in range(CH // 512):
                    for nm, wt, dst, boff in streams:
                        j4 = qki % 4
                        qki += 1
                        A("tensor", lambda h, wt=wt, tb=tb, j4=j4: [
                            h.matmul(PA[:, ts(j4, 512)], wt[:, ts(dt, 128)],
                                     xchunk[:, dt * CH + tb * 512: dt * CH + tb * 512 + 512],
                                     start=(dt == 0), stop=(dt == ND - 1))
                            for dt in range(ND)][-1],
                          reads=["xchunk", "w" + nm], writes=[f"qps{j4}"], tag=ck + f"mm{nm}{tb}")
                        A("scalar", lambda h, dst=dst, tb=tb, boff=boff, j4=j4, p_=p_: h.activation(
                            dst[:, ts(tb, 512)], PA[:, ts(j4, 512)], AF.Identity,
                            bias=biasq[:, boff + p_:boff + p_ + 1]),
                          reads=[f"qps{j4}"], writes=[nm + "pre"], sreads=["biasq"],
                          tag=ck + f"ev{nm}{tb}")
                for mt_ in range(CH // 128):
                    gmt = c_ * (CH // 128) + mt_
                    A("tensor", lambda h, mt_=mt_: h.transpose(
                        PST[:, 0:64].bitcast(BF16), vpre[:, ts(mt_, 128)], idb[:]),
                      reads=["vpre", "idb"], writes=["PST"], tag=ck + f"tr{mt_}")
                    A("vector", lambda h, gmt=gmt: h.tensor_copy(
                        V[:, gmt * 130: gmt * 130 + 130].rearrange("p (a c) -> p a c", a=2)[:, :, 0:64],
                        PST[:, 0:64].bitcast(BF16).rearrange("p (a c) -> p a c", a=2)),
                      reads=["PST"], writes=["Vdat"], tag=ck + f"te{mt_}")
                swaps = [("k", kpre, kswp)] + ([("q", qpre, qswp)] if is_q else [])
                for nm, src_, dst_ in swaps:
                    for g2 in range(2):
                        s0, s1 = 64 * g2, 64 * g2 + 32
                        A("gpsimd", lambda h, src_=src_, dst_=dst_, s0=s0, s1=s1: h.dma_start(
                            out=dst_[s0:s0 + 32, :], in_=src_[s1:s1 + 32, :]),
                          reads=[nm + "pre"], writes=[nm + f"swp{g2}a"], dma=True, tag=ck + f"sw{nm}{g2}a")
                        A("gpsimd", lambda h, src_=src_, dst_=dst_, s0=s0, s1=s1: h.dma_start(
                            out=dst_[s1:s1 + 32, :], in_=src_[s0:s0 + 32, :]),
                          reads=[nm + "pre"], writes=[nm + f"swp{g2}b"], dma=True, tag=ck + f"sw{nm}{g2}b")
                A("gpsimd", lambda h, c_=c_: h.dma_start(out=th[:], in_=posrep[:, ts(c_, CH)]),
                  writes=["th"], dma=True, tag=ck + "io")
                A("vector", lambda h, p_=p_: h.tensor_scalar(th[:], th[:], fq[:, p_:p_ + 1], None, op0=ALU.mult),
                  reads=["th"], writes=["th"], sreads=["fq"], tag=ck + "th")
                A("vector", lambda h: h.tensor_scalar(kfb[:], th[:], INV2PI, None, op0=ALU.mult),
                  reads=["th"], writes=["kfb"], tag=ck + "kf")
                A("vector", lambda h: h.tensor_copy(kib[:], kfb[:]), reads=["kfb"], writes=["kib"], tag=ck + "ki")
                A("vector", lambda h: h.tensor_copy(kfb[:], kib[:]), reads=["kib"], writes=["kfb"], tag=ck + "kf2")
                A("vector", lambda h: h.scalar_tensor_tensor(rr[:], kfb[:], -RR_C1, th[:],
                                                             op0=ALU.mult, op1=ALU.add),
                  reads=["kfb", "th"], writes=["rr"], tag=ck + "r1")
                A("vector", lambda h: h.scalar_tensor_tensor(rr[:], kfb[:], -RR_C2, rr[:],
                                                             op0=ALU.mult, op1=ALU.add),
                  reads=["kfb", "rr"], writes=["rr"], tag=ck + "r2")
                A("vector", lambda h: h.scalar_tensor_tensor(rr[:], kfb[:], -RR_C3, rr[:],
                                                             op0=ALU.mult, op1=ALU.add),
                  reads=["kfb", "rr"], writes=["rr"], tag=ck + "r3")
                for fi in range(2):
                    A("vector", lambda h: h.tensor_scalar(kfb[:], rr[:], PI_, None, op0=ALU.is_gt),
                      reads=["rr"], writes=["kfb"], tag=ck + f"fg{fi}")
                    A("vector", lambda h: h.scalar_tensor_tensor(rr[:], kfb[:], -TWOPI, rr[:],
                                                                 op0=ALU.mult, op1=ALU.add),
                      reads=["kfb", "rr"], writes=["rr"], tag=ck + f"fs{fi}")
                    A("vector", lambda h: h.tensor_scalar(kfb[:], rr[:], -PI_, None, op0=ALU.is_lt),
                      reads=["rr"], writes=["kfb"], tag=ck + f"fh{fi}")
                    A("vector", lambda h: h.scalar_tensor_tensor(rr[:], kfb[:], TWOPI, rr[:],
                                                                 op0=ALU.mult, op1=ALU.add),
                      reads=["kfb", "rr"], writes=["rr"], tag=ck + f"ft{fi}")
                A("vector", lambda h: h.tensor_scalar(rr[:], rr[:], -PI_, PI_, op0=ALU.max, op1=ALU.min),
                  reads=["rr"], writes=["rr"], tag=ck + "cl")
                A("scalar", lambda h: h.activation(sinr[:], rr[:], AF.Sin),
                  reads=["rr"], writes=["sinr"], tag=ck + "sin")
                A("scalar", lambda h: h.activation(th[:], rr[:], AF.Abs),
                  reads=["rr"], writes=["th"], tag=ck + "abs")
                A("scalar", lambda h: h.activation(cosr[:], th[:], AF.Sin, scale=-1.0,
                                                   bias=pihc[:, 0:1]),
                  reads=["th"], writes=["cosr"], sreads=["pihc"], tag=ck + "cos")
                combos = [("k", kpre, kswp, ka, kb, ccols)] + (
                    [("q", qpre, qswp, qa, qb, slice(qoff, qoff + CH))] if is_q else [])
                for nm, pre, swp, ta, tb_, ocols in combos:
                    A("vector", lambda h, pre=pre: h.tensor_tensor(tmpa[:], pre[:], cosr[:], op=ALU.mult),
                      reads=[nm + "pre", "cosr"], writes=["tmpa"], tag=ck + f"cb1{nm}")
                    A("vector", lambda h, swp=swp: h.tensor_tensor(tmpb[:], swp[:], sinr[:], op=ALU.mult),
                      reads=[nm + "swp0a", nm + "swp0b", nm + "swp1a", nm + "swp1b", "sinr"],
                      writes=["tmpb"], tag=ck + f"cb2{nm}")
                    A("vector", lambda h, ta=ta, ocols=ocols: h.tensor_tensor(
                        ta[0:64, ocols], tmpa[0:64, :], tmpb[0:64, :], op=ALU.add),
                      reads=["tmpa", "tmpb"], writes=[nm + "rotA"], tag=ck + f"cb3{nm}")
                    A("vector", lambda h, tb_=tb_, ocols=ocols: h.tensor_tensor(
                        tb_[0:64, ocols], tmpa[64:128, :], tmpb[64:128, :], op=ALU.add),
                      reads=["tmpa", "tmpb"], writes=[nm + "rotB"], tag=ck + f"cb4{nm}")
                for hh in range(2):
                    kx = (ka, kb)[hh]
                    A("scalar", lambda h, kx=kx, ccols=ccols: h.activation(
                        ksqs[:], kx[0:64, ccols], AF.Square),
                      reads=["krot" + "AB"[hh]], writes=["ksqs"], tag=ck + f"ksq{hh}")
                    for piece in range(CH // 512):
                        pidx = c_ * (CH // 512) + piece
                        A("tensor", lambda h, piece=piece: h.matmul(
                            PRB[0:1, 0:512], onesb[0:64, 0:1], ksqs[0:64, ts(piece, 512)],
                            start=True, stop=True),
                          reads=["ksqs", "onesb"], writes=["prb"], tag=ck + f"k2m{hh}.{piece}")
                        A("scalar", lambda h: h.activation(sp512[:], PRB[0:1, 0:512], AF.Copy),
                          reads=["prb"], writes=["sp512"], tag=ck + f"k2e{hh}.{piece}")
                        A("vector", lambda h, hh=hh, pidx=pidx: h.reduce_max(
                            kmaxp[0:1, hh * 16 + pidx:hh * 16 + pidx + 1], sp512[:],
                            axis=mybir.AxisListType.X),
                          reads=["sp512"], writes=["kmaxp"], tag=ck + f"k2r{hh}.{pidx}")
                if is_q:
                    for hh in range(2):
                        qx = (qa, qb)[hh]
                        A("scalar", lambda h, qx=qx, qoff=qoff: h.activation(
                            ksqs[:], qx[0:64, qoff:qoff + CH], AF.Square),
                          reads=["qrot" + "AB"[hh]], writes=["ksqs"], tag=ck + f"qsq{hh}")
                        for piece in range(CH // 512):
                            A("tensor", lambda h, piece=piece: h.matmul(
                                PRB[0:1, 0:512], onesb[0:64, 0:1], ksqs[0:64, ts(piece, 512)],
                                start=True, stop=True),
                              reads=["ksqs", "onesb"], writes=["prb"], tag=ck + f"q2m{hh}.{piece}")
                            A("scalar", lambda h, piece=piece, qoff=qoff, hh=hh: h.activation(
                                q2s[0:1, qoff + piece * 512: qoff + piece * 512 + 512]
                                if hh == 0 else
                                q2s[0:1, qoff + piece * 512: qoff + piece * 512 + 512],
                                PRB[0:1, 0:512], AF.Copy),
                              reads=["prb"], writes=["q2s" + "AB"[hh]], tag=ck + f"q2e{hh}.{piece}")
                        A("scalar", lambda h, qoff=qoff, hh=hh: h.activation(
                            (qnmA if hh == 0 else qnmB)[0:1, qoff:qoff + CH],
                            q2s[0:1, qoff:qoff + CH], AF.Sqrt),
                          reads=["q2s" + "AB"[hh]], writes=["qnm" + "AB"[hh]], tag=ck + f"qnm{hh}")

            for hh in range(2):
                A("vector", lambda h, hh=hh: h.reduce_max(
                    kms[0:1, hh:hh + 1], kmaxp[0:1, hh * 16:hh * 16 + 16],
                    axis=mybir.AxisListType.X),
                  reads=["kmaxp"], writes=["kms2"], tag=pk + f"kmx{hh}")
            A("scalar", lambda h: h.activation(kms[:], kms[:], AF.Sqrt),
              reads=["kms2"], writes=["kms"], tag=pk + "kmsq")
            for hh in range(2):
                qx = (qa, qb)[hh]
                qn = (qnmA, qnmB)[hh]
                A("vector", lambda h, qx=qx, qn=qn, hh=hh: h.tensor_scalar(
                    qx[64:65, :], qn[:], kms[0:1, hh:hh + 1], -1.0,
                    op0=ALU.mult, op1=ALU.mult),
                  reads=["qnm" + "AB"[hh]], writes=["q" + "ab"[hh] + "crow"],
                  sreads=["kms"], tag=pk + f"crow{hh}")

            for hh in range(2):
                kx = (ka, kb)[hh]
                qx = (qa, qb)[hh]
                krd = ["krot" + "AB"[hh], ("karow", "kbrow")[hh]]
                qrd = ["qrot" + "AB"[hh], "q" + "ab"[hh] + "crow"]
                for rb in range(NRB):
                    ob = rb % 2
                    rcols = ts(rb, 512)

                    def emit_qk(h, mp, kx=kx, qx=qx, rcols=rcols):
                        for j in range(2):
                            mt = 2 * mp + j
                            ins = h.matmul(PSL[mp % 2][:, ts(j, 512)], kx[:, ts(mt, 128)],
                                           qx[:, rcols], start=True, stop=True)
                        return ins

                    def emit_exp(h, mp):
                        return h.activation(Pbuf[mp % 2][:], PSL[mp % 2][:], AF.Exp, scale=0.125)

                    def emit_av(h, mp, hh=hh, ob=ob):
                        for j in range(2):
                            mt = 2 * mp + j
                            ins = h.matmul(PSO[ob][0:65, :],
                                           V[:, mt * 130 + 65 * hh: mt * 130 + 65 * hh + 65],
                                           Pbuf[mp % 2][:, ts(j, 512)],
                                           start=(mt == 0), stop=(mt == NMT - 1))
                        return ins

                    hk = pk + f"h{hh}rb{rb}"
                    PK = [["qps0", "qps1"], ["qps2", "qps3"]]
                    A("tensor", lambda h, f=emit_qk: f(h, 0), reads=krd + qrd,
                      writes=PK[0], tag=hk + "qk0")
                    A("tensor", lambda h, f=emit_qk: f(h, 1), reads=krd + qrd,
                      writes=PK[1], tag=hk + "qk1")
                    for mp in range(32):
                        A("scalar", lambda h, mp=mp, f=emit_exp: f(h, mp),
                          reads=PK[mp % 2], writes=[f"P{mp % 2}"], tag=hk + f"ex{mp}")
                        A("tensor", lambda h, mp=mp, f=emit_av: f(h, mp),
                          reads=[f"P{mp % 2}", "Vdat", "Vones"], writes=[f"pso{ob}"], tag=hk + f"av{mp}")
                        if mp + 2 < 32:
                            A("tensor", lambda h, mp=mp, f=emit_qk: f(h, mp + 2),
                              reads=krd + qrd, writes=PK[mp % 2], tag=hk + f"qk{mp + 2}")
                    A("scalar", lambda h, ob=ob: h.activation(sr[:], PSO[ob][64:65, :], AF.Copy),
                      reads=[f"pso{ob}"], writes=["sr"], tag=hk + "sr")
                    A("vector", lambda h: h.reciprocal(rsf[:], sr[:]),
                      reads=["sr"], writes=["rsf"], tag=hk + "rsf")
                    A("vector", lambda h: h.tensor_copy(rs[:], rsf[:]),
                      reads=["rsf"], writes=["rs"], tag=hk + "rs")
                    A("tensor", lambda h: h.matmul(PRB[0:64, 0:512], onesb[0:1, 0:64], rs[:],
                                                   start=True, stop=True),
                      reads=["rs", "onesb"], writes=["prb"], tag=hk + "rb")
                    A("scalar", lambda h: h.activation(rsb[:], PRB[0:64, 0:512], AF.Copy),
                      reads=["prb"], writes=["rsb"], tag=hk + "rc")
                    A("vector", lambda h, ob=ob, hh=hh, rb=rb, p_=p_: h.tensor_tensor(
                        aoT[64 * hh:64 * hh + 64, p_ * TLOC + rb * 512: p_ * TLOC + rb * 512 + 512],
                        PSO[ob][0:64, :], rsb[:], op=ALU.mult),
                      reads=[f"pso{ob}", "rsb"], writes=["aoT"], tag=hk + "oe")

        A("gpsimd", lambda h: h.dma_start(
            out=wo[:], in_=w_out.rearrange("(pp a) (dt c) -> a pp dt c", a=128, c=128)),
          writes=["wo"], dma=True, tag="ldwo")
        for dt in range(ND):
            for g in range(4):
                j2 = (dt * 4 + g) % 2
                A("tensor", lambda h, dt=dt, g=g: [
                    h.matmul(PST[:, 0:512], wo[:, (pp * 8 + dt) * 128:(pp * 8 + dt) * 128 + 128],
                             aoT[:, pp * TLOC + g * 512: pp * TLOC + g * 512 + 512],
                             start=(pp == 0), stop=(pp == NP - 1))
                    for pp in range(NP)][-1],
                  reads=["aoT", "wo"], writes=["PST"], tag=f"op{dt}.{g}")
                A("scalar", lambda h, dt=dt, j2=j2: h.activation(
                    ostg[j2][:], PST[:, 0:512], AF.Identity, bias=bias_o[:, dt:dt + 1]),
                  reads=["PST"], writes=[f"ostg{j2}"], sreads=["bias_o"], tag=f"oe{dt}.{g}")
                A("sync", lambda h, dt=dt, g=g, j2=j2: h.dma_start(
                    out=attn0T[ts(dt, 128), ts(g, 512)], in_=ostg[j2][:]),
                  reads=[f"ostg{j2}"], writes=["stored"], dma=True, tag=f"st{dt}.{g}")
        A("sync", lambda h: h.dma_start(out=dbg_ka, in_=ka[:, 0:256]),
          reads=["krotA", "karow"], writes=["stored"], dma=True, tag="stdbg1")
        A("sync", lambda h: h.dma_start(out=dbg_qa, in_=qa[:, 0:256]),
          reads=["qrotA", "qacrow"], writes=["stored"], dma=True, tag="stdbg2")
        A("sync", lambda h: h.dma_start(out=dbg_V, in_=V[:, 0:260]),
          reads=["Vdat", "Vones"], writes=["stored"], dma=True, tag="stdbg3")
        A("sync", lambda h: h.dma_start(out=dbg_ao, in_=aoT[:, 0:512]),
          reads=["aoT"], writes=["stored"], dma=True, tag="stdbg4")
        fin = A("sync", lambda h: h.nop(nofuse=True), reads=["stored"], tag="fin")
        for op in S.ops:
            if op.dma and op.tag.startswith("st"):
                if op not in fin.deps:
                    fin.deps.append(op)
        S.run_block(nc, ctx)
    return nc


def build_ffn0_launch():
    nc = bass.Bass("TRN2", target_bir_lowering=False)
    xnTd = nc.dram_tensor("xnT_loc", [D, TLOC], BF16, kind="ExternalInput").ap()
    x0Td = nc.dram_tensor("x0T_loc", [D, TLOC], F32, kind="ExternalInput").ap()
    at0d = nc.dram_tensor("attn0T", [D, TLOC], BF16, kind="ExternalInput").ap()
    gwd = nc.dram_tensor("gate_wT", [32, D, 128], F32, kind="ExternalInput").ap()
    uwd = nc.dram_tensor("up_wT", [32, D, 128], F32, kind="ExternalInput").ap()
    dwd = nc.dram_tensor("down_wT", [32, 128, D], F32, kind="ExternalInput").ap()
    sgd = nc.dram_tensor("sgate_wT", [D, 512], F32, kind="ExternalInput").ap()
    sud = nc.dram_tensor("sup_wT", [D, 512], F32, kind="ExternalInput").ap()
    sdd = nc.dram_tensor("sdown_wT", [512, D], F32, kind="ExternalInput").ap()
    rwd = nc.dram_tensor("router_wT", [D, 32], F32, kind="ExternalInput").ap()
    rbd = nc.dram_tensor("router_b", [1, 32], F32, kind="ExternalInput").ap()
    ohd = nc.dram_tensor("onehot32", [32, 32 * 128], BF16, kind="ExternalInput").ap()
    onesbf_d = nc.dram_tensor("onesbf", [128, 128], BF16, kind="ExternalInput").ap()
    idbf_d = nc.dram_tensor("idbf", [128, 128], BF16, kind="ExternalInput").ap()
    x1Td = nc.dram_tensor("x1T", [D, TLOC], F32, kind="ExternalOutput").ap()

    NT = TLOC // 128
    NB4 = TLOC // 512
    EGR = 8

    with contextlib.ExitStack() as ctx:
        sb = lambda *a: ctx.enter_context(nc.sbuf_tensor(*a))
        xnT = sb("xnT", [128, ND * TLOC], BF16)
        ffacc = sb("ffacc", [128, ND * TLOC], F32)
        hwt = [sb(f"hwt{e}", [128, TLOC], BF16) for e in range(EGR)]
        ge = sb("ge", [128, ND * 128], BF16)
        ue = sb("ue", [128, ND * 128], BF16)
        de = [sb(f"de{i}", [128, D], BF16) for i in range(8)]
        sgw = sb("sgw", [128, ND * 512], BF16)
        suw = sb("suw", [128, ND * 512], BF16)
        sdw = sb("sdw", [128, 4 * D], BF16)
        rww = sb("rww", [128, ND * 32], BF16)
        rbb = sb("rbb", [1, 32], BF16)
        oh32 = sb("oh32", [32, 32 * 128], BF16)
        onesb = sb("onesb", [128, 128], BF16)
        idb = sb("idb", [128, 128], BF16)
        lg = sb("lg", [128, 32], F32)
        m8 = sb("m8", [128, 8], F32)
        nv0 = sb("nv0", [128, 1], F32)
        el = sb("el", [128, 32], F32)
        msk = sb("msk", [128, 32], F32)
        ev4 = sb("ev4", [128, 4], F32)
        ds = sb("ds", [128, 1], F32)
        rden = sb("rden", [128, 1], F32)
        compb = sb("compb", [128, NT * 32], BF16)
        combT = sb("combT", [32, TLOC], BF16)
        sgt = sb("sgt", [128, 512], BF16)
        x0t = sb("x0t", [128, TLOC], F32)
        a0t = sb("a0t", [128, TLOC], BF16)
        x1t = sb("x1t", [128, TLOC], F32)
        PA = ctx.enter_context(nc.psum_tensor("PA", [128, 2048], F32))
        PB = ctx.enter_context(nc.psum_tensor("PB", [128, 2048], F32))

        S = Sched()
        A = S.add
        A("sync", lambda h: h.dma_start(out=xnT[:], in_=xnTd.rearrange("(dt p) t -> p dt t", p=128)),
          writes=["xnT"], dma=True, tag="ldxn")
        A("gpsimd", lambda h: h.dma_start(out=rww[:], in_=rwd.rearrange("(dt p) e -> p dt e", p=128)),
          writes=["rww"], dma=True, tag="ldrw")
        A("gpsimd", lambda h: h.dma_start(out=rbb[:], in_=rbd), writes=["rbb"], dma=True, tag="ldrb")
        A("gpsimd", lambda h: h.dma_start(out=oh32[:], in_=ohd), writes=["oh32"], dma=True, tag="ldoh")
        A("gpsimd", lambda h: h.dma_start(out=onesb[:], in_=onesbf_d), writes=["onesb"], dma=True, tag="ldon")
        A("gpsimd", lambda h: h.dma_start(out=idb[:], in_=idbf_d), writes=["idb"], dma=True, tag="ldid")
        A("gpsimd", lambda h: h.dma_start(out=sgw[:], in_=sgd.rearrange("(dt p) s -> p dt s", p=128)),
          writes=["sgw"], dma=True, tag="ldsg")
        A("gpsimd", lambda h: h.dma_start(out=suw[:], in_=sud.rearrange("(dt p) s -> p dt s", p=128)),
          writes=["suw"], dma=True, tag="ldsu")
        A("gpsimd", lambda h: h.dma_start(out=sdw[:], in_=sdd.rearrange("(ht p) d -> p ht d", p=128)),
          writes=["sdw"], dma=True, tag="ldsd")

        for tt in range(NT):
            tk = f"r{tt}"
            A("tensor", lambda h, tt=tt: [
                h.matmul(PA[0:128, 1536:1568], xnT[:, dt * TLOC + tt * 128: dt * TLOC + tt * 128 + 128],
                         rww[:, ts(dt, 32)], start=(dt == 0), stop=False)
                for dt in range(ND)][-1],
              reads=["xnT", "rww"], writes=["prt"], tag=tk + "mm")
            A("tensor", lambda h: h.matmul(PA[0:128, 1536:1568], onesb[0:1, :], rbb[:],
                                           start=False, stop=True),
              reads=["rbb", "onesb"], writes=["prt"], tag=tk + "bias")
            A("scalar", lambda h: h.activation(lg[:], PA[0:128, 1536:1568], AF.Copy),
              reads=["prt"], writes=["lg"], tag=tk + "ev")
            A("vector", lambda h: h.max(m8[:], lg[:]), reads=["lg"], writes=["m8"], tag=tk + "m8")
            A("vector", lambda h: h.tensor_scalar(nv0[:], m8[:, 0:1], -1.0, None, op0=ALU.mult),
              reads=["m8"], writes=["nv0"], tag=tk + "nv")
            A("scalar", lambda h: h.activation(el[:], lg[:], AF.Exp, bias=nv0[:, 0:1]),
              reads=["lg"], sreads=["nv0"], writes=["el"], tag=tk + "el")
            A("vector", lambda h: h.tensor_scalar(msk[:], lg[:], m8[:, 3:4], None, op0=ALU.is_ge),
              reads=["lg"], sreads=["m8"], writes=["msk"], tag=tk + "mk")
            A("scalar", lambda h: h.activation(ev4[:], m8[:, 0:4], AF.Exp, bias=nv0[:, 0:1]),
              reads=["m8"], sreads=["nv0"], writes=["ev4"], tag=tk + "e4")
            A("vector", lambda h: h.reduce_sum(ds[:], ev4[:], axis=mybir.AxisListType.X),
              reads=["ev4"], writes=["ds"], tag=tk + "ds")
            A("vector", lambda h: h.reciprocal(rden[:], ds[:]), reads=["ds"], writes=["rden"], tag=tk + "rd")
            A("vector", lambda h: h.tensor_tensor(el[:], el[:], msk[:], op=ALU.mult),
              reads=["el", "msk"], writes=["el"], tag=tk + "cm1")
            A("vector", lambda h, tt=tt: h.tensor_scalar(
                compb[:, ts(tt, 32)], el[:], rden[:, 0:1], None, op0=ALU.mult),
              reads=["el"], sreads=["rden"], writes=["compb"], tag=tk + "cm2")
            A("tensor", lambda h, tt=tt: h.transpose(
                PB[0:32, 512:576].bitcast(BF16), compb[:, ts(tt, 32)], idb[:]),
              reads=["compb", "idb"], writes=["ptr"], tag=tk + "tr")
            A("vector", lambda h, tt=tt: h.tensor_copy(
                combT[0:32, ts(tt, 128)], PB[0:32, 512:576].bitcast(BF16)),
              reads=["ptr"], writes=["combT"], tag=tk + "te")

        for grp in range(32 // EGR):
            for ei in range(EGR):
                e = grp * EGR + ei
                ek = f"e{e}"
                A("gpsimd", lambda h, e=e: h.dma_start(
                    out=ge[:], in_=gwd[e].rearrange("(dt p) c -> p dt c", p=128)),
                  writes=["ge"], dma=True, tag=ek + "ldg")
                A("gpsimd", lambda h, e=e: h.dma_start(
                    out=ue[:], in_=uwd[e].rearrange("(dt p) c -> p dt c", p=128)),
                  writes=["ue"], dma=True, tag=ek + "ldu")
                for tb in range(NB4):
                    A("tensor", lambda h, tb=tb: [
                        h.matmul(PA[:, 0:512], ge[:, ts(dt, 128)],
                                 xnT[:, dt * TLOC + tb * 512: dt * TLOC + tb * 512 + 512],
                                 start=(dt == 0), stop=(dt == ND - 1))
                        for dt in range(ND)][-1],
                      reads=["xnT", "ge"], writes=["pg"], tag=ek + f"g{tb}")
                    A("tensor", lambda h, tb=tb: [
                        h.matmul(PA[:, 512:1024], ue[:, ts(dt, 128)],
                                 xnT[:, dt * TLOC + tb * 512: dt * TLOC + tb * 512 + 512],
                                 start=(dt == 0), stop=(dt == ND - 1))
                        for dt in range(ND)][-1],
                      reads=["xnT", "ue"], writes=["pu"], tag=ek + f"u{tb}")
                    A("tensor", lambda h, e=e, tb=tb: h.matmul(
                        PA[:, 1024:1536], oh32[:, ts(e, 128)], combT[0:32, ts(tb, 512)],
                        start=True, stop=True),
                      reads=["combT", "oh32"], writes=["pc"], tag=ek + f"c{tb}")
                    A("scalar", lambda h: h.activation(sgt[:], PA[:, 0:512], AF.Silu),
                      reads=["pg"], writes=["sgt"], tag=ek + f"s{tb}")
                    A("vector", lambda h, ei=ei, tb=tb: h.tensor_tensor(
                        hwt[ei][:, ts(tb, 512)], PA[:, 512:1024], sgt[:], op=ALU.mult),
                      reads=["pu", "sgt"], writes=[f"hw{ei}"], tag=ek + f"m{tb}")
                    A("vector", lambda h, ei=ei, tb=tb: h.tensor_tensor(
                        hwt[ei][:, ts(tb, 512)], hwt[ei][:, ts(tb, 512)], PA[:, 1024:1536],
                        op=ALU.mult),
                      reads=[f"hw{ei}", "pc"], writes=[f"hw{ei}"], tag=ek + f"w{tb}")
                A("gpsimd", lambda h, e=e, ei=ei: h.dma_start(out=de[ei][:], in_=dwd[e]),
                  writes=[f"de{ei}"], dma=True, tag=ek + "ldd")
            for dt in range(ND):
                for tb in range(NB4):
                    A("tensor", lambda h, grp=grp, dt=dt, tb=tb: [
                        h.matmul(PB[:, 0:512], de[ei][:, ts(dt, 128)],
                                 hwt[ei][:, ts(tb, 512)], start=(ei == 0), stop=(ei == EGR - 1))
                        for ei in range(EGR)][-1],
                      reads=[f"hw{x}" for x in range(EGR)] + [f"de{x}" for x in range(EGR)],
                      writes=["pd"], tag=f"dn{grp}.{dt}.{tb}")
                    if grp == 0:
                        A("scalar", lambda h, dt=dt, tb=tb: h.activation(
                            ffacc[:, dt * TLOC + tb * 512: dt * TLOC + tb * 512 + 512],
                            PB[:, 0:512], AF.Copy),
                          reads=["pd"], writes=["ffacc"], tag=f"fc{grp}.{dt}.{tb}")
                    else:
                        A("vector", lambda h, dt=dt, tb=tb: h.tensor_tensor(
                            ffacc[:, dt * TLOC + tb * 512: dt * TLOC + tb * 512 + 512],
                            ffacc[:, dt * TLOC + tb * 512: dt * TLOC + tb * 512 + 512],
                            PB[:, 0:512], op=ALU.add),
                          reads=["pd", "ffacc"], writes=["ffacc"], tag=f"fc{grp}.{dt}.{tb}")

        for st in range(4):
            for tb in range(NB4):
                A("tensor", lambda h, st=st, tb=tb: [
                    h.matmul(PA[:, 0:512], sgw[:, dt * 512 + st * 128: dt * 512 + st * 128 + 128],
                             xnT[:, dt * TLOC + tb * 512: dt * TLOC + tb * 512 + 512],
                             start=(dt == 0), stop=(dt == ND - 1))
                    for dt in range(ND)][-1],
                  reads=["xnT", "sgw"], writes=["pg"], tag=f"sg{st}.{tb}")
                A("tensor", lambda h, st=st, tb=tb: [
                    h.matmul(PA[:, 512:1024], suw[:, dt * 512 + st * 128: dt * 512 + st * 128 + 128],
                             xnT[:, dt * TLOC + tb * 512: dt * TLOC + tb * 512 + 512],
                             start=(dt == 0), stop=(dt == ND - 1))
                    for dt in range(ND)][-1],
                  reads=["xnT", "suw"], writes=["pu"], tag=f"su{st}.{tb}")
                A("scalar", lambda h: h.activation(sgt[:], PA[:, 0:512], AF.Silu),
                  reads=["pg"], writes=["sgt"], tag=f"ss{st}.{tb}")
                A("vector", lambda h, st=st, tb=tb: h.tensor_tensor(
                    hwt[st][:, ts(tb, 512)], PA[:, 512:1024], sgt[:], op=ALU.mult),
                  reads=["pu", "sgt"], writes=[f"hw{st}"], tag=f"sm{st}.{tb}")
        for dt in range(ND):
            for tb in range(NB4):
                A("tensor", lambda h, dt=dt, tb=tb: [
                    h.matmul(PB[:, 0:512], sdw[:, st * D + dt * 128: st * D + dt * 128 + 128],
                             hwt[st][:, ts(tb, 512)], start=(st == 0), stop=(st == 3))
                    for st in range(4)][-1],
                  reads=["hw0", "hw1", "hw2", "hw3", "sdw"], writes=["pd"], tag=f"sd{dt}.{tb}")
                A("vector", lambda h, dt=dt, tb=tb: h.tensor_tensor(
                    ffacc[:, dt * TLOC + tb * 512: dt * TLOC + tb * 512 + 512],
                    ffacc[:, dt * TLOC + tb * 512: dt * TLOC + tb * 512 + 512],
                    PB[:, 0:512], op=ALU.add),
                  reads=["pd", "ffacc"], writes=["ffacc"], tag=f"sf{dt}.{tb}")

        for dt in range(ND):
            A("sync", lambda h, dt=dt: h.dma_start(out=x0t[:], in_=x0Td[ts(dt, 128), :]),
              writes=["x0t"], dma=True, tag=f"lx0{dt}")
            A("sync", lambda h, dt=dt: h.dma_start(out=a0t[:], in_=at0d[ts(dt, 128), :]),
              writes=["a0t"], dma=True, tag=f"la0{dt}")
            A("vector", lambda h, dt=dt: h.tensor_tensor(
                x1t[:], x0t[:], ffacc[:, ts(dt, TLOC)], op=ALU.add),
              reads=["x0t", "ffacc"], writes=["x1t"], tag=f"x1a{dt}")
            A("vector", lambda h, dt=dt: h.tensor_tensor(x1t[:], x1t[:], a0t[:], op=ALU.add),
              reads=["x1t", "a0t"], writes=["x1t"], tag=f"x1b{dt}")
            A("sync", lambda h, dt=dt: h.dma_start(out=x1Td[ts(dt, 128), :], in_=x1t[:]),
              reads=["x1t"], writes=["stored"], dma=True, tag=f"st{dt}")
        fin = A("sync", lambda h: h.nop(nofuse=True), reads=["stored"], tag="fin")
        for op in S.ops:
            if op.dma and op.tag.startswith("st") and op not in fin.deps:
                fin.deps.append(op)
        S.run_block(nc, ctx)
    return nc


def phase_ln_fm_bf16(S, nc, xt, xout, work, T, onesbb, gcol=None, bcol=None, key=""):
    NB = T // 512
    sqt = work["sqtb"]; mu = work["mu"]; vart = work["vart"]
    rstd = work["rstd"]; nmr = work["nmr"]; tmp = work["tmp"]
    ps_a = work["ps_a"]; ps_b = work["ps_b"]
    onesc = work["onesc"]
    K = key
    for i in range(ND):
        S.add("scalar", lambda h, i=i: h.activation(sqt[:], xt[i], AF.Square),
              reads=[K + f"x{i}"], writes=[K + "sqt"], tag=K + f"sq{i}")
        for b_ in range(NB):
            S.add("tensor", lambda h, i=i, b_=b_: h.matmul(
                ps_a[0:1, ts(b_, 512)], onesbb[:, 0:1], xt[i][:, ts(b_, 512)],
                start=(i == 0), stop=(i == ND - 1)),
                reads=[K + f"x{i}", "onesbb"], writes=[K + "psa"], tag=K + f"mmS{i}.{b_}")
        for b_ in range(NB):
            S.add("tensor", lambda h, i=i, b_=b_: h.matmul(
                ps_b[0:1, ts(b_, 512)], onesbb[:, 0:1], sqt[:, ts(b_, 512)],
                start=(i == 0), stop=(i == ND - 1)),
                reads=[K + "sqt", "onesbb"], writes=[K + "psb"], tag=K + f"mmQ{i}.{b_}")
    S.add("vector", lambda h: h.tensor_scalar(mu[:], ps_a[0:1, :], 1.0 / D, None, op0=ALU.mult),
          reads=[K + "psa"], writes=[K + "mu"], tag=K + "mu")
    S.add("vector", lambda h: h.tensor_tensor(vart[:], mu[:], mu[:], op=ALU.mult),
          reads=[K + "mu"], writes=[K + "vart"], tag=K + "mu2")
    S.add("vector", lambda h: h.tensor_scalar(vart[:], vart[:], EPS, None, op0=ALU.subtract),
          reads=[K + "vart"], writes=[K + "vart"], tag=K + "m2e")
    S.add("vector", lambda h: h.scalar_tensor_tensor(vart[:], ps_b[0:1, :], 1.0 / D, vart[:],
                                                     op0=ALU.mult, op1=ALU.subtract),
          reads=[K + "psb", K + "vart"], writes=[K + "vart"], tag=K + "var")
    S.add("scalar", lambda h: h.activation(vart[:], vart[:], AF.Ln),
          reads=[K + "vart"], writes=[K + "vart"], tag=K + "ln")
    S.add("scalar", lambda h: h.activation(rstd[:], vart[:], AF.Exp, scale=-0.5),
          reads=[K + "vart"], writes=[K + "rstd"], tag=K + "rstd")
    S.add("vector", lambda h: h.tensor_tensor(nmr[:], mu[:], rstd[:], op=ALU.mult),
          reads=[K + "mu", K + "rstd"], writes=[K + "nmr"], tag=K + "nmr1")
    S.add("vector", lambda h: h.tensor_scalar(nmr[:], nmr[:], -1.0, None, op0=ALU.mult),
          reads=[K + "nmr"], writes=[K + "nmr"], tag=K + "nmr2")
    for b_ in range(NB):
        S.add("tensor", lambda h, b_=b_: h.matmul(
            ps_a[:, 0:512], work["onesf"][0:1, :], rstd[:, ts(b_, 512)], start=True, stop=True),
            reads=[K + "rstd", "onesf"], writes=[K + "psa"], tag=K + f"bcR{b_}")
        S.add("tensor", lambda h, b_=b_: h.matmul(
            ps_b[:, 0:512], work["onesf"][0:1, :], nmr[:, ts(b_, 512)], start=True, stop=True),
            reads=[K + "nmr", "onesf"], writes=[K + "psb"], tag=K + f"bcM{b_}")
        for i in range(ND):
            S.add("vector", lambda h, i=i, b_=b_: h.tensor_tensor(
                tmp[:], xt[i][:, ts(b_, 512)], ps_a[:, 0:512], op=ALU.mult),
                reads=[K + f"x{i}", K + "psa"], writes=[K + "tmp"], tag=K + f"xr{i}.{b_}")
            if gcol is None:
                S.add("vector", lambda h, i=i, b_=b_: h.tensor_tensor(
                    xout[:, i * T + b_ * 512: i * T + b_ * 512 + 512],
                    tmp[:], ps_b[:, 0:512], op=ALU.add),
                    reads=[K + "tmp", K + "psb"], writes=[K + "xout"], tag=K + f"xo{i}.{b_}")
            else:
                S.add("vector", lambda h, i=i, b_=b_: h.tensor_tensor(
                    tmp[:], tmp[:], ps_b[:, 0:512], op=ALU.add),
                    reads=[K + "tmp", K + "psb"], writes=[K + "tmp"], tag=K + f"xp{i}.{b_}")
                S.add("vector", lambda h, i=i, b_=b_: h.tensor_scalar(
                    xout[:, i * T + b_ * 512: i * T + b_ * 512 + 512],
                    tmp[:], gcol[:, i:i + 1], bcol[:, i:i + 1], op0=ALU.mult, op1=ALU.add),
                    reads=[K + "tmp"], writes=[K + "xout"], sreads=[K + "gb"], tag=K + f"xa{i}.{b_}")


def build_front_a_launch():
    nc = bass.Bass("TRN2", target_bir_lowering=False)
    x1Td = nc.dram_tensor("x1T", [D, TLOC], F32, kind="ExternalInput").ap()
    wind = nc.dram_tensor("W_in", [D, 2 * DI], F32, kind="ExternalInput").ap()
    lngd = nc.dram_tensor("ln_gT", [128, ND], F32, kind="ExternalInput").ap()
    lnbd = nc.dram_tensor("ln_bT", [128, ND], F32, kind="ExternalInput").ap()
    onesd = nc.dram_tensor("ones128", [128, 128], F32, kind="ExternalInput").ap()
    onesbd = nc.dram_tensor("onesbf", [128, 128], BF16, kind="ExternalInput").ap()
    xn2d = nc.dram_tensor("xn2T", [D, TLOC], BF16, kind="ExternalOutput").ap()
    xhd = nc.dram_tensor("xhT", [DI, TLOC], BF16, kind="ExternalOutput").ap()
    szd = nc.dram_tensor("szT", [DI, TLOC], BF16, kind="ExternalOutput").ap()

    NB4 = TLOC // 512
    with contextlib.ExitStack() as ctx:
        sb = lambda *a: ctx.enter_context(nc.sbuf_tensor(*a))
        x1t = [sb(f"x1t{i}", [128, TLOC], F32) for i in range(ND)]
        xn2b = sb("xn2b", [128, ND * TLOC], BF16)
        xnnT = sb("xnnT", [128, ND * TLOC], BF16)
        gcol = sb("gcol", [128, ND], F32)
        bcol = sb("bcol", [128, ND], F32)
        wbuf = [sb(f"wbuf{j}", [128, ND * 128], BF16) for j in range(2)]
        ostg = [sb(f"ostg{j}", [128, TLOC], BF16) for j in range(2)]
        work = mk_ln_work(nc, ctx, TLOC)
        work["onesf"] = work["onesc"]
        onesbb = sb("onesbb", [128, 128], BF16)

        S = Sched()
        A = S.add
        A("sync", lambda h: h.dma_start(out=work["onesc"][:], in_=onesd),
          writes=["onesc", "onesf"], dma=True, tag="ldone")
        A("sync", lambda h: h.dma_start(out=onesbb[:], in_=onesbd), writes=["onesbb"], dma=True, tag="ldonb")
        A("gpsimd", lambda h: h.dma_start(out=gcol[:], in_=lngd), writes=["Lgb"], dma=True, tag="ldg")
        A("gpsimd", lambda h: h.dma_start(out=bcol[:], in_=lnbd), writes=["Lgb"], dma=True, tag="ldb")
        for i in range(ND):
            A("sync", lambda h, i=i: h.dma_start(out=x1t[i][:], in_=x1Td[ts(i, 128), :]),
              writes=[f"x{i}"], dma=True, tag=f"ldx{i}")
        phase_ln_feature_major(S, nc, x1t, xn2b, work, TLOC, key="")
        A("sync", lambda h: h.dma_start(
            out=xn2d.rearrange("(dt p) t -> p dt t", p=128),
            in_=xn2b[:].rearrange("p (dt t) -> p dt t", t=TLOC)),
          reads=["xout"], writes=["stxn"], dma=True, tag="sxn2")
        xt2 = [xn2b[:, ts(i, TLOC)] for i in range(ND)]
        for i in range(ND):
            S.last_writer["L" + f"x{i}"] = S.last_writer.get("xout")
        phase_ln_fm_bf16(S, nc, xt2, xnnT, work, TLOC, onesbb, gcol=gcol, bcol=bcol, key="L")
        A("tensor", lambda h: h.nop(),
          reads=["Lxout"], writes=["psa", "psb", "Lpsa", "Lpsb", "pp0", "pp1", "pp2", "pp3"],
          tag="fence")
        for ci in range(32):
            jb = ci % 2
            A("gpsimd", lambda h, ci=ci, jb=jb: h.dma_start(
                out=wbuf[jb][:], in_=wind[:, ts(ci, 128)].rearrange("(dt p) c -> p dt c", p=128)),
              writes=[f"wb{jb}"], dma=True, tag=f"ldw{ci}")
            for tb in range(NB4):
                A("tensor", lambda h, ci=ci, jb=jb, tb=tb: [
                    h.matmul(work["ps_a"][:, ts(tb, 512)],
                             wbuf[jb][:, ts(dt, 128)],
                             xnnT[:, dt * TLOC + tb * 512: dt * TLOC + tb * 512 + 512],
                             start=(dt == 0), stop=(dt == ND - 1))
                    for dt in range(ND)][-1],
                  reads=["Lxout", f"wb{jb}"], writes=[f"pp{tb}"], tag=f"mm{ci}.{tb}")
                if ci < 16:
                    A("scalar", lambda h, ci=ci, jb=jb, tb=tb: h.activation(
                        ostg[jb][:, ts(tb, 512)], work["ps_a"][:, ts(tb, 512)], AF.Copy),
                      reads=[f"pp{tb}"], writes=[f"os{jb}"], tag=f"ev{ci}.{tb}")
                else:
                    A("scalar", lambda h, ci=ci, jb=jb, tb=tb: h.activation(
                        ostg[jb][:, ts(tb, 512)], work["ps_a"][:, ts(tb, 512)], AF.Silu),
                      reads=[f"pp{tb}"], writes=[f"os{jb}"], tag=f"ev{ci}.{tb}")
            dst = xhd if ci < 16 else szd
            row = (ci % 16) * 128
            A("sync", lambda h, dst=dst, row=row, jb=jb: h.dma_start(
                out=dst[row:row + 128, :], in_=ostg[jb][:]),
              reads=[f"os{jb}"], writes=["stored"], dma=True, tag=f"st{ci}")
        fin = A("sync", lambda h: h.nop(nofuse=True), reads=["stored", "stxn"], tag="fin")
        for op in S.ops:
            if op.dma and (op.tag.startswith("st")) and op not in fin.deps:
                fin.deps.append(op)
        S.run_block(nc, ctx)
    return nc


def build_front_b_launch():
    nc = bass.Bass("TRN2", target_bir_lowering=False)
    xhd = nc.dram_tensor("xhT", [DI, TLOC], BF16, kind="ExternalInput").ap()
    xhhd = nc.dram_tensor("xh_halo", [DI, 3], BF16, kind="ExternalInput").ap()
    cwd = nc.dram_tensor("conv_wT", [128, 16 * 4], F32, kind="ExternalInput").ap()
    cbd = nc.dram_tensor("conv_bT", [128, 16], F32, kind="ExternalInput").ap()
    wxd = nc.dram_tensor("W_x", [DI, 96], F32, kind="ExternalInput").ap()
    wdtd = nc.dram_tensor("W_dt", [DTR, DI], F32, kind="ExternalInput").ap()
    bdtd = nc.dram_tensor("b_dtT", [128, 16], F32, kind="ExternalInput").ap()
    xcd = nc.dram_tensor("xcT", [DI, TLOC], BF16, kind="ExternalOutput").ap()
    dtd = nc.dram_tensor("dtT", [DI, TLOC], F16, kind="ExternalOutput").ap()
    bcd = nc.dram_tensor("BCT", [32, TLOC], F32, kind="ExternalOutput").ap()

    NDQ = DI // 128
    NB4 = TLOC // 512
    with contextlib.ExitStack() as ctx:
        sb = lambda *a: ctx.enter_context(nc.sbuf_tensor(*a))
        xht = sb("xht", [128, TLOC + 3], BF16)
        cw = sb("cw", [128, 64], F32)
        cb = sb("cb", [128, 16], F32)
        bdt = sb("bdt", [128, 16], F32)
        cacc = sb("cacc", [128, TLOC], F32)
        xcT = sb("xcTs", [128, NDQ * TLOC], BF16)
        wx = sb("wx", [128, NDQ * 96], BF16)
        wdt = sb("wdt", [64, DI], BF16)
        dbcb = sb("dbcb", [96, TLOC], BF16)
        bcs = sb("bcs", [32, TLOC], F32)
        xsp = sb("xsp", [128, TLOC], F32)
        ab2 = sb("ab2", [128, TLOC], F32)
        dtt = sb("dtt", [128, TLOC], F16)
        PA = ctx.enter_context(nc.psum_tensor("PA", [128, 2048], F32))
        PB = ctx.enter_context(nc.psum_tensor("PB", [128, 2048], F32))

        S = Sched()
        A = S.add
        A("gpsimd", lambda h: h.dma_start(out=cw[:], in_=cwd), writes=["cw"], dma=True, tag="ldcw")
        A("gpsimd", lambda h: h.dma_start(out=cb[:], in_=cbd), writes=["cb"], dma=True, tag="ldcb")
        A("gpsimd", lambda h: h.dma_start(out=bdt[:], in_=bdtd), writes=["bdt"], dma=True, tag="ldbd")
        A("gpsimd", lambda h: h.dma_start(out=wx[:], in_=wxd.rearrange("(dq p) c -> p dq c", p=128)),
          writes=["wx"], dma=True, tag="ldwx")
        A("gpsimd", lambda h: h.dma_start(out=wdt[:], in_=wdtd), writes=["wdt"], dma=True, tag="ldwdt")
        for dq in range(NDQ):
            A("sync", lambda h, dq=dq: h.dma_start(out=xht[:, 0:3], in_=xhhd[ts(dq, 128), :]),
              writes=["xh1"], dma=True, tag=f"lh{dq}")
            A("sync", lambda h, dq=dq: h.dma_start(out=xht[:, 3:], in_=xhd[ts(dq, 128), :]),
              writes=["xh2"], dma=True, tag=f"lx{dq}")
            A("vector", lambda h, dq=dq: h.tensor_scalar(
                cacc[:], xht[:, 0:TLOC], cw[:, dq * 4:dq * 4 + 1], None, op0=ALU.mult),
              reads=["xh1", "xh2"], sreads=["cw"], writes=["cacc"], tag=f"c0.{dq}")
            for k in range(1, 4):
                A("vector", lambda h, dq=dq, k=k: h.scalar_tensor_tensor(
                    cacc[:], xht[:, k:k + TLOC], cw[:, dq * 4 + k:dq * 4 + k + 1], cacc[:],
                    op0=ALU.mult, op1=ALU.add),
                  reads=["xh1", "xh2", "cacc"], sreads=["cw"], writes=["cacc"], tag=f"c{k}.{dq}")
            A("scalar", lambda h, dq=dq: h.activation(
                xcT[:, ts(dq, TLOC)], cacc[:], AF.Silu, bias=cb[:, dq:dq + 1]),
              reads=["cacc"], sreads=["cb"], writes=["xcT"], tag=f"si{dq}")
            A("sync", lambda h, dq=dq: h.dma_start(out=xcd[ts(dq, 128), :], in_=xcT[:, ts(dq, TLOC)]),
              reads=["xcT"], writes=["stored"], dma=True, tag=f"stc{dq}")
        for tb in range(NB4):
            A("tensor", lambda h, tb=tb: [
                h.matmul(PA[0:96, ts(0, 512)], wx[:, dq * 96:dq * 96 + 96],
                         xcT[:, dq * TLOC + tb * 512: dq * TLOC + tb * 512 + 512],
                         start=(dq == 0), stop=(dq == NDQ - 1))
                for dq in range(NDQ)][-1],
              reads=["xcT", "wx"], writes=["pdb"], tag=f"db{tb}")
            A("scalar", lambda h, tb=tb: h.activation(dbcb[:, ts(tb, 512)], PA[0:96, 0:512], AF.Copy),
              reads=["pdb"], writes=["dbcb"], tag=f"dbe{tb}")
            A("scalar", lambda h, tb=tb: h.activation(bcs[0:32, ts(tb, 512)], PA[64:96, 0:512], AF.Copy),
              reads=["pdb"], writes=["bcf"], tag=f"dbf{tb}")
        A("sync", lambda h: h.dma_start(out=bcd, in_=bcs[0:32, :]),
          reads=["bcf"], writes=["stored"], dma=True, tag="stbc")
        for dq in range(NDQ):
            for tb in range(NB4):
                A("tensor", lambda h, dq=dq, tb=tb: h.matmul(
                    PB[:, 512:1024],
                    wdt[:, ts(dq, 128)], dbcb[0:64, ts(tb, 512)], start=True, stop=True),
                  reads=["dbcb", "wdt"], writes=["pdt"], tag=f"dt{dq}.{tb}")
                A("vector", lambda h, dq=dq, tb=tb: h.tensor_scalar(
                    xsp[:, ts(tb, 512)], PB[:, 512:1024], bdt[:, dq:dq + 1], None, op0=ALU.add),
                  reads=["pdt"], sreads=["bdt"], writes=["xsp"], tag=f"dta{dq}.{tb}")
            A("scalar", lambda h: h.activation(ab2[:], xsp[:], AF.Abs),
              reads=["xsp"], writes=["ab2"], tag=f"dtb{dq}")
            A("scalar", lambda h: h.activation(ab2[:], ab2[:], AF.Exp, scale=-1.0),
              reads=["ab2"], writes=["ab2"], tag=f"dtc{dq}")
            A("vector", lambda h: h.tensor_scalar(ab2[:], ab2[:], 1.0, None, op0=ALU.add),
              reads=["ab2"], writes=["ab2"], tag=f"dtd{dq}")
            A("scalar", lambda h: h.activation(ab2[:], ab2[:], AF.Ln),
              reads=["ab2"], writes=["ab2"], tag=f"dte{dq}")
            A("vector", lambda h: h.tensor_scalar(xsp[:], xsp[:], 0.0, None, op0=ALU.max),
              reads=["xsp"], writes=["xsp"], tag=f"dtf{dq}")
            A("vector", lambda h: h.tensor_tensor(dtt[:], xsp[:], ab2[:], op=ALU.add),
              reads=["xsp", "ab2"], writes=["dtt"], tag=f"dtg{dq}")
            A("sync", lambda h, dq=dq: h.dma_start(out=dtd[ts(dq, 128), :], in_=dtt[:]),
              reads=["dtt"], writes=["stored"], dma=True, tag=f"std{dq}")
        fin = A("sync", lambda h: h.nop(nofuse=True), reads=["stored"], tag="fin")
        for op in S.ops:
            if op.dma and op.tag.startswith("st") and op not in fin.deps:
                fin.deps.append(op)
        S.run_block(nc, ctx)
    return nc


def build_scan_launch():
    nc = bass.Bass("TRN2", target_bir_lowering=False)
    xcd = nc.dram_tensor("xcq", [512, L], BF16, kind="ExternalInput").ap()
    dtd = nc.dram_tensor("dtq", [512, L], F16, kind="ExternalInput").ap()
    bcd = nc.dram_tensor("BCq", [32, L], F32, kind="ExternalInput").ap()
    aqd = nc.dram_tensor("Aq", [128, 64], F32, kind="ExternalInput").ap()
    oh16d = nc.dram_tensor("onehot16", [16, 16 * 128], BF16, kind="ExternalInput").ap()
    yqd = nc.dram_tensor("yq", [512, L], BF16, kind="ExternalOutput").ap()

    TB = 1024
    NTB = L // TB
    with contextlib.ExitStack() as ctx:
        sb = lambda *a: ctx.enter_context(nc.sbuf_tensor(*a))
        xct = [sb(f"xct{j}", [128, 4 * TB], BF16) for j in range(2)]
        dtt = [sb(f"dtt{j}", [128, 4 * TB], F16) for j in range(2)]
        bct = [sb(f"bct{j}", [16, TB], BF16) for j in range(2)]
        cct = [sb(f"cct{j}", [16, TB], BF16) for j in range(2)]
        oh16 = sb("oh16", [16, 16 * 128], BF16)
        aq = sb("aq", [128, 64], F32)
        acol = sb("acol", [128, 64], F32)
        dtx = sb("dtx", [128, TB], BF16)
        dA = sb("dA", [128, TB], F32)
        dBu = sb("dBu", [128, TB], F32)
        hb = [sb(f"hb{j}", [128, TB], F32) for j in range(2)]
        tmp = sb("tmps", [128, TB], F32)
        chain = sb("chain", [128, 64], F32)
        ystg = sb("ystg", [128, TB], BF16)
        PA = ctx.enter_context(nc.psum_tensor("PA", [128, 2048], F32))
        PB = ctx.enter_context(nc.psum_tensor("PB", [128, 2048], F32))

        S = Sched()
        A = S.add
        A("gpsimd", lambda h: h.dma_start(out=oh16[:], in_=oh16d), writes=["oh16"], dma=True, tag="ldoh")
        A("gpsimd", lambda h: h.dma_start(out=aq[:], in_=aqd), writes=["aq"], dma=True, tag="ldaq")
        A("scalar", lambda h: h.activation(acol[:], aq[:], AF.Exp),
          reads=["aq"], writes=["acol"], tag="aexp")
        A("vector", lambda h: h.tensor_scalar(acol[:], acol[:], -1.0, None, op0=ALU.mult),
          reads=["acol"], writes=["acol"], tag="aneg")
        A("vector", lambda h: h.memset(chain[:], 0.0), writes=["chain"], tag="ch0")

        for tb in range(NTB):
            j = tb % 2
            A("sync", lambda h, tb=tb, j=j: h.dma_start(
                out=xct[j][:], in_=xcd[:, ts(tb, TB)].rearrange("(dq p) t -> p dq t", p=128)),
              writes=[f"xc{j}"], dma=True, tag=f"lxc{tb}")
            A("sync", lambda h, tb=tb, j=j: h.dma_start(
                out=dtt[j][:], in_=dtd[:, ts(tb, TB)].rearrange("(dq p) t -> p dq t", p=128)),
              writes=[f"dt{j}"], dma=True, tag=f"ldt{tb}")
            A("gpsimd", lambda h, tb=tb, j=j: h.dma_start(out=bct[j][:], in_=bcd[0:16, ts(tb, TB)]),
              writes=[f"bc{j}"], dma=True, tag=f"lbc{tb}")
            A("gpsimd", lambda h, tb=tb, j=j: h.dma_start(out=cct[j][:], in_=bcd[16:32, ts(tb, TB)]),
              writes=[f"cc{j}"], dma=True, tag=f"lcc{tb}")
            for dq in range(4):
                A("vector", lambda h, j=j, dq=dq: h.tensor_tensor(
                    dtx[:], dtt[j][:, ts(dq, TB)], xct[j][:, ts(dq, TB)], op=ALU.mult),
                  reads=[f"dt{j}", f"xc{j}"], writes=["dtx"], tag=f"dx{tb}.{dq}")
                for n in range(NST):
                    kk = f"s{tb}.{dq}.{n}"
                    hj = hb[n % 2]
                    A("tensor", lambda h, j=j, n=n: [
                        h.matmul(PA[:, ts(half, 512)], oh16[:, ts(n, 128)],
                                 bct[j][0:16, ts(half, 512)], start=True, stop=True)
                        for half in range(2)][-1],
                      reads=[f"bc{j}", "oh16"], writes=["pBb"], tag=kk + "B")
                    A("tensor", lambda h, j=j, n=n: [
                        h.matmul(PB[:, ts(half, 512)], oh16[:, ts(n, 128)],
                                 cct[j][0:16, ts(half, 512)], start=True, stop=True)
                        for half in range(2)][-1],
                      reads=[f"cc{j}", "oh16"], writes=["pCb"], tag=kk + "C")
                    A("scalar", lambda h, j=j, dq=dq, n=n: h.activation(
                        dA[:], dtt[j][:, ts(dq, TB)], AF.Exp, scale=acol[:, dq * 16 + n:dq * 16 + n + 1]),
                      reads=[f"dt{j}"], sreads=["acol"], writes=["dA"], tag=kk + "a")
                    A("vector", lambda h: h.tensor_tensor(dBu[:], dtx[:], PA[:, 0:TB], op=ALU.mult),
                      reads=["dtx", "pBb"], writes=["dBu"], tag=kk + "u")
                    A("vector", lambda h, hj=hj, dq=dq, n=n: h.tensor_tensor_scan(
                        hj[:], dA[:], dBu[:], chain[:, dq * 16 + n:dq * 16 + n + 1],
                        op0=ALU.mult, op1=ALU.add),
                      reads=["dA", "dBu"], sreads=[f"ch{dq}.{n}"],
                      writes=[f"h{n % 2}"], tag=kk + "h")
                    A("vector", lambda h, hj=hj, dq=dq, n=n: h.tensor_copy(
                        chain[:, dq * 16 + n:dq * 16 + n + 1], hj[:, TB - 1:TB]),
                      reads=[f"h{n % 2}"], writes=[f"ch{dq}.{n}", "chain"], tag=kk + "cc")
                    if n == 0:
                        A("vector", lambda h, hj=hj: h.tensor_tensor(
                            PA[:, 1024:2048], hj[:], PB[:, 0:TB], op=ALU.mult),
                          reads=[f"h{n % 2}", "pCb"], writes=["pY"], tag=kk + "y0")
                    else:
                        A("vector", lambda h, hj=hj: h.tensor_tensor(
                            tmp[:], hj[:], PB[:, 0:TB], op=ALU.mult),
                          reads=[f"h{n % 2}", "pCb"], writes=["tmps"], tag=kk + "y1")
                        A("vector", lambda h: h.tensor_tensor(
                            PA[:, 1024:2048], PA[:, 1024:2048], tmp[:], op=ALU.add),
                          reads=["tmps", "pY"], writes=["pY"], tag=kk + "y2")
                A("scalar", lambda h: h.activation(ystg[:], PA[:, 1024:2048], AF.Copy),
                  reads=["pY"], writes=["ystg"], tag=f"ye{tb}.{dq}")
                A("sync", lambda h, tb=tb, dq=dq: h.dma_start(
                    out=yqd[ts(dq, 128), ts(tb, TB)], in_=ystg[:]),
                  reads=["ystg"], writes=["stored"], dma=True, tag=f"st{tb}.{dq}")
        fin = A("sync", lambda h: h.nop(nofuse=True), reads=["stored"], tag="fin")
        for op in S.ops:
            if op.dma and op.tag.startswith("st") and op not in fin.deps:
                fin.deps.append(op)
        S.run_block(nc, ctx)
    return nc


def build_back_launch():
    nc = bass.Bass("TRN2", target_bir_lowering=False)
    x1Td = nc.dram_tensor("x1T", [D, TLOC], F32, kind="ExternalInput").ap()
    xn2d = nc.dram_tensor("xn2T", [D, TLOC], BF16, kind="ExternalInput").ap()
    yTd = nc.dram_tensor("yT_loc", [DI, TLOC], BF16, kind="ExternalInput").ap()
    xcd = nc.dram_tensor("xcT", [DI, TLOC], BF16, kind="ExternalInput").ap()
    szd = nc.dram_tensor("szT", [DI, TLOC], BF16, kind="ExternalInput").ap()
    dcold = nc.dram_tensor("DcolT", [128, 16], F32, kind="ExternalInput").ap()
    wod = nc.dram_tensor("W_out1", [DI, D], BF16, kind="ExternalInput").ap()
    sw1d = nc.dram_tensor("shared_w1T", [D, 4096], BF16, kind="ExternalInput").ap()
    sb1d = nc.dram_tensor("shared_b1T", [128, 32], F32, kind="ExternalInput").ap()
    sw2d = nc.dram_tensor("shared_w2T", [4096, D], BF16, kind="ExternalInput").ap()
    sb2d = nc.dram_tensor("shared_b2T", [128, 8], F32, kind="ExternalInput").ap()
    rwd = nc.dram_tensor("router1_wT", [D, 8], BF16, kind="ExternalInput").ap()
    rbd = nc.dram_tensor("router1_b", [1, 8], BF16, kind="ExternalInput").ap()
    w1d = nc.dram_tensor("spec_w1T", [8, D, 4096], BF16, kind="ExternalInput").ap()
    b1d = nc.dram_tensor("spec_b1T", [128, 8 * 32], F32, kind="ExternalInput").ap()
    w2d = nc.dram_tensor("spec_w2T", [8, 4096, D], BF16, kind="ExternalInput").ap()
    b2d = nc.dram_tensor("spec_b2h", [8, D], BF16, kind="ExternalInput").ap()
    oh8d = nc.dram_tensor("onehot8", [8, 8 * 128], BF16, kind="ExternalInput").ap()
    onesbd = nc.dram_tensor("onesbf", [128, 128], BF16, kind="ExternalInput").ap()
    idbd = nc.dram_tensor("idbf", [128, 128], BF16, kind="ExternalInput").ap()
    outd = nc.dram_tensor("outT", [D, TLOC], F32, kind="ExternalOutput").ap()

    NDQ = DI // 128
    NB4 = TLOC // 512
    NT = TLOC // 128
    NFQ = 8
    with contextlib.ExitStack() as ctx:
        sb = lambda *a: ctx.enter_context(nc.sbuf_tensor(*a))
        xn2T = sb("xn2Ts", [128, ND * TLOC], BF16)
        accT = sb("accT", [128, ND * TLOC], F32)
        y3T = sb("y3T", [128, 8 * TLOC], BF16)
        wo = sb("wo", [128, 8 * ND * 128], BF16)
        yt = sb("yt", [128, TLOC], BF16)
        xct = sb("xct", [128, TLOC], BF16)
        szt = sb("szt", [128, TLOC], BF16)
        x1t = sb("x1t", [128, TLOC], F32)
        dcol = sb("dcol", [128, 16], F32)
        sb2c = sb("sb2c", [128, 8], F32)
        sb1c = sb("sb1c", [128, 32], F32)
        b1c = sb("b1c", [128, 8 * 32], F32)
        b2h = sb("b2h", [8, D], BF16)
        oh8 = sb("oh8", [8, 8 * 128], BF16)
        onesb = sb("onesb", [128, 128], BF16)
        idb = sb("idb", [128, 128], BF16)
        rww = sb("rww", [128, ND * 8], BF16)
        rbb = sb("rbb", [1, 8], BF16)
        lg = sb("lg", [128, 8], F32)
        m8 = sb("m8", [128, 8], F32)
        nv0 = sb("nv0", [128, 1], F32)
        el = sb("el", [128, 8], F32)
        msk = sb("msk", [128, 8], F32)
        ev2 = sb("ev2", [128, 2], F32)
        ds = sb("ds", [128, 1], F32)
        rden = sb("rden", [128, 1], F32)
        compb = sb("compb", [128, NT * 8], BF16)
        combT = sb("combT", [8, TLOC], BF16)
        w1b = [sb(f"w1b{j}", [128, ND * 512], BF16) for j in range(2)]
        w2b = wo
        PA = ctx.enter_context(nc.psum_tensor("PA", [128, 2048], F32))
        PB = ctx.enter_context(nc.psum_tensor("PB", [128, 2048], F32))

        S = Sched()
        A = S.add
        for nm, dst, srcd in [("dcol", dcol, dcold), ("sb2c", sb2c, sb2d), ("sb1c", sb1c, sb1d),
                              ("b1c", b1c, b1d), ("oh8", oh8, oh8d), ("onesb", onesb, onesbd),
                              ("idb", idb, idbd), ("rbb", rbb, rbd)]:
            A("gpsimd", lambda h, dst=dst, srcd=srcd: h.dma_start(out=dst[:], in_=srcd),
              writes=[nm], dma=True, tag="ld" + nm)
        A("gpsimd", lambda h: h.dma_start(out=b2h[:], in_=b2d), writes=["b2h"], dma=True, tag="ldb2h")
        A("gpsimd", lambda h: h.dma_start(out=rww[:], in_=rwd.rearrange("(dt p) e -> p dt e", p=128)),
          writes=["rww"], dma=True, tag="ldrw")

        A("sync", lambda h: h.dma_start(out=xn2T[:], in_=xn2d.rearrange("(dt p) t -> p dt t", p=128)),
          writes=["xn2T"], dma=True, tag="ldxn2")
        for dt in range(ND):
            A("sync", lambda h, dt=dt: h.dma_start(out=x1t[:], in_=x1Td[ts(dt, 128), :]),
              writes=["x1t"], dma=True, tag=f"lx1{dt}")
            A("vector", lambda h, dt=dt: h.tensor_tensor(
                accT[:, ts(dt, TLOC)], x1t[:], xn2T[:, ts(dt, TLOC)], op=ALU.add),
              reads=["x1t", "xn2T"], writes=["accT"], tag=f"ai{dt}")
            A("vector", lambda h, dt=dt: h.tensor_scalar(
                accT[:, ts(dt, TLOC)], accT[:, ts(dt, TLOC)], sb2c[:, dt:dt + 1], None, op0=ALU.add),
              reads=["accT"], sreads=["sb2c"], writes=["accT"], tag=f"ab{dt}")
        for half in range(2):
            A("gpsimd", lambda h, half=half: h.dma_start(
                out=wo[:], in_=wod[half * 1024:(half + 1) * 1024, :]
                .rearrange("(dq p) (dt c) -> p dq dt c", p=128, c=128)),
              writes=["wo"], dma=True, tag=f"ldwo{half}")
            for dq8 in range(8):
                dq = half * 8 + dq8
                A("sync", lambda h, dq=dq: h.dma_start(out=yt[:], in_=yTd[ts(dq, 128), :]),
                  writes=["yt"], dma=True, tag=f"ly{dq}")
                A("sync", lambda h, dq=dq: h.dma_start(out=xct[:], in_=xcd[ts(dq, 128), :]),
                  writes=["xctb"], dma=True, tag=f"lc{dq}")
                A("sync", lambda h, dq=dq: h.dma_start(out=szt[:], in_=szd[ts(dq, 128), :]),
                  writes=["sztb"], dma=True, tag=f"lz{dq}")
                A("vector", lambda h, dq=dq, dq8=dq8: h.scalar_tensor_tensor(
                    y3T[:, ts(dq8, TLOC)], xct[:], dcol[:, dq:dq + 1], yt[:], op0=ALU.mult, op1=ALU.add),
                  reads=["xctb", "yt"], sreads=["dcol"], writes=[f"y3_{dq8}"], tag=f"y2.{dq}")
                A("vector", lambda h, dq8=dq8: h.tensor_tensor(
                    y3T[:, ts(dq8, TLOC)], y3T[:, ts(dq8, TLOC)], szt[:], op=ALU.mult),
                  reads=[f"y3_{dq8}", "sztb"], writes=[f"y3_{dq8}"], tag=f"y3.{dq}")
            for dt in range(ND):
                for tb in range(NB4):
                    A("tensor", lambda h, dt=dt, tb=tb: [
                        h.matmul(PA[:, 0:512], wo[:, (dq8 * ND + dt) * 128:(dq8 * ND + dt) * 128 + 128],
                                 y3T[:, dq8 * TLOC + tb * 512: dq8 * TLOC + tb * 512 + 512],
                                 start=(dq8 == 0), stop=(dq8 == 7))
                        for dq8 in range(8)][-1],
                      reads=[f"y3_{x}" for x in range(8)] + ["wo"], writes=["pw"],
                      tag=f"wo{half}.{dt}.{tb}")
                    A("vector", lambda h, dt=dt, tb=tb: h.tensor_tensor(
                        accT[:, dt * TLOC + tb * 512: dt * TLOC + tb * 512 + 512],
                        accT[:, dt * TLOC + tb * 512: dt * TLOC + tb * 512 + 512],
                        PA[:, 0:512], op=ALU.add),
                      reads=["pw", "accT"], writes=["accT"], tag=f"wa{half}.{dt}.{tb}")
        for tt in range(NT):
            tk = f"r{tt}"
            A("tensor", lambda h, tt=tt: [
                h.matmul(PA[0:128, 1536:1544], xn2T[:, dt * TLOC + tt * 128: dt * TLOC + tt * 128 + 128],
                         rww[:, ts(dt, 8)], start=(dt == 0), stop=False)
                for dt in range(ND)][-1],
              reads=["xn2T", "rww"], writes=["prt"], tag=tk + "mm")
            A("tensor", lambda h: h.matmul(PA[0:128, 1536:1544], onesb[0:1, :], rbb[:],
                                           start=False, stop=True),
              reads=["rbb", "onesb"], writes=["prt"], tag=tk + "bias")
            A("scalar", lambda h: h.activation(lg[:], PA[0:128, 1536:1544], AF.Copy),
              reads=["prt"], writes=["lg"], tag=tk + "ev")
            A("vector", lambda h: h.max(m8[:], lg[:]), reads=["lg"], writes=["m8"], tag=tk + "m8")
            A("vector", lambda h: h.tensor_scalar(nv0[:], m8[:, 0:1], -1.0, None, op0=ALU.mult),
              reads=["m8"], writes=["nv0"], tag=tk + "nv")
            A("scalar", lambda h: h.activation(el[:], lg[:], AF.Exp, bias=nv0[:, 0:1]),
              reads=["lg"], sreads=["nv0"], writes=["el"], tag=tk + "el")
            A("vector", lambda h: h.tensor_scalar(msk[:], lg[:], m8[:, 1:2], None, op0=ALU.is_ge),
              reads=["lg"], sreads=["m8"], writes=["msk"], tag=tk + "mk")
            A("scalar", lambda h: h.activation(ev2[:], m8[:, 0:2], AF.Exp, bias=nv0[:, 0:1]),
              reads=["m8"], sreads=["nv0"], writes=["ev2"], tag=tk + "e2")
            A("vector", lambda h: h.reduce_sum(ds[:], ev2[:], axis=mybir.AxisListType.X),
              reads=["ev2"], writes=["ds"], tag=tk + "ds")
            A("vector", lambda h: h.reciprocal(rden[:], ds[:]), reads=["ds"], writes=["rden"], tag=tk + "rd")
            A("vector", lambda h: h.tensor_tensor(el[:], el[:], msk[:], op=ALU.mult),
              reads=["el", "msk"], writes=["el"], tag=tk + "cm1")
            A("vector", lambda h, tt=tt: h.tensor_scalar(
                compb[:, ts(tt, 8)], el[:], rden[:, 0:1], None, op0=ALU.mult),
              reads=["el"], sreads=["rden"], writes=["compb"], tag=tk + "cm2")
            A("tensor", lambda h, tt=tt: h.transpose(
                PB[0:8, 1536:1600].bitcast(BF16), compb[:, ts(tt, 8)], idb[:]),
              reads=["compb", "idb"], writes=["ptr"], tag=tk + "tr")
            A("vector", lambda h, tt=tt: h.tensor_copy(
                combT[0:8, ts(tt, 128)], PB[0:8, 1536:1600].bitcast(BF16)),
              reads=["ptr"], writes=["combT"], tag=tk + "te")
        A("tensor", lambda h: h.nop(nofuse=True),
          reads=["combT"], writes=["prt", "ptr", "pw", "ph", "pcb", "pd"], tag="fence")
        w1i = 0
        for e in range(-1, 8):
            ekk = f"E{e}"
            if e >= 0:
                A("tensor", lambda h, e=e: [
                    h.matmul(PB[:, half * 512: half * 512 + 512], oh8[:, ts(e, 128)],
                             combT[0:8, half * 512: half * 512 + 512], start=True, stop=True)
                    for half in range(4)][-1],
                  reads=["combT", "oh8"], writes=["pcb"], tag=ekk + "cb")
            for q4 in range(4):
                w1src = sw1d if e < 0 else w1d[e]
                w2src = sw2d if e < 0 else w2d[e]
                A("gpsimd", lambda h, w2src=w2src, q4=q4: h.dma_start(
                    out=w2b[:], in_=w2src[ts(q4, 1024), :].rearrange("(ft p) c -> p ft c", p=128)),
                  writes=["wo", "w2bk"], dma=True, tag=ekk + f"lw2{q4}")
                for qh in range(2):
                    jb = w1i % 2
                    w1i += 1
                    A("gpsimd", lambda h, jb=jb, w1src=w1src, q4=q4, qh=qh: h.dma_start(
                        out=w1b[jb][:],
                        in_=w1src[:, q4 * 1024 + qh * 512: q4 * 1024 + qh * 512 + 512]
                        .rearrange("(dt p) c -> p dt c", p=128)),
                      writes=[f"w1b{jb}"], dma=True, tag=ekk + f"lw1{q4}.{qh}")
                    for fh in range(4):
                        ft = qh * 4 + fh
                        gft = q4 * NFQ + ft
                        for tb in range(NB4):
                            A("tensor", lambda h, jb=jb, fh=fh, tb=tb: [
                                h.matmul(PA[:, 512:1024],
                                         w1b[jb][:, dt * 512 + fh * 128: dt * 512 + fh * 128 + 128],
                                         xn2T[:, dt * TLOC + tb * 512: dt * TLOC + tb * 512 + 512],
                                         start=(dt == 0), stop=(dt == ND - 1))
                                for dt in range(ND)][-1],
                              reads=["xn2T", f"w1b{jb}"], writes=["ph"], tag=ekk + f"h{gft}.{tb}")
                            bca = (sb1c, gft) if e < 0 else (b1c, e * 32 + gft)
                            A("scalar", lambda h, ft=ft, tb=tb, bca=bca: h.activation(
                                y3T[:, ft * TLOC + tb * 512: ft * TLOC + tb * 512 + 512],
                                PA[:, 512:1024], AF.Gelu_apprx_tanh,
                                bias=bca[0][:, bca[1]:bca[1] + 1]),
                              reads=["ph"], sreads=["sb1c" if e < 0 else "b1c"],
                              writes=[f"hq{ft}", f"y3_{ft}"], tag=ekk + f"g{gft}.{tb}")
                        if e >= 0:
                            A("vector", lambda h, ft=ft: h.tensor_tensor(
                                y3T[:, ft * TLOC: ft * TLOC + TLOC],
                                y3T[:, ft * TLOC: ft * TLOC + TLOC],
                                PB[:, 0:2048], op=ALU.mult),
                              reads=[f"hq{ft}", "pcb"], writes=[f"hq{ft}", f"y3_{ft}"],
                              tag=ekk + f"w{gft}")
                for dt in range(ND):
                    for tb in range(NB4):
                        def emit_down(h, dt=dt, tb=tb, e=e, q4=q4):
                            for ft in range(NFQ):
                                ins = h.matmul(
                                    PA[:, 1024:1536],
                                    w2b[:, ft * 1024 + dt * 128: ft * 1024 + dt * 128 + 128],
                                    y3T[:, ft * TLOC + tb * 512: ft * TLOC + tb * 512 + 512],
                                    start=(ft == 0), stop=(ft == NFQ - 1 and not (e >= 0 and q4 == 0)))
                            if e >= 0 and q4 == 0:
                                ins = h.matmul(PA[:, 1024:1536], b2h[:, ts(dt, 128)],
                                               combT[0:8, ts(tb, 512)], start=False, stop=True)
                            return ins
                        A("tensor", emit_down,
                          reads=[f"hq{x}" for x in range(NFQ)] + ["w2bk", "b2h", "combT"],
                          writes=["pd"], tag=ekk + f"d{q4}.{dt}.{tb}")
                        A("vector", lambda h, dt=dt, tb=tb: h.tensor_tensor(
                            accT[:, dt * TLOC + tb * 512: dt * TLOC + tb * 512 + 512],
                            accT[:, dt * TLOC + tb * 512: dt * TLOC + tb * 512 + 512],
                            PA[:, 1024:1536], op=ALU.add),
                          reads=["pd", "accT"], writes=["accT"], tag=ekk + f"a{q4}.{dt}.{tb}")
        for dt in range(ND):
            A("sync", lambda h, dt=dt: h.dma_start(out=outd[ts(dt, 128), :], in_=accT[:, ts(dt, TLOC)]),
              reads=["accT"], writes=["stored"], dma=True, tag=f"sto{dt}")
        fin = A("sync", lambda h: h.nop(nofuse=True), reads=["stored"], tag="fin")
        for op in S.ops:
            if op.dma and op.tag.startswith("st") and op not in fin.deps:
                fin.deps.append(op)
        S.run_block(nc, ctx)
    return nc


def prep_common(z):
    BFD = np.dtype(ml_dtypes.bfloat16)
    p = {}
    p["x0T"] = np.ascontiguousarray(z["L0_attn_x"].transpose(0, 2, 1))
    p["ones128"] = np.ones((128, 128), dtype=np.float32)
    p["onesbf"] = np.ones((128, 128), dtype=BFD)
    p["idbf"] = np.eye(128, dtype=BFD)
    p["w_qkv"] = z["L0_attn_w_qkv"]
    p["b_qkvT"] = np.ascontiguousarray(z["L0_attn_b_qkv"].reshape(24, 128).T)
    p["w_out"] = z["L0_attn_w_out"]
    p["b_outT"] = np.ascontiguousarray(z["L0_attn_b_out"].reshape(8, 128).T)
    freqs = z["L0_attn_rope_freqs"][:, :, 0]
    fc = np.zeros((128, 8), dtype=np.float32)
    for pp in range(8):
        fc[0:32, pp] = -freqs[2 * pp]
        fc[32:64, pp] = freqs[2 * pp]
        fc[64:96, pp] = -freqs[2 * pp + 1]
        fc[96:128, pp] = freqs[2 * pp + 1]
    p["freqcols2"] = fc
    pos = np.arange(L, dtype=np.float32)
    p["pos"] = pos
    p["gate_wT"] = np.ascontiguousarray(z["L0_ffn_gate_w"].transpose(0, 2, 1))
    p["up_wT"] = np.ascontiguousarray(z["L0_ffn_up_w"].transpose(0, 2, 1))
    p["down_wT"] = np.ascontiguousarray(z["L0_ffn_down_w"].transpose(0, 2, 1))
    p["sgate_wT"] = np.ascontiguousarray(z["L0_ffn_sgate_w"].T)
    p["sup_wT"] = np.ascontiguousarray(z["L0_ffn_sup_w"].T)
    p["sdown_wT"] = np.ascontiguousarray(z["L0_ffn_sdown_w"].T)
    p["router_wT"] = np.ascontiguousarray(z["L0_ffn_router_w"].T)
    p["router_b"] = np.ascontiguousarray(z["L0_ffn_router_bias"][None, :])
    oh32 = np.zeros((32, 32 * 128), dtype=BFD)
    for e in range(32):
        oh32[e, e * 128:(e + 1) * 128] = 1
    p["onehot32"] = oh32
    oh16 = np.zeros((16, 16 * 128), dtype=BFD)
    for n in range(16):
        oh16[n, n * 128:(n + 1) * 128] = 1
    p["onehot16"] = oh16
    oh8 = np.zeros((8, 8 * 128), dtype=BFD)
    for e in range(8):
        oh8[e, e * 128:(e + 1) * 128] = 1
    p["onehot8"] = oh8
    p["W_in"] = z["L1_attn_W_in"]
    p["ln_gT"] = np.ascontiguousarray(z["L1_attn_ln_g"].reshape(8, 128).T)
    p["ln_bT"] = np.ascontiguousarray(z["L1_attn_ln_b"].reshape(8, 128).T)
    p["conv_wT"] = np.ascontiguousarray(
        z["L1_attn_conv_w"].reshape(16, 128, 4).transpose(1, 0, 2).reshape(128, 64))
    p["conv_bT"] = np.ascontiguousarray(z["L1_attn_conv_b"].reshape(16, 128).T)
    p["W_x"] = z["L1_attn_W_x"]
    p["W_dt"] = z["L1_attn_W_dt"]
    p["b_dtT"] = np.ascontiguousarray(z["L1_attn_b_dt"].reshape(16, 128).T)
    p["A_log"] = z["L1_attn_A_log"]
    p["DcolT"] = np.ascontiguousarray(z["L1_attn_D"].reshape(16, 128).T)
    p["W_out1"] = z["L1_attn_W_out"].astype(BF)
    p["shared_w1T"] = np.ascontiguousarray(z["L1_ffn_shared_w1"].T).astype(BF)
    p["shared_b1T"] = np.ascontiguousarray(z["L1_ffn_shared_b1"].reshape(32, 128).T)
    p["shared_w2T"] = np.ascontiguousarray(z["L1_ffn_shared_w2"].T).astype(BF)
    p["shared_b2T"] = np.ascontiguousarray(z["L1_ffn_shared_b2"].reshape(8, 128).T)
    p["router1_wT"] = np.ascontiguousarray(z["L1_ffn_router_w"].T).astype(BF)
    p["router1_b"] = np.ascontiguousarray(z["L1_ffn_router_b"][None, :]).astype(BF)
    p["spec_w1T"] = np.ascontiguousarray(z["L1_ffn_spec_w1"].transpose(0, 2, 1)).astype(BF)
    p["spec_b1T"] = np.ascontiguousarray(
        z["L1_ffn_spec_b1"].reshape(8, 32, 128).transpose(2, 0, 1).reshape(128, 256))
    p["spec_w2T"] = np.ascontiguousarray(z["L1_ffn_spec_w2"].transpose(0, 2, 1)).astype(BF)
    p["spec_b2h"] = np.ascontiguousarray(z["L1_ffn_spec_b2"]).astype(BF)
    return p


def kernel_multilaunch(**inputs):
    BFD = np.dtype(ml_dtypes.bfloat16)
    p = prep_common(inputs)
    cores = [divmod(c, 4) for c in range(NCORES)]

    nc = build_ln_launch()
    ims = [dict(x0T=np.ascontiguousarray(p["x0T"][b_, :, ts(s, TLOC)]),
                ones128=p["ones128"]) for b_, s in cores]
    res = run_bass_kernel_spmd(nc, ims, core_ids=list(range(NCORES)))
    xnT_loc = [r["xnT"] for r in res.results]
    xnT_full = [np.concatenate([xnT_loc[b_ * 4 + s] for s in range(4)], axis=1) for b_ in range(B)]

    nc = build_attn_launch(0)
    ims = []
    for b_, s in cores:
        xr = np.ascontiguousarray(np.roll(xnT_full[b_], -s * TLOC, axis=1))
        pr = np.ascontiguousarray(np.broadcast_to(np.roll(p["pos"], -s * TLOC), (128, L)))
        ims.append(dict(xnT=xr, posrep=pr, w_qkv=p["w_qkv"], b_qkvT=p["b_qkvT"],
                        w_out=p["w_out"], b_outT=p["b_outT"], freqcols2=p["freqcols2"],
                        onesbf=p["onesbf"], idbf=p["idbf"]))
    res = run_bass_kernel_spmd(nc, ims, core_ids=list(range(NCORES)))
    attn0T = [r["attn0T"] for r in res.results]

    nc = build_ffn0_launch()
    ims = []
    for c, (b_, s) in enumerate(cores):
        ims.append(dict(xnT_loc=xnT_loc[c],
                        x0T_loc=np.ascontiguousarray(p["x0T"][b_, :, ts(s, TLOC)]),
                        attn0T=attn0T[c],
                        gate_wT=p["gate_wT"], up_wT=p["up_wT"], down_wT=p["down_wT"],
                        sgate_wT=p["sgate_wT"], sup_wT=p["sup_wT"], sdown_wT=p["sdown_wT"],
                        router_wT=p["router_wT"], router_b=p["router_b"],
                        onehot32=p["onehot32"], onesbf=p["onesbf"], idbf=p["idbf"]))
    res = run_bass_kernel_spmd(nc, ims, core_ids=list(range(NCORES)))
    x1T = [r["x1T"] for r in res.results]

    nc = build_front_a_launch()
    ims = [dict(x1T=x1T[c], W_in=p["W_in"], ln_gT=p["ln_gT"], ln_bT=p["ln_bT"],
                ones128=p["ones128"], onesbf=p["onesbf"]) for c in range(NCORES)]
    res = run_bass_kernel_spmd(nc, ims, core_ids=list(range(NCORES)))
    xn2T = [r["xn2T"] for r in res.results]
    xhT = [r["xhT"] for r in res.results]
    szT = [r["szT"] for r in res.results]

    nc = build_front_b_launch()
    ims = []
    for c, (b_, s) in enumerate(cores):
        if s == 0:
            halo = np.zeros((DI, 3), dtype=BFD)
        else:
            halo = np.ascontiguousarray(xhT[c - 1][:, TLOC - 3:])
        ims.append(dict(xhT=xhT[c], xh_halo=halo, conv_wT=p["conv_wT"], conv_bT=p["conv_bT"],
                        W_x=p["W_x"], W_dt=p["W_dt"], b_dtT=p["b_dtT"]))
    res = run_bass_kernel_spmd(nc, ims, core_ids=list(range(NCORES)))
    xcT = [r["xcT"] for r in res.results]
    dtT = [r["dtT"] for r in res.results]
    BCT = [r["BCT"] for r in res.results]

    nc = build_scan_launch()
    ims = []
    qcores = [divmod(c, 4) for c in range(NCORES)]
    for b_, q in qcores:
        dsl = slice(q * 512, (q + 1) * 512)
        xcq = np.concatenate([xcT[b_ * 4 + s][dsl] for s in range(4)], axis=1)
        dtq = np.concatenate([dtT[b_ * 4 + s][dsl] for s in range(4)], axis=1)
        BCq = np.concatenate([BCT[b_ * 4 + s] for s in range(4)], axis=1)
        Aq = np.ascontiguousarray(
            p["A_log"][dsl].reshape(4, 128, 16).transpose(1, 0, 2).reshape(128, 64))
        ims.append(dict(xcq=np.ascontiguousarray(xcq), dtq=np.ascontiguousarray(dtq),
                        BCq=np.ascontiguousarray(BCq), Aq=Aq, onehot16=p["onehot16"]))
    res = run_bass_kernel_spmd(nc, ims, core_ids=list(range(NCORES)))
    yq = [r["yq"] for r in res.results]

    nc = build_back_launch()
    ims = []
    for c, (b_, s) in enumerate(cores):
        yT = np.concatenate([yq[b_ * 4 + q][:, ts(s, TLOC)] for q in range(4)], axis=0)
        ims.append(dict(x1T=x1T[c], xn2T=xn2T[c], yT_loc=np.ascontiguousarray(yT),
                        xcT=xcT[c], szT=szT[c], DcolT=p["DcolT"], W_out1=p["W_out1"],
                        shared_w1T=p["shared_w1T"], shared_b1T=p["shared_b1T"],
                        shared_w2T=p["shared_w2T"], shared_b2T=p["shared_b2T"],
                        router1_wT=p["router1_wT"], router1_b=p["router1_b"],
                        spec_w1T=p["spec_w1T"], spec_b1T=p["spec_b1T"],
                        spec_w2T=p["spec_w2T"], spec_b2h=p["spec_b2h"],
                        onehot8=p["onehot8"], onesbf=p["onesbf"], idbf=p["idbf"]))
    res = run_bass_kernel_spmd(nc, ims, core_ids=list(range(NCORES)))
    outT = [r["outT"] for r in res.results]

    out = np.empty((B, L, D), dtype=np.float32)
    for c, (b_, s) in enumerate(cores):
        out[b_, ts(s, TLOC), :] = outT[c].T
    return out


def emit_ffn0_body(S, nc, ctx, io):
    A = S.add
    xnTd = io["xnT_src"]; x0Td = io["x0Td"]; at0d = io["at0d"]
    gwd = io["gwd"]; uwd = io["uwd"]; dwd = io["dwd"]
    sgd = io["sgd"]; sud = io["sud"]; sdd = io["sdd"]
    rwd = io["rwd"]; rbd = io["rbd"]; ohd = io["ohd"]
    onesbf_d = io["onesbf_d"]; idbf_d = io["idbf_d"]
    x1Td = io["x1Td"]; x1key = io["x1key"]; at0key = io.get("at0key")
    NT = TLOC // 128
    NB4 = TLOC // 512
    EGR = 8

    if True:
        sb = lambda nm, shp, dt2: ctx.enter_context(nc.sbuf_tensor("p3" + nm, shp, dt2))
        xnT = sb("xnT", [128, ND * TLOC], BF16)
        ffacc = sb("ffacc", [128, ND * TLOC], F32)
        hwt = [sb(f"hwt{e}", [128, TLOC], BF16) for e in range(EGR)]
        ge = sb("ge", [128, ND * 128], BF16)
        ue = sb("ue", [128, ND * 128], BF16)
        de = [sb(f"de{i}", [128, D], BF16) for i in range(8)]
        sgw = sb("sgw", [128, ND * 512], BF16)
        suw = sb("suw", [128, ND * 512], BF16)
        sdw = sb("sdw", [128, 4 * D], BF16)
        rww = sb("rww", [128, ND * 32], BF16)
        rbb = sb("rbb", [1, 32], BF16)
        oh32 = sb("oh32", [32, 32 * 128], BF16)
        onesb = sb("onesb", [128, 128], BF16)
        idb = sb("idb", [128, 128], BF16)
        lg = sb("lg", [128, 32], F32)
        m8 = sb("m8", [128, 8], F32)
        nv0 = sb("nv0", [128, 1], F32)
        el = sb("el", [128, 32], F32)
        msk = sb("msk", [128, 32], F32)
        ev4 = sb("ev4", [128, 4], F32)
        ds = sb("ds", [128, 1], F32)
        rden = sb("rden", [128, 1], F32)
        compb = sb("compb", [128, NT * 32], BF16)
        combT = sb("combT", [32, TLOC], BF16)
        sgt = sb("sgt", [128, 512], BF16)
        x0t = sb("x0t", [128, 1024], F32)
        a0t = sb("a0t", [128, 1024], BF16)
        x1t = sb("x1t", [128, 1024], F32)
        PA = io["PA"]; PB = io["PB"]

        A("sync", lambda h: h.dma_start(out=xnT[:], in_=xnTd.rearrange("(dt p) t -> p dt t", p=128)),
          reads=[io.get("xn_srckey")] if io.get("xn_srckey") else [],
          writes=["xnT"], dma=True, tag="p3ldxn")
        A("gpsimd", lambda h: h.dma_start(out=rww[:], in_=rwd.rearrange("(dt p) e -> p dt e", p=128)),
          writes=["rww"], dma=True, tag="ldrw")
        A("gpsimd", lambda h: h.dma_start(out=rbb[:], in_=rbd), writes=["rbb"], dma=True, tag="ldrb")
        A("gpsimd", lambda h: h.dma_start(out=oh32[:], in_=ohd), writes=["oh32"], dma=True, tag="ldoh")
        A("gpsimd", lambda h: h.dma_start(out=onesb[:], in_=onesbf_d), writes=["onesb"], dma=True, tag="ldon")
        A("gpsimd", lambda h: h.dma_start(out=idb[:], in_=idbf_d), writes=["idb"], dma=True, tag="ldid")
        A("gpsimd", lambda h: h.dma_start(out=sgw[:], in_=sgd.rearrange("(dt p) s -> p dt s", p=128)),
          writes=["sgw"], dma=True, tag="ldsg")
        A("gpsimd", lambda h: h.dma_start(out=suw[:], in_=sud.rearrange("(dt p) s -> p dt s", p=128)),
          writes=["suw"], dma=True, tag="ldsu")
        A("gpsimd", lambda h: h.dma_start(out=sdw[:], in_=sdd.rearrange("(ht p) d -> p ht d", p=128)),
          writes=["sdw"], dma=True, tag="ldsd")

        for tt in range(NT):
            tk = f"r{tt}"
            A("tensor", lambda h, tt=tt: [
                h.matmul(PA[0:128, 1536:1568], xnT[:, dt * TLOC + tt * 128: dt * TLOC + tt * 128 + 128],
                         rww[:, ts(dt, 32)], start=(dt == 0), stop=False)
                for dt in range(ND)][-1],
              reads=["xnT", "rww"], writes=["prt"], tag=tk + "mm")
            A("tensor", lambda h: h.matmul(PA[0:128, 1536:1568], onesb[0:1, :], rbb[:],
                                           start=False, stop=True),
              reads=["rbb", "onesb"], writes=["prt"], tag=tk + "bias")
            A("scalar", lambda h: h.activation(lg[:], PA[0:128, 1536:1568], AF.Copy),
              reads=["prt"], writes=["lg"], tag=tk + "ev")
            A("vector", lambda h: h.max(m8[:], lg[:]), reads=["lg"], writes=["m8"], tag=tk + "m8")
            A("vector", lambda h: h.tensor_scalar(nv0[:], m8[:, 0:1], -1.0, None, op0=ALU.mult),
              reads=["m8"], writes=["nv0"], tag=tk + "nv")
            A("scalar", lambda h: h.activation(el[:], lg[:], AF.Exp, bias=nv0[:, 0:1]),
              reads=["lg"], sreads=["nv0"], writes=["el"], tag=tk + "el")
            A("vector", lambda h: h.tensor_scalar(msk[:], lg[:], m8[:, 3:4], None, op0=ALU.is_ge),
              reads=["lg"], sreads=["m8"], writes=["msk"], tag=tk + "mk")
            A("scalar", lambda h: h.activation(ev4[:], m8[:, 0:4], AF.Exp, bias=nv0[:, 0:1]),
              reads=["m8"], sreads=["nv0"], writes=["ev4"], tag=tk + "e4")
            A("vector", lambda h: h.reduce_sum(ds[:], ev4[:], axis=mybir.AxisListType.X),
              reads=["ev4"], writes=["ds"], tag=tk + "ds")
            A("vector", lambda h: h.reciprocal(rden[:], ds[:]), reads=["ds"], writes=["rden"], tag=tk + "rd")
            A("vector", lambda h: h.tensor_tensor(el[:], el[:], msk[:], op=ALU.mult),
              reads=["el", "msk"], writes=["el"], tag=tk + "cm1")
            A("vector", lambda h, tt=tt: h.tensor_scalar(
                compb[:, ts(tt, 32)], el[:], rden[:, 0:1], None, op0=ALU.mult),
              reads=["el"], sreads=["rden"], writes=["compb"], tag=tk + "cm2")
            A("tensor", lambda h, tt=tt: h.transpose(
                PB[0:32, 512:576].bitcast(BF16), compb[:, ts(tt, 32)], idb[:]),
              reads=["compb", "idb"], writes=["ptr"], tag=tk + "tr")
            A("vector", lambda h, tt=tt: h.tensor_copy(
                combT[0:32, ts(tt, 128)], PB[0:32, 512:576].bitcast(BF16)),
              reads=["ptr"], writes=["combT"], tag=tk + "te")

        for grp in range(32 // EGR):
            for ei in range(EGR):
                e = grp * EGR + ei
                ek = f"e{e}"
                A("gpsimd", lambda h, e=e: h.dma_start(
                    out=ge[:], in_=gwd[e].rearrange("(dt p) c -> p dt c", p=128)),
                  writes=["ge"], dma=True, tag=ek + "ldg")
                A("gpsimd", lambda h, e=e: h.dma_start(
                    out=ue[:], in_=uwd[e].rearrange("(dt p) c -> p dt c", p=128)),
                  writes=["ue"], dma=True, tag=ek + "ldu")
                for tb in range(NB4):
                    A("tensor", lambda h, tb=tb: [
                        h.matmul(PA[:, 0:512], ge[:, ts(dt, 128)],
                                 xnT[:, dt * TLOC + tb * 512: dt * TLOC + tb * 512 + 512],
                                 start=(dt == 0), stop=(dt == ND - 1))
                        for dt in range(ND)][-1],
                      reads=["xnT", "ge"], writes=["pg"], tag=ek + f"g{tb}")
                    A("tensor", lambda h, tb=tb: [
                        h.matmul(PA[:, 512:1024], ue[:, ts(dt, 128)],
                                 xnT[:, dt * TLOC + tb * 512: dt * TLOC + tb * 512 + 512],
                                 start=(dt == 0), stop=(dt == ND - 1))
                        for dt in range(ND)][-1],
                      reads=["xnT", "ue"], writes=["pu"], tag=ek + f"u{tb}")
                    A("tensor", lambda h, e=e, tb=tb: h.matmul(
                        PA[:, 1024:1536], oh32[:, ts(e, 128)], combT[0:32, ts(tb, 512)],
                        start=True, stop=True),
                      reads=["combT", "oh32"], writes=["pc"], tag=ek + f"c{tb}")
                    A("scalar", lambda h: h.activation(sgt[:], PA[:, 0:512], AF.Silu),
                      reads=["pg"], writes=["sgt"], tag=ek + f"s{tb}")
                    A("vector", lambda h, ei=ei, tb=tb: h.tensor_tensor(
                        hwt[ei][:, ts(tb, 512)], PA[:, 512:1024], sgt[:], op=ALU.mult),
                      reads=["pu", "sgt"], writes=[f"hw{ei}"], tag=ek + f"m{tb}")
                    A("vector", lambda h, ei=ei, tb=tb: h.tensor_tensor(
                        hwt[ei][:, ts(tb, 512)], hwt[ei][:, ts(tb, 512)], PA[:, 1024:1536],
                        op=ALU.mult),
                      reads=[f"hw{ei}", "pc"], writes=[f"hw{ei}"], tag=ek + f"w{tb}")
                A("gpsimd", lambda h, e=e, ei=ei: h.dma_start(out=de[ei][:], in_=dwd[e]),
                  writes=[f"de{ei}"], dma=True, tag=ek + "ldd")
            for dt in range(ND):
                for tb in range(NB4):
                    A("tensor", lambda h, grp=grp, dt=dt, tb=tb: [
                        h.matmul(PB[:, 0:512], de[ei][:, ts(dt, 128)],
                                 hwt[ei][:, ts(tb, 512)], start=(ei == 0), stop=(ei == EGR - 1))
                        for ei in range(EGR)][-1],
                      reads=[f"hw{x}" for x in range(EGR)] + [f"de{x}" for x in range(EGR)],
                      writes=["pd"], tag=f"dn{grp}.{dt}.{tb}")
                    if grp == 0:
                        A("scalar", lambda h, dt=dt, tb=tb: h.activation(
                            ffacc[:, dt * TLOC + tb * 512: dt * TLOC + tb * 512 + 512],
                            PB[:, 0:512], AF.Copy),
                          reads=["pd"], writes=["ffacc"], tag=f"fc{grp}.{dt}.{tb}")
                    else:
                        A("vector", lambda h, dt=dt, tb=tb: h.tensor_tensor(
                            ffacc[:, dt * TLOC + tb * 512: dt * TLOC + tb * 512 + 512],
                            ffacc[:, dt * TLOC + tb * 512: dt * TLOC + tb * 512 + 512],
                            PB[:, 0:512], op=ALU.add),
                          reads=["pd", "ffacc"], writes=["ffacc"], tag=f"fc{grp}.{dt}.{tb}")

        for st in range(4):
            for tb in range(NB4):
                A("tensor", lambda h, st=st, tb=tb: [
                    h.matmul(PA[:, 0:512], sgw[:, dt * 512 + st * 128: dt * 512 + st * 128 + 128],
                             xnT[:, dt * TLOC + tb * 512: dt * TLOC + tb * 512 + 512],
                             start=(dt == 0), stop=(dt == ND - 1))
                    for dt in range(ND)][-1],
                  reads=["xnT", "sgw"], writes=["pg"], tag=f"sg{st}.{tb}")
                A("tensor", lambda h, st=st, tb=tb: [
                    h.matmul(PA[:, 512:1024], suw[:, dt * 512 + st * 128: dt * 512 + st * 128 + 128],
                             xnT[:, dt * TLOC + tb * 512: dt * TLOC + tb * 512 + 512],
                             start=(dt == 0), stop=(dt == ND - 1))
                    for dt in range(ND)][-1],
                  reads=["xnT", "suw"], writes=["pu"], tag=f"su{st}.{tb}")
                A("scalar", lambda h: h.activation(sgt[:], PA[:, 0:512], AF.Silu),
                  reads=["pg"], writes=["sgt"], tag=f"ss{st}.{tb}")
                A("vector", lambda h, st=st, tb=tb: h.tensor_tensor(
                    hwt[st][:, ts(tb, 512)], PA[:, 512:1024], sgt[:], op=ALU.mult),
                  reads=["pu", "sgt"], writes=[f"hw{st}"], tag=f"sm{st}.{tb}")
        for dt in range(ND):
            for tb in range(NB4):
                A("tensor", lambda h, dt=dt, tb=tb: [
                    h.matmul(PB[:, 0:512], sdw[:, st * D + dt * 128: st * D + dt * 128 + 128],
                             hwt[st][:, ts(tb, 512)], start=(st == 0), stop=(st == 3))
                    for st in range(4)][-1],
                  reads=["hw0", "hw1", "hw2", "hw3", "sdw"], writes=["pd"], tag=f"sd{dt}.{tb}")
                A("vector", lambda h, dt=dt, tb=tb: h.tensor_tensor(
                    ffacc[:, dt * TLOC + tb * 512: dt * TLOC + tb * 512 + 512],
                    ffacc[:, dt * TLOC + tb * 512: dt * TLOC + tb * 512 + 512],
                    PB[:, 0:512], op=ALU.add),
                  reads=["pd", "ffacc"], writes=["ffacc"], tag=f"sf{dt}.{tb}")

        for dt in range(ND):
          for hf in range(2):
            hsl = slice(hf * 1024, hf * 1024 + 1024)
            A("sync", lambda h, dt=dt, hsl=hsl: h.dma_start(out=x0t[:], in_=x0Td[ts(dt, 128), hsl]),
              writes=["x0t"], dma=True, tag=f"lx0{dt}.{hf}")
            A("sync", lambda h, dt=dt, hsl=hsl: h.dma_start(out=a0t[:], in_=at0d[ts(dt, 128), hsl]),
              reads=[at0key] if at0key else [], writes=["a0t"], dma=True, tag=f"p3la0{dt}.{hf}")
            A("vector", lambda h, dt=dt, hf=hf: h.tensor_tensor(
                x1t[:], x0t[:], ffacc[:, dt * TLOC + hf * 1024: dt * TLOC + hf * 1024 + 1024],
                op=ALU.add),
              reads=["x0t", "ffacc"], writes=["x1t"], tag=f"x1a{dt}.{hf}")
            A("vector", lambda h: h.tensor_tensor(x1t[:], x1t[:], a0t[:], op=ALU.add),
              reads=["x1t", "a0t"], writes=["x1t"], tag=f"x1b{dt}.{hf}")
            A("sync", lambda h, dt=dt, hsl=hsl: h.dma_start(out=x1Td[ts(dt, 128), hsl], in_=x1t[:]),
              reads=["x1t"], writes=[x1key], dma=True, tag=f"p3st{dt}.{hf}")


def emit_front_a_body(S, nc, ctx, io):
    A = S.add
    x1Td = io["x1Td"]; wind = io["wind"]; lngd = io["lngd"]; lnbd = io["lnbd"]
    onesd = io["onesd"]; onesbd = io["onesbd"]
    xn2d = io["xn2d"]; xhd = io["xhd"]; szd = io["szd"]
    x1key = io.get("x1key")
    NB4 = TLOC // 512
    if True:
        sb = lambda nm, shp, dt2: ctx.enter_context(nc.sbuf_tensor("p4" + nm, shp, dt2))
        x1t = [sb(f"x1t{i}", [128, TLOC], F32) for i in range(ND)]
        xn2b = sb("xn2b", [128, ND * TLOC], BF16)
        xnnT = sb("xnnT", [128, ND * TLOC], BF16)
        gcol = sb("gcol", [128, ND], F32)
        bcol = sb("bcol", [128, ND], F32)
        wbuf = [sb(f"wbuf{j}", [128, ND * 128], BF16) for j in range(2)]
        ostg = [sb(f"ostg{j}", [128, TLOC], BF16) for j in range(2)]
        work = mk_ln_work(nc, ctx, TLOC, pref="p4", psum=(io["PA"], io["PB"]))
        work["onesf"] = work["onesc"]
        onesbb = sb("onesbb", [128, 128], BF16)

        A("sync", lambda h: h.dma_start(out=work["onesc"][:], in_=onesd),
          writes=["onesc", "onesf"], dma=True, tag="ldone")
        A("sync", lambda h: h.dma_start(out=onesbb[:], in_=onesbd), writes=["onesbb"], dma=True, tag="ldonb")
        A("gpsimd", lambda h: h.dma_start(out=gcol[:], in_=lngd), writes=["p4Lgb"], dma=True, tag="p4ldg")
        A("gpsimd", lambda h: h.dma_start(out=bcol[:], in_=lnbd), writes=["p4Lgb"], dma=True, tag="p4ldb")
        for i in range(ND):
            A("sync", lambda h, i=i: h.dma_start(out=x1t[i][:], in_=x1Td[ts(i, 128), :]),
              reads=[x1key] if x1key else [], writes=[f"p4x{i}"], dma=True, tag=f"p4ldx{i}")
        phase_ln_feature_major(S, nc, x1t, xn2b, work, TLOC, key="p4")
        A("sync", lambda h: h.dma_start(
            out=xn2d.rearrange("(dt p) t -> p dt t", p=128),
            in_=xn2b[:].rearrange("p (dt t) -> p dt t", t=TLOC)),
          reads=["p4xout"], writes=["xn2dk"], dma=True, tag="p4sxn2")
        xt2 = [xn2b[:, ts(i, TLOC)] for i in range(ND)]
        for i in range(ND):
            S.last_writer["p4L" + f"x{i}"] = S.last_writer.get("p4xout")
        phase_ln_fm_bf16(S, nc, xt2, xnnT, work, TLOC, onesbb, gcol=gcol, bcol=bcol, key="p4L")
        A("tensor", lambda h: h.nop(nofuse=True),
          reads=["p4Lxout"], writes=["p4psa", "p4psb", "p4Lpsa", "p4Lpsb", "pp0", "pp1", "pp2", "pp3"],
          tag="p4fence")
        for ci in range(32):
            jb = ci % 2
            A("gpsimd", lambda h, ci=ci, jb=jb: h.dma_start(
                out=wbuf[jb][:], in_=wind[:, ts(ci, 128)].rearrange("(dt p) c -> p dt c", p=128)),
              writes=[f"wb{jb}"], dma=True, tag=f"ldw{ci}")
            for tb in range(NB4):
                A("tensor", lambda h, ci=ci, jb=jb, tb=tb: [
                    h.matmul(work["ps_a"][:, ts(tb, 512)],
                             wbuf[jb][:, ts(dt, 128)],
                             xnnT[:, dt * TLOC + tb * 512: dt * TLOC + tb * 512 + 512],
                             start=(dt == 0), stop=(dt == ND - 1))
                    for dt in range(ND)][-1],
                  reads=["p4Lxout", f"wb{jb}"], writes=[f"pp{tb}"], tag=f"mm{ci}.{tb}")
                if ci < 16:
                    A("scalar", lambda h, ci=ci, jb=jb, tb=tb: h.activation(
                        ostg[jb][:, ts(tb, 512)], work["ps_a"][:, ts(tb, 512)], AF.Copy),
                      reads=[f"pp{tb}"], writes=[f"os{jb}"], tag=f"ev{ci}.{tb}")
                else:
                    A("scalar", lambda h, ci=ci, jb=jb, tb=tb: h.activation(
                        ostg[jb][:, ts(tb, 512)], work["ps_a"][:, ts(tb, 512)], AF.Silu),
                      reads=[f"pp{tb}"], writes=[f"os{jb}"], tag=f"ev{ci}.{tb}")
            dst = xhd if ci < 16 else szd
            row = (ci % 16) * 128
            A("sync", lambda h, dst=dst, row=row, jb=jb: h.dma_start(
                out=dst[row:row + 128, :], in_=ostg[jb][:]),
              reads=[f"os{jb}"], writes=["xhd" if ci < 16 else "szd"], dma=True, tag=f"p4st{ci}")
            if ci < 16 and io.get("tail") is not None:
                A("sync", lambda h, row=row, jb=jb: h.dma_start(
                    out=io["tail"][row:row + 128, 0:8], in_=ostg[jb][:, TLOC - 8:]),
                  reads=[f"os{jb}"], writes=["tld"], dma=True, tag=f"p4tl{ci}")


def emit_front_b_body(S, nc, ctx, io):
    A = S.add
    xhd = io["xhd"]; cwd = io["cwd"]; cbd = io["cbd"]
    wxd = io["wxd"]; wdtd = io["wdtd"]; bdtd = io["bdtd"]
    xcd = io["xcd"]; dtd = io["dtd"]; bcd = io["bcd"]
    xhkey = io.get("xhkey")
    NDQ = DI // 128
    NB4 = TLOC // 512
    if True:
        sb = lambda nm, shp, dt2: ctx.enter_context(nc.sbuf_tensor("p5" + nm, shp, dt2))
        xht = sb("xht", [128, TLOC + 3], BF16)
        cw = sb("cw", [128, 64], F32)
        cb = sb("cb", [128, 16], F32)
        bdt = sb("bdt", [128, 16], F32)
        cacc = sb("cacc", [128, TLOC], F32)
        xcT = sb("xcTs", [128, NDQ * TLOC], BF16)
        wx = sb("wx", [128, NDQ * 96], BF16)
        wdt = sb("wdt", [64, DI], BF16)
        dbcb = sb("dbcb", [96, TLOC], BF16)
        bcs = sb("bcs", [32, TLOC], F32)
        xsp = sb("xsp", [128, TLOC], F32)
        ab2 = sb("ab2", [128, TLOC], F32)
        dtt = sb("dtt", [128, TLOC], F16)
        PA = io["PA"]; PB = io["PB"]
        xhalo = sb("xhalo", [128, 8], BF16)

        A("gpsimd", lambda h: h.dma_start(out=cw[:], in_=cwd), writes=["cw"], dma=True, tag="ldcw")
        A("gpsimd", lambda h: h.dma_start(out=cb[:], in_=cbd), writes=["cb"], dma=True, tag="ldcb")
        A("gpsimd", lambda h: h.dma_start(out=bdt[:], in_=bdtd), writes=["bdt"], dma=True, tag="ldbd")
        A("gpsimd", lambda h: h.dma_start(out=wx[:], in_=wxd.rearrange("(dq p) c -> p dq c", p=128)),
          writes=["wx"], dma=True, tag="ldwx")
        A("gpsimd", lambda h: h.dma_start(out=wdt[:], in_=wdtd), writes=["wdt"], dma=True, tag="ldwdt")
        for dq in range(NDQ):
            A("gpsimd", lambda h: h.memset(xhalo[:], 0.0), writes=["xhalo"], tag=f"p5hm{dq}")
            A("gpsimd", io["gather"](xhalo[:], io["tl_out"], io["idxh"][:, dq:dq + 1],
                                     NCORES * DI - 1),
              reads=["tloutd", "idxh", "xhalo"], writes=["xhalo"], dma=True, tag=f"p5hg{dq}")
            A("vector", lambda h: h.tensor_copy(xht[:, 0:3], xhalo[:, 5:8]),
              reads=["xhalo"], writes=["xh1"], tag=f"p5hc{dq}")
            A("sync", lambda h, dq=dq: h.dma_start(out=xht[:, 3:], in_=xhd[ts(dq, 128), :]),
              reads=[xhkey] if xhkey else [], writes=["xh2"], dma=True, tag=f"p5lx{dq}")
            A("vector", lambda h, dq=dq: h.tensor_scalar(
                cacc[:], xht[:, 0:TLOC], cw[:, dq * 4:dq * 4 + 1], None, op0=ALU.mult),
              reads=["xh1", "xh2"], sreads=["cw"], writes=["cacc"], tag=f"c0.{dq}")
            for k in range(1, 4):
                A("vector", lambda h, dq=dq, k=k: h.scalar_tensor_tensor(
                    cacc[:], xht[:, k:k + TLOC], cw[:, dq * 4 + k:dq * 4 + k + 1], cacc[:],
                    op0=ALU.mult, op1=ALU.add),
                  reads=["xh1", "xh2", "cacc"], sreads=["cw"], writes=["cacc"], tag=f"c{k}.{dq}")
            A("scalar", lambda h, dq=dq: h.activation(
                xcT[:, ts(dq, TLOC)], cacc[:], AF.Silu, bias=cb[:, dq:dq + 1]),
              reads=["cacc"], sreads=["cb"], writes=["xcT"], tag=f"si{dq}")
            A("sync", lambda h, dq=dq: h.dma_start(out=xcd[ts(dq, 128), :], in_=xcT[:, ts(dq, TLOC)]),
              reads=["xcT"], writes=["xcd"], dma=True, tag=f"p5stc{dq}")
        for tb in range(NB4):
            A("tensor", lambda h, tb=tb: [
                h.matmul(PA[0:96, ts(0, 512)], wx[:, dq * 96:dq * 96 + 96],
                         xcT[:, dq * TLOC + tb * 512: dq * TLOC + tb * 512 + 512],
                         start=(dq == 0), stop=(dq == NDQ - 1))
                for dq in range(NDQ)][-1],
              reads=["xcT", "wx"], writes=["pdb"], tag=f"db{tb}")
            A("scalar", lambda h, tb=tb: h.activation(dbcb[:, ts(tb, 512)], PA[0:96, 0:512], AF.Copy),
              reads=["pdb"], writes=["dbcb"], tag=f"dbe{tb}")
            A("scalar", lambda h, tb=tb: h.activation(bcs[0:32, ts(tb, 512)], PA[64:96, 0:512], AF.Copy),
              reads=["pdb"], writes=["bcf"], tag=f"dbf{tb}")
        A("sync", lambda h: h.dma_start(out=bcd, in_=bcs[0:32, :]),
          reads=["bcf"], writes=["bcd"], dma=True, tag="p5stbc")
        for dq in range(NDQ):
            for tb in range(NB4):
                A("tensor", lambda h, dq=dq, tb=tb: h.matmul(
                    PB[:, 512:1024],
                    wdt[:, ts(dq, 128)], dbcb[0:64, ts(tb, 512)], start=True, stop=True),
                  reads=["dbcb", "wdt"], writes=["pdt"], tag=f"dt{dq}.{tb}")
                A("vector", lambda h, dq=dq, tb=tb: h.tensor_scalar(
                    xsp[:, ts(tb, 512)], PB[:, 512:1024], bdt[:, dq:dq + 1], None, op0=ALU.add),
                  reads=["pdt"], sreads=["bdt"], writes=["xsp"], tag=f"dta{dq}.{tb}")
            A("scalar", lambda h: h.activation(ab2[:], xsp[:], AF.Abs),
              reads=["xsp"], writes=["ab2"], tag=f"dtb{dq}")
            A("scalar", lambda h: h.activation(ab2[:], ab2[:], AF.Exp, scale=-1.0),
              reads=["ab2"], writes=["ab2"], tag=f"dtc{dq}")
            A("vector", lambda h: h.tensor_scalar(ab2[:], ab2[:], 1.0, None, op0=ALU.add),
              reads=["ab2"], writes=["ab2"], tag=f"dtd{dq}")
            A("scalar", lambda h: h.activation(ab2[:], ab2[:], AF.Ln),
              reads=["ab2"], writes=["ab2"], tag=f"dte{dq}")
            A("vector", lambda h: h.tensor_scalar(xsp[:], xsp[:], 0.0, None, op0=ALU.max),
              reads=["xsp"], writes=["xsp"], tag=f"dtf{dq}")
            A("vector", lambda h: h.tensor_tensor(dtt[:], xsp[:], ab2[:], op=ALU.add),
              reads=["xsp", "ab2"], writes=["dtt"], tag=f"dtg{dq}")
            A("sync", lambda h, dq=dq: h.dma_start(out=dtd[ts(dq, 128), :], in_=dtt[:]),
              reads=["dtt"], writes=["dtd"], dma=True, tag=f"p5std{dq}")


def emit_scan_body(S, nc, ctx, io):
    A = S.add
    aqd = io["aqd"]; oh16d = io["oh16d"]; yqd = io["yqd"]
    TB = 1024
    NTB = L // TB
    if True:
        sb = lambda nm, shp, dt2: ctx.enter_context(nc.sbuf_tensor("p6" + nm, shp, dt2))
        xct = [sb(f"xct{j}", [128, 4 * TB], BF16) for j in range(2)]
        dtt = [sb(f"dtt{j}", [128, 4 * TB], F16) for j in range(2)]
        bct = [sb(f"bct{j}", [16, TB], BF16) for j in range(2)]
        cct = [sb(f"cct{j}", [16, TB], BF16) for j in range(2)]
        oh16 = sb("oh16", [16, 16 * 128], BF16)
        aq = sb("aq", [128, 64], F32)
        acol = sb("acol", [128, 64], F32)
        dtx = sb("dtx", [128, TB], BF16)
        dA = sb("dA", [128, TB], F32)
        dBu = sb("dBu", [128, TB], F32)
        hb = [sb(f"hb{j}", [128, TB], F32) for j in range(2)]
        tmp = sb("tmps", [128, TB], F32)
        chain = sb("chain", [128, 64], F32)
        ystg = sb("ystg", [128, TB], BF16)
        PA = io["PA"]; PB = io["PB"]

        A("gpsimd", lambda h: h.dma_start(out=oh16[:], in_=oh16d), writes=["oh16"], dma=True, tag="p6ldoh")
        A("gpsimd", lambda h: h.dma_start(out=aq[:], in_=aqd), writes=["aq"], dma=True, tag="ldaq")
        A("scalar", lambda h: h.activation(acol[:], aq[:], AF.Exp),
          reads=["aq"], writes=["acol"], tag="aexp")
        A("vector", lambda h: h.tensor_scalar(acol[:], acol[:], -1.0, None, op0=ALU.mult),
          reads=["acol"], writes=["acol"], tag="aneg")
        A("vector", lambda h: h.memset(chain[:], 0.0), writes=["chain"], tag="ch0")

        for tb in range(NTB):
            j = tb % 2
            for dqi in range(4):
                A("gpsimd", io["gather"](xct[j][:, ts(dqi, TB)], io["xc_out"],
                                         io["idxs"][:, tb * 4 + dqi:tb * 4 + dqi + 1]),
                  reads=["xcoutd", "idxs"], writes=[f"xc{j}"], dma=True, tag=f"p6lxc{tb}.{dqi}")
                A("gpsimd", io["gather"](dtt[j][:, ts(dqi, TB)], io["dt_out"],
                                         io["idxs"][:, tb * 4 + dqi:tb * 4 + dqi + 1]),
                  reads=["dtoutd", "idxs"], writes=[f"dt{j}"], dma=True, tag=f"p6ldt{tb}.{dqi}")
            A("gpsimd", io["gather"](bct[j][:], io["bc_out"], io["idxb"][0:16, tb:tb + 1]),
              reads=["bcoutd", "idxb"], writes=[f"bc{j}"], dma=True, tag=f"p6lbc{tb}")
            A("gpsimd", io["gather"](cct[j][:], io["bc_out"], io["idxb"][0:16, 8 + tb:8 + tb + 1]),
              reads=["bcoutd", "idxb"], writes=[f"cc{j}"], dma=True, tag=f"p6lcc{tb}")
            for dq in range(4):
                A("vector", lambda h, j=j, dq=dq: h.tensor_tensor(
                    dtx[:], dtt[j][:, ts(dq, TB)], xct[j][:, ts(dq, TB)], op=ALU.mult),
                  reads=[f"dt{j}", f"xc{j}"], writes=["dtx"], tag=f"dx{tb}.{dq}")
                for n in range(NST):
                    kk = f"s{tb}.{dq}.{n}"
                    hj = hb[n % 2]
                    A("tensor", lambda h, j=j, n=n: [
                        h.matmul(PA[:, ts(half, 512)], oh16[:, ts(n, 128)],
                                 bct[j][0:16, ts(half, 512)], start=True, stop=True)
                        for half in range(2)][-1],
                      reads=[f"bc{j}", "oh16"], writes=["pBb"], tag=kk + "B")
                    A("tensor", lambda h, j=j, n=n: [
                        h.matmul(PB[:, ts(half, 512)], oh16[:, ts(n, 128)],
                                 cct[j][0:16, ts(half, 512)], start=True, stop=True)
                        for half in range(2)][-1],
                      reads=[f"cc{j}", "oh16"], writes=["pCb"], tag=kk + "C")
                    A("scalar", lambda h, j=j, dq=dq, n=n: h.activation(
                        dA[:], dtt[j][:, ts(dq, TB)], AF.Exp, scale=acol[:, dq * 16 + n:dq * 16 + n + 1]),
                      reads=[f"dt{j}"], sreads=["acol"], writes=["dA"], tag=kk + "a")
                    A("vector", lambda h: h.tensor_tensor(dBu[:], dtx[:], PA[:, 0:TB], op=ALU.mult),
                      reads=["dtx", "pBb"], writes=["dBu"], tag=kk + "u")
                    A("vector", lambda h, hj=hj, dq=dq, n=n: h.tensor_tensor_scan(
                        hj[:], dA[:], dBu[:], chain[:, dq * 16 + n:dq * 16 + n + 1],
                        op0=ALU.mult, op1=ALU.add),
                      reads=["dA", "dBu"], sreads=[f"ch{dq}.{n}"],
                      writes=[f"h{n % 2}"], tag=kk + "h")
                    A("vector", lambda h, hj=hj, dq=dq, n=n: h.tensor_copy(
                        chain[:, dq * 16 + n:dq * 16 + n + 1], hj[:, TB - 1:TB]),
                      reads=[f"h{n % 2}"], writes=[f"ch{dq}.{n}", "chain"], tag=kk + "cc")
                    if n == 0:
                        A("vector", lambda h, hj=hj: h.tensor_tensor(
                            PA[:, 1024:2048], hj[:], PB[:, 0:TB], op=ALU.mult),
                          reads=[f"h{n % 2}", "pCb"], writes=["pY"], tag=kk + "y0")
                    else:
                        A("vector", lambda h, hj=hj: h.tensor_tensor(
                            tmp[:], hj[:], PB[:, 0:TB], op=ALU.mult),
                          reads=[f"h{n % 2}", "pCb"], writes=["tmps"], tag=kk + "y1")
                        A("vector", lambda h: h.tensor_tensor(
                            PA[:, 1024:2048], PA[:, 1024:2048], tmp[:], op=ALU.add),
                          reads=["tmps", "pY"], writes=["pY"], tag=kk + "y2")
                A("scalar", lambda h: h.activation(ystg[:], PA[:, 1024:2048], AF.Copy),
                  reads=["pY"], writes=["ystg"], tag=f"ye{tb}.{dq}")
                A("sync", lambda h, tb=tb, dq=dq: h.dma_start(
                    out=yqd[ts(dq, 128), ts(tb, TB)], in_=ystg[:]),
                  reads=["ystg"], writes=["yqd"], dma=True, tag=f"p6st{tb}.{dq}")


def emit_back_body(S, nc, ctx, io):
    A = S.add
    x1Td = io["x1Td"]; xn2d = io["xn2d"]; xcd = io["xcd"]; szd = io["szd"]
    dcold = io["dcold"]; wod = io["wod"]
    sw1d = io["sw1d"]; sb1d = io["sb1d"]; sw2d = io["sw2d"]; sb2d = io["sb2d"]
    rwd = io["rwd"]; rbd = io["rbd"]
    w1d = io["w1d"]; b1d = io["b1d"]; w2d = io["w2d"]; b2d = io["b2d"]
    oh8d = io["oh8d"]; onesbd = io["onesbd"]; idbd = io["idbd"]; outd = io["outd"]
    NDQ = DI // 128
    NB4 = TLOC // 512
    NT = TLOC // 128
    NFQ = 8
    if True:
        sb = lambda nm, shp, dt2: ctx.enter_context(nc.sbuf_tensor("p7" + nm, shp, dt2))
        xn2T = sb("xn2Ts", [128, ND * TLOC], BF16)
        accT = sb("accT", [128, ND * TLOC], F32)
        y3T = sb("y3T", [128, 8 * TLOC], BF16)
        wo = sb("wo", [128, 8 * ND * 128], BF16)
        yt = sb("yt", [128, TLOC], BF16)
        xct = sb("xct", [128, TLOC], BF16)
        szt = sb("szt", [128, TLOC], BF16)
        x1t = sb("x1t", [128, TLOC], F32)
        dcol = sb("dcol", [128, 16], F32)
        sb2c = sb("sb2c", [128, 8], F32)
        sb1c = sb("sb1c", [128, 32], F32)
        b1c = sb("b1c", [128, 8 * 32], F32)
        b2h = sb("b2h", [8, D], BF16)
        oh8 = sb("oh8", [8, 8 * 128], BF16)
        onesb = sb("onesb", [128, 128], BF16)
        idb = sb("idb", [128, 128], BF16)
        rww = sb("rww", [128, ND * 8], BF16)
        rbb = sb("rbb", [1, 8], BF16)
        lg = sb("lg", [128, 8], F32)
        m8 = sb("m8", [128, 8], F32)
        nv0 = sb("nv0", [128, 1], F32)
        el = sb("el", [128, 8], F32)
        msk = sb("msk", [128, 8], F32)
        ev2 = sb("ev2", [128, 2], F32)
        ds = sb("ds", [128, 1], F32)
        rden = sb("rden", [128, 1], F32)
        compb = sb("compb", [128, NT * 8], BF16)
        combT = sb("combT", [8, TLOC], BF16)
        w1b = [sb(f"w1b{j}", [128, ND * 512], BF16) for j in range(2)]
        w2b = wo
        PA = io["PA"]; PB = io["PB"]
        ph_sl = [(PA, 512), (PA, 0)]
        pd_sl = [(PA, 1024), (PA, 1536)]
        ph_nm = ["ph0", "ph1"]
        pd_nm = ["pd0", "pd1"]

        for nm, dst, srcd in [("dcol", dcol, dcold), ("sb2c", sb2c, sb2d), ("sb1c", sb1c, sb1d),
                              ("b1c", b1c, b1d), ("oh8", oh8, oh8d), ("onesb", onesb, onesbd),
                              ("idb", idb, idbd), ("rbb", rbb, rbd)]:
            A("sync", lambda h, dst=dst, srcd=srcd: h.dma_start(out=dst[:], in_=srcd),
              writes=[nm], dma=True, tag="ld" + nm)
        A("sync", lambda h: h.dma_start(out=b2h[:], in_=b2d), writes=["b2h"], dma=True, tag="ldb2h")
        A("sync", lambda h: h.dma_start(out=rww[:], in_=rwd.rearrange("(dt p) e -> p dt e", p=128)),
          writes=["rww"], dma=True, tag="ldrw")

        A("sync", lambda h: h.dma_start(out=xn2T[:], in_=xn2d.rearrange("(dt p) t -> p dt t", p=128)),
          reads=["xn2dk"], writes=["xn2T"], dma=True, tag="p7ldxn2")
        for dt in range(ND):
            A("sync", lambda h, dt=dt: h.dma_start(out=x1t[:], in_=x1Td[ts(dt, 128), :]),
              reads=["x1d"], writes=["x1t"], dma=True, tag=f"p7lx1{dt}")
            A("vector", lambda h, dt=dt: h.tensor_tensor(
                accT[:, ts(dt, TLOC)], x1t[:], xn2T[:, ts(dt, TLOC)], op=ALU.add),
              reads=["x1t", "xn2T"], writes=["accT"], tag=f"ai{dt}")
            A("vector", lambda h, dt=dt: h.tensor_scalar(
                accT[:, ts(dt, TLOC)], accT[:, ts(dt, TLOC)], sb2c[:, dt:dt + 1], None, op0=ALU.add),
              reads=["accT"], sreads=["sb2c"], writes=["accT"], tag=f"ab{dt}")
        for tt in range(NT):
            tk = f"r{tt}"
            A("tensor", lambda h, tt=tt: [
                h.matmul(PA[0:128, 1536:1544], xn2T[:, dt * TLOC + tt * 128: dt * TLOC + tt * 128 + 128],
                         rww[:, ts(dt, 8)], start=(dt == 0), stop=False)
                for dt in range(ND)][-1],
              reads=["xn2T", "rww"], writes=["prt"], tag=tk + "mm")
            A("tensor", lambda h: h.matmul(PA[0:128, 1536:1544], onesb[0:1, :], rbb[:],
                                           start=False, stop=True),
              reads=["rbb", "onesb"], writes=["prt"], tag=tk + "bias")
            A("scalar", lambda h: h.activation(lg[:], PA[0:128, 1536:1544], AF.Copy),
              reads=["prt"], writes=["lg"], tag=tk + "ev")
            A("vector", lambda h: h.max(m8[:], lg[:]), reads=["lg"], writes=["m8"], tag=tk + "m8")
            A("vector", lambda h: h.tensor_scalar(nv0[:], m8[:, 0:1], -1.0, None, op0=ALU.mult),
              reads=["m8"], writes=["nv0"], tag=tk + "nv")
            A("scalar", lambda h: h.activation(el[:], lg[:], AF.Exp, bias=nv0[:, 0:1]),
              reads=["lg"], sreads=["nv0"], writes=["el"], tag=tk + "el")
            A("vector", lambda h: h.tensor_scalar(msk[:], lg[:], m8[:, 1:2], None, op0=ALU.is_ge),
              reads=["lg"], sreads=["m8"], writes=["msk"], tag=tk + "mk")
            A("scalar", lambda h: h.activation(ev2[:], m8[:, 0:2], AF.Exp, bias=nv0[:, 0:1]),
              reads=["m8"], sreads=["nv0"], writes=["ev2"], tag=tk + "e2")
            A("vector", lambda h: h.reduce_sum(ds[:], ev2[:], axis=mybir.AxisListType.X),
              reads=["ev2"], writes=["ds"], tag=tk + "ds")
            A("vector", lambda h: h.reciprocal(rden[:], ds[:]), reads=["ds"], writes=["rden"], tag=tk + "rd")
            A("vector", lambda h: h.tensor_tensor(el[:], el[:], msk[:], op=ALU.mult),
              reads=["el", "msk"], writes=["el"], tag=tk + "cm1")
            A("vector", lambda h, tt=tt: h.tensor_scalar(
                compb[:, ts(tt, 8)], el[:], rden[:, 0:1], None, op0=ALU.mult),
              reads=["el"], sreads=["rden"], writes=["compb"], tag=tk + "cm2")
            A("tensor", lambda h, tt=tt: h.transpose(
                PB[0:8, 1536:1600].bitcast(BF16), compb[:, ts(tt, 8)], idb[:]),
              reads=["compb", "idb"], writes=["ptr"], tag=tk + "tr")
            A("vector", lambda h, tt=tt: h.tensor_copy(
                combT[0:8, ts(tt, 128)], PB[0:8, 1536:1600].bitcast(BF16)),
              reads=["ptr"], writes=["combT"], tag=tk + "te")
        A("tensor", lambda h: h.nop(nofuse=True),
          reads=["combT"], writes=["prt", "ptr", "pcb", "ph0", "ph1", "pd0", "pd1"], tag="fence")
        w1i = 0
        hG = 0
        dG = 0
        for e in range(-1, 8):
            ekk = f"E{e}"
            if e >= 0:
                A("tensor", lambda h, e=e: [
                    h.matmul(PB[:, half * 512: half * 512 + 512], oh8[:, ts(e, 128)],
                             combT[0:8, half * 512: half * 512 + 512], start=True, stop=True)
                    for half in range(4)][-1],
                  reads=["combT", "oh8"], writes=["pcb"], tag=ekk + "cb")
            for q4 in range(4):
                w1src = sw1d if e < 0 else w1d[e]
                w2src = sw2d if e < 0 else w2d[e]
                A("sync", lambda h, w2src=w2src, q4=q4: h.dma_start(
                    out=w2b[:], in_=w2src[ts(q4, 1024), :].rearrange("(ft p) c -> p ft c", p=128)),
                  writes=["wo", "w2bk"], dma=True, tag=ekk + f"lw2{q4}")
                for qh in range(2):
                    jb = w1i % 2
                    w1i += 1
                    A("sync", lambda h, jb=jb, w1src=w1src, q4=q4, qh=qh: h.dma_start(
                        out=w1b[jb][:],
                        in_=w1src[:, q4 * 1024 + qh * 512: q4 * 1024 + qh * 512 + 512]
                        .rearrange("(dt p) c -> p dt c", p=128)),
                      writes=[f"w1b{jb}"], dma=True, tag=ekk + f"lw1{q4}.{qh}")
                    for fh in range(4):
                        ft = qh * 4 + fh
                        gft = q4 * NFQ + ft
                        for tb in range(NB4):
                            hp = hG % 2
                            hG += 1
                            hpa, hof = ph_sl[hp]
                            A("tensor", lambda h, jb=jb, fh=fh, tb=tb, hpa=hpa, hof=hof: [
                                h.matmul(hpa[:, hof:hof + 512],
                                         w1b[jb][:, dt * 512 + fh * 128: dt * 512 + fh * 128 + 128],
                                         xn2T[:, dt * TLOC + tb * 512: dt * TLOC + tb * 512 + 512],
                                         start=(dt == 0), stop=(dt == ND - 1))
                                for dt in range(ND)][-1],
                              reads=["xn2T", f"w1b{jb}"], writes=[ph_nm[hp]], tag=ekk + f"h{gft}.{tb}")
                            bca = (sb1c, gft) if e < 0 else (b1c, e * 32 + gft)
                            A("scalar", lambda h, ft=ft, tb=tb, bca=bca, hpa=hpa, hof=hof: h.activation(
                                y3T[:, ft * TLOC + tb * 512: ft * TLOC + tb * 512 + 512],
                                hpa[:, hof:hof + 512], AF.Gelu_apprx_tanh,
                                bias=bca[0][:, bca[1]:bca[1] + 1]),
                              reads=[ph_nm[hp]], sreads=["sb1c" if e < 0 else "b1c"],
                              writes=[f"hq{ft}", f"y3_{ft}"], tag=ekk + f"g{gft}.{tb}")
                        if e >= 0:
                            A("vector", lambda h, ft=ft: h.tensor_tensor(
                                y3T[:, ft * TLOC: ft * TLOC + TLOC],
                                y3T[:, ft * TLOC: ft * TLOC + TLOC],
                                PB[:, 0:2048], op=ALU.mult),
                              reads=[f"hq{ft}", "pcb"], writes=[f"hq{ft}", f"y3_{ft}"],
                              tag=ekk + f"w{gft}")
                for dt in range(ND):
                    for tb in range(NB4):
                        dp = dG % 2
                        dG += 1
                        dpa, dof = pd_sl[dp]
                        def emit_down(h, dt=dt, tb=tb, e=e, q4=q4, dpa=dpa, dof=dof):
                            for ft in range(NFQ):
                                ins = h.matmul(
                                    dpa[:, dof:dof + 512],
                                    w2b[:, ft * 1024 + dt * 128: ft * 1024 + dt * 128 + 128],
                                    y3T[:, ft * TLOC + tb * 512: ft * TLOC + tb * 512 + 512],
                                    start=(ft == 0), stop=(ft == NFQ - 1 and not (e >= 0 and q4 == 0)))
                            if e >= 0 and q4 == 0:
                                ins = h.matmul(dpa[:, dof:dof + 512], b2h[:, ts(dt, 128)],
                                               combT[0:8, ts(tb, 512)], start=False, stop=True)
                            return ins
                        A("tensor", emit_down,
                          reads=[f"hq{x}" for x in range(NFQ)] + ["w2bk", "b2h", "combT"],
                          writes=[pd_nm[dp]], tag=ekk + f"d{q4}.{dt}.{tb}")
                        A("vector", lambda h, dt=dt, tb=tb, dpa=dpa, dof=dof: h.tensor_tensor(
                            accT[:, dt * TLOC + tb * 512: dt * TLOC + tb * 512 + 512],
                            accT[:, dt * TLOC + tb * 512: dt * TLOC + tb * 512 + 512],
                            dpa[:, dof:dof + 512], op=ALU.add),
                          reads=[pd_nm[dp], "accT"], writes=["accT"], tag=ekk + f"a{q4}.{dt}.{tb}")
        wG = 0
        for half in range(2):
            A("sync", lambda h, half=half: h.dma_start(
                out=wo[:], in_=wod[half * 1024:(half + 1) * 1024, :]
                .rearrange("(dq p) (dt c) -> p dq dt c", p=128, c=128)),
              writes=["wo", "w2bk"], dma=True, tag=f"ldwo{half}")
            for dq8 in range(8):
                dq = half * 8 + dq8
                A("gpsimd", io["gather"](yt[:], io["y_out"], io["idxy"][:, dq:dq + 1]),
                  reads=["youtd", "idxy"], writes=["yt"], dma=True, tag=f"p7ly{dq}")
                A("sync", lambda h, dq=dq: h.dma_start(out=xct[:], in_=xcd[ts(dq, 128), :]),
                  reads=["xcd"], writes=["xctb"], dma=True, tag=f"p7lc{dq}")
                A("sync", lambda h, dq=dq: h.dma_start(out=szt[:], in_=szd[ts(dq, 128), :]),
                  reads=["szd"], writes=["sztb"], dma=True, tag=f"p7lz{dq}")
                A("vector", lambda h, dq=dq, dq8=dq8: h.scalar_tensor_tensor(
                    y3T[:, ts(dq8, TLOC)], xct[:], dcol[:, dq:dq + 1], yt[:], op0=ALU.mult, op1=ALU.add),
                  reads=["xctb", "yt"], sreads=["dcol"], writes=[f"y3_{dq8}", f"hq{dq8}"], tag=f"y2.{dq}")
                A("vector", lambda h, dq8=dq8: h.tensor_tensor(
                    y3T[:, ts(dq8, TLOC)], y3T[:, ts(dq8, TLOC)], szt[:], op=ALU.mult),
                  reads=[f"y3_{dq8}", "sztb"], writes=[f"y3_{dq8}", f"hq{dq8}"], tag=f"y3.{dq}")
            for dt in range(ND):
                for tb in range(NB4):
                    wp = wG % 2
                    wG += 1
                    wpa, wof = ph_sl[wp]
                    A("tensor", lambda h, dt=dt, tb=tb, wpa=wpa, wof=wof: [
                        h.matmul(wpa[:, wof:wof + 512], wo[:, (dq8 * ND + dt) * 128:(dq8 * ND + dt) * 128 + 128],
                                 y3T[:, dq8 * TLOC + tb * 512: dq8 * TLOC + tb * 512 + 512],
                                 start=(dq8 == 0), stop=(dq8 == 7))
                        for dq8 in range(8)][-1],
                      reads=[f"y3_{x}" for x in range(8)] + ["wo"], writes=[ph_nm[wp]],
                      tag=f"wo{half}.{dt}.{tb}")
                    A("vector", lambda h, dt=dt, tb=tb, wpa=wpa, wof=wof: h.tensor_tensor(
                        accT[:, dt * TLOC + tb * 512: dt * TLOC + tb * 512 + 512],
                        accT[:, dt * TLOC + tb * 512: dt * TLOC + tb * 512 + 512],
                        wpa[:, wof:wof + 512], op=ALU.add),
                      reads=[ph_nm[wp], "accT"], writes=["accT"], tag=f"wa{half}.{dt}.{tb}")
        for dt in range(ND):
            A("sync", lambda h, dt=dt: h.dma_start(out=outd[ts(dt, 128), :], in_=accT[:, ts(dt, TLOC)]),
              reads=["accT"], writes=["stored"], dma=True, tag=f"p7sto{dt}")
        fin = A("sync", lambda h: h.nop(nofuse=True), reads=["stored"], tag="fin")
        for op in S.ops:
            if op.dma and op.tag.startswith("p7sto") and op not in fin.deps:
                fin.deps.append(op)


def emit_attn_body(S, nc, env):
    A = env["A"]
    xchunk = env["xchunk"]
    wq = env["wq"]
    wk = env["wk"]
    wv = env["wv"]
    biasq = env["biasq"]
    bias_o = env["bias_o"]
    fq = env["fq"]
    onesb = env["onesb"]
    idb = env["idb"]
    pihc = env["pihc"]
    kpre = env["kpre"]
    kswp = env["kswp"]
    qpre = env["qpre"]
    qswp = env["qswp"]
    vpre = env["vpre"]
    ka = env["ka"]
    kb = env["kb"]
    qa = env["qa"]
    qb = env["qb"]
    V = env["V"]
    th = env["th"]
    kfb = env["kfb"]
    kib = env["kib"]
    rr = env["rr"]
    cosr = env["cosr"]
    sinr = env["sinr"]
    tmpa = env["tmpa"]
    tmpb = env["tmpb"]
    ksqs = env["ksqs"]
    sp512 = env["sp512"]
    kmaxp = env["kmaxp"]
    kms = env["kms"]
    q2s = env["q2s"]
    qnmA = env["qnmA"]
    qnmB = env["qnmB"]
    P0 = env["P0"]
    P1v = env["P1v"]
    sr = env["sr"]
    rsf = env["rsf"]
    rs = env["rs"]
    rsb = env["rsb"]
    aoT = env["aoT"]
    wo = env["wo"]
    ostg = env["ostg"]
    PA = env["PA"]
    PB = env["PB"]
    b_qkvT = env["b_qkvT"]
    b_outT = env["b_outT"]
    fcolsd = env["fcolsd"]
    onesbf_d = env["onesbf_d"]
    idbf_d = env["idbf_d"]
    posrep = env["posrep"]
    w_qkv = env["w_qkv"]
    w_outd = env["w_outd"]
    xn_view = env["xn_view"]
    idxa = env["idxa"]
    at0_i = env["at0_i"]
    gather = env["gather"]
    w_out = w_outd
    at0_st = at0_i
    span = 0
    NP = 8; CH = 1024; NCH = L // CH; NMT = 64; NRB = 4

    A("gpsimd", lambda h: h.dma_start(out=biasq[:], in_=b_qkvT), writes=["biasq"], dma=True, tag="ldbq")
    A("gpsimd", lambda h: h.dma_start(out=bias_o[:], in_=b_outT), writes=["bias_o"], dma=True, tag="ldbo")
    A("gpsimd", lambda h: h.dma_start(out=fq[:], in_=fcolsd), writes=["fq"], dma=True, tag="ldfq")
    A("gpsimd", lambda h: h.dma_start(out=onesb[:], in_=onesbf_d), writes=["onesb"], dma=True, tag="ldob")
    A("gpsimd", lambda h: h.dma_start(out=idb[:], in_=idbf_d), writes=["idb"], dma=True, tag="ldid")
    A("gpsimd", lambda h: h.memset(pihc[:], PIH), writes=["pihc"], tag="mspih")
    A("gpsimd", lambda h: h.memset(ka[64:65, :], 1.0), writes=["karow"], tag="mska")
    A("gpsimd", lambda h: h.memset(kb[64:65, :], 1.0), writes=["kbrow"], tag="mskb")
    vap = V[:].rearrange("p (m c) -> p m c", c=130)
    A("gpsimd", lambda h: h.memset(vap[:, :, 64:65], 1.0), writes=["Vones"], tag="msv1")
    A("gpsimd", lambda h: h.memset(vap[:, :, 129:130], 1.0), writes=["Vones"], tag="msv2")

    PSL = [PA[:, 0:1024], PA[:, 1024:2048]]
    PSO = [PB[:, 0:512], PB[:, 512:1024]]
    PRB = PB[:, 1024:1536]
    PST = PB[:, 1536:2048]
    Pbuf = [P0, P1v]

    for p_ in range(NP):
        c0 = p_ * 128
        pk = f"p{p_}"
        for nm, wt, coff in (("wq", wq, c0), ("wk", wk, D + c0), ("wv", wv, 2 * D + c0)):
            A("gpsimd", lambda h, wt=wt, coff=coff: h.dma_start(
                out=wt[:], in_=w_qkv[:, coff:coff + 128].rearrange("(dt p) c -> p dt c", p=128)),
              writes=[nm], dma=True, tag=pk + "ld" + nm)

        for c_ in range(NCH):
            ck = pk + f"c{c_}"
            ccols = ts(c_, CH)
            is_q = (c_ // 2 == span)
            qoff = (c_ % 2) * CH
            for dt_i in range(ND):
                A("gpsimd", gather(xchunk[:, ts(dt_i, CH)], xn_view,
                                   idxa[:, c_ * 8 + dt_i:c_ * 8 + dt_i + 1]),
                  reads=["xnoutd", "idxa"], writes=["xchunk"], dma=True,
                  tag=ck + f"ldx{dt_i}")
            streams = [("k", wk, kpre, 8), ("v", wv, vpre, 16)]
            if is_q:
                streams.append(("q", wq, qpre, 0))
            qki = 0
            for tb in range(CH // 512):
                for nm, wt, dst, boff in streams:
                    j4 = qki % 4
                    qki += 1
                    A("tensor", lambda h, wt=wt, tb=tb, j4=j4: [
                        h.matmul(PA[:, ts(j4, 512)], wt[:, ts(dt, 128)],
                                 xchunk[:, dt * CH + tb * 512: dt * CH + tb * 512 + 512],
                                 start=(dt == 0), stop=(dt == ND - 1))
                        for dt in range(ND)][-1],
                      reads=["xchunk", "w" + nm], writes=[f"qps{j4}"], tag=ck + f"mm{nm}{tb}")
                    A("scalar", lambda h, dst=dst, tb=tb, boff=boff, j4=j4, p_=p_: h.activation(
                        dst[:, ts(tb, 512)], PA[:, ts(j4, 512)], AF.Identity,
                        bias=biasq[:, boff + p_:boff + p_ + 1]),
                      reads=[f"qps{j4}"], writes=[nm + "pre"], sreads=["biasq"],
                      tag=ck + f"ev{nm}{tb}")
            for mt_ in range(CH // 128):
                gmt = c_ * (CH // 128) + mt_
                A("tensor", lambda h, mt_=mt_: h.transpose(
                    PST[:, 0:64].bitcast(BF16), vpre[:, ts(mt_, 128)], idb[:]),
                  reads=["vpre", "idb"], writes=["PST"], tag=ck + f"tr{mt_}")
                A("vector", lambda h, gmt=gmt: h.tensor_copy(
                    V[:, gmt * 130: gmt * 130 + 130].rearrange("p (a c) -> p a c", a=2)[:, :, 0:64],
                    PST[:, 0:64].bitcast(BF16).rearrange("p (a c) -> p a c", a=2)),
                  reads=["PST"], writes=["Vdat"], tag=ck + f"te{mt_}")
            swaps = [("k", kpre, kswp)] + ([("q", qpre, qswp)] if is_q else [])
            for nm, src_, dst_ in swaps:
                for g2 in range(2):
                    s0, s1 = 64 * g2, 64 * g2 + 32
                    A("gpsimd", lambda h, src_=src_, dst_=dst_, s0=s0, s1=s1: h.dma_start(
                        out=dst_[s0:s0 + 32, :], in_=src_[s1:s1 + 32, :]),
                      reads=[nm + "pre"], writes=[nm + f"swp{g2}a"], dma=True, tag=ck + f"sw{nm}{g2}a")
                    A("gpsimd", lambda h, src_=src_, dst_=dst_, s0=s0, s1=s1: h.dma_start(
                        out=dst_[s1:s1 + 32, :], in_=src_[s0:s0 + 32, :]),
                      reads=[nm + "pre"], writes=[nm + f"swp{g2}b"], dma=True, tag=ck + f"sw{nm}{g2}b")
            A("gpsimd", lambda h, c_=c_: h.dma_start(out=th[:], in_=posrep[:, ts(c_, CH)]),
              writes=["th"], dma=True, tag=ck + "io")
            A("vector", lambda h, p_=p_: h.tensor_scalar(th[:], th[:], fq[:, p_:p_ + 1], None, op0=ALU.mult),
              reads=["th"], writes=["th"], sreads=["fq"], tag=ck + "th")
            A("vector", lambda h: h.tensor_scalar(kfb[:], th[:], INV2PI, None, op0=ALU.mult),
              reads=["th"], writes=["kfb"], tag=ck + "kf")
            A("vector", lambda h: h.tensor_copy(kib[:], kfb[:]), reads=["kfb"], writes=["kib"], tag=ck + "ki")
            A("vector", lambda h: h.tensor_copy(kfb[:], kib[:]), reads=["kib"], writes=["kfb"], tag=ck + "kf2")
            A("vector", lambda h: h.scalar_tensor_tensor(rr[:], kfb[:], -RR_C1, th[:],
                                                         op0=ALU.mult, op1=ALU.add),
              reads=["kfb", "th"], writes=["rr"], tag=ck + "r1")
            A("vector", lambda h: h.scalar_tensor_tensor(rr[:], kfb[:], -RR_C2, rr[:],
                                                         op0=ALU.mult, op1=ALU.add),
              reads=["kfb", "rr"], writes=["rr"], tag=ck + "r2")
            A("vector", lambda h: h.scalar_tensor_tensor(rr[:], kfb[:], -RR_C3, rr[:],
                                                         op0=ALU.mult, op1=ALU.add),
              reads=["kfb", "rr"], writes=["rr"], tag=ck + "r3")
            for fi in range(2):
                A("vector", lambda h: h.tensor_scalar(kfb[:], rr[:], PI_, None, op0=ALU.is_gt),
                  reads=["rr"], writes=["kfb"], tag=ck + f"fg{fi}")
                A("vector", lambda h: h.scalar_tensor_tensor(rr[:], kfb[:], -TWOPI, rr[:],
                                                             op0=ALU.mult, op1=ALU.add),
                  reads=["kfb", "rr"], writes=["rr"], tag=ck + f"fs{fi}")
                A("vector", lambda h: h.tensor_scalar(kfb[:], rr[:], -PI_, None, op0=ALU.is_lt),
                  reads=["rr"], writes=["kfb"], tag=ck + f"fh{fi}")
                A("vector", lambda h: h.scalar_tensor_tensor(rr[:], kfb[:], TWOPI, rr[:],
                                                             op0=ALU.mult, op1=ALU.add),
                  reads=["kfb", "rr"], writes=["rr"], tag=ck + f"ft{fi}")
            A("vector", lambda h: h.tensor_scalar(rr[:], rr[:], -PI_, PI_, op0=ALU.max, op1=ALU.min),
              reads=["rr"], writes=["rr"], tag=ck + "cl")
            A("scalar", lambda h: h.activation(sinr[:], rr[:], AF.Sin),
              reads=["rr"], writes=["sinr"], tag=ck + "sin")
            A("scalar", lambda h: h.activation(th[:], rr[:], AF.Abs),
              reads=["rr"], writes=["th"], tag=ck + "abs")
            A("scalar", lambda h: h.activation(cosr[:], th[:], AF.Sin, scale=-1.0,
                                               bias=pihc[:, 0:1]),
              reads=["th"], writes=["cosr"], sreads=["pihc"], tag=ck + "cos")
            combos = [("k", kpre, kswp, ka, kb, ccols)] + (
                [("q", qpre, qswp, qa, qb, slice(qoff, qoff + CH))] if is_q else [])
            for nm, pre, swp, ta, tb_, ocols in combos:
                A("vector", lambda h, pre=pre: h.tensor_tensor(tmpa[:], pre[:], cosr[:], op=ALU.mult),
                  reads=[nm + "pre", "cosr"], writes=["tmpa"], tag=ck + f"cb1{nm}")
                A("vector", lambda h, swp=swp: h.tensor_tensor(tmpb[:], swp[:], sinr[:], op=ALU.mult),
                  reads=[nm + "swp0a", nm + "swp0b", nm + "swp1a", nm + "swp1b", "sinr"],
                  writes=["tmpb"], tag=ck + f"cb2{nm}")
                A("vector", lambda h, ta=ta, ocols=ocols: h.tensor_tensor(
                    ta[0:64, ocols], tmpa[0:64, :], tmpb[0:64, :], op=ALU.add),
                  reads=["tmpa", "tmpb"], writes=[nm + "rotA"], tag=ck + f"cb3{nm}")
                A("vector", lambda h, tb_=tb_, ocols=ocols: h.tensor_tensor(
                    tb_[0:64, ocols], tmpa[64:128, :], tmpb[64:128, :], op=ALU.add),
                  reads=["tmpa", "tmpb"], writes=[nm + "rotB"], tag=ck + f"cb4{nm}")
            for hh in range(2):
                kx = (ka, kb)[hh]
                A("scalar", lambda h, kx=kx, ccols=ccols: h.activation(
                    ksqs[:], kx[0:64, ccols], AF.Square),
                  reads=["krot" + "AB"[hh]], writes=["ksqs"], tag=ck + f"ksq{hh}")
                for piece in range(CH // 512):
                    pidx = c_ * (CH // 512) + piece
                    A("tensor", lambda h, piece=piece: h.matmul(
                        PRB[0:1, 0:512], onesb[0:64, 0:1], ksqs[0:64, ts(piece, 512)],
                        start=True, stop=True),
                      reads=["ksqs", "onesb"], writes=["prb"], tag=ck + f"k2m{hh}.{piece}")
                    A("scalar", lambda h: h.activation(sp512[:], PRB[0:1, 0:512], AF.Copy),
                      reads=["prb"], writes=["sp512"], tag=ck + f"k2e{hh}.{piece}")
                    A("vector", lambda h, hh=hh, pidx=pidx: h.reduce_max(
                        kmaxp[0:1, hh * 16 + pidx:hh * 16 + pidx + 1], sp512[:],
                        axis=mybir.AxisListType.X),
                      reads=["sp512"], writes=["kmaxp"], tag=ck + f"k2r{hh}.{pidx}")
            if is_q:
                for hh in range(2):
                    qx = (qa, qb)[hh]
                    A("scalar", lambda h, qx=qx, qoff=qoff: h.activation(
                        ksqs[:], qx[0:64, qoff:qoff + CH], AF.Square),
                      reads=["qrot" + "AB"[hh]], writes=["ksqs"], tag=ck + f"qsq{hh}")
                    for piece in range(CH // 512):
                        A("tensor", lambda h, piece=piece: h.matmul(
                            PRB[0:1, 0:512], onesb[0:64, 0:1], ksqs[0:64, ts(piece, 512)],
                            start=True, stop=True),
                          reads=["ksqs", "onesb"], writes=["prb"], tag=ck + f"q2m{hh}.{piece}")
                        A("scalar", lambda h, piece=piece, qoff=qoff, hh=hh: h.activation(
                            q2s[0:1, qoff + piece * 512: qoff + piece * 512 + 512]
                            if hh == 0 else
                            q2s[0:1, qoff + piece * 512: qoff + piece * 512 + 512],
                            PRB[0:1, 0:512], AF.Copy),
                          reads=["prb"], writes=["q2s" + "AB"[hh]], tag=ck + f"q2e{hh}.{piece}")
                    A("scalar", lambda h, qoff=qoff, hh=hh: h.activation(
                        (qnmA if hh == 0 else qnmB)[0:1, qoff:qoff + CH],
                        q2s[0:1, qoff:qoff + CH], AF.Sqrt),
                      reads=["q2s" + "AB"[hh]], writes=["qnm" + "AB"[hh]], tag=ck + f"qnm{hh}")

        for hh in range(2):
            A("vector", lambda h, hh=hh: h.reduce_max(
                kms[0:1, hh:hh + 1], kmaxp[0:1, hh * 16:hh * 16 + 16],
                axis=mybir.AxisListType.X),
              reads=["kmaxp"], writes=["kms2"], tag=pk + f"kmx{hh}")
        A("scalar", lambda h: h.activation(kms[:], kms[:], AF.Sqrt),
          reads=["kms2"], writes=["kms"], tag=pk + "kmsq")
        for hh in range(2):
            qx = (qa, qb)[hh]
            qn = (qnmA, qnmB)[hh]
            A("vector", lambda h, qx=qx, qn=qn, hh=hh: h.tensor_scalar(
                qx[64:65, :], qn[:], kms[0:1, hh:hh + 1], -1.0,
                op0=ALU.mult, op1=ALU.mult),
              reads=["qnm" + "AB"[hh]], writes=["q" + "ab"[hh] + "crow"],
              sreads=["kms"], tag=pk + f"crow{hh}")

        for hh in range(2):
            kx = (ka, kb)[hh]
            qx = (qa, qb)[hh]
            krd = ["krot" + "AB"[hh], ("karow", "kbrow")[hh]]
            qrd = ["qrot" + "AB"[hh], "q" + "ab"[hh] + "crow"]
            for rb in range(NRB):
                ob = rb % 2
                rcols = ts(rb, 512)

                def emit_qk(h, mp, kx=kx, qx=qx, rcols=rcols):
                    for j in range(2):
                        mt = 2 * mp + j
                        ins = h.matmul(PSL[mp % 2][:, ts(j, 512)], kx[:, ts(mt, 128)],
                                       qx[:, rcols], start=True, stop=True)
                    return ins

                def emit_exp(h, mp):
                    return h.activation(Pbuf[mp % 2][:], PSL[mp % 2][:], AF.Exp, scale=0.125)

                def emit_av(h, mp, hh=hh, ob=ob):
                    for j in range(2):
                        mt = 2 * mp + j
                        ins = h.matmul(PSO[ob][0:65, :],
                                       V[:, mt * 130 + 65 * hh: mt * 130 + 65 * hh + 65],
                                       Pbuf[mp % 2][:, ts(j, 512)],
                                       start=(mt == 0), stop=(mt == NMT - 1))
                    return ins

                hk = pk + f"h{hh}rb{rb}"
                PK = [["qps0", "qps1"], ["qps2", "qps3"]]
                A("tensor", lambda h, f=emit_qk: f(h, 0), reads=krd + qrd,
                  writes=PK[0], tag=hk + "qk0")
                A("tensor", lambda h, f=emit_qk: f(h, 1), reads=krd + qrd,
                  writes=PK[1], tag=hk + "qk1")
                for mp in range(32):
                    A("scalar", lambda h, mp=mp, f=emit_exp: f(h, mp),
                      reads=PK[mp % 2], writes=[f"P{mp % 2}"], tag=hk + f"ex{mp}")
                    A("tensor", lambda h, mp=mp, f=emit_av: f(h, mp),
                      reads=[f"P{mp % 2}", "Vdat", "Vones"], writes=[f"pso{ob}"], tag=hk + f"av{mp}")
                    if mp + 2 < 32:
                        A("tensor", lambda h, mp=mp, f=emit_qk: f(h, mp + 2),
                          reads=krd + qrd, writes=PK[mp % 2], tag=hk + f"qk{mp + 2}")
                A("scalar", lambda h, ob=ob: h.activation(sr[:], PSO[ob][64:65, :], AF.Copy),
                  reads=[f"pso{ob}"], writes=["sr"], tag=hk + "sr")
                A("vector", lambda h: h.reciprocal(rsf[:], sr[:]),
                  reads=["sr"], writes=["rsf"], tag=hk + "rsf")
                A("vector", lambda h: h.tensor_copy(rs[:], rsf[:]),
                  reads=["rsf"], writes=["rs"], tag=hk + "rs")
                A("tensor", lambda h: h.matmul(PRB[0:64, 0:512], onesb[0:1, 0:64], rs[:],
                                               start=True, stop=True),
                  reads=["rs", "onesb"], writes=["prb"], tag=hk + "rb")
                A("scalar", lambda h: h.activation(rsb[:], PRB[0:64, 0:512], AF.Copy),
                  reads=["prb"], writes=["rsb"], tag=hk + "rc")
                A("vector", lambda h, ob=ob, hh=hh, rb=rb, p_=p_: h.tensor_tensor(
                    aoT[64 * hh:64 * hh + 64, p_ * TLOC + rb * 512: p_ * TLOC + rb * 512 + 512],
                    PSO[ob][0:64, :], rsb[:], op=ALU.mult),
                  reads=[f"pso{ob}", "rsb"], writes=["aoT"], tag=hk + "oe")

    A("gpsimd", lambda h: h.dma_start(
        out=wo[:], in_=w_out.rearrange("(pp a) (dt c) -> a pp dt c", a=128, c=128)),
      writes=["wo"], dma=True, tag="ldwo")
    for dt in range(ND):
        for g in range(4):
            j2 = (dt * 4 + g) % 2
            A("tensor", lambda h, dt=dt, g=g: [
                h.matmul(PST[:, 0:512], wo[:, (pp * 8 + dt) * 128:(pp * 8 + dt) * 128 + 128],
                         aoT[:, pp * TLOC + g * 512: pp * TLOC + g * 512 + 512],
                         start=(pp == 0), stop=(pp == NP - 1))
                for pp in range(NP)][-1],
              reads=["aoT", "wo"], writes=["PST"], tag=f"op{dt}.{g}")
            A("scalar", lambda h, dt=dt, j2=j2: h.activation(
                ostg[j2][:], PST[:, 0:512], AF.Identity, bias=bias_o[:, dt:dt + 1]),
              reads=["PST"], writes=[f"ostg{j2}"], sreads=["bias_o"], tag=f"oe{dt}.{g}")
            A("sync", lambda h, dt=dt, g=g, j2=j2: h.dma_start(
                out=at0_st[ts(dt, 128), ts(g, 512)], in_=ostg[j2][:]),
              reads=[f"ostg{j2}"], writes=["at0d"], dma=True, tag=f"p2st{dt}.{g}")

def build_fused():
    nc = bass.Bass("TRN2", target_bir_lowering=False, num_devices=NCORES)
    EI, EO = "ExternalInput", "ExternalOutput"
    dt_ = nc.dram_tensor
    x0Td = dt_("x0T_loc", [D, TLOC], F32, kind=EI).ap()
    posrep = dt_("posrep", [128, L], F32, kind=EI).ap()
    w_qkv = dt_("w_qkv", [D, 3 * D], F32, kind=EI).ap()
    b_qkvT = dt_("b_qkvT", [128, 24], F32, kind=EI).ap()
    w_outd = dt_("w_out", [D, D], F32, kind=EI).ap()
    b_outT = dt_("b_outT", [128, 8], F32, kind=EI).ap()
    fcolsd = dt_("freqcols2", [128, 8], F32, kind=EI).ap()
    onesd = dt_("ones128", [128, 128], F32, kind=EI).ap()
    onesbf_d = dt_("onesbf", [128, 128], BF16, kind=EI).ap()
    idbf_d = dt_("idbf", [128, 128], BF16, kind=EI).ap()
    gwd = dt_("gate_wT", [32, D, 128], F32, kind=EI).ap()
    uwd = dt_("up_wT", [32, D, 128], F32, kind=EI).ap()
    dwd = dt_("down_wT", [32, 128, D], F32, kind=EI).ap()
    sgd = dt_("sgate_wT", [D, 512], F32, kind=EI).ap()
    sud = dt_("sup_wT", [D, 512], F32, kind=EI).ap()
    sdd = dt_("sdown_wT", [512, D], F32, kind=EI).ap()
    rw0d = dt_("router_wT", [D, 32], F32, kind=EI).ap()
    rb0d = dt_("router_b", [1, 32], F32, kind=EI).ap()
    ohd32 = dt_("onehot32", [32, 32 * 128], BF16, kind=EI).ap()
    oh16d = dt_("onehot16", [16, 16 * 128], BF16, kind=EI).ap()
    oh8d = dt_("onehot8", [8, 8 * 128], BF16, kind=EI).ap()
    wind = dt_("W_in", [D, 2 * DI], F32, kind=EI).ap()
    lngd = dt_("ln_gT", [128, ND], F32, kind=EI).ap()
    lnbd = dt_("ln_bT", [128, ND], F32, kind=EI).ap()
    cwd = dt_("conv_wT", [128, 16 * 4], F32, kind=EI).ap()
    cbd = dt_("conv_bT", [128, 16], F32, kind=EI).ap()
    wxd = dt_("W_x", [DI, 96], F32, kind=EI).ap()
    wdtd = dt_("W_dt", [DTR, DI], F32, kind=EI).ap()
    bdtd = dt_("b_dtT", [128, 16], F32, kind=EI).ap()
    aqd = dt_("Aq", [128, 64], F32, kind=EI).ap()
    dcold = dt_("DcolT", [128, 16], F32, kind=EI).ap()
    wo1d = dt_("W_out1", [DI, D], BF16, kind=EI).ap()
    sw1d = dt_("shared_w1T", [D, 4096], BF16, kind=EI).ap()
    sb1d = dt_("shared_b1T", [128, 32], F32, kind=EI).ap()
    sw2d = dt_("shared_w2T", [4096, D], BF16, kind=EI).ap()
    sb2d = dt_("shared_b2T", [128, 8], F32, kind=EI).ap()
    rw1d = dt_("router1_wT", [D, 8], BF16, kind=EI).ap()
    rb1d = dt_("router1_b", [1, 8], BF16, kind=EI).ap()
    w1d = dt_("spec_w1T", [8, D, 4096], BF16, kind=EI).ap()
    b1d = dt_("spec_b1T", [128, 8 * 32], F32, kind=EI).ap()
    w2d = dt_("spec_w2T", [8, 4096, D], BF16, kind=EI).ap()
    b2d = dt_("spec_b2h", [8, D], BF16, kind=EI).ap()
    U32 = mybir.dt.uint32
    idxad = dt_("idx_attn", [128, 64], U32, kind=EI).ap()
    idxhd = dt_("idx_halo", [128, 16], U32, kind=EI).ap()
    idxsd = dt_("idx_scan", [128, 32], U32, kind=EI).ap()
    idxbd = dt_("idx_bc", [16, 16], U32, kind=EI).ap()
    idxyd = dt_("idx_y", [128, 16], U32, kind=EI).ap()
    outd = dt_("outT", [D, TLOC], F32, kind=EO).ap()
    IN = "Internal"
    ag_xn_in = dt_("ag_xn_in", [D, TLOC], BF16, kind=IN)
    ag_xn_out = dt_("ag_xn_out", [NCORES * D, TLOC], BF16, kind=IN)
    ag_tl_in = dt_("ag_tl_in", [DI, 8], BF16, kind=IN)
    ag_tl_out = dt_("ag_tl_out", [NCORES * DI, 8], BF16, kind=IN)
    ag_xc_in = dt_("ag_xc_in", [DI, TLOC], BF16, kind=IN)
    ag_xc_out = dt_("ag_xc_out", [NCORES * DI, TLOC], BF16, kind=IN)
    ag_dt_in = dt_("ag_dt_in", [DI, TLOC], F16, kind=IN)
    ag_dt_out = dt_("ag_dt_out", [NCORES * DI, TLOC], F16, kind=IN)
    ag_bc_in = dt_("ag_bc_in", [32, TLOC], F32, kind=IN)
    ag_bc_out = dt_("ag_bc_out", [NCORES * 32, TLOC], F32, kind=IN)
    ag_y_in = dt_("ag_y_in", [512, L], BF16, kind=IN)
    ag_y_out = dt_("ag_y_out", [NCORES * 512, L], BF16, kind=IN)
    at0_i = dt_("at0_i", [D, TLOC], BF16, kind=IN).ap()
    x1_i = dt_("x1_i", [D, TLOC], F32, kind=IN).ap()
    xn2_i = dt_("xn2_i", [D, TLOC], BF16, kind=IN).ap()
    xh_i = dt_("xh_i", [DI, TLOC], BF16, kind=IN).ap()
    sz_i = dt_("sz_i", [DI, TLOC], BF16, kind=IN).ap()

    S = Sched()
    A = S.add
    RG = [list(range(NCORES))]

    def gather(out_ap, in_ap, idx_ap, oobmax=None):
        def f(h, out_ap=out_ap, in_ap=in_ap, idx_ap=idx_ap):
            import bass_rust as _br
            return h.indirect_dma_start(
                out=out_ap, out_offset=None, in_=in_ap,
                in_offset=_br.IndirectOffsetOnAxis(ap=idx_ap, axis=0),
                bounds_check=oobmax, oob_is_err=False if oobmax is not None else True)
        return f

    with contextlib.ExitStack() as octx:
        sbo = lambda *a: octx.enter_context(nc.sbuf_tensor(*a))
        PA = octx.enter_context(nc.psum_tensor("PA", [128, 2048], F32))
        PB = octx.enter_context(nc.psum_tensor("PB", [128, 2048], F32))
        idxa = sbo("idxa", [128, 64], U32)
        idxh = sbo("idxh", [128, 16], U32)
        idxs = sbo("idxs", [128, 32], U32)
        idxb = sbo("idxb", [16, 16], U32)
        idxy = sbo("idxy", [128, 16], U32)
        for nm, dst, srcd in [("idxa", idxa, idxad), ("idxh", idxh, idxhd),
                              ("idxs", idxs, idxsd), ("idxb", idxb, idxbd),
                              ("idxy", idxy, idxyd)]:
            A("gpsimd", lambda h, dst=dst, srcd=srcd: h.dma_start(out=dst[:], in_=srcd),
              writes=[nm], dma=True, tag="ld" + nm)

        with contextlib.ExitStack() as ctx:
            sb = lambda *a: ctx.enter_context(nc.sbuf_tensor(*a))
            xt = [sb(f"p1xt{i}", [128, TLOC], F32) for i in range(ND)]
            xout = sb("p1xout", [128, ND * TLOC], BF16)
            work = mk_ln_work(nc, ctx, TLOC, pref="p1", psum=(PA, PB))
            A("sync", lambda h: h.dma_start(out=work["onesc"][:], in_=onesd),
              writes=["onesc"], dma=True, tag="p1ldones")
            for i in range(ND):
                A("sync", lambda h, i=i: h.dma_start(out=xt[i][:], in_=x0Td[ts(i, 128), :]),
                  writes=[f"x{i}"], dma=True, tag=f"p1ld{i}")
            phase_ln_feature_major(S, nc, xt, xout, work, TLOC, key="")
            A("sync", lambda h: h.dma_start(
                out=ag_xn_in.ap().rearrange("(dt p) t -> p dt t", p=128),
                in_=xout[:].rearrange("p (dt t) -> p dt t", t=TLOC)),
              reads=["xout"], writes=["xnind"], dma=True, tag="p1st")
            S.barrier()
        A("gpsimd", lambda h: h.collective_compute(
            "AllGather", ALU.bypass, replica_groups=RG,
            ins=[ag_xn_in.ap().opt()], outs=[ag_xn_out.ap().opt()]),
          reads=["xnind"], writes=["xnoutd"], cc=True, tag="CC_xn")

        NP = 8
        CH = 1024
        NCH = L // CH
        NMT = 64
        NRB = 4
        xn_view = ag_xn_out.ap().rearrange("r (h c) -> (r h) c", c=CH)
        with contextlib.ExitStack() as ctx:
            sb = lambda *a: ctx.enter_context(nc.sbuf_tensor(*a))
            xchunk = sb("xchunk", [128, ND * CH], BF16)
            wq = sb("wq", [128, ND * 128], BF16)
            wk = sb("wk", [128, ND * 128], BF16)
            wv = sb("wv", [128, ND * 128], BF16)
            biasq = sb("biasq", [128, 24], F32)
            bias_o = sb("bias_o", [128, 8], F32)
            fq = sb("fq", [128, 8], F32)
            onesb = sb("onesb", [128, 128], BF16)
            idb = sb("idb", [128, 128], BF16)
            pihc = sb("pihc", [128, 1], F32)
            kpre = sb("kpre", [128, CH], BF16)
            kswp = sb("kswp", [128, CH], BF16)
            qpre = sb("qpre", [128, CH], BF16)
            qswp = sb("qswp", [128, CH], BF16)
            vpre = sb("vpre", [128, CH], BF16)
            ka = sb("ka", [65, L], BF16)
            kb = sb("kb", [65, L], BF16)
            qa = sb("qa", [65, TLOC], BF16)
            qb = sb("qb", [65, TLOC], BF16)
            V = sb("V", [128, NMT * 130], BF16)
            th = sb("th", [128, CH], F32)
            kfb = sb("kfb", [128, CH], F32)
            kib = sb("kib", [128, CH], I32)
            rr = sb("rr", [128, CH], F32)
            cosr = sb("cosr", [128, CH], BF16)
            sinr = sb("sinr", [128, CH], BF16)
            tmpa = sb("tmpa", [128, CH], BF16)
            tmpb = sb("tmpb", [128, CH], BF16)
            ksqs = sb("ksqs", [64, CH], BF16)
            sp512 = sb("sp512", [1, 512], F32)
            kmaxp = sb("kmaxp", [1, 64], F32)
            kms = sb("kms", [1, 2], F32)
            q2s = sb("q2s", [1, TLOC], BF16)
            qnmA = sb("qnmA", [1, TLOC], BF16)
            qnmB = sb("qnmB", [1, TLOC], BF16)
            P0 = sb("P0", [128, 1024], BF16)
            P1v = sb("P1v", [128, 1024], BF16)
            sr = sb("sr", [1, 512], F32)
            rsf = sb("rsf", [1, 512], F32)
            rs = sb("rs", [1, 512], BF16)
            rsb = sb("rsb", [64, 512], BF16)
            aoT = sb("aoT", [128, NP * TLOC], BF16)
            wo = sb("wo", [128, 64 * 128], BF16)
            ostg = [sb(f"ostg{j}", [128, 512], BF16) for j in range(2)]
            emit_attn_body(S, nc, locals())
            S.barrier()

        with contextlib.ExitStack() as ctx:
            sb = lambda *a: ctx.enter_context(nc.sbuf_tensor(*a))
            emit_ffn0_body(S, nc, ctx, dict(
                PA=PA, PB=PB,
                xnT_src=ag_xn_in.ap(), xn_srckey="xnind",
                x0Td=x0Td, at0d=at0_i, at0key="at0d",
                gwd=gwd, uwd=uwd, dwd=dwd, sgd=sgd, sud=sud, sdd=sdd,
                rwd=rw0d, rbd=rb0d, ohd=ohd32, onesbf_d=onesbf_d, idbf_d=idbf_d,
                x1Td=x1_i, x1key="x1d"))
            S.barrier()

        with contextlib.ExitStack() as ctx:
            emit_front_a_body(S, nc, ctx, dict(
                PA=PA, PB=PB,
                x1Td=x1_i, x1key="x1d", wind=wind, lngd=lngd, lnbd=lnbd,
                onesd=onesd, onesbd=onesbf_d,
                xn2d=xn2_i, xhd=xh_i, szd=sz_i, tail=ag_tl_in.ap()))
            S.barrier()
        A("gpsimd", lambda h: h.collective_compute(
            "AllGather", ALU.bypass, replica_groups=RG,
            ins=[ag_tl_in.ap().opt()], outs=[ag_tl_out.ap().opt()]),
          reads=["tld"], writes=["tloutd"], cc=True, tag="CC_tl")

        with contextlib.ExitStack() as ctx:
            emit_front_b_body(S, nc, ctx, dict(
                PA=PA, PB=PB, gather=gather,
                xhd=xh_i, xhkey="xhd", tl_out=ag_tl_out.ap(), idxh=idxh,
                cwd=cwd, cbd=cbd, wxd=wxd, wdtd=wdtd, bdtd=bdtd,
                xcd=ag_xc_in.ap(), dtd=ag_dt_in.ap(), bcd=ag_bc_in.ap()))
            S.barrier()
        for nm, ain, aout, rk, wk_ in [("xc", ag_xc_in, ag_xc_out, "xcd", "xcoutd"),
                                       ("dtq", ag_dt_in, ag_dt_out, "dtd", "dtoutd"),
                                       ("bc", ag_bc_in, ag_bc_out, "bcd", "bcoutd")]:
            A("gpsimd", lambda h, ain=ain, aout=aout: h.collective_compute(
                "AllGather", ALU.bypass, replica_groups=RG,
                ins=[ain.ap().opt()], outs=[aout.ap().opt()]),
              reads=[rk], writes=[wk_], cc=True, tag="CC_" + nm)

        with contextlib.ExitStack() as ctx:
            emit_scan_body(S, nc, ctx, dict(
                PA=PA, PB=PB, gather=gather,
                xc_out=ag_xc_out.ap().rearrange("r (h c) -> (r h) c", c=1024),
                dt_out=ag_dt_out.ap().rearrange("r (h c) -> (r h) c", c=1024),
                bc_out=ag_bc_out.ap().rearrange("r (h c) -> (r h) c", c=1024),
                idxs=idxs, idxb=idxb, aqd=aqd, oh16d=oh16d,
                yqd=ag_y_in.ap()))
            S.barrier()
        A("gpsimd", lambda h: h.collective_compute(
            "AllGather", ALU.bypass, replica_groups=RG,
            ins=[ag_y_in.ap().opt()], outs=[ag_y_out.ap().opt()]),
          reads=["yqd"], writes=["youtd"], cc=True, tag="CC_y")

        with contextlib.ExitStack() as ctx:
            emit_back_body(S, nc, ctx, dict(
                PA=PA, PB=PB, gather=gather,
                x1Td=x1_i, xn2d=xn2_i, y_out=ag_y_out.ap().rearrange("r (h c) -> (r h) c", c=TLOC),
                idxy=idxy, xcd=ag_xc_in.ap(), szd=sz_i,
                dcold=dcold, wod=wo1d, sw1d=sw1d, sb1d=sb1d, sw2d=sw2d, sb2d=sb2d,
                rwd=rw1d, rbd=rb1d, w1d=w1d, b1d=b1d, w2d=w2d, b2d=b2d,
                oh8d=oh8d, onesbd=onesbf_d, idbd=idbf_d, outd=outd))
        S.run_block(nc, octx)
    return nc


def fused_core_inputs(p, c):
    b_, s = divmod(c, 4)
    q = s
    d = {}
    d["x0T_loc"] = np.ascontiguousarray(p["x0T"][b_, :, ts(s, TLOC)])
    d["posrep"] = np.ascontiguousarray(
        np.broadcast_to(np.roll(p["pos"], -s * TLOC), (128, L)))
    for k_ in ["w_qkv", "b_qkvT", "w_out", "b_outT", "freqcols2", "ones128", "onesbf",
               "idbf", "gate_wT", "up_wT", "down_wT", "sgate_wT", "sup_wT", "sdown_wT",
               "router_wT", "router_b", "onehot32", "onehot16", "onehot8", "W_in",
               "ln_gT", "ln_bT", "conv_wT", "conv_bT", "W_x", "W_dt", "b_dtT",
               "DcolT", "W_out1", "shared_w1T", "shared_b1T", "shared_w2T", "shared_b2T",
               "router1_wT", "router1_b", "spec_w1T", "spec_b1T", "spec_w2T", "spec_b2h"]:
        d[k_] = p[k_]
    dsl = slice(q * 512, (q + 1) * 512)
    d["Aq"] = np.ascontiguousarray(
        p["A_log"][dsl].reshape(4, 128, 16).transpose(1, 0, 2).reshape(128, 64))
    pvec = np.arange(128, dtype=np.uint64)
    ia = np.zeros((128, 64), dtype=np.uint32)
    for c_ in range(8):
        o = (c_ + 2 * s) % 8
        for dt in range(8):
            ia[:, c_ * 8 + dt] = (((4 * b_ + o // 2) * 1024 + dt * 128 + pvec) * 2
                                  + (o % 2)).astype(np.uint32)
    d["idx_attn"] = ia
    ih = np.zeros((128, 16), dtype=np.uint32)
    for dq in range(16):
        if s == 0:
            ih[:, dq] = 0x7FFFFFFF
        else:
            ih[:, dq] = ((4 * b_ + s - 1) * 2048 + dq * 128 + pvec).astype(np.uint32)
    d["idx_halo"] = ih
    isc = np.zeros((128, 32), dtype=np.uint32)
    for tb in range(8):
        for dqi in range(4):
            src_s = tb // 2
            isc[:, tb * 4 + dqi] = (((4 * b_ + src_s) * 2048 + q * 512 + dqi * 128 + pvec) * 2
                                    + (tb % 2)).astype(np.uint32)
    d["idx_scan"] = isc
    ib = np.zeros((16, 16), dtype=np.uint32)
    r16 = np.arange(16, dtype=np.uint64)
    for tb in range(8):
        src_s = tb // 2
        ib[:, tb] = (((4 * b_ + src_s) * 32 + r16) * 2 + (tb % 2)).astype(np.uint32)
        ib[:, 8 + tb] = (((4 * b_ + src_s) * 32 + 16 + r16) * 2 + (tb % 2)).astype(np.uint32)
    d["idx_bc"] = ib
    iy = np.zeros((128, 16), dtype=np.uint32)
    for dq in range(16):
        qq, dq8 = divmod(dq, 4)
        iy[:, dq] = (((4 * b_ + qq) * 512 + dq8 * 128 + pvec) * 4 + s).astype(np.uint32)
    d["idx_y"] = iy
    return d


def kernel(**inputs):
    p = prep_common(inputs)
    p["A_log"] = inputs["L1_attn_A_log"]
    nc = build_fused()
    ims = [fused_core_inputs(p, c) for c in range(NCORES)]
    res = run_bass_kernel_spmd(nc, ims, core_ids=list(range(NCORES)))
    out = np.empty((B, L, D), dtype=np.float32)
    for c in range(NCORES):
        b_, s = divmod(c, 4)
        out[b_, ts(s, TLOC), :] = res.results[c]["outT"].T
    return out
